# version 03
# baseline (speedup 1.0000x reference)
import contextlib
import math

import numpy as np
import ml_dtypes

import concourse.bass as bass
import concourse.mybir as mybir
from concourse.alu_op_type import AluOpType
from concourse import library_config as lc
from concourse.bass_utils import run_bass_kernel_spmd
from concourse.library_overlay import lower_extended_insts

F32 = mybir.dt.float32
BF16 = mybir.dt.bfloat16
I16 = mybir.dt.int16
U32 = mybir.dt.uint32
AF = mybir.ActivationFunctionType
OP = AluOpType
AX = mybir.AxisListType.X

B, T, C = 4, 2048, 2048
NCORE = 8
TOK = 1024
HH, HD = 128, 256
E, CAP = 16, 1024
IEXP, ISH = 1408, 2816
EPC = 2
LN_EPS = 1e-5
RMS_EPS = 1.1920929e-07
LAMBDA_INIT = 0.2
ISQ = 1.0 / math.sqrt(128.0)
TTOT = B * T
PADROW = TTOT

PAIRS = [[0, 1], [2, 3], [4, 5], [6, 7]]
ALL8 = [list(range(NCORE))]

ENGINES = ("tensor", "vector", "scalar", "gpsimd", "sync")
CHAIN = {"tensor": "PE", "vector": "DV", "scalar": "AC", "gpsimd": "PL", "sync": "SY"}
QSEM = {"sync": "QS", "scalar": "QA", "gpsimd": "QG"}


class Rec:

    def __init__(self, nc, ctx):
        self.nc = nc
        self.ctx = ctx
        self.sems = {}
        for name in ["PE", "DV", "AC", "PL", "SY", "CO"]:
            self.sems[name] = [ctx.enter_context(nc.semaphore(name)), 0]
        self.ops = {e: [] for e in ENGINES}
        self.hw = {e: {} for e in ENGINES}
        self.drained = {e: 0 for e in ENGINES}
        self.block_start = {}

    def stream(self, name):
        if name not in self.sems:
            self.sems[name] = [self.ctx.enter_context(self.nc.semaphore(name)), 0]
        return self.sems[name]

    def now(self, kind):
        return (kind, self.sems[kind][1])

    def _after(self, eng, after):
        waits, need_drain = [], False
        for ev in after:
            if ev is None:
                continue
            k, v = ev
            if v <= 0:
                continue
            if k == CHAIN[eng]:
                if v > self.drained[eng]:
                    need_drain = True
            else:
                waits.append((k, v))
        if need_drain:
            self.drained[eng] = self.sems[CHAIN[eng]][1]
        wmax = {}
        for (k, v) in waits:
            wmax[k] = max(wmax.get(k, 0), v)
        return sorted(wmax.items()), need_drain

    def op(self, eng, fn, after=(), sem=None, n=1):
        kind = sem or CHAIN[eng]
        waits, need_drain = self._after(eng, after)
        self.sems[kind][1] += n
        val = self.sems[kind][1]
        self.ops[eng].append((fn, waits, need_drain, kind, n))
        return (kind, val)

    def dma(self, eng, stream, out, in_, after=(), **kw):
        self.stream(stream)
        return self.op(eng, lambda e: e.dma_start(out=out, in_=in_, **kw),
                       after=after, sem=stream, n=16)

    def raw(self, eng, fn, after=()):
        waits, need_drain = self._after(eng, after)
        self.ops[eng].append((fn, waits, need_drain, None, 0))

    def bump(self, kind, n):
        self.sems[kind][1] += n
        return (kind, self.sems[kind][1])

    def emit(self):
        nc = self.nc
        if not any(self.ops[e] for e in ENGINES):
            return
        fence = [(k, v) for k, v in self.block_start.items() if v > 0]
        with nc.Block() as blk:
            for eng_name in ENGINES:
                ops = self.ops[eng_name]
                if not ops:
                    continue

                def make_body(eng_name, ops):
                    def body(eng):
                        hw = self.hw[eng_name]
                        for (k, v) in fence:
                            if hw.get(k, 0) < v:
                                eng.wait_ge(self.sems[k][0], v)
                                hw[k] = v
                        for (fn, waits, need_drain, kind, n) in ops:
                            for (k, v) in waits:
                                if hw.get(k, 0) < v:
                                    eng.wait_ge(self.sems[k][0], v)
                                    hw[k] = v
                            if need_drain:
                                eng.drain()
                            inst = fn(eng)
                            if kind is not None and inst is not None:
                                inst.then_inc(self.sems[kind][0], n)
                    return body

                getattr(blk, eng_name)(make_body(eng_name, ops))
        self.ops = {e: [] for e in ENGINES}
        self.block_start = {k: s[1] for k, s in self.sems.items()
                            if k not in ("PE", "DV", "AC", "PL", "SY")}


def _consts():
    ku = np.arange(896)[None, :] >= (np.arange(128)[:, None] + 384)
    stair = np.asarray(ku.astype(ml_dtypes.bfloat16))
    idnb = np.asarray(np.eye(128, dtype=ml_dtypes.bfloat16))
    idn32 = np.eye(128, dtype=np.float32)
    onesb = np.asarray(np.ones((128, 1), dtype=ml_dtypes.bfloat16))
    tok128 = (np.arange(128)[:, None] // 16 * 1024 + np.arange(TOK)[None, :] + 1).astype(np.float32)
    pp = np.arange(128)
    csum = ((pp[:, None] % 16 == pp[None, :] % 16) & (pp[:, None] // 16 < pp[None, :] // 16)).astype(np.float32)
    idqs = []
    for b4 in range(4):
        w = np.zeros((128, 32), np.int16)
        for i in range(512):
            for g in range(8):
                w[(i % 16) + 16 * g, i // 16] = 512 * b4 + i
        idqs.append(w)
    return stair, idnb, idn32, onesb, tok128, csum, idqs


def _ln_stats(rec, stat, src_ap, sq_scratch, e_src, prev, eps_ap=None):
    e_s1 = rec.op("scalar", lambda s: s.activation(sq_scratch[:, :], src_ap, AF.Square, accum_out=stat[:, 1:2]),
                  after=[e_src, prev])
    e_s2 = rec.op("vector", lambda v: v.tensor_reduce(stat[:, 0:1], src_ap, axis=AX, op=OP.add),
                  after=[e_src, prev])
    e_m = rec.op("vector", lambda v: v.tensor_scalar(stat[:, 2:3], stat[:, 0:1], 1.0 / C, None, OP.mult), after=[e_s2])
    e_v1 = rec.op("vector", lambda v: v.tensor_tensor(stat[:, 3:4], stat[:, 2:3], stat[:, 2:3], OP.mult), after=[e_m])
    e_v2 = rec.op("vector", lambda v: v.scalar_tensor_tensor(stat[:, 4:5], stat[:, 1:2], 1.0 / C, stat[:, 3:4],
                                                             OP.mult, OP.subtract), after=[e_v1, e_s1])
    e_sd = rec.op("scalar", lambda s: s.activation(stat[:, 5:6], stat[:, 4:5], AF.Sqrt, bias=eps_ap), after=[e_v2])
    e_ri = rec.op("vector", lambda v: v.reciprocal(stat[:, 6:7], stat[:, 5:6]), after=[e_sd])
    return e_ri


def build_nc():
    nc = bass.Bass("TRN2", target_bir_lowering=False)
    ctx = contextlib.ExitStack()
    dram = lambda name, shape, dt, kind="Internal": nc.dram_tensor(name, shape, dt, kind=kind).ap()

    xb = dram("xb", [T, C], F32, "ExternalInput")
    x_own = dram("x_own", [TOK, C], F32, "ExternalInput")
    wq_t = dram("wq_t", [2, 128, 16, 512], F32, "ExternalInput")
    wk_t = dram("wk_t", [2, 128, 16, 512], F32, "ExternalInput")
    wv_t = dram("wv_t", [2, 128, 16, 512], F32, "ExternalInput")
    wo_t = dram("wo_t", [4, 128, 16, 512], F32, "ExternalInput")
    lqk = dram("lqk", [4, HD], F32, "ExternalInput")
    rotpq = dram("rotpq", [128, 8], F32, "ExternalInput")
    gate_w = dram("gate_w", [E, C], F32, "ExternalInput")
    weg = dram("weg", [EPC, 11, 128, 16, 128], F32, "ExternalInput")
    weu = dram("weu", [EPC, 11, 128, 16, 128], F32, "ExternalInput")
    wed = dram("wed", [EPC, 11, 4, 128, 512], F32, "ExternalInput")
    swg = dram("swg", [22, 128, 16, 128], F32, "ExternalInput")
    swu = dram("swu", [22, 128, 16, 128], F32, "ExternalInput")
    swd = dram("swd", [22, 4, 128, 512], F32, "ExternalInput")
    out_shard = dram("out_shard", [TOK, C], F32, "ExternalOutput")

    stair_h, idnb_h, idn32_h, onesb_h, tok128_h, csum_h, idqs_h = _consts()
    cSTAIR = nc.inline_tensor(stair_h, name="cSTAIR").ap()
    cIDNB = nc.inline_tensor(idnb_h, name="cIDNB").ap()
    cIDN32 = nc.inline_tensor(idn32_h, name="cIDN32").ap()
    cONESB = nc.inline_tensor(onesb_h, name="cONESB").ap()
    cTOK128 = nc.inline_tensor(tok128_h, name="cTOK128").ap()
    cIDQ = [nc.inline_tensor(idqs_h[b4], name=f"cIDQ{b4}").ap() for b4 in range(4)]
    cCSUM = nc.inline_tensor(csum_h, name="cCSUM").ap()

    x1s = dram("x1s", [T, C], BF16)
    wq_b = dram("wq_b", [2, 128, 16, 512], BF16)
    wk_b = dram("wk_b", [2, 128, 16, 512], BF16)
    wv_b = dram("wv_b", [2, 128, 16, 512], BF16)
    wo_b = dram("wo_b", [4, 128, 16, 512], BF16)
    weg_b = dram("weg_b", [EPC, 11, 128, 16, 128], BF16)
    weu_b = dram("weu_b", [EPC, 11, 128, 16, 128], BF16)
    wed_b = dram("wed_b", [EPC, 11, 4, 128, 512], BF16)
    swg_b = dram("swg_b", [22, 128, 16, 128], BF16)
    swu_b = dram("swu_b", [22, 128, 16, 128], BF16)
    swd_b = dram("swd_b", [22, 4, 128, 512], BF16)
    x2s = dram("x2s", [TOK, C], F32)
    yhalf = dram("yhalf", [128, 8, 1024], BF16)
    ypair = dram("ypair", [2, 128, 8, 1024], BF16)
    x3mine = dram("x3mine", [TOK, C], BF16)
    x3all = dram("x3all", [TTOT + 1, C], BF16)
    wm_mine = dram("wm_mine", [E, TOK], F32)
    wm_all = dram("wm_all", [NCORE, E, TOK], F32)
    yp = dram("yp", [TTOT + 1, C], BF16)
    y_rs = dram("y_rs", [TOK, C], BF16)
    strm = dram("strm", [2, 9216], F32)
    wbb = dram("wbb", [EPC, CAP], F32)

    sbt = lambda name, shape, dt: ctx.enter_context(nc.sbuf_tensor(name, shape, dt))
    ROTS = sbt("ROTS", [128, 8], F32)
    IDQS = [sbt(f"IDQS{b4}", [128, 32], I16) for b4 in range(4)]
    STAIR = sbt("STAIR", [128, 896], BF16)
    IDNB = sbt("IDNB", [128, 128], BF16)
    IDN32 = sbt("IDN32", [128, 128], F32)
    ONESB = sbt("ONESB", [128, 1], BF16)
    ZEROT = sbt("ZEROT", [128, 4096], BF16)
    LAM = sbt("LAM", [1, 4], F32)
    GWT = sbt("GWT", [128, 16, 16], BF16)
    EPS = sbt("EPS", [128, 2], F32)

    rec = Rec(nc, ctx)
    EV0 = ("PE", 0)

    rank_reg = [None]

    def init_g(g):
        g.load_library(lc.mlp)
        rank_reg[0] = g.to_reg(g.partition_id())
    rec.raw("gpsimd", init_g)

    e_wq = rec.dma("gpsimd", "c_wq", wq_b[:], wq_t[:])
    e_wk = rec.dma("gpsimd", "c_wk", wk_b[:], wk_t[:])
    e_wv = rec.dma("gpsimd", "c_wv", wv_b[:], wv_t[:])
    e_wo = rec.dma("gpsimd", "c_wo", wo_b[:], wo_t[:])
    e_weg = rec.dma("gpsimd", "c_weg", weg_b[:], weg[:])
    e_weu = rec.dma("gpsimd", "c_weu", weu_b[:], weu[:])
    e_wed = rec.dma("gpsimd", "c_wed", wed_b[:], wed[:])
    e_swg = rec.dma("gpsimd", "c_swg", swg_b[:], swg[:])
    e_swu = rec.dma("gpsimd", "c_swu", swu_b[:], swu[:])
    e_swd = rec.dma("gpsimd", "c_swd", swd_b[:], swd[:])

    e_c = {}
    for nm, dst, src in (("st", STAIR, cSTAIR), ("ib", IDNB, cIDNB),
                         ("i32", IDN32, cIDN32), ("ob", ONESB, cONESB)):
        e_c[nm] = rec.dma("sync", "cst_" + nm, dst[:], src[:])
    e_c["rot"] = rec.dma("sync", "cst_rot", ROTS[:], rotpq[:])
    for b4 in range(4):
        e_c[f"idq{b4}"] = rec.dma("sync", f"cst_idq{b4}", IDQS[b4][:], cIDQ[b4][:])
    e_zt = rec.op("vector", lambda v: v.memset(ZEROT[:], 0.0))
    rec.op("vector", lambda v: v.memset(EPS[:, 0:1], LN_EPS))
    rec.op("vector", lambda v: v.memset(EPS[:, 1:2], RMS_EPS))

    ZELEM = 128 * 4096
    flat_yp = yp.rearrange("r c -> (r c)")
    nzp = ((TTOT + 1) * C) // ZELEM
    e_zero = []
    for z in range(nzp):
        seg = flat_yp[z * ZELEM:(z + 1) * ZELEM].rearrange("(p f) -> p f", p=128)
        e_zero.append(rec.dma("sync", "zero", seg, ZEROT[:], after=[e_zt]))
    rem = (TTOT + 1) * C - nzp * ZELEM
    if rem:
        seg = flat_yp[nzp * ZELEM:].rearrange("(p f) -> p f", p=128)
        e_zero.append(rec.dma("sync", "zero", seg, ZEROT[:, 0:rem // 128], after=[e_zt]))
    e_zero.append(rec.dma("sync", "zero", x3all[PADROW:PADROW + 1, :], ZEROT[0:1, 0:C], after=[e_zt]))

    lvs = sbt("lvs", [1, 4 * HD], F32)
    lpr = sbt("lpr", [1, 2 * HD], F32)
    lred = sbt("lred", [1, 4], F32)
    e_lv = rec.dma("sync", "lqk", lvs[:], lqk.rearrange("a b -> (a b)").rearrange("(o f) -> o f", o=1))
    e_p1 = rec.op("vector", lambda v: v.tensor_tensor(lpr[:, 0:HD], lvs[:, 0:HD], lvs[:, HD:2 * HD], OP.mult), after=[e_lv])
    e_p2 = rec.op("vector", lambda v: v.tensor_tensor(lpr[:, HD:2 * HD], lvs[:, 2 * HD:3 * HD], lvs[:, 3 * HD:4 * HD], OP.mult), after=[e_lv, e_p1])
    e_r1 = rec.op("vector", lambda v: v.tensor_reduce(lred[:, 0:1], lpr[:, 0:HD], axis=AX, op=OP.add), after=[e_p2])
    e_r2 = rec.op("vector", lambda v: v.tensor_reduce(lred[:, 1:2], lpr[:, HD:2 * HD], axis=AX, op=OP.add), after=[e_r1])
    e_ex = rec.op("scalar", lambda s: s.activation(lred[:, 0:2], lred[:, 0:2], AF.Exp), after=[e_r2])
    e_l0 = rec.op("vector", lambda v: v.tensor_tensor(LAM[0:1, 0:1], lred[:, 0:1], lred[:, 1:2], OP.subtract), after=[e_ex])
    e_l1 = rec.op("vector", lambda v: v.tensor_scalar(LAM[0:1, 0:1], LAM[0:1, 0:1], LAMBDA_INIT, None, OP.add), after=[e_l0])
    e_lam = rec.op("vector", lambda v: v.tensor_scalar(LAM[0:1, 1:2], LAM[0:1, 0:1], -1.0, None, OP.mult), after=[e_l1])

    gwsb = sbt("gwsb", [16, C], BF16)
    e_gw = rec.dma("gpsimd", "gwl", gwsb[:], gate_w[:])
    with nc.psum_tensor("pgt", [128, 16], BF16) as pgt:
        ev_prev = EV0
        for k in range(16):
            e_t = rec.op("tensor", lambda t, k=k: t.transpose(pgt[:, :], gwsb[:, k * 128:(k + 1) * 128], IDNB[0:16, 0:16]),
                         after=[e_gw, e_c["ib"], ev_prev])
            ev_prev = rec.op("vector", lambda v, k=k: v.tensor_copy(GWT[:, k, :], pgt[:, :]), after=[e_t])
        e_gwt = ev_prev
        rec.emit()

    with (
        nc.sbuf_tensor("xt", [128, 2, C], F32) as xt,
        nc.sbuf_tensor("x1t", [128, 2, C], BF16) as x1t,
        nc.sbuf_tensor("sqd", [128, C], BF16) as sqd,
        nc.sbuf_tensor("st8", [128, 8], F32) as st8,
    ):
        prev_use = [EV0, EV0]
        prev_stat = EV0
        last_w = EV0
        for i in range(16):
            sl = i % 2
            e_x = rec.dma("sync", f"xls{sl}", xt[:, sl, :], xb[i * 128:(i + 1) * 128, :], after=[prev_use[sl]])
            e_ri = _ln_stats(rec, st8, xt[:, sl, :], sqd, e_x, prev_stat, EPS[:, 0:1])
            e_ap = rec.op("vector", lambda v, sl=sl: v.tensor_scalar(x1t[:, sl, :], xt[:, sl, :], st8[:, 2:3], st8[:, 6:7],
                                                                     OP.subtract, OP.mult), after=[e_ri])
            prev_stat = e_ap
            e_w = rec.dma("scalar", f"x1w{sl}", x1s[i * 128:(i + 1) * 128, :], x1t[:, sl, :], after=[e_ap])
            prev_use[sl] = e_w
            last_w = e_w
        e_x1_done = [prev_use[0], prev_use[1]]
        rec.emit()

    with nc.sbuf_tensor("YT", [128, 8, T], BF16) as YT:
        for p in range(2):
            with (
                nc.sbuf_tensor(f"QT{p}", [128, 4, T], BF16) as QT,
                nc.sbuf_tensor(f"KT{p}", [128, 4, T], BF16) as KT,
                nc.sbuf_tensor(f"VS{p}", [128, 16, 512], BF16) as VS,
                nc.sbuf_tensor(f"WTS{p}", [128, 16, 512], BF16) as WTS,
            ):
                with nc.sbuf_tensor(f"X1T{p}", [128, 4, 16, 512], BF16) as X1T:
                    rec.stream("x1T")
                    e_x1T_l = []
                    prevg = e_x1_done
                    for g4 in range(4):
                        ev = rec.op("gpsimd", lambda g, g4=g4: g.dma_gather(
                            X1T[:, g4, :, :], x1s[:], IDQS[g4][:], num_idxs=512, num_idxs_reg=512,
                            elem_size=C, transpose=True),
                            after=list(prevg) + [e_c[f"idq{g4}"]] if g4 == 0 else [prevg, e_c[f"idq{g4}"]],
                            sem="x1T", n=16)
                        e_x1T_l.append(ev)
                        prevg = ev
                    e_x1T = [e_x1T_l[-1]] * 16
                    e_wts_1 = rec.dma("scalar", "wts", WTS[:, :, :], wv_b[p], after=[e_wv])
                    e_wts = [e_wts_1] * 16
                    e_vs = {}
                    with nc.psum_tensor(f"pv{p}", [128, 8, 512], F32) as pv:
                        for tb in range(16):
                            j = tb % 8
                            last = None
                            for cc in range(16):
                                aft = [e_x1T[cc], e_wts[cc]]
                                if cc == 0 and tb >= 8:
                                    aft.append(e_vs[tb - 8])
                                last = rec.op("tensor", lambda t, j=j, tb=tb, cc=cc: t.matmul(
                                    pv[:, j, :], X1T[:, tb // 4, cc, (tb % 4) * 128:(tb % 4 + 1) * 128], WTS[:, cc, :],
                                    start=(cc == 0), stop=(cc == 15)), after=aft)
                            e_vs[tb] = rec.op("vector", lambda v, j=j, tb=tb: v.tensor_copy(VS[:, tb, :], pv[:, j, :]),
                                              after=[last])
                    e_vs_all = [e_vs[tb] for tb in range(16)]
                    e_vs_last = rec.now("DV")

                    def qk_proj(dst, w_b, e_w_ready, e_free, nm, init_ev):
                        e_w_1 = rec.dma("scalar", "wts", WTS[:, :, :], w_b[p], after=[e_w_ready, e_free])
                        e_w = [e_w_1] * 16
                        evs = []
                        with nc.psum_tensor(nm, [128, 2, 512], F32) as pqk:
                            prev = [init_ev, init_ev]
                            for cc in range(4):
                                for tc in range(4):
                                    sl = (cc * 4 + tc) % 2
                                    last = None
                                    for k in range(16):
                                        aft = [e_w[k], e_x1T[k]]
                                        if k == 0:
                                            aft.append(prev[sl])
                                        last = rec.op("tensor", lambda t, cc=cc, tc=tc, k=k, sl=sl: t.matmul(
                                            pqk[:, sl, :], WTS[:, k, cc * 128:(cc + 1) * 128],
                                            X1T[:, tc, k, :],
                                            start=(k == 0), stop=(k == 15)), after=aft)
                                    prev[sl] = rec.op("vector", lambda v, cc=cc, tc=tc, sl=sl: v.tensor_copy(
                                        dst[:, cc, tc * 512:(tc + 1) * 512], pqk[:, sl, :]), after=[last])
                                    evs.append(prev[sl])
                        return evs

                    e_q_evs = qk_proj(QT, wq_b, e_wq, rec.now("PE"), f"pqkq{p}", e_vs[15])
                    e_k_evs = qk_proj(KT, wk_b, e_wk, rec.now("PE"), f"pqkk{p}", e_q_evs[-1])
                rec.emit()

                rope_ev = {}
                with (
                    nc.sbuf_tensor(f"sqr{p}", [128, T], BF16) as sqr,
                    nc.sbuf_tensor(f"rr{p}", [1, 2 * T], F32) as rr,
                    nc.sbuf_tensor(f"rrb{p}", [1, T], BF16) as rrb,
                    nc.sbuf_tensor(f"rb{p}", [128, T], BF16) as rb,
                    nc.sbuf_tensor(f"TMPA{p}", [128, T], BF16) as TMPA,
                    nc.sbuf_tensor(f"TMPB{p}", [128, T], BF16) as TMPB,
                    nc.psum_tensor(f"pr{p}", [1, 4, 512], F32) as pr,
                ):
                    prev_rope = e_k_evs[-1]
                    prev_rb_rd = EV0
                    for mi, (src, src_evs) in enumerate(((QT, e_q_evs), (KT, e_k_evs))):
                        for ch in range(4):
                            tile = src[:, ch, :]
                            lh = ch // 2
                            e_sq = rec.op("scalar", lambda s, tile=tile: s.activation(sqr[:, :], tile, AF.Square),
                                          after=src_evs + [prev_rope])
                            e_ms = []
                            for q4 in range(4):
                                e_ms.append(rec.op("tensor", lambda t, q4=q4: t.matmul(
                                    pr[:, q4, :], ONESB[:, :], sqr[:, q4 * 512:(q4 + 1) * 512],
                                    start=True, stop=True), after=[e_sq, e_c["ob"], prev_rope]))
                            e_rs = rec.op("scalar", lambda s: s.activation(rr[:, 0:T], pr[:, :, :], AF.Sqrt,
                                                                           scale=1.0 / 128.0, bias=EPS[0:1, 1:2]), after=e_ms)
                            e_rc = rec.op("vector", lambda v: v.reciprocal(rr[:, T:2 * T], rr[:, 0:T]), after=[e_rs])
                            e_rb = rec.op("vector", lambda v: v.tensor_copy(rrb[:, :], rr[:, T:2 * T]), after=[e_rc])
                            e_bc = rec.op("gpsimd", lambda g: g.partition_broadcast(rb[:, :], rrb[:, :], channels=128),
                                          after=[e_rb, prev_rb_rd])
                            e_sw1 = rec.dma("sync", "swp", TMPB[0:64, :], tile[64:128, :], after=src_evs + [prev_rope])
                            e_sw2 = rec.dma("sync", "swp", TMPB[64:128, :], tile[0:64, :], after=[e_sw1])
                            e_t1 = rec.op("vector", lambda v, tile=tile, lh=lh: v.tensor_scalar(
                                TMPA[:, :], tile, ROTS[:, lh:lh + 1], None, OP.mult),
                                after=[e_c["rot"], e_sw2, prev_rope])
                            e_t2 = rec.op("vector", lambda v, tile=tile, lh=lh: v.scalar_tensor_tensor(
                                tile, TMPB[:, :], ROTS[:, 4 + lh:5 + lh], TMPA[:, :], OP.mult, OP.add),
                                after=[e_t1, e_sw2])
                            e_nb = rec.op("vector", lambda v, tile=tile: v.tensor_tensor(tile, tile, rb[:, :], OP.mult),
                                          after=[e_t2, e_bc])
                            prev_rb_rd = e_nb
                            rope_ev[(mi, ch)] = e_nb
                            prev_rope = e_nb
                    e_rope_all = prev_rope
                    rec.emit()

                with (
                    nc.sbuf_tensor(f"PT{p}", [128, 2, 512], BF16) as PT,
                    nc.sbuf_tensor(f"zrow{p}", [1, 2, 512], F32) as zrow,
                    nc.sbuf_tensor(f"zbf{p}", [128, 2, 512], F32) as zbf,
                    nc.sbuf_tensor(f"csc{p}", [128, 2, 512], F32) as csc,
                    nc.psum_tensor(f"pO{p}", [128, 4, 512], F32) as pO,
                    nc.psum_tensor(f"pS{p}", [128, 2, 512], F32) as pS,
                    nc.psum_tensor(f"pZ{p}", [1, 2, 512], F32) as pZ,
                ):
                    prev_exp = [e_rope_all, e_rope_all]
                    prev_zread = e_rope_all
                    prev_zrow_rd = e_rope_all
                    prev_cmb = [e_rope_all] * 4
                    for lh in range(2):
                        for qc in range(4):
                            nkb = (qc + 1) * 4
                            e_zl = None
                            e_ol = {}
                            for kb in range(nkb):
                                for j in range(2):
                                    qk_ev = [rope_ev[(0, 2 * lh + j)], rope_ev[(1, 2 * lh + j)]]
                                    e_s = rec.op("tensor", lambda t, lh=lh, j=j, kb=kb, qc=qc: t.matmul(
                                        pS[:, j, :], KT[:, 2 * lh + j, kb * 128:(kb + 1) * 128],
                                        QT[:, 2 * lh + j, qc * 512:(qc + 1) * 512],
                                        start=True, stop=True),
                                        after=qk_ev + [prev_exp[j]])
                                    e_e = rec.op("scalar", lambda s, j=j: s.activation(PT[:, j, :], pS[:, j, :], AF.Exp, scale=ISQ),
                                                 after=[e_s])
                                    delta = kb * 128 - qc * 512
                                    if 0 <= delta <= 384:
                                        e_e = rec.op("vector", lambda v, j=j, delta=delta: v.tensor_tensor(
                                            PT[:, j, :], PT[:, j, :], STAIR[:, 384 - delta:896 - delta], OP.mult),
                                            after=[e_e, e_c["st"]])
                                    prev_exp[j] = e_e
                                    e_z = rec.op("tensor", lambda t, j=j, kb=kb, nkb=nkb: t.matmul(
                                        pZ[:, j, :], ONESB[:, :], PT[:, j, :],
                                        start=(kb == 0), stop=(kb == nkb - 1)),
                                        after=[e_e] + ([prev_zread] if kb == 0 else []))
                                    e_zl = e_z
                                    for d2 in range(2):
                                        aft = [e_e, e_vs_last]
                                        if kb == 0:
                                            aft.append(prev_cmb[2 * j + d2])
                                        e_ol[2 * j + d2] = rec.op("tensor", lambda t, lh=lh, j=j, d2=d2, kb=kb, nkb=nkb: t.matmul(
                                            pO[:, 2 * j + d2, :], VS[:, kb, 256 * lh + 128 * d2:256 * lh + 128 * (d2 + 1)],
                                            PT[:, j, :], start=(kb == 0), stop=(kb == nkb - 1)), after=aft)
                            e_zc = rec.op("vector", lambda v: v.tensor_copy(zrow[:, :, :], pZ[:, :, :]),
                                          after=[e_zl, prev_zrow_rd])
                            prev_zread = e_zc
                            e_zr = rec.op("vector", lambda v: v.reciprocal(zrow[:, :, :], zrow[:, :, :]), after=[e_zc])
                            e_zs = rec.op("vector", lambda v: v.tensor_scalar(zrow[:, 1, :], zrow[:, 1, :], LAM[0:1, 1:2], None, OP.mult),
                                          after=[e_zr, e_lam])
                            e_zb = rec.op("gpsimd", lambda g: g.partition_broadcast(zbf[:, :, :], zrow[:, :, :], channels=128),
                                          after=[e_zs])
                            prev_zrow_rd = e_zb
                            prev_y = e_zb
                            for d2 in range(2):
                                e_c1 = rec.op("vector", lambda v, d2=d2: v.tensor_tensor(csc[:, 0, :], pO[:, d2, :], zbf[:, 0, :], OP.mult),
                                              after=[e_zb, e_ol[d2], prev_cmb[d2], prev_y])
                                e_c2 = rec.op("vector", lambda v, d2=d2: v.tensor_tensor(csc[:, 1, :], pO[:, 2 + d2, :], zbf[:, 1, :], OP.mult),
                                              after=[e_c1, e_ol[2 + d2]])
                                ych = 4 * p + 2 * lh + d2
                                e_y = rec.op("vector", lambda v, ych=ych, qc=qc: v.tensor_tensor(
                                    YT[:, ych, qc * 512:(qc + 1) * 512], csc[:, 0, :], csc[:, 1, :], OP.add),
                                    after=[e_c2])
                                prev_cmb[d2] = e_y
                                prev_cmb[2 + d2] = e_y
                                prev_y = e_y
                    rec.emit()
        e_yt_done = rec.now("DV")

        rec.stream("yex")

        def emit_send(g):
            for c in range(NCORE):
                with g.If_eq(rank_reg[0], c):
                    off = 1024 if c % 2 == 0 else 0
                    g.dma_start(out=yhalf[:], in_=YT[:, :, off:off + 1024]).then_inc(rec.sems["yex"][0], 16)
                g.end_ifs()
        rec.raw("gpsimd", emit_send, after=[e_yt_done])
        e_send = rec.bump("yex", 16)

        e_ag_y = rec.op("gpsimd", lambda g: g.collective_compute(
            "AllGather", OP.bypass, ins=[yhalf[:]], outs=[ypair[:]], replica_groups=PAIRS),
            after=[e_send], sem="CO", n=1)

        e_fsh_dep = []
        with (
            nc.sbuf_tensor("YF", [128, 16, 1024], BF16) as YF,
            nc.sbuf_tensor("wt5", [128, 2, 16, 512], BF16) as wt5,
            nc.sbuf_tensor("a_sb", [128, 8, C], BF16) as a_sb,
        ):
            rec.stream("yasm")

            def emit_asm(g):
                for c in range(NCORE):
                    with g.If_eq(rank_reg[0], c):
                        if c % 2 == 0:
                            g.dma_start(out=YF[:, 0:8, :], in_=YT[:, :, 0:1024]).then_inc(rec.sems["yasm"][0], 16)
                            g.dma_start(out=YF[:, 8:16, :], in_=ypair[1]).then_inc(rec.sems["yasm"][0], 16)
                        else:
                            g.dma_start(out=YF[:, 0:8, :], in_=ypair[0]).then_inc(rec.sems["yasm"][0], 16)
                            g.dma_start(out=YF[:, 8:16, :], in_=YT[:, :, 1024:2048]).then_inc(rec.sems["yasm"][0], 16)
                    g.end_ifs()
            rec.raw("gpsimd", emit_asm, after=[e_ag_y, e_yt_done])
            e_yf = rec.bump("yasm", 32)

            with nc.psum_tensor("pa", [128, 8, 512], F32) as pa:
                rec.stream("wo")
                prev_ev = {tb: EV0 for tb in range(8)}
                slot_last_mm = [EV0, EV0]
                prev_lw = EV0
                for coq in range(4):
                    sl = coq % 2
                    e_lw = rec.dma("sync", f"wo{sl}", wt5[:, sl, :, :], wo_b[coq],
                                   after=[e_wo, slot_last_mm[sl]])
                    prev_lw = e_lw
                    last_mm = None
                    for cch in range(16):
                        for tb in range(8):
                            aft = [e_lw, e_yf]
                            if cch == 0:
                                aft.append(prev_ev[tb])
                            last_mm = rec.op("tensor", lambda t, tb=tb, cch=cch, sl=sl: t.matmul(
                                pa[:, tb, :], YF[:, cch, tb * 128:(tb + 1) * 128], wt5[:, sl, cch, :],
                                start=(cch == 0), stop=(cch == 15)), after=aft)
                    slot_last_mm[sl] = last_mm
                    pe_now = rec.now("PE")
                    for tb in range(8):
                        prev_ev[tb] = rec.op("scalar", lambda s, tb=tb, coq=coq: s.activation(
                            a_sb[:, tb, coq * 512:(coq + 1) * 512], pa[:, tb, :], AF.Copy, bias=0.0, scale=0.8),
                            after=[pe_now])
                e_a_done = [prev_ev[tb] for tb in range(8)]
            rec.emit()

            with (
                nc.sbuf_tensor("xo", [128, C], F32) as xo,
                nc.sbuf_tensor("lnb", [128, C], F32) as lnb,
                nc.sbuf_tensor("x2t", [128, C], F32) as x2t,
                nc.sbuf_tensor("x3t", [128, C], BF16) as x3t,
                nc.sbuf_tensor("sqd2", [128, C], BF16) as sqd2,
                nc.sbuf_tensor("st2", [128, 8], F32) as st2,
            ):
                e_x3w = []
                prev = EV0
                prev2 = EV0
                for tb in range(8):
                    atile = a_sb[:, tb, :]
                    e_xo = rec.dma("sync", "xow", xo[:, :], x_own[tb * 128:(tb + 1) * 128, :], after=[prev])
                    e_ri = _ln_stats(rec, st2, atile, sqd2, e_a_done[tb], prev, EPS[:, 0:1])
                    e_ln = rec.op("vector", lambda v, atile=atile: v.tensor_scalar(lnb[:, :], atile, st2[:, 2:3], st2[:, 6:7],
                                                                                   OP.subtract, OP.mult), after=[e_ri])
                    e_x2 = rec.op("vector", lambda v: v.tensor_tensor(x2t[:, :], xo[:, :], lnb[:, :], OP.add), after=[e_ln, e_xo, prev2])
                    e_w2 = rec.dma("scalar", "x2w", x2s[tb * 128:(tb + 1) * 128, :], x2t[:, :], after=[e_x2])
                    e_rib = _ln_stats(rec, st2, x2t[:, :], sqd2, e_x2, e_ln, EPS[:, 0:1])
                    e_x3 = rec.op("vector", lambda v: v.tensor_scalar(x3t[:, :], x2t[:, :], st2[:, 2:3], st2[:, 6:7],
                                                                      OP.subtract, OP.mult), after=[e_rib])
                    e_w3 = rec.dma("scalar", "x3w", x3mine[tb * 128:(tb + 1) * 128, :], x3t[:, :], after=[e_x3])
                    e_x3w.append(e_w3)
                    prev = e_w3
                    prev2 = e_w2
                rec.emit()

    e_ag_x3 = rec.op("gpsimd", lambda g: g.collective_compute(
        "AllGather", OP.bypass, ins=[x3mine[:]],
        outs=[x3all[0:TTOT, :].rearrange("(s t) c -> s t c", s=NCORE)],
        replica_groups=ALL8), after=e_x3w, sem="CO", n=1)

    FSH = ctx.enter_context(nc.sbuf_tensor("FSH", [128, 8, C], BF16))
    idxT0 = ctx.enter_context(nc.sbuf_tensor("idxT0", [128, 64], I16))
    idxT1 = ctx.enter_context(nc.sbuf_tensor("idxT1", [128, 64], I16))
    wbt0 = ctx.enter_context(nc.sbuf_tensor("wbt0", [128, 8], F32))
    wbt1 = ctx.enter_context(nc.sbuf_tensor("wbt1", [128, 8], F32))
    IDXT = [idxT0, idxT1]
    WBT = [wbt0, wbt1]
    route_evs = []

    MT = ctx.enter_context(nc.sbuf_tensor("MT", [128, 22, TOK], BF16))
    if True:
        e_mt = []
        with nc.sbuf_tensor("X3T", [128, 2, 16, 512], BF16) as X3T:
            rec.stream("x3T")
            e_x3a = rec.op("gpsimd", lambda g: g.dma_gather(
                X3T[:, 0, :, :], x3mine[:], IDQS[0][:], num_idxs=512, num_idxs_reg=512,
                elem_size=C, transpose=True), after=[e_x3w[-1], e_c["idq0"]], sem="x3T", n=16)
            e_x3b = rec.op("gpsimd", lambda g: g.dma_gather(
                X3T[:, 1, :, :], x3mine[:], IDQS[1][:], num_idxs=512, num_idxs_reg=512,
                elem_size=C, transpose=True), after=[e_x3a, e_c["idq1"]], sem="x3T", n=16)
            e_x3T = [e_x3b] * 16

            with (
                nc.sbuf_tensor("LTs", [16, TOK], F32) as LTs,
                nc.sbuf_tensor("Ltok", [128, 8, 16], F32) as Ltok,
                nc.sbuf_tensor("gsc", [128, 8, 16], F32) as gsc,
                nc.sbuf_tensor("gst", [128, 10], F32) as gst,
                nc.sbuf_tensor("m8s", [128, 8], F32) as m8s,
                nc.sbuf_tensor("WMs", [16, TOK], F32) as WMs,
                nc.psum_tensor("pL", [16, 2, 512], F32) as pL,
                nc.psum_tensor("pTt", [128, 16], F32) as pTt,
                nc.psum_tensor("pwt", [16, 128], F32) as pwt,
            ):
                e_lcp = []
                for tc2 in range(2):
                    last = None
                    for k in range(16):
                        last = rec.op("tensor", lambda t, tc2=tc2, k=k: t.matmul(
                            pL[:, tc2, :], GWT[:, k, :], X3T[:, tc2, k, :],
                            start=(k == 0), stop=(k == 15)), after=[e_gwt, e_x3T[k]])
                    e_lcp.append(rec.op("vector", lambda v, tc2=tc2: v.tensor_copy(
                        LTs[:, tc2 * 512:(tc2 + 1) * 512], pL[:, tc2, :]), after=[last]))

                prev_t = EV0
                prev_w = EV0
                for tb in range(8):
                    e_tr = rec.op("tensor", lambda t, tb=tb: t.transpose(pTt[:, :], LTs[:, tb * 128:(tb + 1) * 128], IDN32[0:16, 0:16]),
                                  after=e_lcp + [e_c["i32"], prev_t])
                    e_ev = rec.op("vector", lambda v, tb=tb: v.tensor_copy(Ltok[:, tb, :], pTt[:, :]), after=[e_tr])
                    prev_t = e_ev
                    lt = Ltok[:, tb, :]
                    e_mx = rec.op("vector", lambda v, lt=lt: v.tensor_reduce(gst[:, 0:1], lt, axis=AX, op=OP.max), after=[e_ev])
                    e_nm = rec.op("vector", lambda v: v.tensor_scalar(gst[:, 1:2], gst[:, 0:1], -1.0, None, OP.mult), after=[e_mx])
                    e_p = rec.op("scalar", lambda s, lt=lt, tb=tb: s.activation(gsc[:, tb, :], lt, AF.Exp, bias=gst[:, 1:2],
                                                                                accum_out=gst[:, 2:3]), after=[e_nm])
                    e_m8 = rec.op("vector", lambda v, tb=tb: v.max(out=m8s[:, :], in_=gsc[:, tb, :]), after=[e_p])
                    e_m = rec.op("vector", lambda v, tb=tb: v.tensor_scalar(Ltok[:, tb, :], gsc[:, tb, :], m8s[:, 3:4], None, OP.is_ge),
                                 after=[e_m8])
                    e_pm = rec.op("vector", lambda v, tb=tb: v.tensor_tensor(gsc[:, tb, :], gsc[:, tb, :], Ltok[:, tb, :], OP.mult),
                                  after=[e_m])
                    e_s4 = rec.op("vector", lambda v, tb=tb: v.tensor_reduce(gst[:, 3:4], gsc[:, tb, :], axis=AX, op=OP.add), after=[e_pm])
                    e_zz = rec.op("vector", lambda v: v.tensor_scalar(gst[:, 4:5], gst[:, 2:3], 1e-20, None, OP.mult), after=[e_p, e_s4])
                    e_dn = rec.op("vector", lambda v: v.tensor_tensor(gst[:, 5:6], gst[:, 3:4], gst[:, 4:5], OP.add), after=[e_zz])
                    e_di = rec.op("vector", lambda v: v.reciprocal(gst[:, 6:7], gst[:, 5:6]), after=[e_dn])
                    e_w = rec.op("vector", lambda v, tb=tb: v.tensor_scalar(gsc[:, tb, :], gsc[:, tb, :], gst[:, 6:7], None, OP.mult),
                                 after=[e_di])
                    e_tw = rec.op("tensor", lambda t, tb=tb: t.transpose(pwt[:, :], gsc[:, tb, :], IDN32[:, :]),
                                  after=[e_w, prev_w])
                    e_wv2 = rec.op("vector", lambda v, tb=tb: v.tensor_copy(WMs[:, tb * 128:(tb + 1) * 128], pwt[:, :]), after=[e_tw])
                    prev_w = e_wv2
                e_wmd = rec.dma("scalar", "wmw", wm_mine[:], WMs[:], after=[prev_w])
                rec.emit()

            e_ag_wm = rec.op("gpsimd", lambda g: g.collective_compute(
                "AllGather", OP.bypass, ins=[wm_mine[:]], outs=[wm_all[:]], replica_groups=ALL8),
                after=[e_wmd], sem="CO", n=1)

            with (
                nc.sbuf_tensor("wst", [128, 2, 16, 128], BF16) as wst,
                nc.sbuf_tensor("gsil", [128, TOK], BF16) as gsil,
                nc.psum_tensor("pm1", [128, 2, 512], F32) as pm1,
                nc.psum_tensor("pm2", [128, 2, 512], F32) as pm2,
            ):
                prev_g = [EV0, EV0]
                prev_mul = EV0
                for ic in range(22):
                    sl = ic % 2
                    e_wg = rec.dma("sync", f"sst{sl}", wst[:, sl, :, :], swg_b[ic],
                                   after=[prev_g[sl], e_swg])
                    lastg = [None, None]
                    for h2 in range(2):
                        for k in range(16):
                            lastg[h2] = rec.op("tensor", lambda t, sl=sl, k=k, h2=h2: t.matmul(
                                pm1[:, h2, :], wst[:, sl, k, :], X3T[:, h2, k, :],
                                start=(k == 0), stop=(k == 15)),
                                after=[e_wg, e_x3T[k]] + ([prev_g[sl]] if k == 0 else []))
                    e_sil = rec.op("scalar", lambda s: s.activation(gsil[:, 0:512], pm1[:, 0, :], AF.Silu), after=[lastg[0], prev_mul])
                    e_sil2 = rec.op("scalar", lambda s: s.activation(gsil[:, 512:1024], pm1[:, 1, :], AF.Silu), after=[lastg[1], prev_mul])
                    e_wu = rec.dma("sync", f"sst{sl}", wst[:, sl, :, :], swu_b[ic],
                                   after=[e_sil, e_sil2, lastg[0], lastg[1], e_swu])
                    lastu = [None, None]
                    for h2 in range(2):
                        for k in range(16):
                            lastu[h2] = rec.op("tensor", lambda t, sl=sl, k=k, h2=h2: t.matmul(
                                pm2[:, h2, :], wst[:, sl, k, :], X3T[:, h2, k, :],
                                start=(k == 0), stop=(k == 15)),
                                after=[e_wu, e_x3T[k]] + ([prev_g[sl]] if k == 0 else []))
                    e_mul1 = rec.op("vector", lambda v, ic=ic: v.tensor_tensor(MT[:, ic, 0:512], pm2[:, 0, :], gsil[:, 0:512], OP.mult),
                                    after=[lastu[0], e_sil])
                    e_mul2 = rec.op("vector", lambda v, ic=ic: v.tensor_tensor(MT[:, ic, 512:1024], pm2[:, 1, :], gsil[:, 512:1024], OP.mult),
                                    after=[lastu[1], e_sil2, e_mul1])
                    prev_g[sl] = e_mul2
                    prev_mul = e_mul2
                    e_mt.append(e_mul2)
                rec.emit()

        with (
            nc.sbuf_tensor("Wfull", [128, TOK], F32) as Wfull,
            nc.sbuf_tensor("mfull", [128, TOK], F32) as mfull,
            nc.sbuf_tensor("ones2", [128, TOK], F32) as ones2,
            nc.sbuf_tensor("Rx", [128, TOK], F32) as Rx,
            nc.sbuf_tensor("keptf", [128, TOK], F32) as keptf,
            nc.sbuf_tensor("TOKc", [128, TOK], F32) as TOKc,
            nc.sbuf_tensor("CSUM", [128, 128], F32) as CSUM,
            nc.sbuf_tensor("vtok", [128, TOK], F32) as vtok,
            nc.sbuf_tensor("vw", [128, TOK], F32) as vw,
            nc.sbuf_tensor("tot", [128, 2], F32) as tot,
            nc.sbuf_tensor("sent", [1, 2 * TOK], F32) as sent,
            nc.sbuf_tensor("sgi", [16, 2, 576], F32) as sgi,
            nc.sbuf_tensor("sgo", [16, 144], F32) as sgo,
            nc.sbuf_tensor("sgw", [16, 144], F32) as sgw,
            nc.sbuf_tensor("nfo", [1, 4], U32) as nfo,
            nc.psum_tensor("pbase", [128, 1], F32) as pbase,
        ):
            e_wf_l = [rec.dma("sync", "wmr", Wfull[s * 16:(s + 1) * 16, :], wm_all[s], after=[e_ag_wm]) for s in range(NCORE)]
            e_wf = [e_wf_l[-1]] * NCORE
            e_tc = rec.dma("sync", "tokc", TOKc[:, :], cTOK128[:])
            e_cs = rec.dma("sync", "csum", CSUM[:, :], cCSUM[:])
            e_mf = rec.op("vector", lambda v: v.tensor_scalar(mfull[:, :], Wfull[:, :], 0.0, None, OP.is_gt), after=e_wf)
            e_o2 = rec.op("vector", lambda v: v.memset(ones2[:, :], 1.0))
            e_sn0 = rec.op("vector", lambda v: v.memset(sent[:, 0:TOK], float(PADROW)))
            e_sn1 = rec.op("vector", lambda v: v.memset(sent[:, TOK:2 * TOK], 0.0))
            e_sc = rec.op("vector", lambda v: v.tensor_tensor_scan(Rx[:, :], ones2[:, :], mfull[:, :], 0.0, OP.mult, OP.add),
                          after=[e_mf, e_o2])
            e_tt = rec.op("vector", lambda v: v.tensor_copy(tot[:, 0:1], Rx[:, TOK - 1:TOK]), after=[e_sc])
            e_bm = rec.op("tensor", lambda t: t.matmul(pbase[:, :], CSUM[:, :], tot[:, 0:1], start=True, stop=True),
                          after=[e_tt, e_cs])
            e_bc2 = rec.op("vector", lambda v: v.tensor_copy(tot[:, 1:2], pbase[:, :]), after=[e_bm])
            e_ad = rec.op("vector", lambda v: v.tensor_scalar(Rx[:, :], Rx[:, :], tot[:, 1:2], None, OP.add), after=[e_bc2])
            e_rx = rec.op("vector", lambda v: v.tensor_tensor(Rx[:, :], Rx[:, :], mfull[:, :], OP.subtract), after=[e_ad])
            e_ltc = rec.op("vector", lambda v: v.tensor_scalar(keptf[:, :], Rx[:, :], float(CAP), None, OP.is_lt), after=[e_rx])
            e_kp = rec.op("vector", lambda v: v.tensor_tensor(keptf[:, :], keptf[:, :], mfull[:, :], OP.mult), after=[e_ltc])
            e_vt0 = rec.op("vector", lambda v: v.tensor_tensor(vtok[:, :], keptf[:, :], TOKc[:, :], OP.mult),
                           after=[e_kp, e_tc])
            e_vt = rec.op("vector", lambda v: v.tensor_scalar(vtok[:, :], vtok[:, :], -1.0, None, OP.add), after=[e_vt0])
            e_vwa = rec.op("vector", lambda v: v.tensor_tensor(vw[:, :], keptf[:, :], Wfull[:, :], OP.mult), after=[e_kp])
            e_vwb = rec.op("vector", lambda v: v.tensor_scalar(TOKc[:, :], keptf[:, :], -1.0, None, OP.add), after=[e_vwa, e_vt0])
            e_vw0 = rec.op("vector", lambda v: v.tensor_tensor(vw[:, :], vw[:, :], TOKc[:, :], OP.add), after=[e_vwb])

            e_st0 = rec.dma("gpsimd", "sent", strm[0:1, TTOT:9216], sent[:, 0:TOK], after=[e_sn0])
            e_st1 = rec.dma("gpsimd", "sent", strm[1:2, TTOT:9216], sent[:, TOK:2 * TOK], after=[e_sn1])

            prev_strm = [e_vt, e_vw0, e_st0, e_st1]
            lib_sg = [False]
            for ei in range(EPC):
                rec.stream("rows")

                def emit_row_ei(g, ei=ei):
                    for c in range(NCORE):
                        with g.If_eq(rank_reg[0], c):
                            r = 2 * c + ei
                            for s in range(NCORE):
                                pr_ = s * 16 + r
                                g.dma_start(out=strm[0:1, s * TOK:(s + 1) * TOK],
                                            in_=vtok[pr_:pr_ + 1, :]).then_inc(rec.sems["rows"][0], 16)
                                g.dma_start(out=strm[1:2, s * TOK:(s + 1) * TOK],
                                            in_=vw[pr_:pr_ + 1, :]).then_inc(rec.sems["rows"][0], 16)
                        g.end_ifs()
                rec.raw("gpsimd", emit_row_ei, after=prev_strm)
                e_rows = rec.bump("rows", 16 * 16)
                e_wrap0 = rec.dma("gpsimd", "wrap", sgi[:, 0, :], strm[0, :].rearrange("(f p) -> p f", p=16), after=[e_rows], allow_slow_non_contiguous=True)
                e_wrap1 = rec.dma("gpsimd", "wrap", sgi[:, 1, :], strm[1, :].rearrange("(f p) -> p f", p=16), after=[e_wrap0], allow_slow_non_contiguous=True)
                if not lib_sg[0]:
                    rec.raw("gpsimd", lambda g: g.load_library(lc.sparse_gather), after=[e_wrap1])
                    lib_sg[0] = True
                e_sg0 = rec.op("gpsimd", lambda g: g.sparse_gather(sgo[:, :], sgi[:, 0, :], num_found=nfo[0:1, 0:1]),
                               after=[e_wrap1])
                e_sg1 = rec.op("gpsimd", lambda g: g.sparse_gather(sgw[:, :], sgi[:, 1, :], num_found=nfo[0:1, 1:2]),
                               after=[e_sg0])
                e_cl0 = rec.op("vector", lambda v: v.tensor_scalar(sgo[:, 0:64], sgo[:, 0:64], 0.0, float(PADROW), OP.max, OP.min),
                               after=[e_sg0])
                e_i16 = rec.op("vector", lambda v, ei=ei: v.tensor_copy(IDXT[ei][0:16, :], sgo[:, 0:64]), after=[e_cl0])
                reps = [e_i16]
                for kk in range(1, 8):
                    reps.append(rec.dma("sync", "irep", IDXT[ei][16 * kk:16 * (kk + 1), :], IDXT[ei][0:16, :], after=[e_i16]))
                e_wb0 = rec.dma("gpsimd", "wbb", wbb[ei].rearrange("(f p) -> p f", p=16), sgw[:, 0:64], after=[e_sg1], allow_slow_non_contiguous=True)
                e_wb1 = rec.dma("gpsimd", "wbb", WBT[ei][:, :], wbb[ei].rearrange("(f p) -> p f", p=128), after=[e_wb0], allow_slow_non_contiguous=True)
                route_evs.append((reps, e_wb1))
                prev_strm = [e_wrap1, e_sg1, e_cl0, e_i16, e_wb0]
            rec.raw("gpsimd", lambda g: g.drain())
            rec.raw("gpsimd", lambda g: g.load_library(lc.mlp), after=[route_evs[-1][1]])
            rec.emit()


    e_scat = []
    with (
        nc.sbuf_tensor("XGT", [128, 2, 16, 512], BF16) as XGT,
        nc.sbuf_tensor("wge", [128, 16, 128], BF16) as wge,
        nc.sbuf_tensor("MT2", [128, 11, CAP], BF16) as MT2,
        nc.sbuf_tensor("gsil2", [128, CAP], BF16) as gsil2,
        nc.sbuf_tensor("eout", [128, 8, C], BF16) as eout,
        nc.sbuf_tensor("wde", [128, 2, 512], BF16) as wde,
    ):
        prev_scatter = EV0
        prev_xgt_users = EV0
        prev_eo_all = EV0
        for ei in range(EPC):
            reps, e_wb = route_evs[ei]
            rec.stream("xg")
            e_xga = rec.op("gpsimd", lambda g, ei=ei: g.dma_gather(
                XGT[:, 0, :, :], x3all[:], IDXT[ei][:, 0:32], num_idxs=512, num_idxs_reg=512,
                elem_size=C, transpose=True),
                after=[e_ag_x3, prev_xgt_users] + reps, sem="xg", n=16)
            e_xg = rec.op("gpsimd", lambda g, ei=ei: g.dma_gather(
                XGT[:, 1, :, :], x3all[:], IDXT[ei][:, 32:64], num_idxs=512, num_idxs_reg=512,
                elem_size=C, transpose=True),
                after=[e_xga], sem="xg", n=16)
            prev_ic = [prev_eo_all, prev_eo_all]
            prev_mul2 = prev_eo_all
            e_mt2 = []
            pgctx = contextlib.ExitStack()
            pg1 = pgctx.enter_context(nc.psum_tensor(f"pg1_{ei}", [128, 2, 512], F32))
            pg2 = pgctx.enter_context(nc.psum_tensor(f"pg2_{ei}", [128, 2, 512], F32))
            for ic in range(11):
                sl = ic % 2
                e_wg2 = rec.dma("sync", "wge", wge[:, :, :], weg_b[ei, ic],
                                after=[prev_ic[0], prev_ic[1], e_weg])
                lastg = [None, None]
                for h2 in range(2):
                    for k in range(16):
                        lastg[h2] = rec.op("tensor", lambda t, k=k, h2=h2: t.matmul(
                            pg1[:, h2, :], wge[:, k, :], XGT[:, h2, k, :],
                            start=(k == 0), stop=(k == 15)),
                            after=[e_wg2, e_xg] + ([prev_ic[sl]] if k == 0 else []))
                e_si1 = rec.op("scalar", lambda s: s.activation(gsil2[:, 0:512], pg1[:, 0, :], AF.Silu), after=[lastg[0], prev_mul2])
                e_si2 = rec.op("scalar", lambda s: s.activation(gsil2[:, 512:1024], pg1[:, 1, :], AF.Silu), after=[lastg[1], prev_mul2])
                e_wu2 = rec.dma("sync", "wge", wge[:, :, :], weu_b[ei, ic],
                                after=[e_si1, e_si2, lastg[0], lastg[1], e_weu])
                lastu = [None, None]
                for h2 in range(2):
                    for k in range(16):
                        lastu[h2] = rec.op("tensor", lambda t, k=k, h2=h2: t.matmul(
                            pg2[:, h2, :], wge[:, k, :], XGT[:, h2, k, :],
                            start=(k == 0), stop=(k == 15)),
                            after=[e_wu2, e_xg])
                e_m1 = rec.op("vector", lambda v, ic=ic: v.tensor_tensor(MT2[:, ic, 0:512], pg2[:, 0, :], gsil2[:, 0:512], OP.mult),
                              after=[lastu[0], e_si1])
                e_m2 = rec.op("vector", lambda v, ic=ic: v.tensor_tensor(MT2[:, ic, 512:1024], pg2[:, 1, :], gsil2[:, 512:1024], OP.mult),
                              after=[lastu[1], e_si2, e_m1])
                prev_ic[sl] = e_m2
                prev_mul2 = e_m2
                e_mt2.append(e_m2)
            prev_xgt_users = e_mt2[-1]
            pgctx.close()

            pectx = contextlib.ExitStack()
            pe8 = pectx.enter_context(nc.psum_tensor(f"pe8_{ei}", [128, 8, 512], F32))
            prev_eo = {sb: e_mt2[-1] for sb in range(8)}
            slot_last2 = [EV0, EV0]
            for coq in range(4):
                for ic in range(11):
                    sl = ic % 2
                    e_ldd = rec.dma("sync", f"wde{sl}", wde[:, sl, :], wed_b[ei, ic, coq],
                                    after=[slot_last2[sl], e_wed])
                    last_mm = None
                    for sb in range(8):
                        aft = [e_ldd, e_mt2[ic]]
                        if ic == 0:
                            aft.append(prev_eo[sb])
                        last_mm = rec.op("tensor", lambda t, sb=sb, ic=ic, sl=sl: t.matmul(
                            pe8[:, sb, :], MT2[:, ic, sb * 128:(sb + 1) * 128], wde[:, sl, :],
                            start=(ic == 0), stop=(ic == 10)), after=aft)
                    slot_last2[sl] = last_mm
                pe_now = rec.now("PE")
                for sb in range(8):
                    prev_eo[sb] = rec.op("scalar", lambda s, sb=sb, coq=coq, ei=ei: s.activation(
                        eout[:, sb, coq * 512:(coq + 1) * 512], pe8[:, sb, :], AF.Copy, bias=0.0,
                        scale=WBT[ei][:, sb:sb + 1]), after=[pe_now, e_wb, prev_scatter])
            e_eo = [prev_eo[sb] for sb in range(8)]
            rec.stream("scat")
            e_sc2 = rec.op("gpsimd", lambda g, ei=ei: g.dma_scatter_add(
                yp[:], eout[:, :, :], IDXT[ei][:, :], num_idxs=CAP, num_idxs_reg=CAP, elem_size=C),
                after=e_eo + e_zero + [prev_scatter] + reps, sem="scat", n=16)
            prev_scatter = e_sc2
            prev_eo_all = e_sc2
            e_scat.append(e_sc2)
            pectx.close()
        rec.emit()

    e_rs = rec.op("gpsimd", lambda g: g.collective_compute(
        "ReduceScatter", OP.add, ins=[yp[0:TTOT, :]], outs=[y_rs[:]], replica_groups=ALL8),
        after=e_scat, sem="CO", n=1)

    with (
        nc.sbuf_tensor("wds", [128, 2, 512], BF16) as wds,
        nc.psum_tensor("pf", [128, 8, 512], F32) as pf,
    ):
        prev_f = {tb: EV0 for tb in range(8)}
        slot_last = [EV0, EV0]
        for coq in range(4):
            for ic in range(22):
                sl = ic % 2
                e_ld = rec.dma("sync", f"swd{sl}", wds[:, sl, :], swd_b[ic, coq],
                               after=[slot_last[sl], e_swd])
                last_mm = None
                for tb in range(8):
                    aft = [e_ld, e_mt[ic]]
                    if ic == 0:
                        aft.append(prev_f[tb])
                    last_mm = rec.op("tensor", lambda t, tb=tb, ic=ic, sl=sl, coq=coq: t.matmul(
                        pf[:, tb, :], MT[:, ic, tb * 128:(tb + 1) * 128], wds[:, sl, :],
                        start=(ic == 0), stop=(ic == 21)), after=aft)
                slot_last[sl] = last_mm
            pe_now = rec.now("PE")
            for tb in range(8):
                prev_f[tb] = rec.op("scalar", lambda s, tb=tb, coq=coq: s.activation(
                    FSH[:, tb, coq * 512:(coq + 1) * 512], pf[:, tb, :], AF.Copy, bias=0.0, scale=1.0),
                    after=[pe_now])
        e_fsh = [prev_f[tb] for tb in range(8)]
        rec.emit()


    with (
        nc.sbuf_tensor("ym", [128, C], BF16) as ym,
        nc.sbuf_tensor("ff", [128, C], F32) as ff,
        nc.sbuf_tensor("x2r", [128, C], F32) as x2r,
        nc.sbuf_tensor("oo", [128, C], F32) as oo,
        nc.sbuf_tensor("sqd3", [128, C], BF16) as sqd3,
        nc.sbuf_tensor("st3", [128, 8], F32) as st3,
    ):
        e_outs = []
        prev = EV0
        for tb in range(8):
            e_ym = rec.dma("sync", "ymr", ym[:, :], y_rs[tb * 128:(tb + 1) * 128, :], after=[e_rs, prev])
            e_x2r = rec.dma("sync", "x2r", x2r[:, :], x2s[tb * 128:(tb + 1) * 128, :], after=[prev, ("x2w", 128)])
            e_ff = rec.op("vector", lambda v, tb=tb: v.tensor_tensor(ff[:, :], ym[:, :], FSH[:, tb, :], OP.add),
                          after=[e_ym, e_fsh[tb], prev])
            e_ri = _ln_stats(rec, st3, ff[:, :], sqd3, e_ff, prev, EPS[:, 0:1])
            e_ln = rec.op("vector", lambda v: v.tensor_scalar(ff[:, :], ff[:, :], st3[:, 2:3], st3[:, 6:7],
                                                              OP.subtract, OP.mult), after=[e_ri])
            e_o = rec.op("vector", lambda v: v.tensor_tensor(oo[:, :], x2r[:, :], ff[:, :], OP.add), after=[e_ln, e_x2r])
            e_w = rec.dma("sync", "outw", out_shard[tb * 128:(tb + 1) * 128, :], oo[:, :], after=[e_o])
            e_outs.append(e_w)
            prev = e_w
        rec.raw("sync", lambda s: None, after=e_outs)
        rec.raw("gpsimd", lambda g: None, after=e_outs)
        rec.emit()

    ctx.close()
    return nc


def kernel(**inputs):
    x = np.ascontiguousarray(inputs["L0_attn_x"])
    wq = np.ascontiguousarray(inputs["L0_attn_wq"])
    wk = np.ascontiguousarray(inputs["L0_attn_wk"])
    wv = np.ascontiguousarray(inputs["L0_attn_wv"])
    wo = np.ascontiguousarray(inputs["L0_attn_wo"])
    lqk = np.ascontiguousarray(np.stack([inputs["L0_attn_lq1"], inputs["L0_attn_lk1"],
                                         inputs["L0_attn_lq2"], inputs["L0_attn_lk2"]], axis=0))
    gate = np.ascontiguousarray(inputs["L0_ffn_gate_w"])
    wg = np.ascontiguousarray(inputs["L0_ffn_w_gate"])
    wu = np.ascontiguousarray(inputs["L0_ffn_w_up"])
    wd = np.ascontiguousarray(inputs["L0_ffn_w_down"])
    swg_i = np.ascontiguousarray(inputs["L0_ffn_sw_gate"])
    swu_i = np.ascontiguousarray(inputs["L0_ffn_sw_up"])
    swd_i = np.ascontiguousarray(inputs["L0_ffn_sw_down"])

    nc = build_nc()
    lower_extended_insts(nc)

    in_maps = []
    for c in range(NCORE):
        b, half = c // 2, c % 2
        hr = slice(1024 * half, 1024 * (half + 1))
        inv = (1.0 / 10000.0) ** (np.arange(0, HH, 2, dtype=np.float64) / HH)
        rot = np.zeros((128, 8), np.float32)
        for lh in range(4):
            h = 4 * half + lh
            th = h * inv
            rot[0:64, lh] = np.cos(th)
            rot[64:128, lh] = np.cos(th)
            rot[0:64, 4 + lh] = np.sin(th)
            rot[64:128, 4 + lh] = -np.sin(th)
        def tile_w(m):
            n = m.shape[0] // 512
            return np.ascontiguousarray(m.reshape(n, 512, 16, 128).transpose(0, 3, 2, 1))

        wg2 = wg[2 * c:2 * c + 2]
        wu2 = wu[2 * c:2 * c + 2]
        wd2 = wd[2 * c:2 * c + 2]
        in_maps.append({
            "xb": x[b],
            "rotpq": rot,
            "x_own": np.ascontiguousarray(x[b][1024 * half:1024 * (half + 1)]),
            "wq_t": tile_w(wq[hr]),
            "wk_t": tile_w(wk[hr]),
            "wv_t": tile_w(wv[hr]),
            "wo_t": tile_w(wo),
            "lqk": lqk,
            "gate_w": gate,
            "weg": np.ascontiguousarray(wg2.reshape(2, 16, 128, 11, 128).transpose(0, 3, 2, 1, 4)),
            "weu": np.ascontiguousarray(wu2.reshape(2, 16, 128, 11, 128).transpose(0, 3, 2, 1, 4)),
            "wed": np.ascontiguousarray(wd2.reshape(2, 11, 128, 4, 512).transpose(0, 1, 3, 2, 4)),
            "swg": np.ascontiguousarray(swg_i.reshape(16, 128, 22, 128).transpose(2, 1, 0, 3)),
            "swu": np.ascontiguousarray(swu_i.reshape(16, 128, 22, 128).transpose(2, 1, 0, 3)),
            "swd": np.ascontiguousarray(swd_i.reshape(22, 128, 4, 512).transpose(0, 2, 1, 3)),
        })
    res = run_bass_kernel_spmd(nc, in_maps, core_ids=list(range(NCORE)))
    shards = [res.results[c]["out_shard"] for c in range(NCORE)]
    out = np.concatenate(shards, axis=0).reshape(B, T, C)
    return out


# version 04
# speedup vs baseline: 1.1103x; 1.1103x over previous
import contextlib
import math

import numpy as np
import ml_dtypes

import concourse.bass as bass
import concourse.mybir as mybir
from concourse.alu_op_type import AluOpType
from concourse import library_config as lc
from concourse.bass_utils import run_bass_kernel_spmd
from concourse.library_overlay import lower_extended_insts

F32 = mybir.dt.float32
BF16 = mybir.dt.bfloat16
I16 = mybir.dt.int16
U32 = mybir.dt.uint32
AF = mybir.ActivationFunctionType
OP = AluOpType
AX = mybir.AxisListType.X

B, T, C = 4, 2048, 2048
NCORE = 8
TOK = 1024
HH, HD = 128, 256
E, CAP = 16, 1024
IEXP, ISH = 1408, 2816
EPC = 2
LN_EPS = 1e-5
RMS_EPS = 1.1920929e-07
LAMBDA_INIT = 0.2
ISQ = 1.0 / math.sqrt(128.0)
TTOT = B * T
PADROW = TTOT

PAIRS = [[0, 1], [2, 3], [4, 5], [6, 7]]
ALL8 = [list(range(NCORE))]

ENGINES = ("tensor", "vector", "scalar", "gpsimd", "sync")
CHAIN = {"tensor": "PE", "vector": "DV", "scalar": "AC", "gpsimd": "PL", "sync": "SY"}
QSEM = {"sync": "QS", "scalar": "QA", "gpsimd": "QG"}


class Rec:

    def __init__(self, nc, ctx):
        self.nc = nc
        self.ctx = ctx
        self.sems = {}
        for name in ["PE", "DV", "AC", "PL", "SY", "CO"]:
            self.sems[name] = [ctx.enter_context(nc.semaphore(name)), 0]
        self.ops = {e: [] for e in ENGINES}
        self.hw = {e: {} for e in ENGINES}
        self.drained = {e: 0 for e in ENGINES}
        self.block_start = {}
        self.bg = set()

    def stream(self, name, bg=False):
        if name not in self.sems:
            self.sems[name] = [self.ctx.enter_context(self.nc.semaphore(name)), 0]
        if bg:
            self.bg.add(name)
        return self.sems[name]

    def now(self, kind):
        return (kind, self.sems[kind][1])

    def _after(self, eng, after):
        waits, need_drain = [], False
        for ev in after:
            if ev is None:
                continue
            k, v = ev
            if v <= 0:
                continue
            if k == CHAIN[eng]:
                if v > self.drained[eng]:
                    need_drain = True
            else:
                waits.append((k, v))
        if need_drain:
            self.drained[eng] = self.sems[CHAIN[eng]][1]
        wmax = {}
        for (k, v) in waits:
            wmax[k] = max(wmax.get(k, 0), v)
        return sorted(wmax.items()), need_drain

    def op(self, eng, fn, after=(), sem=None, n=1):
        kind = sem or CHAIN[eng]
        waits, need_drain = self._after(eng, after)
        self.sems[kind][1] += n
        val = self.sems[kind][1]
        self.ops[eng].append((fn, waits, need_drain, kind, n))
        return (kind, val)

    def dma(self, eng, stream, out, in_, after=(), bg=False, **kw):
        self.stream(stream, bg=bg)
        return self.op(eng, lambda e: e.dma_start(out=out, in_=in_, **kw),
                       after=after, sem=stream, n=16)

    def raw(self, eng, fn, after=()):
        waits, need_drain = self._after(eng, after)
        self.ops[eng].append((fn, waits, need_drain, None, 0))

    def bump(self, kind, n):
        self.sems[kind][1] += n
        return (kind, self.sems[kind][1])

    def emit(self):
        nc = self.nc
        if not any(self.ops[e] for e in ENGINES):
            return
        fence = [(k, v) for k, v in self.block_start.items()
                 if v > 0 and k not in self.bg]
        with nc.Block() as blk:
            for eng_name in ENGINES:
                ops = self.ops[eng_name]
                if not ops:
                    continue

                def make_body(eng_name, ops):
                    def body(eng):
                        hw = self.hw[eng_name]
                        for (k, v) in fence:
                            if hw.get(k, 0) < v:
                                eng.wait_ge(self.sems[k][0], v)
                                hw[k] = v
                        for (fn, waits, need_drain, kind, n) in ops:
                            for (k, v) in waits:
                                if hw.get(k, 0) < v:
                                    eng.wait_ge(self.sems[k][0], v)
                                    hw[k] = v
                            if need_drain:
                                eng.drain()
                            inst = fn(eng)
                            if kind is not None and inst is not None:
                                inst.then_inc(self.sems[kind][0], n)
                    return body

                getattr(blk, eng_name)(make_body(eng_name, ops))
        self.ops = {e: [] for e in ENGINES}
        self.block_start = {k: s[1] for k, s in self.sems.items()
                            if k not in ("PE", "DV", "AC", "PL", "SY")}


def _consts():
    ku = np.arange(896)[None, :] >= (np.arange(128)[:, None] + 384)
    stair = np.asarray(ku.astype(ml_dtypes.bfloat16))
    idnb = np.asarray(np.eye(128, dtype=ml_dtypes.bfloat16))
    idn32 = np.eye(128, dtype=np.float32)
    onesb = np.asarray(np.ones((128, 1), dtype=ml_dtypes.bfloat16))
    tok128 = (np.arange(128)[:, None] // 16 * 1024 + np.arange(TOK)[None, :] + 1).astype(np.float32)
    pp = np.arange(128)
    csum = ((pp[:, None] % 16 == pp[None, :] % 16) & (pp[:, None] // 16 < pp[None, :] // 16)).astype(np.float32)
    idqs = []
    for b4 in range(4):
        w = np.zeros((128, 32), np.int16)
        for i in range(512):
            for g in range(8):
                w[(i % 16) + 16 * g, i // 16] = 512 * b4 + i
        idqs.append(w)
    return stair, idnb, idn32, onesb, tok128, csum, idqs


def _ln_stats(rec, stat, src_ap, sq_scratch, e_src, prev, eps_ap=None):
    e_s1 = rec.op("scalar", lambda s: s.activation(sq_scratch[:, :], src_ap, AF.Square, accum_out=stat[:, 1:2]),
                  after=[e_src, prev])
    e_s2 = rec.op("vector", lambda v: v.tensor_reduce(stat[:, 0:1], src_ap, axis=AX, op=OP.add),
                  after=[e_src, prev])
    e_m = rec.op("vector", lambda v: v.tensor_scalar(stat[:, 2:3], stat[:, 0:1], 1.0 / C, None, OP.mult), after=[e_s2])
    e_v1 = rec.op("vector", lambda v: v.tensor_tensor(stat[:, 3:4], stat[:, 2:3], stat[:, 2:3], OP.mult), after=[e_m])
    e_v2 = rec.op("vector", lambda v: v.scalar_tensor_tensor(stat[:, 4:5], stat[:, 1:2], 1.0 / C, stat[:, 3:4],
                                                             OP.mult, OP.subtract), after=[e_v1, e_s1])
    e_sd = rec.op("scalar", lambda s: s.activation(stat[:, 5:6], stat[:, 4:5], AF.Sqrt, bias=eps_ap), after=[e_v2])
    e_ri = rec.op("vector", lambda v: v.reciprocal(stat[:, 6:7], stat[:, 5:6]), after=[e_sd])
    return e_ri


def build_nc():
    nc = bass.Bass("TRN2", target_bir_lowering=False)
    ctx = contextlib.ExitStack()
    dram = lambda name, shape, dt, kind="Internal": nc.dram_tensor(name, shape, dt, kind=kind).ap()

    xb = dram("xb", [T, C], F32, "ExternalInput")
    x_own = dram("x_own", [TOK, C], F32, "ExternalInput")
    wq_t = dram("wq_t", [2, 128, 16, 512], F32, "ExternalInput")
    wk_t = dram("wk_t", [2, 128, 16, 512], F32, "ExternalInput")
    wv_t = dram("wv_t", [2, 128, 16, 512], F32, "ExternalInput")
    wo_t = dram("wo_t", [4, 128, 16, 512], F32, "ExternalInput")
    lqk = dram("lqk", [4, HD], F32, "ExternalInput")
    rotpq = dram("rotpq", [128, 8], F32, "ExternalInput")
    gate_w = dram("gate_w", [E, C], F32, "ExternalInput")
    weg = dram("weg", [EPC, 11, 128, 16, 128], F32, "ExternalInput")
    weu = dram("weu", [EPC, 11, 128, 16, 128], F32, "ExternalInput")
    wed = dram("wed", [EPC, 11, 4, 128, 512], F32, "ExternalInput")
    swg = dram("swg", [22, 128, 16, 128], F32, "ExternalInput")
    swu = dram("swu", [22, 128, 16, 128], F32, "ExternalInput")
    swd = dram("swd", [22, 4, 128, 512], F32, "ExternalInput")
    out_shard = dram("out_shard", [TOK, C], F32, "ExternalOutput")

    stair_h, idnb_h, idn32_h, onesb_h, tok128_h, csum_h, idqs_h = _consts()
    cSTAIR = nc.inline_tensor(stair_h, name="cSTAIR").ap()
    cIDNB = nc.inline_tensor(idnb_h, name="cIDNB").ap()
    cIDN32 = nc.inline_tensor(idn32_h, name="cIDN32").ap()
    cONESB = nc.inline_tensor(onesb_h, name="cONESB").ap()
    cTOK128 = nc.inline_tensor(tok128_h, name="cTOK128").ap()
    cIDQ = [nc.inline_tensor(idqs_h[b4], name=f"cIDQ{b4}").ap() for b4 in range(4)]
    cCSUM = nc.inline_tensor(csum_h, name="cCSUM").ap()

    x1s = dram("x1s", [T, C], BF16)
    wq_b = dram("wq_b", [2, 128, 16, 512], BF16)
    wk_b = dram("wk_b", [2, 128, 16, 512], BF16)
    wv_b = dram("wv_b", [2, 128, 16, 512], BF16)
    wo_b = dram("wo_b", [4, 128, 16, 512], BF16)
    weg_b = dram("weg_b", [EPC, 11, 128, 16, 128], BF16)
    weu_b = dram("weu_b", [EPC, 11, 128, 16, 128], BF16)
    wed_b = dram("wed_b", [EPC, 11, 4, 128, 512], BF16)
    swg_b = dram("swg_b", [22, 128, 16, 128], BF16)
    swu_b = dram("swu_b", [22, 128, 16, 128], BF16)
    swd_b = dram("swd_b", [22, 4, 128, 512], BF16)
    x2s = dram("x2s", [TOK, C], F32)
    yhalf = dram("yhalf", [128, 8, 1024], BF16)
    ypair = dram("ypair", [2, 128, 8, 1024], BF16)
    x3mine = dram("x3mine", [TOK, C], BF16)
    x3all = dram("x3all", [TTOT + 1, C], BF16)
    wm_mine = dram("wm_mine", [E, TOK], F32)
    wm_all = dram("wm_all", [NCORE, E, TOK], F32)
    yp = dram("yp", [TTOT + 1, C], BF16)
    y_rs = dram("y_rs", [TOK, C], BF16)
    strm = dram("strm", [2, 9216], F32)
    wbb = dram("wbb", [EPC, CAP], F32)

    sbt = lambda name, shape, dt: ctx.enter_context(nc.sbuf_tensor(name, shape, dt))
    ROTS = sbt("ROTS", [128, 8], F32)
    IDQS = [sbt(f"IDQS{b4}", [128, 32], I16) for b4 in range(4)]
    STAIR = sbt("STAIR", [128, 896], BF16)
    IDNB = sbt("IDNB", [128, 128], BF16)
    IDN32 = sbt("IDN32", [128, 128], F32)
    ONESB = sbt("ONESB", [128, 1], BF16)
    ZEROT = sbt("ZEROT", [128, 4096], BF16)
    LAM = sbt("LAM", [1, 4], F32)
    GWT = sbt("GWT", [128, 16, 16], BF16)
    EPS = sbt("EPS", [128, 2], F32)

    rec = Rec(nc, ctx)
    EV0 = ("PE", 0)

    rank_reg = [None]

    def init_g(g):
        g.load_library(lc.mlp)
        rank_reg[0] = g.to_reg(g.partition_id())
    rec.raw("gpsimd", init_g)

    e_wv = rec.dma("gpsimd", "c_wv", wv_b[:], wv_t[:], bg=True)
    e_wq = rec.dma("gpsimd", "c_wq", wq_b[:], wq_t[:], bg=True)
    e_wk = rec.dma("gpsimd", "c_wk", wk_b[:], wk_t[:], bg=True)

    e_c = {}
    for nm, dst, src in (("st", STAIR, cSTAIR), ("ib", IDNB, cIDNB),
                         ("i32", IDN32, cIDN32), ("ob", ONESB, cONESB)):
        e_c[nm] = rec.dma("sync", "cst_" + nm, dst[:], src[:])
    e_c["rot"] = rec.dma("sync", "cst_rot", ROTS[:], rotpq[:])
    for b4 in range(4):
        e_c[f"idq{b4}"] = rec.dma("sync", f"cst_idq{b4}", IDQS[b4][:], cIDQ[b4][:])
    e_zt = rec.op("vector", lambda v: v.memset(ZEROT[:], 0.0))
    rec.op("vector", lambda v: v.memset(EPS[:, 0:1], LN_EPS))
    rec.op("vector", lambda v: v.memset(EPS[:, 1:2], RMS_EPS))

    ZELEM = 128 * 4096
    flat_yp = yp.rearrange("r c -> (r c)")
    nzp = ((TTOT + 1) * C) // ZELEM
    e_zero = []
    for z in range(nzp):
        seg = flat_yp[z * ZELEM:(z + 1) * ZELEM].rearrange("(p f) -> p f", p=128)
        e_zero.append(rec.dma("sync", "zero", seg, ZEROT[:], after=[e_zt], bg=True))
    rem = (TTOT + 1) * C - nzp * ZELEM
    if rem:
        seg = flat_yp[nzp * ZELEM:].rearrange("(p f) -> p f", p=128)
        e_zero.append(rec.dma("sync", "zero", seg, ZEROT[:, 0:rem // 128], after=[e_zt], bg=True))
    e_zero.append(rec.dma("sync", "zero", x3all[PADROW:PADROW + 1, :], ZEROT[0:1, 0:C], after=[e_zt], bg=True))

    lvs = sbt("lvs", [1, 4 * HD], F32)
    lpr = sbt("lpr", [1, 2 * HD], F32)
    lred = sbt("lred", [1, 4], F32)
    e_lv = rec.dma("sync", "lqk", lvs[:], lqk.rearrange("a b -> (a b)").rearrange("(o f) -> o f", o=1))
    e_p1 = rec.op("vector", lambda v: v.tensor_tensor(lpr[:, 0:HD], lvs[:, 0:HD], lvs[:, HD:2 * HD], OP.mult), after=[e_lv])
    e_p2 = rec.op("vector", lambda v: v.tensor_tensor(lpr[:, HD:2 * HD], lvs[:, 2 * HD:3 * HD], lvs[:, 3 * HD:4 * HD], OP.mult), after=[e_lv, e_p1])
    e_r1 = rec.op("vector", lambda v: v.tensor_reduce(lred[:, 0:1], lpr[:, 0:HD], axis=AX, op=OP.add), after=[e_p2])
    e_r2 = rec.op("vector", lambda v: v.tensor_reduce(lred[:, 1:2], lpr[:, HD:2 * HD], axis=AX, op=OP.add), after=[e_r1])
    e_ex = rec.op("scalar", lambda s: s.activation(lred[:, 0:2], lred[:, 0:2], AF.Exp), after=[e_r2])
    e_l0 = rec.op("vector", lambda v: v.tensor_tensor(LAM[0:1, 0:1], lred[:, 0:1], lred[:, 1:2], OP.subtract), after=[e_ex])
    e_l1 = rec.op("vector", lambda v: v.tensor_scalar(LAM[0:1, 0:1], LAM[0:1, 0:1], LAMBDA_INIT, None, OP.add), after=[e_l0])
    e_lam = rec.op("vector", lambda v: v.tensor_scalar(LAM[0:1, 1:2], LAM[0:1, 0:1], -1.0, None, OP.mult), after=[e_l1])

    gwsb = sbt("gwsb", [16, C], BF16)
    e_gw = rec.dma("gpsimd", "gwl", gwsb[:], gate_w[:])
    with nc.psum_tensor("pgt", [128, 16], BF16) as pgt:
        ev_prev = EV0
        for k in range(16):
            e_t = rec.op("tensor", lambda t, k=k: t.transpose(pgt[:, :], gwsb[:, k * 128:(k + 1) * 128], IDNB[0:16, 0:16]),
                         after=[e_gw, e_c["ib"], ev_prev])
            ev_prev = rec.op("vector", lambda v, k=k: v.tensor_copy(GWT[:, k, :], pgt[:, :]), after=[e_t])
        e_gwt = ev_prev
        rec.emit()

    with (
        nc.sbuf_tensor("xt", [128, 2, C], F32) as xt,
        nc.sbuf_tensor("x1t", [128, 2, C], BF16) as x1t,
        nc.sbuf_tensor("sqd", [128, C], BF16) as sqd,
        nc.sbuf_tensor("st8", [128, 8], F32) as st8,
    ):
        prev_use = [EV0, EV0]
        prev_stat = EV0
        last_w = EV0
        for i in range(16):
            sl = i % 2
            e_x = rec.dma("sync", f"xls{sl}", xt[:, sl, :], xb[i * 128:(i + 1) * 128, :], after=[prev_use[sl]])
            e_ri = _ln_stats(rec, st8, xt[:, sl, :], sqd, e_x, prev_stat, EPS[:, 0:1])
            e_ap = rec.op("vector", lambda v, sl=sl: v.tensor_scalar(x1t[:, sl, :], xt[:, sl, :], st8[:, 2:3], st8[:, 6:7],
                                                                     OP.subtract, OP.mult), after=[e_ri])
            prev_stat = e_ap
            e_w = rec.dma("scalar", f"x1w{sl}", x1s[i * 128:(i + 1) * 128, :], x1t[:, sl, :], after=[e_ap])
            prev_use[sl] = e_w
            last_w = e_w
        e_x1_done = [prev_use[0], prev_use[1]]
        rec.emit()

    e_cast = {}
    bg_chunks = []
    for i in range(4):
        bg_chunks.append(("wo", wo_b[i], wo_t[i]))
    for lohi in range(2):
        bg_chunks.append(("swg", swg_b[11 * lohi:11 * (lohi + 1)], swg[11 * lohi:11 * (lohi + 1)]))
        bg_chunks.append(("swu", swu_b[11 * lohi:11 * (lohi + 1)], swu[11 * lohi:11 * (lohi + 1)]))
    for nm, db, sb_ in (("weg", weg_b, weg), ("weu", weu_b, weu), ("wed", wed_b, wed)):
        for ei_ in range(EPC):
            for lohi in range(2):
                sl_ = slice(6 * lohi, 6 + 5 * lohi)
                bg_chunks.append((nm, db[ei_, sl_], sb_[ei_, sl_]))
    for lohi in range(2):
        bg_chunks.append(("swd", swd_b[11 * lohi:11 * (lohi + 1)], swd[11 * lohi:11 * (lohi + 1)]))

    def bg_pump(k=1):
        for _ in range(k):
            if bg_chunks:
                key, dst, src = bg_chunks.pop(0)
                e_cast[key] = rec.dma("gpsimd", "cast_" + key, dst, src, bg=True)

    with nc.sbuf_tensor("YT", [128, 8, T], BF16) as YT:
        for p in range(2):
            with (
                nc.sbuf_tensor(f"QT{p}", [128, 4, T], BF16) as QT,
                nc.sbuf_tensor(f"KT{p}", [128, 4, T], BF16) as KT,
                nc.sbuf_tensor(f"VS{p}", [128, 16, 512], BF16) as VS,
                nc.sbuf_tensor(f"WTS{p}", [128, 16, 512], BF16) as WTS,
            ):
                with nc.sbuf_tensor(f"X1T{p}", [128, 4, 16, 512], BF16) as X1T:
                    rec.stream("x1T")
                    e_x1T_l = []
                    prevg = e_x1_done
                    for g4 in range(4):
                        ev = rec.op("gpsimd", lambda g, g4=g4: g.dma_gather(
                            X1T[:, g4, :, :], x1s[:], IDQS[g4][:], num_idxs=512, num_idxs_reg=512,
                            elem_size=C, transpose=True),
                            after=list(prevg) + [e_c[f"idq{g4}"]] if g4 == 0 else [prevg, e_c[f"idq{g4}"]],
                            sem="x1T", n=16)
                        e_x1T_l.append(ev)
                        prevg = ev
                    e_x1T = [e_x1T_l[-1]] * 16
                    e_wts_1 = rec.dma("scalar", "wts", WTS[:, :, :], wv_b[p], after=[e_wv])
                    e_wts = [e_wts_1] * 16
                    e_vs = {}
                    with nc.psum_tensor(f"pv{p}", [128, 8, 512], F32) as pv:
                        for tb in range(16):
                            j = tb % 8
                            last = None
                            for cc in range(16):
                                aft = [e_x1T[cc], e_wts[cc]]
                                if cc == 0 and tb >= 8:
                                    aft.append(e_vs[tb - 8])
                                last = rec.op("tensor", lambda t, j=j, tb=tb, cc=cc: t.matmul(
                                    pv[:, j, :], X1T[:, tb // 4, cc, (tb % 4) * 128:(tb % 4 + 1) * 128], WTS[:, cc, :],
                                    start=(cc == 0), stop=(cc == 15)), after=aft)
                            e_vs[tb] = rec.op("vector", lambda v, j=j, tb=tb: v.tensor_copy(VS[:, tb, :], pv[:, j, :]),
                                              after=[last])
                    e_vs_all = [e_vs[tb] for tb in range(16)]
                    e_vs_last = rec.now("DV")
                    bg_pump()

                    def qk_proj(dst, w_b, e_w_ready, e_free, nm, init_ev):
                        e_w_1 = rec.dma("scalar", "wts", WTS[:, :, :], w_b[p], after=[e_w_ready, e_free])
                        e_w = [e_w_1] * 16
                        evs = []
                        with nc.psum_tensor(nm, [128, 2, 512], F32) as pqk:
                            prev = [init_ev, init_ev]
                            for cc in range(4):
                                for tc in range(4):
                                    sl = (cc * 4 + tc) % 2
                                    last = None
                                    for k in range(16):
                                        aft = [e_w[k], e_x1T[k]]
                                        if k == 0:
                                            aft.append(prev[sl])
                                        last = rec.op("tensor", lambda t, cc=cc, tc=tc, k=k, sl=sl: t.matmul(
                                            pqk[:, sl, :], WTS[:, k, cc * 128:(cc + 1) * 128],
                                            X1T[:, tc, k, :],
                                            start=(k == 0), stop=(k == 15)), after=aft)
                                    prev[sl] = rec.op("vector", lambda v, cc=cc, tc=tc, sl=sl: v.tensor_copy(
                                        dst[:, cc, tc * 512:(tc + 1) * 512], pqk[:, sl, :]), after=[last])
                                    evs.append(prev[sl])
                        return evs

                    e_q_evs = qk_proj(QT, wq_b, e_wq, rec.now("PE"), f"pqkq{p}", e_vs[15])
                    bg_pump()
                    e_k_evs = qk_proj(KT, wk_b, e_wk, rec.now("PE"), f"pqkk{p}", e_q_evs[-1])
                    bg_pump()
                rec.emit()

                rope_ev = {}
                with (
                    nc.sbuf_tensor(f"sqr{p}", [128, T], BF16) as sqr,
                    nc.sbuf_tensor(f"rr{p}", [1, 2 * T], F32) as rr,
                    nc.sbuf_tensor(f"rrb{p}", [1, T], BF16) as rrb,
                    nc.sbuf_tensor(f"rb{p}", [128, T], BF16) as rb,
                    nc.sbuf_tensor(f"TMPA{p}", [128, T], BF16) as TMPA,
                    nc.sbuf_tensor(f"TMPB{p}", [128, T], BF16) as TMPB,
                    nc.psum_tensor(f"pr{p}", [1, 4, 512], F32) as pr,
                ):
                    prev_rope = e_k_evs[-1]
                    prev_rb_rd = EV0
                    for mi, (src, src_evs) in enumerate(((QT, e_q_evs), (KT, e_k_evs))):
                        for ch in range(4):
                            tile = src[:, ch, :]
                            lh = ch // 2
                            e_sq = rec.op("scalar", lambda s, tile=tile: s.activation(sqr[:, :], tile, AF.Square),
                                          after=src_evs + [prev_rope])
                            e_ms = []
                            for q4 in range(4):
                                e_ms.append(rec.op("tensor", lambda t, q4=q4: t.matmul(
                                    pr[:, q4, :], ONESB[:, :], sqr[:, q4 * 512:(q4 + 1) * 512],
                                    start=True, stop=True), after=[e_sq, e_c["ob"], prev_rope]))
                            e_rs = rec.op("scalar", lambda s: s.activation(rr[:, 0:T], pr[:, :, :], AF.Sqrt,
                                                                           scale=1.0 / 128.0, bias=EPS[0:1, 1:2]), after=e_ms)
                            e_rc = rec.op("vector", lambda v: v.reciprocal(rr[:, T:2 * T], rr[:, 0:T]), after=[e_rs])
                            e_rb = rec.op("vector", lambda v: v.tensor_copy(rrb[:, :], rr[:, T:2 * T]), after=[e_rc])
                            e_bc = rec.op("gpsimd", lambda g: g.partition_broadcast(rb[:, :], rrb[:, :], channels=128),
                                          after=[e_rb, prev_rb_rd])
                            e_sw1 = rec.dma("sync", "swp", TMPB[0:64, :], tile[64:128, :], after=src_evs + [prev_rope])
                            e_sw2 = rec.dma("sync", "swp", TMPB[64:128, :], tile[0:64, :], after=[e_sw1])
                            e_t1 = rec.op("vector", lambda v, tile=tile, lh=lh: v.tensor_scalar(
                                TMPA[:, :], tile, ROTS[:, lh:lh + 1], None, OP.mult),
                                after=[e_c["rot"], e_sw2, prev_rope])
                            e_t2 = rec.op("vector", lambda v, tile=tile, lh=lh: v.scalar_tensor_tensor(
                                tile, TMPB[:, :], ROTS[:, 4 + lh:5 + lh], TMPA[:, :], OP.mult, OP.add),
                                after=[e_t1, e_sw2])
                            e_nb = rec.op("vector", lambda v, tile=tile: v.tensor_tensor(tile, tile, rb[:, :], OP.mult),
                                          after=[e_t2, e_bc])
                            prev_rb_rd = e_nb
                            rope_ev[(mi, ch)] = e_nb
                            prev_rope = e_nb
                    e_rope_all = prev_rope
                    bg_pump(2)
                    rec.emit()

                with (
                    nc.sbuf_tensor(f"PT{p}", [128, 2, 512], BF16) as PT,
                    nc.sbuf_tensor(f"zrow{p}", [1, 2, 512], F32) as zrow,
                    nc.sbuf_tensor(f"zbf{p}", [128, 2, 512], F32) as zbf,
                    nc.sbuf_tensor(f"csc{p}", [128, 2, 512], F32) as csc,
                    nc.psum_tensor(f"pO{p}", [128, 4, 512], F32) as pO,
                    nc.psum_tensor(f"pS{p}", [128, 2, 512], F32) as pS,
                    nc.psum_tensor(f"pZ{p}", [1, 2, 512], F32) as pZ,
                ):
                    prev_exp = [e_rope_all, e_rope_all]
                    prev_zread = e_rope_all
                    prev_zrow_rd = e_rope_all
                    prev_cmb = [e_rope_all] * 4
                    for lh in range(2):
                        for qc in range(4):
                            nkb = (qc + 1) * 4
                            e_zl = None
                            e_ol = {}
                            for kb in range(nkb):
                                for j in range(2):
                                    qk_ev = [rope_ev[(0, 2 * lh + j)], rope_ev[(1, 2 * lh + j)]]
                                    e_s = rec.op("tensor", lambda t, lh=lh, j=j, kb=kb, qc=qc: t.matmul(
                                        pS[:, j, :], KT[:, 2 * lh + j, kb * 128:(kb + 1) * 128],
                                        QT[:, 2 * lh + j, qc * 512:(qc + 1) * 512],
                                        start=True, stop=True),
                                        after=qk_ev + [prev_exp[j]])
                                    e_e = rec.op("scalar", lambda s, j=j: s.activation(PT[:, j, :], pS[:, j, :], AF.Exp, scale=ISQ),
                                                 after=[e_s])
                                    delta = kb * 128 - qc * 512
                                    if 0 <= delta <= 384:
                                        e_e = rec.op("vector", lambda v, j=j, delta=delta: v.tensor_tensor(
                                            PT[:, j, :], PT[:, j, :], STAIR[:, 384 - delta:896 - delta], OP.mult),
                                            after=[e_e, e_c["st"]])
                                    prev_exp[j] = e_e
                                    e_z = rec.op("tensor", lambda t, j=j, kb=kb, nkb=nkb: t.matmul(
                                        pZ[:, j, :], ONESB[:, :], PT[:, j, :],
                                        start=(kb == 0), stop=(kb == nkb - 1)),
                                        after=[e_e] + ([prev_zread] if kb == 0 else []))
                                    e_zl = e_z
                                    for d2 in range(2):
                                        aft = [e_e, e_vs_last]
                                        if kb == 0:
                                            aft.append(prev_cmb[2 * j + d2])
                                        e_ol[2 * j + d2] = rec.op("tensor", lambda t, lh=lh, j=j, d2=d2, kb=kb, nkb=nkb: t.matmul(
                                            pO[:, 2 * j + d2, :], VS[:, kb, 256 * lh + 128 * d2:256 * lh + 128 * (d2 + 1)],
                                            PT[:, j, :], start=(kb == 0), stop=(kb == nkb - 1)), after=aft)
                            e_zc = rec.op("vector", lambda v: v.tensor_copy(zrow[:, :, :], pZ[:, :, :]),
                                          after=[e_zl, prev_zrow_rd])
                            prev_zread = e_zc
                            e_zr = rec.op("vector", lambda v: v.reciprocal(zrow[:, :, :], zrow[:, :, :]), after=[e_zc])
                            e_zs = rec.op("vector", lambda v: v.tensor_scalar(zrow[:, 1, :], zrow[:, 1, :], LAM[0:1, 1:2], None, OP.mult),
                                          after=[e_zr, e_lam])
                            e_zb = rec.op("gpsimd", lambda g: g.partition_broadcast(zbf[:, :, :], zrow[:, :, :], channels=128),
                                          after=[e_zs])
                            prev_zrow_rd = e_zb
                            prev_y = e_zb
                            for d2 in range(2):
                                e_c1 = rec.op("vector", lambda v, d2=d2: v.tensor_tensor(csc[:, 0, :], pO[:, d2, :], zbf[:, 0, :], OP.mult),
                                              after=[e_zb, e_ol[d2], prev_cmb[d2], prev_y])
                                e_c2 = rec.op("vector", lambda v, d2=d2: v.tensor_tensor(csc[:, 1, :], pO[:, 2 + d2, :], zbf[:, 1, :], OP.mult),
                                              after=[e_c1, e_ol[2 + d2]])
                                ych = 4 * p + 2 * lh + d2
                                e_y = rec.op("vector", lambda v, ych=ych, qc=qc: v.tensor_tensor(
                                    YT[:, ych, qc * 512:(qc + 1) * 512], csc[:, 0, :], csc[:, 1, :], OP.add),
                                    after=[e_c2])
                                prev_cmb[d2] = e_y
                                prev_cmb[2 + d2] = e_y
                                prev_y = e_y
                            bg_pump()
                    rec.emit()
        e_yt_done = rec.now("DV")
        bg_pump(len(bg_chunks))
        e_wo, e_swg, e_swu = e_cast["wo"], e_cast["swg"], e_cast["swu"]
        e_weg, e_weu, e_wed, e_swd = e_cast["weg"], e_cast["weu"], e_cast["wed"], e_cast["swd"]

        rec.stream("yex")

        def emit_send(g):
            for c in range(NCORE):
                with g.If_eq(rank_reg[0], c):
                    off = 1024 if c % 2 == 0 else 0
                    g.dma_start(out=yhalf[:], in_=YT[:, :, off:off + 1024]).then_inc(rec.sems["yex"][0], 16)
                g.end_ifs()
        rec.raw("gpsimd", emit_send, after=[e_yt_done])
        e_send = rec.bump("yex", 16)

        e_ag_y = rec.op("gpsimd", lambda g: g.collective_compute(
            "AllGather", OP.bypass, ins=[yhalf[:]], outs=[ypair[:]], replica_groups=PAIRS),
            after=[e_send], sem="CO", n=1)

        e_fsh_dep = []
        with (
            nc.sbuf_tensor("YF", [128, 16, 1024], BF16) as YF,
            nc.sbuf_tensor("wt5", [128, 2, 16, 512], BF16) as wt5,
            nc.sbuf_tensor("a_sb", [128, 8, C], BF16) as a_sb,
        ):
            rec.stream("yasm")

            def emit_asm(g):
                for c in range(NCORE):
                    with g.If_eq(rank_reg[0], c):
                        if c % 2 == 0:
                            g.dma_start(out=YF[:, 0:8, :], in_=YT[:, :, 0:1024]).then_inc(rec.sems["yasm"][0], 16)
                            g.dma_start(out=YF[:, 8:16, :], in_=ypair[1]).then_inc(rec.sems["yasm"][0], 16)
                        else:
                            g.dma_start(out=YF[:, 0:8, :], in_=ypair[0]).then_inc(rec.sems["yasm"][0], 16)
                            g.dma_start(out=YF[:, 8:16, :], in_=YT[:, :, 1024:2048]).then_inc(rec.sems["yasm"][0], 16)
                    g.end_ifs()
            rec.raw("gpsimd", emit_asm, after=[e_ag_y, e_yt_done])
            e_yf = rec.bump("yasm", 32)

            with nc.psum_tensor("pa", [128, 8, 512], F32) as pa:
                rec.stream("wo")
                prev_ev = {tb: EV0 for tb in range(8)}
                slot_last_mm = [EV0, EV0]
                prev_lw = EV0
                for coq in range(4):
                    sl = coq % 2
                    e_lw = rec.dma("sync", f"wo{sl}", wt5[:, sl, :, :], wo_b[coq],
                                   after=[e_wo, slot_last_mm[sl]])
                    prev_lw = e_lw
                    last_mm = None
                    for cch in range(16):
                        for tb in range(8):
                            aft = [e_lw, e_yf]
                            if cch == 0:
                                aft.append(prev_ev[tb])
                            last_mm = rec.op("tensor", lambda t, tb=tb, cch=cch, sl=sl: t.matmul(
                                pa[:, tb, :], YF[:, cch, tb * 128:(tb + 1) * 128], wt5[:, sl, cch, :],
                                start=(cch == 0), stop=(cch == 15)), after=aft)
                    slot_last_mm[sl] = last_mm
                    pe_now = rec.now("PE")
                    for tb in range(8):
                        prev_ev[tb] = rec.op("scalar", lambda s, tb=tb, coq=coq: s.activation(
                            a_sb[:, tb, coq * 512:(coq + 1) * 512], pa[:, tb, :], AF.Copy, bias=0.0, scale=0.8),
                            after=[pe_now])
                e_a_done = [prev_ev[tb] for tb in range(8)]
            rec.emit()

            with (
                nc.sbuf_tensor("xo", [128, C], F32) as xo,
                nc.sbuf_tensor("lnb", [128, C], F32) as lnb,
                nc.sbuf_tensor("x2t", [128, C], F32) as x2t,
                nc.sbuf_tensor("x3t", [128, C], BF16) as x3t,
                nc.sbuf_tensor("sqd2", [128, C], BF16) as sqd2,
                nc.sbuf_tensor("st2", [128, 8], F32) as st2,
            ):
                e_x3w = []
                prev = EV0
                prev2 = EV0
                for tb in range(8):
                    atile = a_sb[:, tb, :]
                    e_xo = rec.dma("sync", "xow", xo[:, :], x_own[tb * 128:(tb + 1) * 128, :], after=[prev])
                    e_ri = _ln_stats(rec, st2, atile, sqd2, e_a_done[tb], prev, EPS[:, 0:1])
                    e_ln = rec.op("vector", lambda v, atile=atile: v.tensor_scalar(lnb[:, :], atile, st2[:, 2:3], st2[:, 6:7],
                                                                                   OP.subtract, OP.mult), after=[e_ri])
                    e_x2 = rec.op("vector", lambda v: v.tensor_tensor(x2t[:, :], xo[:, :], lnb[:, :], OP.add), after=[e_ln, e_xo, prev2])
                    e_w2 = rec.dma("scalar", "x2w", x2s[tb * 128:(tb + 1) * 128, :], x2t[:, :], after=[e_x2])
                    e_rib = _ln_stats(rec, st2, x2t[:, :], sqd2, e_x2, e_ln, EPS[:, 0:1])
                    e_x3 = rec.op("vector", lambda v: v.tensor_scalar(x3t[:, :], x2t[:, :], st2[:, 2:3], st2[:, 6:7],
                                                                      OP.subtract, OP.mult), after=[e_rib])
                    e_w3 = rec.dma("scalar", "x3w", x3mine[tb * 128:(tb + 1) * 128, :], x3t[:, :], after=[e_x3])
                    e_x3w.append(e_w3)
                    prev = e_w3
                    prev2 = e_w2
                rec.emit()

    e_ag_x3 = rec.op("gpsimd", lambda g: g.collective_compute(
        "AllGather", OP.bypass, ins=[x3mine[:]],
        outs=[x3all[0:TTOT, :].rearrange("(s t) c -> s t c", s=NCORE)],
        replica_groups=ALL8), after=e_x3w, sem="CO", n=1)

    FSH = ctx.enter_context(nc.sbuf_tensor("FSH", [128, 8, C], BF16))
    idxT0 = ctx.enter_context(nc.sbuf_tensor("idxT0", [128, 64], I16))
    idxT1 = ctx.enter_context(nc.sbuf_tensor("idxT1", [128, 64], I16))
    wbt0 = ctx.enter_context(nc.sbuf_tensor("wbt0", [128, 8], F32))
    wbt1 = ctx.enter_context(nc.sbuf_tensor("wbt1", [128, 8], F32))
    IDXT = [idxT0, idxT1]
    WBT = [wbt0, wbt1]
    route_evs = []

    MT = ctx.enter_context(nc.sbuf_tensor("MT", [128, 22, TOK], BF16))
    if True:
        e_mt = []
        with nc.sbuf_tensor("X3T", [128, 2, 16, 512], BF16) as X3T:
            rec.stream("x3T")
            e_x3a = rec.op("gpsimd", lambda g: g.dma_gather(
                X3T[:, 0, :, :], x3mine[:], IDQS[0][:], num_idxs=512, num_idxs_reg=512,
                elem_size=C, transpose=True), after=[e_x3w[-1], e_c["idq0"]], sem="x3T", n=16)
            e_x3b = rec.op("gpsimd", lambda g: g.dma_gather(
                X3T[:, 1, :, :], x3mine[:], IDQS[1][:], num_idxs=512, num_idxs_reg=512,
                elem_size=C, transpose=True), after=[e_x3a, e_c["idq1"]], sem="x3T", n=16)
            e_x3T = [e_x3b] * 16

            with (
                nc.sbuf_tensor("LTs", [16, TOK], F32) as LTs,
                nc.sbuf_tensor("Ltok", [128, 8, 16], F32) as Ltok,
                nc.sbuf_tensor("gsc", [128, 8, 16], F32) as gsc,
                nc.sbuf_tensor("gst", [128, 10], F32) as gst,
                nc.sbuf_tensor("m8s", [128, 8], F32) as m8s,
                nc.sbuf_tensor("WMs", [16, TOK], F32) as WMs,
                nc.psum_tensor("pL", [16, 2, 512], F32) as pL,
                nc.psum_tensor("pTt", [128, 16], F32) as pTt,
                nc.psum_tensor("pwt", [16, 128], F32) as pwt,
            ):
                e_lcp = []
                for tc2 in range(2):
                    last = None
                    for k in range(16):
                        last = rec.op("tensor", lambda t, tc2=tc2, k=k: t.matmul(
                            pL[:, tc2, :], GWT[:, k, :], X3T[:, tc2, k, :],
                            start=(k == 0), stop=(k == 15)), after=[e_gwt, e_x3T[k]])
                    e_lcp.append(rec.op("vector", lambda v, tc2=tc2: v.tensor_copy(
                        LTs[:, tc2 * 512:(tc2 + 1) * 512], pL[:, tc2, :]), after=[last]))

                prev_t = EV0
                prev_w = EV0
                for tb in range(8):
                    e_tr = rec.op("tensor", lambda t, tb=tb: t.transpose(pTt[:, :], LTs[:, tb * 128:(tb + 1) * 128], IDN32[0:16, 0:16]),
                                  after=e_lcp + [e_c["i32"], prev_t])
                    e_ev = rec.op("vector", lambda v, tb=tb: v.tensor_copy(Ltok[:, tb, :], pTt[:, :]), after=[e_tr])
                    prev_t = e_ev
                    lt = Ltok[:, tb, :]
                    e_mx = rec.op("vector", lambda v, lt=lt: v.tensor_reduce(gst[:, 0:1], lt, axis=AX, op=OP.max), after=[e_ev])
                    e_nm = rec.op("vector", lambda v: v.tensor_scalar(gst[:, 1:2], gst[:, 0:1], -1.0, None, OP.mult), after=[e_mx])
                    e_p = rec.op("scalar", lambda s, lt=lt, tb=tb: s.activation(gsc[:, tb, :], lt, AF.Exp, bias=gst[:, 1:2],
                                                                                accum_out=gst[:, 2:3]), after=[e_nm])
                    e_m8 = rec.op("vector", lambda v, tb=tb: v.max(out=m8s[:, :], in_=gsc[:, tb, :]), after=[e_p])
                    e_m = rec.op("vector", lambda v, tb=tb: v.tensor_scalar(Ltok[:, tb, :], gsc[:, tb, :], m8s[:, 3:4], None, OP.is_ge),
                                 after=[e_m8])
                    e_pm = rec.op("vector", lambda v, tb=tb: v.tensor_tensor(gsc[:, tb, :], gsc[:, tb, :], Ltok[:, tb, :], OP.mult),
                                  after=[e_m])
                    e_s4 = rec.op("vector", lambda v, tb=tb: v.tensor_reduce(gst[:, 3:4], gsc[:, tb, :], axis=AX, op=OP.add), after=[e_pm])
                    e_zz = rec.op("vector", lambda v: v.tensor_scalar(gst[:, 4:5], gst[:, 2:3], 1e-20, None, OP.mult), after=[e_p, e_s4])
                    e_dn = rec.op("vector", lambda v: v.tensor_tensor(gst[:, 5:6], gst[:, 3:4], gst[:, 4:5], OP.add), after=[e_zz])
                    e_di = rec.op("vector", lambda v: v.reciprocal(gst[:, 6:7], gst[:, 5:6]), after=[e_dn])
                    e_w = rec.op("vector", lambda v, tb=tb: v.tensor_scalar(gsc[:, tb, :], gsc[:, tb, :], gst[:, 6:7], None, OP.mult),
                                 after=[e_di])
                    e_tw = rec.op("tensor", lambda t, tb=tb: t.transpose(pwt[:, :], gsc[:, tb, :], IDN32[:, :]),
                                  after=[e_w, prev_w])
                    e_wv2 = rec.op("vector", lambda v, tb=tb: v.tensor_copy(WMs[:, tb * 128:(tb + 1) * 128], pwt[:, :]), after=[e_tw])
                    prev_w = e_wv2
                e_wmd = rec.dma("scalar", "wmw", wm_mine[:], WMs[:], after=[prev_w])
                rec.emit()

            e_ag_wm = rec.op("gpsimd", lambda g: g.collective_compute(
                "AllGather", OP.bypass, ins=[wm_mine[:]], outs=[wm_all[:]], replica_groups=ALL8),
                after=[e_wmd], sem="CO", n=1)

            with (
                nc.sbuf_tensor("wst", [128, 2, 16, 128], BF16) as wst,
                nc.sbuf_tensor("gsil", [128, TOK], BF16) as gsil,
                nc.psum_tensor("pm1", [128, 2, 512], F32) as pm1,
                nc.psum_tensor("pm2", [128, 2, 512], F32) as pm2,
            ):
                prev_g = [EV0, EV0]
                prev_mul = EV0
                for ic in range(22):
                    sl = ic % 2
                    e_wg = rec.dma("sync", f"sst{sl}", wst[:, sl, :, :], swg_b[ic],
                                   after=[prev_g[sl], e_swg])
                    lastg = [None, None]
                    for h2 in range(2):
                        for k in range(16):
                            lastg[h2] = rec.op("tensor", lambda t, sl=sl, k=k, h2=h2: t.matmul(
                                pm1[:, h2, :], wst[:, sl, k, :], X3T[:, h2, k, :],
                                start=(k == 0), stop=(k == 15)),
                                after=[e_wg, e_x3T[k]] + ([prev_g[sl]] if k == 0 else []))
                    e_sil = rec.op("scalar", lambda s: s.activation(gsil[:, 0:512], pm1[:, 0, :], AF.Silu), after=[lastg[0], prev_mul])
                    e_sil2 = rec.op("scalar", lambda s: s.activation(gsil[:, 512:1024], pm1[:, 1, :], AF.Silu), after=[lastg[1], prev_mul])
                    e_wu = rec.dma("sync", f"sst{sl}", wst[:, sl, :, :], swu_b[ic],
                                   after=[e_sil, e_sil2, lastg[0], lastg[1], e_swu])
                    lastu = [None, None]
                    for h2 in range(2):
                        for k in range(16):
                            lastu[h2] = rec.op("tensor", lambda t, sl=sl, k=k, h2=h2: t.matmul(
                                pm2[:, h2, :], wst[:, sl, k, :], X3T[:, h2, k, :],
                                start=(k == 0), stop=(k == 15)),
                                after=[e_wu, e_x3T[k]] + ([prev_g[sl]] if k == 0 else []))
                    e_mul1 = rec.op("vector", lambda v, ic=ic: v.tensor_tensor(MT[:, ic, 0:512], pm2[:, 0, :], gsil[:, 0:512], OP.mult),
                                    after=[lastu[0], e_sil])
                    e_mul2 = rec.op("vector", lambda v, ic=ic: v.tensor_tensor(MT[:, ic, 512:1024], pm2[:, 1, :], gsil[:, 512:1024], OP.mult),
                                    after=[lastu[1], e_sil2, e_mul1])
                    prev_g[sl] = e_mul2
                    prev_mul = e_mul2
                    e_mt.append(e_mul2)
                rec.emit()

        with (
            nc.sbuf_tensor("Wfull", [128, TOK], F32) as Wfull,
            nc.sbuf_tensor("mfull", [128, TOK], F32) as mfull,
            nc.sbuf_tensor("ones2", [128, TOK], F32) as ones2,
            nc.sbuf_tensor("Rx", [128, TOK], F32) as Rx,
            nc.sbuf_tensor("keptf", [128, TOK], F32) as keptf,
            nc.sbuf_tensor("TOKc", [128, TOK], F32) as TOKc,
            nc.sbuf_tensor("CSUM", [128, 128], F32) as CSUM,
            nc.sbuf_tensor("vtok", [128, TOK], F32) as vtok,
            nc.sbuf_tensor("vw", [128, TOK], F32) as vw,
            nc.sbuf_tensor("tot", [128, 2], F32) as tot,
            nc.sbuf_tensor("sent", [1, 2 * TOK], F32) as sent,
            nc.sbuf_tensor("sgi", [16, 2, 576], F32) as sgi,
            nc.sbuf_tensor("sgo", [16, 144], F32) as sgo,
            nc.sbuf_tensor("sgw", [16, 144], F32) as sgw,
            nc.sbuf_tensor("nfo", [1, 4], U32) as nfo,
            nc.psum_tensor("pbase", [128, 1], F32) as pbase,
        ):
            e_wf_l = [rec.dma("sync", "wmr", Wfull[s * 16:(s + 1) * 16, :], wm_all[s], after=[e_ag_wm]) for s in range(NCORE)]
            e_wf = [e_wf_l[-1]] * NCORE
            e_tc = rec.dma("sync", "tokc", TOKc[:, :], cTOK128[:])
            e_cs = rec.dma("sync", "csum", CSUM[:, :], cCSUM[:])
            e_mf = rec.op("vector", lambda v: v.tensor_scalar(mfull[:, :], Wfull[:, :], 0.0, None, OP.is_gt), after=e_wf)
            e_o2 = rec.op("vector", lambda v: v.memset(ones2[:, :], 1.0))
            e_sn0 = rec.op("vector", lambda v: v.memset(sent[:, 0:TOK], float(PADROW)))
            e_sn1 = rec.op("vector", lambda v: v.memset(sent[:, TOK:2 * TOK], 0.0))
            e_sc = rec.op("vector", lambda v: v.tensor_tensor_scan(Rx[:, :], ones2[:, :], mfull[:, :], 0.0, OP.mult, OP.add),
                          after=[e_mf, e_o2])
            e_tt = rec.op("vector", lambda v: v.tensor_copy(tot[:, 0:1], Rx[:, TOK - 1:TOK]), after=[e_sc])
            e_bm = rec.op("tensor", lambda t: t.matmul(pbase[:, :], CSUM[:, :], tot[:, 0:1], start=True, stop=True),
                          after=[e_tt, e_cs])
            e_bc2 = rec.op("vector", lambda v: v.tensor_copy(tot[:, 1:2], pbase[:, :]), after=[e_bm])
            e_ad = rec.op("vector", lambda v: v.tensor_scalar(Rx[:, :], Rx[:, :], tot[:, 1:2], None, OP.add), after=[e_bc2])
            e_rx = rec.op("vector", lambda v: v.tensor_tensor(Rx[:, :], Rx[:, :], mfull[:, :], OP.subtract), after=[e_ad])
            e_ltc = rec.op("vector", lambda v: v.tensor_scalar(keptf[:, :], Rx[:, :], float(CAP), None, OP.is_lt), after=[e_rx])
            e_kp = rec.op("vector", lambda v: v.tensor_tensor(keptf[:, :], keptf[:, :], mfull[:, :], OP.mult), after=[e_ltc])
            e_vt0 = rec.op("vector", lambda v: v.tensor_tensor(vtok[:, :], keptf[:, :], TOKc[:, :], OP.mult),
                           after=[e_kp, e_tc])
            e_vt = rec.op("vector", lambda v: v.tensor_scalar(vtok[:, :], vtok[:, :], -1.0, None, OP.add), after=[e_vt0])
            e_vwa = rec.op("vector", lambda v: v.tensor_tensor(vw[:, :], keptf[:, :], Wfull[:, :], OP.mult), after=[e_kp])
            e_vwb = rec.op("vector", lambda v: v.tensor_scalar(TOKc[:, :], keptf[:, :], -1.0, None, OP.add), after=[e_vwa, e_vt0])
            e_vw0 = rec.op("vector", lambda v: v.tensor_tensor(vw[:, :], vw[:, :], TOKc[:, :], OP.add), after=[e_vwb])

            e_st0 = rec.dma("gpsimd", "sent", strm[0:1, TTOT:9216], sent[:, 0:TOK], after=[e_sn0])
            e_st1 = rec.dma("gpsimd", "sent", strm[1:2, TTOT:9216], sent[:, TOK:2 * TOK], after=[e_sn1])

            prev_strm = [e_vt, e_vw0, e_st0, e_st1]
            lib_sg = [False]
            for ei in range(EPC):
                rec.stream("rows")

                def emit_row_ei(g, ei=ei):
                    for c in range(NCORE):
                        with g.If_eq(rank_reg[0], c):
                            r = 2 * c + ei
                            for s in range(NCORE):
                                pr_ = s * 16 + r
                                g.dma_start(out=strm[0:1, s * TOK:(s + 1) * TOK],
                                            in_=vtok[pr_:pr_ + 1, :]).then_inc(rec.sems["rows"][0], 16)
                                g.dma_start(out=strm[1:2, s * TOK:(s + 1) * TOK],
                                            in_=vw[pr_:pr_ + 1, :]).then_inc(rec.sems["rows"][0], 16)
                        g.end_ifs()
                rec.raw("gpsimd", emit_row_ei, after=prev_strm)
                e_rows = rec.bump("rows", 16 * 16)
                e_wrap0 = rec.dma("gpsimd", "wrap", sgi[:, 0, :], strm[0, :].rearrange("(f p) -> p f", p=16), after=[e_rows], allow_slow_non_contiguous=True)
                e_wrap1 = rec.dma("gpsimd", "wrap", sgi[:, 1, :], strm[1, :].rearrange("(f p) -> p f", p=16), after=[e_wrap0], allow_slow_non_contiguous=True)
                if not lib_sg[0]:
                    rec.raw("gpsimd", lambda g: g.load_library(lc.sparse_gather), after=[e_wrap1])
                    lib_sg[0] = True
                e_sg0 = rec.op("gpsimd", lambda g: g.sparse_gather(sgo[:, :], sgi[:, 0, :], num_found=nfo[0:1, 0:1]),
                               after=[e_wrap1])
                e_sg1 = rec.op("gpsimd", lambda g: g.sparse_gather(sgw[:, :], sgi[:, 1, :], num_found=nfo[0:1, 1:2]),
                               after=[e_sg0])
                e_cl0 = rec.op("vector", lambda v: v.tensor_scalar(sgo[:, 0:64], sgo[:, 0:64], 0.0, float(PADROW), OP.max, OP.min),
                               after=[e_sg0])
                e_i16 = rec.op("vector", lambda v, ei=ei: v.tensor_copy(IDXT[ei][0:16, :], sgo[:, 0:64]), after=[e_cl0])
                reps = [e_i16]
                for kk in range(1, 8):
                    reps.append(rec.dma("sync", "irep", IDXT[ei][16 * kk:16 * (kk + 1), :], IDXT[ei][0:16, :], after=[e_i16]))
                e_wb0 = rec.dma("gpsimd", "wbb", wbb[ei].rearrange("(f p) -> p f", p=16), sgw[:, 0:64], after=[e_sg1], allow_slow_non_contiguous=True)
                e_wb1 = rec.dma("gpsimd", "wbb", WBT[ei][:, :], wbb[ei].rearrange("(f p) -> p f", p=128), after=[e_wb0], allow_slow_non_contiguous=True)
                route_evs.append((reps, e_wb1))
                prev_strm = [e_wrap1, e_sg1, e_cl0, e_i16, e_wb0]
            rec.raw("gpsimd", lambda g: g.drain())
            rec.raw("gpsimd", lambda g: g.load_library(lc.mlp), after=[route_evs[-1][1]])
            rec.emit()


    e_scat = []
    with (
        nc.sbuf_tensor("XGT", [128, 2, 16, 512], BF16) as XGT,
        nc.sbuf_tensor("wge", [128, 16, 128], BF16) as wge,
        nc.sbuf_tensor("MT2", [128, 11, CAP], BF16) as MT2,
        nc.sbuf_tensor("gsil2", [128, CAP], BF16) as gsil2,
        nc.sbuf_tensor("eout", [128, 8, C], BF16) as eout,
        nc.sbuf_tensor("wde", [128, 2, 512], BF16) as wde,
    ):
        prev_scatter = EV0
        prev_xgt_users = EV0
        prev_eo_all = EV0
        for ei in range(EPC):
            reps, e_wb = route_evs[ei]
            rec.stream("xg")
            e_xga = rec.op("gpsimd", lambda g, ei=ei: g.dma_gather(
                XGT[:, 0, :, :], x3all[:], IDXT[ei][:, 0:32], num_idxs=512, num_idxs_reg=512,
                elem_size=C, transpose=True),
                after=[e_ag_x3, prev_xgt_users, e_zero[-1]] + reps, sem="xg", n=16)
            e_xg = rec.op("gpsimd", lambda g, ei=ei: g.dma_gather(
                XGT[:, 1, :, :], x3all[:], IDXT[ei][:, 32:64], num_idxs=512, num_idxs_reg=512,
                elem_size=C, transpose=True),
                after=[e_xga], sem="xg", n=16)
            prev_ic = [prev_eo_all, prev_eo_all]
            prev_mul2 = prev_eo_all
            e_mt2 = []
            pgctx = contextlib.ExitStack()
            pg1 = pgctx.enter_context(nc.psum_tensor(f"pg1_{ei}", [128, 2, 512], F32))
            pg2 = pgctx.enter_context(nc.psum_tensor(f"pg2_{ei}", [128, 2, 512], F32))
            for ic in range(11):
                sl = ic % 2
                e_wg2 = rec.dma("sync", "wge", wge[:, :, :], weg_b[ei, ic],
                                after=[prev_ic[0], prev_ic[1], e_weg])
                lastg = [None, None]
                for h2 in range(2):
                    for k in range(16):
                        lastg[h2] = rec.op("tensor", lambda t, k=k, h2=h2: t.matmul(
                            pg1[:, h2, :], wge[:, k, :], XGT[:, h2, k, :],
                            start=(k == 0), stop=(k == 15)),
                            after=[e_wg2, e_xg] + ([prev_ic[sl]] if k == 0 else []))
                e_si1 = rec.op("scalar", lambda s: s.activation(gsil2[:, 0:512], pg1[:, 0, :], AF.Silu), after=[lastg[0], prev_mul2])
                e_si2 = rec.op("scalar", lambda s: s.activation(gsil2[:, 512:1024], pg1[:, 1, :], AF.Silu), after=[lastg[1], prev_mul2])
                e_wu2 = rec.dma("sync", "wge", wge[:, :, :], weu_b[ei, ic],
                                after=[e_si1, e_si2, lastg[0], lastg[1], e_weu])
                lastu = [None, None]
                for h2 in range(2):
                    for k in range(16):
                        lastu[h2] = rec.op("tensor", lambda t, k=k, h2=h2: t.matmul(
                            pg2[:, h2, :], wge[:, k, :], XGT[:, h2, k, :],
                            start=(k == 0), stop=(k == 15)),
                            after=[e_wu2, e_xg])
                e_m1 = rec.op("vector", lambda v, ic=ic: v.tensor_tensor(MT2[:, ic, 0:512], pg2[:, 0, :], gsil2[:, 0:512], OP.mult),
                              after=[lastu[0], e_si1])
                e_m2 = rec.op("vector", lambda v, ic=ic: v.tensor_tensor(MT2[:, ic, 512:1024], pg2[:, 1, :], gsil2[:, 512:1024], OP.mult),
                              after=[lastu[1], e_si2, e_m1])
                prev_ic[sl] = e_m2
                prev_mul2 = e_m2
                e_mt2.append(e_m2)
            prev_xgt_users = e_mt2[-1]
            pgctx.close()

            pectx = contextlib.ExitStack()
            pe8 = pectx.enter_context(nc.psum_tensor(f"pe8_{ei}", [128, 8, 512], F32))
            prev_eo = {sb: e_mt2[-1] for sb in range(8)}
            slot_last2 = [EV0, EV0]
            for coq in range(4):
                for ic in range(11):
                    sl = ic % 2
                    e_ldd = rec.dma("sync", f"wde{sl}", wde[:, sl, :], wed_b[ei, ic, coq],
                                    after=[slot_last2[sl], e_wed])
                    last_mm = None
                    for sb in range(8):
                        aft = [e_ldd, e_mt2[ic]]
                        if ic == 0:
                            aft.append(prev_eo[sb])
                        last_mm = rec.op("tensor", lambda t, sb=sb, ic=ic, sl=sl: t.matmul(
                            pe8[:, sb, :], MT2[:, ic, sb * 128:(sb + 1) * 128], wde[:, sl, :],
                            start=(ic == 0), stop=(ic == 10)), after=aft)
                    slot_last2[sl] = last_mm
                pe_now = rec.now("PE")
                for sb in range(8):
                    prev_eo[sb] = rec.op("scalar", lambda s, sb=sb, coq=coq, ei=ei: s.activation(
                        eout[:, sb, coq * 512:(coq + 1) * 512], pe8[:, sb, :], AF.Copy, bias=0.0,
                        scale=WBT[ei][:, sb:sb + 1]), after=[pe_now, e_wb, prev_scatter])
            e_eo = [prev_eo[sb] for sb in range(8)]
            rec.stream("scat")
            e_sc2 = rec.op("gpsimd", lambda g, ei=ei: g.dma_scatter_add(
                yp[:], eout[:, :, :], IDXT[ei][:, :], num_idxs=CAP, num_idxs_reg=CAP, elem_size=C),
                after=e_eo + e_zero + [prev_scatter] + reps, sem="scat", n=16)
            prev_scatter = e_sc2
            prev_eo_all = e_sc2
            e_scat.append(e_sc2)
            pectx.close()
        rec.emit()

    e_rs = rec.op("gpsimd", lambda g: g.collective_compute(
        "ReduceScatter", OP.add, ins=[yp[0:TTOT, :]], outs=[y_rs[:]], replica_groups=ALL8),
        after=e_scat, sem="CO", n=1)

    with (
        nc.sbuf_tensor("wds", [128, 2, 512], BF16) as wds,
        nc.psum_tensor("pf", [128, 8, 512], F32) as pf,
    ):
        prev_f = {tb: EV0 for tb in range(8)}
        slot_last = [EV0, EV0]
        for coq in range(4):
            for ic in range(22):
                sl = ic % 2
                e_ld = rec.dma("sync", f"swd{sl}", wds[:, sl, :], swd_b[ic, coq],
                               after=[slot_last[sl], e_swd])
                last_mm = None
                for tb in range(8):
                    aft = [e_ld, e_mt[ic]]
                    if ic == 0:
                        aft.append(prev_f[tb])
                    last_mm = rec.op("tensor", lambda t, tb=tb, ic=ic, sl=sl, coq=coq: t.matmul(
                        pf[:, tb, :], MT[:, ic, tb * 128:(tb + 1) * 128], wds[:, sl, :],
                        start=(ic == 0), stop=(ic == 21)), after=aft)
                slot_last[sl] = last_mm
            pe_now = rec.now("PE")
            for tb in range(8):
                prev_f[tb] = rec.op("scalar", lambda s, tb=tb, coq=coq: s.activation(
                    FSH[:, tb, coq * 512:(coq + 1) * 512], pf[:, tb, :], AF.Copy, bias=0.0, scale=1.0),
                    after=[pe_now])
        e_fsh = [prev_f[tb] for tb in range(8)]
        rec.emit()


    with (
        nc.sbuf_tensor("ym", [128, C], BF16) as ym,
        nc.sbuf_tensor("ff", [128, C], F32) as ff,
        nc.sbuf_tensor("x2r", [128, C], F32) as x2r,
        nc.sbuf_tensor("oo", [128, C], F32) as oo,
        nc.sbuf_tensor("sqd3", [128, C], BF16) as sqd3,
        nc.sbuf_tensor("st3", [128, 8], F32) as st3,
    ):
        e_outs = []
        prev = EV0
        for tb in range(8):
            e_ym = rec.dma("sync", "ymr", ym[:, :], y_rs[tb * 128:(tb + 1) * 128, :], after=[e_rs, prev])
            e_x2r = rec.dma("sync", "x2r", x2r[:, :], x2s[tb * 128:(tb + 1) * 128, :], after=[prev, ("x2w", 128)])
            e_ff = rec.op("vector", lambda v, tb=tb: v.tensor_tensor(ff[:, :], ym[:, :], FSH[:, tb, :], OP.add),
                          after=[e_ym, e_fsh[tb], prev])
            e_ri = _ln_stats(rec, st3, ff[:, :], sqd3, e_ff, prev, EPS[:, 0:1])
            e_ln = rec.op("vector", lambda v: v.tensor_scalar(ff[:, :], ff[:, :], st3[:, 2:3], st3[:, 6:7],
                                                              OP.subtract, OP.mult), after=[e_ri])
            e_o = rec.op("vector", lambda v: v.tensor_tensor(oo[:, :], x2r[:, :], ff[:, :], OP.add), after=[e_ln, e_x2r])
            e_w = rec.dma("sync", "outw", out_shard[tb * 128:(tb + 1) * 128, :], oo[:, :], after=[e_o])
            e_outs.append(e_w)
            prev = e_w
        rec.raw("sync", lambda s: None, after=e_outs)
        rec.raw("gpsimd", lambda g: None, after=e_outs)
        rec.emit()

    ctx.close()
    return nc


def kernel(**inputs):
    x = np.ascontiguousarray(inputs["L0_attn_x"])
    wq = np.ascontiguousarray(inputs["L0_attn_wq"])
    wk = np.ascontiguousarray(inputs["L0_attn_wk"])
    wv = np.ascontiguousarray(inputs["L0_attn_wv"])
    wo = np.ascontiguousarray(inputs["L0_attn_wo"])
    lqk = np.ascontiguousarray(np.stack([inputs["L0_attn_lq1"], inputs["L0_attn_lk1"],
                                         inputs["L0_attn_lq2"], inputs["L0_attn_lk2"]], axis=0))
    gate = np.ascontiguousarray(inputs["L0_ffn_gate_w"])
    wg = np.ascontiguousarray(inputs["L0_ffn_w_gate"])
    wu = np.ascontiguousarray(inputs["L0_ffn_w_up"])
    wd = np.ascontiguousarray(inputs["L0_ffn_w_down"])
    swg_i = np.ascontiguousarray(inputs["L0_ffn_sw_gate"])
    swu_i = np.ascontiguousarray(inputs["L0_ffn_sw_up"])
    swd_i = np.ascontiguousarray(inputs["L0_ffn_sw_down"])

    nc = build_nc()
    lower_extended_insts(nc)

    in_maps = []
    for c in range(NCORE):
        b, half = c // 2, c % 2
        hr = slice(1024 * half, 1024 * (half + 1))
        inv = (1.0 / 10000.0) ** (np.arange(0, HH, 2, dtype=np.float64) / HH)
        rot = np.zeros((128, 8), np.float32)
        for lh in range(4):
            h = 4 * half + lh
            th = h * inv
            rot[0:64, lh] = np.cos(th)
            rot[64:128, lh] = np.cos(th)
            rot[0:64, 4 + lh] = np.sin(th)
            rot[64:128, 4 + lh] = -np.sin(th)
        def tile_w(m):
            n = m.shape[0] // 512
            return np.ascontiguousarray(m.reshape(n, 512, 16, 128).transpose(0, 3, 2, 1))

        wg2 = wg[2 * c:2 * c + 2]
        wu2 = wu[2 * c:2 * c + 2]
        wd2 = wd[2 * c:2 * c + 2]
        in_maps.append({
            "xb": x[b],
            "rotpq": rot,
            "x_own": np.ascontiguousarray(x[b][1024 * half:1024 * (half + 1)]),
            "wq_t": tile_w(wq[hr]),
            "wk_t": tile_w(wk[hr]),
            "wv_t": tile_w(wv[hr]),
            "wo_t": tile_w(wo),
            "lqk": lqk,
            "gate_w": gate,
            "weg": np.ascontiguousarray(wg2.reshape(2, 16, 128, 11, 128).transpose(0, 3, 2, 1, 4)),
            "weu": np.ascontiguousarray(wu2.reshape(2, 16, 128, 11, 128).transpose(0, 3, 2, 1, 4)),
            "wed": np.ascontiguousarray(wd2.reshape(2, 11, 128, 4, 512).transpose(0, 1, 3, 2, 4)),
            "swg": np.ascontiguousarray(swg_i.reshape(16, 128, 22, 128).transpose(2, 1, 0, 3)),
            "swu": np.ascontiguousarray(swu_i.reshape(16, 128, 22, 128).transpose(2, 1, 0, 3)),
            "swd": np.ascontiguousarray(swd_i.reshape(22, 128, 4, 512).transpose(0, 2, 1, 3)),
        })
    res = run_bass_kernel_spmd(nc, in_maps, core_ids=list(range(NCORE)))
    shards = [res.results[c]["out_shard"] for c in range(NCORE)]
    out = np.concatenate(shards, axis=0).reshape(B, T, C)
    return out


# version 05
# speedup vs baseline: 1.1880x; 1.0699x over previous
import contextlib
import math

import numpy as np
import ml_dtypes

import concourse.bass as bass
import concourse.mybir as mybir
from concourse.alu_op_type import AluOpType
from concourse import library_config as lc
from concourse.bass_utils import run_bass_kernel_spmd
from concourse.library_overlay import lower_extended_insts

F32 = mybir.dt.float32
BF16 = mybir.dt.bfloat16
I16 = mybir.dt.int16
U32 = mybir.dt.uint32
AF = mybir.ActivationFunctionType
OP = AluOpType
AX = mybir.AxisListType.X

B, T, C = 4, 2048, 2048
NCORE = 8
TOK = 1024
HH, HD = 128, 256
E, CAP = 16, 1024
IEXP, ISH = 1408, 2816
EPC = 2
LN_EPS = 1e-5
RMS_EPS = 1.1920929e-07
LAMBDA_INIT = 0.2
ISQ = 1.0 / math.sqrt(128.0)
TTOT = B * T
PADROW = TTOT

PAIRS = [[0, 1], [2, 3], [4, 5], [6, 7]]
ALL8 = [list(range(NCORE))]

ENGINES = ("tensor", "vector", "scalar", "gpsimd", "sync")
CHAIN = {"tensor": "PE", "vector": "DV", "scalar": "AC", "gpsimd": "PL", "sync": "SY"}
QSEM = {"sync": "QS", "scalar": "QA", "gpsimd": "QG"}


class Rec:

    def __init__(self, nc, ctx):
        self.nc = nc
        self.ctx = ctx
        self.sems = {}
        for name in ["PE", "DV", "AC", "PL", "SY", "CO"]:
            self.sems[name] = [ctx.enter_context(nc.semaphore(name)), 0]
        self.ops = {e: [] for e in ENGINES}
        self.hw = {e: {} for e in ENGINES}
        self.drained = {e: 0 for e in ENGINES}
        self.block_start = {}
        self.bg = {"CO"}

    def stream(self, name, bg=False):
        if name not in self.sems:
            self.sems[name] = [self.ctx.enter_context(self.nc.semaphore(name)), 0]
        if bg:
            self.bg.add(name)
        return self.sems[name]

    def now(self, kind):
        return (kind, self.sems[kind][1])

    def _after(self, eng, after):
        waits, need_drain = [], False
        for ev in after:
            if ev is None:
                continue
            k, v = ev
            if v <= 0:
                continue
            if k == CHAIN[eng]:
                if v > self.drained[eng]:
                    need_drain = True
            else:
                waits.append((k, v))
        if need_drain:
            self.drained[eng] = self.sems[CHAIN[eng]][1]
        wmax = {}
        for (k, v) in waits:
            wmax[k] = max(wmax.get(k, 0), v)
        return sorted(wmax.items()), need_drain

    def op(self, eng, fn, after=(), sem=None, n=1):
        kind = sem or CHAIN[eng]
        waits, need_drain = self._after(eng, after)
        self.sems[kind][1] += n
        val = self.sems[kind][1]
        self.ops[eng].append((fn, waits, need_drain, kind, n))
        return (kind, val)

    def dma(self, eng, stream, out, in_, after=(), bg=False, **kw):
        self.stream(stream, bg=bg)
        return self.op(eng, lambda e: e.dma_start(out=out, in_=in_, **kw),
                       after=after, sem=stream, n=16)

    def raw(self, eng, fn, after=()):
        waits, need_drain = self._after(eng, after)
        self.ops[eng].append((fn, waits, need_drain, None, 0))

    def bump(self, kind, n):
        self.sems[kind][1] += n
        return (kind, self.sems[kind][1])

    def emit(self):
        nc = self.nc
        if not any(self.ops[e] for e in ENGINES):
            return
        fence = [(k, v) for k, v in self.block_start.items()
                 if v > 0 and k not in self.bg]
        with nc.Block() as blk:
            for eng_name in ENGINES:
                ops = self.ops[eng_name]
                if not ops:
                    continue

                def make_body(eng_name, ops):
                    def body(eng):
                        hw = self.hw[eng_name]
                        for (k, v) in fence:
                            if hw.get(k, 0) < v:
                                eng.wait_ge(self.sems[k][0], v)
                                hw[k] = v
                        for (fn, waits, need_drain, kind, n) in ops:
                            for (k, v) in waits:
                                if hw.get(k, 0) < v:
                                    eng.wait_ge(self.sems[k][0], v)
                                    hw[k] = v
                            if need_drain:
                                eng.drain()
                            inst = fn(eng)
                            if kind is not None and inst is not None:
                                inst.then_inc(self.sems[kind][0], n)
                    return body

                getattr(blk, eng_name)(make_body(eng_name, ops))
        self.ops = {e: [] for e in ENGINES}
        self.block_start = {k: s[1] for k, s in self.sems.items()
                            if k not in ("PE", "DV", "AC", "PL", "SY")}


def _consts():
    ku = np.arange(896)[None, :] >= (np.arange(128)[:, None] + 384)
    stair = np.asarray(ku.astype(ml_dtypes.bfloat16))
    idnb = np.asarray(np.eye(128, dtype=ml_dtypes.bfloat16))
    idn32 = np.eye(128, dtype=np.float32)
    onesb = np.asarray(np.ones((128, 1), dtype=ml_dtypes.bfloat16))
    tok128 = (np.arange(128)[:, None] % 8 * 1024 + np.arange(TOK)[None, :] + 1).astype(np.float32)
    pp = np.arange(128)
    csum = ((pp[:, None] // 8 == pp[None, :] // 8) & (pp[:, None] % 8 < pp[None, :] % 8)).astype(np.float32)
    idqs = []
    for b4 in range(4):
        w = np.zeros((128, 32), np.int16)
        for i in range(512):
            for g in range(8):
                w[(i % 16) + 16 * g, i // 16] = 512 * b4 + i
        idqs.append(w)
    return stair, idnb, idn32, onesb, tok128, csum, idqs


def _ln_stats(rec, stat, src_ap, sq_scratch, e_src, prev, eps_ap=None):
    e_s1 = rec.op("scalar", lambda s: s.activation(sq_scratch[:, :], src_ap, AF.Square, accum_out=stat[:, 1:2]),
                  after=[e_src, prev])
    e_s2 = rec.op("vector", lambda v: v.tensor_reduce(stat[:, 0:1], src_ap, axis=AX, op=OP.add),
                  after=[e_src, prev])
    e_m = rec.op("vector", lambda v: v.tensor_scalar(stat[:, 2:3], stat[:, 0:1], 1.0 / C, None, OP.mult), after=[e_s2])
    e_v1 = rec.op("vector", lambda v: v.tensor_tensor(stat[:, 3:4], stat[:, 2:3], stat[:, 2:3], OP.mult), after=[e_m])
    e_v2 = rec.op("vector", lambda v: v.scalar_tensor_tensor(stat[:, 4:5], stat[:, 1:2], 1.0 / C, stat[:, 3:4],
                                                             OP.mult, OP.subtract), after=[e_v1, e_s1])
    e_sd = rec.op("scalar", lambda s: s.activation(stat[:, 5:6], stat[:, 4:5], AF.Sqrt, bias=eps_ap), after=[e_v2])
    e_ri = rec.op("vector", lambda v: v.reciprocal(stat[:, 6:7], stat[:, 5:6]), after=[e_sd])
    return e_ri


def build_nc():
    nc = bass.Bass("TRN2", target_bir_lowering=False)
    ctx = contextlib.ExitStack()
    dram = lambda name, shape, dt, kind="Internal": nc.dram_tensor(name, shape, dt, kind=kind).ap()

    xb = dram("xb", [T, C], F32, "ExternalInput")
    x_own = dram("x_own", [TOK, C], F32, "ExternalInput")
    wq_t = dram("wq_t", [2, 128, 16, 512], F32, "ExternalInput")
    wk_t = dram("wk_t", [2, 128, 16, 512], F32, "ExternalInput")
    wv_t = dram("wv_t", [2, 128, 16, 512], F32, "ExternalInput")
    wo_t = dram("wo_t", [4, 128, 16, 512], F32, "ExternalInput")
    lqk = dram("lqk", [4, HD], F32, "ExternalInput")
    rotpq = dram("rotpq", [128, 8], F32, "ExternalInput")
    gate_w = dram("gate_w", [E, C], F32, "ExternalInput")
    weg = dram("weg", [EPC, 11, 128, 16, 128], F32, "ExternalInput")
    weu = dram("weu", [EPC, 11, 128, 16, 128], F32, "ExternalInput")
    wed = dram("wed", [EPC, 11, 4, 128, 512], F32, "ExternalInput")
    swg = dram("swg", [22, 128, 16, 128], F32, "ExternalInput")
    swu = dram("swu", [22, 128, 16, 128], F32, "ExternalInput")
    swd = dram("swd", [22, 4, 128, 512], F32, "ExternalInput")
    out_shard = dram("out_shard", [TOK, C], F32, "ExternalOutput")

    stair_h, idnb_h, idn32_h, onesb_h, tok128_h, csum_h, idqs_h = _consts()
    cSTAIR = nc.inline_tensor(stair_h, name="cSTAIR").ap()
    cIDNB = nc.inline_tensor(idnb_h, name="cIDNB").ap()
    cIDN32 = nc.inline_tensor(idn32_h, name="cIDN32").ap()
    cONESB = nc.inline_tensor(onesb_h, name="cONESB").ap()
    cTOK128 = nc.inline_tensor(tok128_h, name="cTOK128").ap()
    cIDQ = [nc.inline_tensor(idqs_h[b4], name=f"cIDQ{b4}").ap() for b4 in range(4)]
    cCSUM = nc.inline_tensor(csum_h, name="cCSUM").ap()

    x1s = dram("x1s", [T, C], BF16)
    wq_b = dram("wq_b", [2, 128, 16, 512], BF16)
    wk_b = dram("wk_b", [2, 128, 16, 512], BF16)
    wv_b = dram("wv_b", [2, 128, 16, 512], BF16)
    wo_b = dram("wo_b", [4, 128, 16, 512], BF16)
    weg_b = dram("weg_b", [EPC, 11, 128, 16, 128], BF16)
    weu_b = dram("weu_b", [EPC, 11, 128, 16, 128], BF16)
    wed_b = dram("wed_b", [EPC, 11, 4, 128, 512], BF16)
    swg_b = dram("swg_b", [22, 128, 16, 128], BF16)
    swu_b = dram("swu_b", [22, 128, 16, 128], BF16)
    swd_b = dram("swd_b", [22, 4, 128, 512], BF16)
    x2s = dram("x2s", [TOK, C], F32)
    yhalf = dram("yhalf", [128, 8, 1024], BF16)
    ypair = dram("ypair", [2, 128, 8, 1024], BF16)
    x3mine = dram("x3mine", [TOK, C], BF16)
    x3all = dram("x3all", [TTOT + 1, C], BF16)
    wm_mine = dram("wm_mine", [E, TOK], F32)
    wm_all = dram("wm_all", [NCORE, E, TOK], F32)
    yp = dram("yp", [TTOT + 1, C], BF16)
    y_rs = dram("y_rs", [TOK, C], BF16)
    strm = dram("strm", [2, 9216], F32)
    wbb = dram("wbb", [EPC, CAP], F32)

    sbt = lambda name, shape, dt: ctx.enter_context(nc.sbuf_tensor(name, shape, dt))
    ROTS = sbt("ROTS", [128, 8], F32)
    IDQS = [sbt(f"IDQS{b4}", [128, 32], I16) for b4 in range(4)]
    STAIR = sbt("STAIR", [128, 896], BF16)
    IDNB = sbt("IDNB", [128, 128], BF16)
    IDN32 = sbt("IDN32", [128, 128], F32)
    ONESB = sbt("ONESB", [128, 1], BF16)
    ZEROT = sbt("ZEROT", [128, 4096], BF16)
    LAM = sbt("LAM", [1, 4], F32)
    GWT = sbt("GWT", [128, 16, 16], BF16)
    EPS = sbt("EPS", [128, 2], F32)

    rec = Rec(nc, ctx)
    EV0 = ("PE", 0)

    rank_reg = [None]

    def init_g(g):
        g.load_library(lc.mlp)
        rank_reg[0] = g.to_reg(g.partition_id())
    rec.raw("gpsimd", init_g)


    e_c = {}
    for nm, dst, src in (("st", STAIR, cSTAIR), ("ib", IDNB, cIDNB),
                         ("i32", IDN32, cIDN32), ("ob", ONESB, cONESB)):
        e_c[nm] = rec.dma("sync", "cst_" + nm, dst[:], src[:])
    e_c["rot"] = rec.dma("sync", "cst_rot", ROTS[:], rotpq[:])
    for b4 in range(4):
        e_c[f"idq{b4}"] = rec.dma("sync", f"cst_idq{b4}", IDQS[b4][:], cIDQ[b4][:])
    e_zt = rec.op("vector", lambda v: v.memset(ZEROT[:], 0.0))
    rec.op("vector", lambda v: v.memset(EPS[:, 0:1], LN_EPS))
    rec.op("vector", lambda v: v.memset(EPS[:, 1:2], RMS_EPS))

    ZELEM = 128 * 4096
    flat_yp = yp.rearrange("r c -> (r c)")
    nzp = ((TTOT + 1) * C) // ZELEM
    zero_chunks = []
    for z in range(nzp):
        zero_chunks.append((flat_yp[z * ZELEM:(z + 1) * ZELEM].rearrange("(p f) -> p f", p=128), ZEROT[:]))
    rem = (TTOT + 1) * C - nzp * ZELEM
    if rem:
        zero_chunks.append((flat_yp[nzp * ZELEM:].rearrange("(p f) -> p f", p=128),
                            ZEROT[:, 0:rem // 128]))
    zero_chunks.append((x3all[PADROW:PADROW + 1, :], ZEROT[0:1, 0:C]))

    lvs = sbt("lvs", [1, 4 * HD], F32)
    lpr = sbt("lpr", [1, 2 * HD], F32)
    lred = sbt("lred", [1, 4], F32)
    e_lv = rec.dma("sync", "lqk", lvs[:], lqk.rearrange("a b -> (a b)").rearrange("(o f) -> o f", o=1))
    e_p1 = rec.op("vector", lambda v: v.tensor_tensor(lpr[:, 0:HD], lvs[:, 0:HD], lvs[:, HD:2 * HD], OP.mult), after=[e_lv])
    e_p2 = rec.op("vector", lambda v: v.tensor_tensor(lpr[:, HD:2 * HD], lvs[:, 2 * HD:3 * HD], lvs[:, 3 * HD:4 * HD], OP.mult), after=[e_lv, e_p1])
    e_r1 = rec.op("vector", lambda v: v.tensor_reduce(lred[:, 0:1], lpr[:, 0:HD], axis=AX, op=OP.add), after=[e_p2])
    e_r2 = rec.op("vector", lambda v: v.tensor_reduce(lred[:, 1:2], lpr[:, HD:2 * HD], axis=AX, op=OP.add), after=[e_r1])
    e_ex = rec.op("scalar", lambda s: s.activation(lred[:, 0:2], lred[:, 0:2], AF.Exp), after=[e_r2])
    e_l0 = rec.op("vector", lambda v: v.tensor_tensor(LAM[0:1, 0:1], lred[:, 0:1], lred[:, 1:2], OP.subtract), after=[e_ex])
    e_l1 = rec.op("vector", lambda v: v.tensor_scalar(LAM[0:1, 0:1], LAM[0:1, 0:1], LAMBDA_INIT, None, OP.add), after=[e_l0])
    e_lam = rec.op("vector", lambda v: v.tensor_scalar(LAM[0:1, 1:2], LAM[0:1, 0:1], -1.0, None, OP.mult), after=[e_l1])

    gwsb = sbt("gwsb", [16, C], BF16)
    e_gw = rec.dma("gpsimd", "gwl", gwsb[:], gate_w[:])
    with nc.psum_tensor("pgt", [128, 16], BF16) as pgt:
        ev_prev = EV0
        for k in range(16):
            e_t = rec.op("tensor", lambda t, k=k: t.transpose(pgt[:, :], gwsb[:, k * 128:(k + 1) * 128], IDNB[0:16, 0:16]),
                         after=[e_gw, e_c["ib"], ev_prev])
            ev_prev = rec.op("vector", lambda v, k=k: v.tensor_copy(GWT[:, k, :], pgt[:, :]), after=[e_t])
        e_gwt = ev_prev
        rec.emit()

    with (
        nc.sbuf_tensor("xt", [128, 2, C], F32) as xt,
        nc.sbuf_tensor("x1t", [128, 2, C], BF16) as x1t,
        nc.sbuf_tensor("sqd", [128, C], BF16) as sqd,
        nc.sbuf_tensor("st8", [128, 8], F32) as st8,
    ):
        e_wv = rec.dma("gpsimd", "c_wv", wv_b[:], wv_t[:], bg=True)
        e_wq = rec.dma("gpsimd", "c_wq", wq_b[:], wq_t[:], bg=True)
        e_wk = rec.dma("gpsimd", "c_wk", wk_b[:], wk_t[:], bg=True)
        prev_use = [EV0, EV0]
        prev_stat = EV0
        last_w = EV0
        for i in range(16):
            sl = i % 2
            e_x = rec.dma("sync", f"xls{sl}", xt[:, sl, :], xb[i * 128:(i + 1) * 128, :], after=[prev_use[sl]])
            e_ri = _ln_stats(rec, st8, xt[:, sl, :], sqd, e_x, prev_stat, EPS[:, 0:1])
            e_ap = rec.op("vector", lambda v, sl=sl: v.tensor_scalar(x1t[:, sl, :], xt[:, sl, :], st8[:, 2:3], st8[:, 6:7],
                                                                     OP.subtract, OP.mult), after=[e_ri])
            prev_stat = e_ap
            e_w = rec.dma("scalar", f"x1w{sl}", x1s[i * 128:(i + 1) * 128, :], x1t[:, sl, :], after=[e_ap])
            prev_use[sl] = e_w
            last_w = e_w
        e_x1_done = [prev_use[0], prev_use[1]]
        rec.emit()

    e_cast = {}
    bg_chunks = []
    for i in range(4):
        bg_chunks.append(("wo", wo_b[i], wo_t[i]))
    for lohi in range(2):
        bg_chunks.append(("swg", swg_b[11 * lohi:11 * (lohi + 1)], swg[11 * lohi:11 * (lohi + 1)]))
        bg_chunks.append(("swu", swu_b[11 * lohi:11 * (lohi + 1)], swu[11 * lohi:11 * (lohi + 1)]))
    for nm, db, sb_ in (("weg", weg_b, weg), ("weu", weu_b, weu), ("wed", wed_b, wed)):
        for ei_ in range(EPC):
            for lohi in range(2):
                sl_ = slice(6 * lohi, 6 + 5 * lohi)
                bg_chunks.append((nm, db[ei_, sl_], sb_[ei_, sl_]))
    for lohi in range(2):
        bg_chunks.append(("swd", swd_b[11 * lohi:11 * (lohi + 1)], swd[11 * lohi:11 * (lohi + 1)]))
    for dst, src_ in zero_chunks:
        bg_chunks.append(("zero", dst, src_, e_zt))

    def bg_pump(k=1):
        for _ in range(k):
            if bg_chunks:
                ch = bg_chunks.pop(0)
                key, dst, src = ch[0], ch[1], ch[2]
                aft = [ch[3]] if len(ch) > 3 else ()
                e_cast[key] = rec.dma("gpsimd", "cast_" + key, dst, src, bg=True, after=aft)

    with nc.sbuf_tensor("YT", [128, 8, T], BF16) as YT:
        for p in range(2):
            with (
                nc.sbuf_tensor(f"QT{p}", [128, 4, T], BF16) as QT,
                nc.sbuf_tensor(f"KT{p}", [128, 4, T], BF16) as KT,
                nc.sbuf_tensor(f"VS{p}", [128, 16, 512], BF16) as VS,
                nc.sbuf_tensor(f"WTS{p}", [128, 16, 512], BF16) as WTS,
            ):
                with nc.sbuf_tensor(f"X1T{p}", [128, 4, 16, 512], BF16) as X1T:
                    rec.stream("x1T")
                    e_x1T_l = []
                    prevg = e_x1_done
                    for g4 in range(4):
                        ev = rec.op("gpsimd", lambda g, g4=g4: g.dma_gather(
                            X1T[:, g4, :, :], x1s[:], IDQS[g4][:], num_idxs=512, num_idxs_reg=512,
                            elem_size=C, transpose=True),
                            after=list(prevg) + [e_c[f"idq{g4}"]] if g4 == 0 else [prevg, e_c[f"idq{g4}"]],
                            sem="x1T", n=16)
                        e_x1T_l.append(ev)
                        prevg = ev
                    e_x1T = [e_x1T_l[-1]] * 16
                    e_wts_1 = rec.dma("scalar", "wts", WTS[:, :, :], wv_b[p], after=[e_wv])
                    e_wts = [e_wts_1] * 16
                    e_vs = {}
                    with nc.psum_tensor(f"pv{p}", [128, 8, 512], F32) as pv:
                        for tb in range(16):
                            j = tb % 8
                            last = None
                            for cc in range(16):
                                aft = [e_x1T[cc], e_wts[cc]]
                                if cc == 0 and tb >= 8:
                                    aft.append(e_vs[tb - 8])
                                last = rec.op("tensor", lambda t, j=j, tb=tb, cc=cc: t.matmul(
                                    pv[:, j, :], X1T[:, tb // 4, cc, (tb % 4) * 128:(tb % 4 + 1) * 128], WTS[:, cc, :],
                                    start=(cc == 0), stop=(cc == 15)), after=aft)
                            e_vs[tb] = rec.op("vector", lambda v, j=j, tb=tb: v.tensor_copy(VS[:, tb, :], pv[:, j, :]),
                                              after=[last])
                    e_vs_all = [e_vs[tb] for tb in range(16)]
                    e_vs_last = rec.now("DV")
                    bg_pump()

                    def qk_proj(dst, w_b, e_w_ready, e_free, nm, init_ev):
                        e_w_1 = rec.dma("scalar", "wts", WTS[:, :, :], w_b[p], after=[e_w_ready, e_free])
                        e_w = [e_w_1] * 16
                        evs = []
                        with nc.psum_tensor(nm, [128, 2, 512], F32) as pqk:
                            prev = [init_ev, init_ev]
                            for cc in range(4):
                                for tc in range(4):
                                    sl = (cc * 4 + tc) % 2
                                    last = None
                                    for k in range(16):
                                        aft = [e_w[k], e_x1T[k]]
                                        if k == 0:
                                            aft.append(prev[sl])
                                        last = rec.op("tensor", lambda t, cc=cc, tc=tc, k=k, sl=sl: t.matmul(
                                            pqk[:, sl, :], WTS[:, k, cc * 128:(cc + 1) * 128],
                                            X1T[:, tc, k, :],
                                            start=(k == 0), stop=(k == 15)), after=aft)
                                    prev[sl] = rec.op("vector", lambda v, cc=cc, tc=tc, sl=sl: v.tensor_copy(
                                        dst[:, cc, tc * 512:(tc + 1) * 512], pqk[:, sl, :]), after=[last])
                                    evs.append(prev[sl])
                        return evs

                    e_q_evs = qk_proj(QT, wq_b, e_wq, rec.now("PE"), f"pqkq{p}", e_vs[15])
                    bg_pump()
                    e_k_evs = qk_proj(KT, wk_b, e_wk, rec.now("PE"), f"pqkk{p}", e_q_evs[-1])
                    bg_pump()
                rec.emit()

                rope_ev = {}
                with (
                    nc.sbuf_tensor(f"sqr{p}", [128, T], BF16) as sqr,
                    nc.sbuf_tensor(f"rr{p}", [1, 2 * T], F32) as rr,
                    nc.sbuf_tensor(f"rrb{p}", [1, T], BF16) as rrb,
                    nc.sbuf_tensor(f"rb{p}", [128, T], BF16) as rb,
                    nc.sbuf_tensor(f"TMPA{p}", [128, T], BF16) as TMPA,
                    nc.sbuf_tensor(f"TMPB{p}", [128, T], BF16) as TMPB,
                    nc.psum_tensor(f"pr{p}", [1, 4, 512], F32) as pr,
                ):
                    prev_rope = e_k_evs[-1]
                    prev_rb_rd = EV0
                    for mi, (src, src_evs) in enumerate(((QT, e_q_evs), (KT, e_k_evs))):
                        for ch in range(4):
                            tile = src[:, ch, :]
                            lh = ch // 2
                            e_sq = rec.op("scalar", lambda s, tile=tile: s.activation(sqr[:, :], tile, AF.Square),
                                          after=src_evs + [prev_rope])
                            e_ms = []
                            for q4 in range(4):
                                e_ms.append(rec.op("tensor", lambda t, q4=q4: t.matmul(
                                    pr[:, q4, :], ONESB[:, :], sqr[:, q4 * 512:(q4 + 1) * 512],
                                    start=True, stop=True), after=[e_sq, e_c["ob"], prev_rope]))
                            e_rs = rec.op("scalar", lambda s: s.activation(rr[:, 0:T], pr[:, :, :], AF.Sqrt,
                                                                           scale=1.0 / 128.0, bias=EPS[0:1, 1:2]), after=e_ms)
                            e_rc = rec.op("vector", lambda v: v.reciprocal(rr[:, T:2 * T], rr[:, 0:T]), after=[e_rs])
                            e_rb = rec.op("vector", lambda v: v.tensor_copy(rrb[:, :], rr[:, T:2 * T]), after=[e_rc])
                            e_bc = rec.op("gpsimd", lambda g: g.partition_broadcast(rb[:, :], rrb[:, :], channels=128),
                                          after=[e_rb, prev_rb_rd])
                            e_sw1 = rec.dma("sync", "swp", TMPB[0:64, :], tile[64:128, :], after=src_evs + [prev_rope])
                            e_sw2 = rec.dma("sync", "swp", TMPB[64:128, :], tile[0:64, :], after=[e_sw1])
                            e_t1 = rec.op("vector", lambda v, tile=tile, lh=lh: v.tensor_scalar(
                                TMPA[:, :], tile, ROTS[:, lh:lh + 1], None, OP.mult),
                                after=[e_c["rot"], e_sw2, prev_rope])
                            e_t2 = rec.op("vector", lambda v, tile=tile, lh=lh: v.scalar_tensor_tensor(
                                tile, TMPB[:, :], ROTS[:, 4 + lh:5 + lh], TMPA[:, :], OP.mult, OP.add),
                                after=[e_t1, e_sw2])
                            e_nb = rec.op("vector", lambda v, tile=tile: v.tensor_tensor(tile, tile, rb[:, :], OP.mult),
                                          after=[e_t2, e_bc])
                            prev_rb_rd = e_nb
                            rope_ev[(mi, ch)] = e_nb
                            prev_rope = e_nb
                    e_rope_all = prev_rope
                    bg_pump(2)
                    rec.emit()

                with (
                    nc.sbuf_tensor(f"PT{p}", [128, 2, 512], BF16) as PT,
                    nc.sbuf_tensor(f"zrow{p}", [1, 2, 512], F32) as zrow,
                    nc.sbuf_tensor(f"zbf{p}", [128, 2, 512], F32) as zbf,
                    nc.sbuf_tensor(f"csc{p}", [128, 2, 512], F32) as csc,
                    nc.psum_tensor(f"pO{p}", [128, 4, 512], F32) as pO,
                    nc.psum_tensor(f"pS{p}", [128, 2, 512], F32) as pS,
                    nc.psum_tensor(f"pZ{p}", [1, 2, 512], F32) as pZ,
                ):
                    prev_exp = [e_rope_all, e_rope_all]
                    prev_zread = e_rope_all
                    prev_zrow_rd = e_rope_all
                    prev_cmb = [e_rope_all] * 4
                    for lh in range(2):
                        for qc in range(4):
                            nkb = (qc + 1) * 4
                            e_zl = None
                            e_ol = {}
                            for kb in range(nkb):
                                for j in range(2):
                                    qk_ev = [rope_ev[(0, 2 * lh + j)], rope_ev[(1, 2 * lh + j)]]
                                    e_s = rec.op("tensor", lambda t, lh=lh, j=j, kb=kb, qc=qc: t.matmul(
                                        pS[:, j, :], KT[:, 2 * lh + j, kb * 128:(kb + 1) * 128],
                                        QT[:, 2 * lh + j, qc * 512:(qc + 1) * 512],
                                        start=True, stop=True),
                                        after=qk_ev + [prev_exp[j]])
                                    e_e = rec.op("scalar", lambda s, j=j: s.activation(PT[:, j, :], pS[:, j, :], AF.Exp, scale=ISQ),
                                                 after=[e_s])
                                    delta = kb * 128 - qc * 512
                                    if 0 <= delta <= 384:
                                        e_e = rec.op("vector", lambda v, j=j, delta=delta: v.tensor_tensor(
                                            PT[:, j, :], PT[:, j, :], STAIR[:, 384 - delta:896 - delta], OP.mult),
                                            after=[e_e, e_c["st"]])
                                    prev_exp[j] = e_e
                                    e_z = rec.op("tensor", lambda t, j=j, kb=kb, nkb=nkb: t.matmul(
                                        pZ[:, j, :], ONESB[:, :], PT[:, j, :],
                                        start=(kb == 0), stop=(kb == nkb - 1)),
                                        after=[e_e] + ([prev_zread] if kb == 0 else []))
                                    e_zl = e_z
                                    for d2 in range(2):
                                        aft = [e_e, e_vs_last]
                                        if kb == 0:
                                            aft.append(prev_cmb[2 * j + d2])
                                        e_ol[2 * j + d2] = rec.op("tensor", lambda t, lh=lh, j=j, d2=d2, kb=kb, nkb=nkb: t.matmul(
                                            pO[:, 2 * j + d2, :], VS[:, kb, 256 * lh + 128 * d2:256 * lh + 128 * (d2 + 1)],
                                            PT[:, j, :], start=(kb == 0), stop=(kb == nkb - 1)), after=aft)
                            e_zc = rec.op("vector", lambda v: v.tensor_copy(zrow[:, :, :], pZ[:, :, :]),
                                          after=[e_zl, prev_zrow_rd])
                            prev_zread = e_zc
                            e_zr = rec.op("vector", lambda v: v.reciprocal(zrow[:, :, :], zrow[:, :, :]), after=[e_zc])
                            e_zs = rec.op("vector", lambda v: v.tensor_scalar(zrow[:, 1, :], zrow[:, 1, :], LAM[0:1, 1:2], None, OP.mult),
                                          after=[e_zr, e_lam])
                            e_zb = rec.op("gpsimd", lambda g: g.partition_broadcast(zbf[:, :, :], zrow[:, :, :], channels=128),
                                          after=[e_zs])
                            prev_zrow_rd = e_zb
                            prev_y = e_zb
                            for d2 in range(2):
                                e_c1 = rec.op("vector", lambda v, d2=d2: v.tensor_tensor(csc[:, 0, :], pO[:, d2, :], zbf[:, 0, :], OP.mult),
                                              after=[e_zb, e_ol[d2], prev_cmb[d2], prev_y])
                                e_c2 = rec.op("vector", lambda v, d2=d2: v.tensor_tensor(csc[:, 1, :], pO[:, 2 + d2, :], zbf[:, 1, :], OP.mult),
                                              after=[e_c1, e_ol[2 + d2]])
                                ych = 4 * p + 2 * lh + d2
                                e_y = rec.op("vector", lambda v, ych=ych, qc=qc: v.tensor_tensor(
                                    YT[:, ych, qc * 512:(qc + 1) * 512], csc[:, 0, :], csc[:, 1, :], OP.add),
                                    after=[e_c2])
                                prev_cmb[d2] = e_y
                                prev_cmb[2 + d2] = e_y
                                prev_y = e_y
                            bg_pump()
                    rec.emit()
        e_yt_done = rec.now("DV")
        bg_pump(len(bg_chunks))
        e_wo, e_swg, e_swu = e_cast["wo"], e_cast["swg"], e_cast["swu"]
        e_weg, e_weu, e_wed, e_swd = e_cast["weg"], e_cast["weu"], e_cast["wed"], e_cast["swd"]

        rec.stream("yex")

        def emit_send(g):
            for c in range(NCORE):
                with g.If_eq(rank_reg[0], c):
                    off = 1024 if c % 2 == 0 else 0
                    g.dma_start(out=yhalf[:], in_=YT[:, :, off:off + 1024]).then_inc(rec.sems["yex"][0], 16)
                g.end_ifs()
        rec.raw("gpsimd", emit_send, after=[e_yt_done])
        e_send = rec.bump("yex", 16)

        e_ag_y = rec.op("gpsimd", lambda g: g.collective_compute(
            "AllGather", OP.bypass, ins=[yhalf[:]], outs=[ypair[:]], replica_groups=PAIRS),
            after=[e_send], sem="CO", n=1)

        e_fsh_dep = []
        with (
            nc.sbuf_tensor("YF", [128, 16, 1024], BF16) as YF,
            nc.sbuf_tensor("wt5", [128, 2, 16, 512], BF16) as wt5,
            nc.sbuf_tensor("a_sb", [128, 8, C], BF16) as a_sb,
        ):
            rec.stream("yasm")

            def emit_asm(g):
                for c in range(NCORE):
                    with g.If_eq(rank_reg[0], c):
                        if c % 2 == 0:
                            g.dma_start(out=YF[:, 0:8, :], in_=YT[:, :, 0:1024]).then_inc(rec.sems["yasm"][0], 16)
                            g.dma_start(out=YF[:, 8:16, :], in_=ypair[1]).then_inc(rec.sems["yasm"][0], 16)
                        else:
                            g.dma_start(out=YF[:, 0:8, :], in_=ypair[0]).then_inc(rec.sems["yasm"][0], 16)
                            g.dma_start(out=YF[:, 8:16, :], in_=YT[:, :, 1024:2048]).then_inc(rec.sems["yasm"][0], 16)
                    g.end_ifs()
            rec.raw("gpsimd", emit_asm, after=[e_ag_y, e_yt_done])
            e_yf = rec.bump("yasm", 32)

            with nc.psum_tensor("pa", [128, 8, 512], F32) as pa:
                rec.stream("wo")
                prev_ev = {tb: EV0 for tb in range(8)}
                slot_last_mm = [EV0, EV0]
                prev_lw = EV0
                for coq in range(4):
                    sl = coq % 2
                    e_lw = rec.dma("sync", f"wo{sl}", wt5[:, sl, :, :], wo_b[coq],
                                   after=[e_wo, slot_last_mm[sl]])
                    prev_lw = e_lw
                    last_mm = None
                    for cch in range(16):
                        for tb in range(8):
                            aft = [e_lw, e_yf]
                            if cch == 0:
                                aft.append(prev_ev[tb])
                            last_mm = rec.op("tensor", lambda t, tb=tb, cch=cch, sl=sl: t.matmul(
                                pa[:, tb, :], YF[:, cch, tb * 128:(tb + 1) * 128], wt5[:, sl, cch, :],
                                start=(cch == 0), stop=(cch == 15)), after=aft)
                    slot_last_mm[sl] = last_mm
                    pe_now = rec.now("PE")
                    for tb in range(8):
                        prev_ev[tb] = rec.op("scalar", lambda s, tb=tb, coq=coq: s.activation(
                            a_sb[:, tb, coq * 512:(coq + 1) * 512], pa[:, tb, :], AF.Copy, bias=0.0, scale=0.8),
                            after=[pe_now])
                e_a_done = [prev_ev[tb] for tb in range(8)]
            rec.emit()

            with (
                nc.sbuf_tensor("xo", [128, C], F32) as xo,
                nc.sbuf_tensor("lnb", [128, C], F32) as lnb,
                nc.sbuf_tensor("x2t", [128, C], F32) as x2t,
                nc.sbuf_tensor("x3t", [128, C], BF16) as x3t,
                nc.sbuf_tensor("sqd2", [128, C], BF16) as sqd2,
                nc.sbuf_tensor("st2", [128, 8], F32) as st2,
            ):
                e_x3w = []
                prev = EV0
                prev2 = EV0
                for tb in range(8):
                    atile = a_sb[:, tb, :]
                    e_xo = rec.dma("sync", "xow", xo[:, :], x_own[tb * 128:(tb + 1) * 128, :], after=[prev])
                    e_ri = _ln_stats(rec, st2, atile, sqd2, e_a_done[tb], prev, EPS[:, 0:1])
                    e_ln = rec.op("vector", lambda v, atile=atile: v.tensor_scalar(lnb[:, :], atile, st2[:, 2:3], st2[:, 6:7],
                                                                                   OP.subtract, OP.mult), after=[e_ri])
                    e_x2 = rec.op("vector", lambda v: v.tensor_tensor(x2t[:, :], xo[:, :], lnb[:, :], OP.add), after=[e_ln, e_xo, prev2])
                    e_w2 = rec.dma("scalar", "x2w", x2s[tb * 128:(tb + 1) * 128, :], x2t[:, :], after=[e_x2])
                    e_rib = _ln_stats(rec, st2, x2t[:, :], sqd2, e_x2, e_ln, EPS[:, 0:1])
                    e_x3 = rec.op("vector", lambda v: v.tensor_scalar(x3t[:, :], x2t[:, :], st2[:, 2:3], st2[:, 6:7],
                                                                      OP.subtract, OP.mult), after=[e_rib])
                    e_w3 = rec.dma("scalar", "x3w", x3mine[tb * 128:(tb + 1) * 128, :], x3t[:, :], after=[e_x3])
                    e_x3w.append(e_w3)
                    prev = e_w3
                    prev2 = e_w2
                rec.emit()

    e_ag_x3 = rec.op("gpsimd", lambda g: g.collective_compute(
        "AllGather", OP.bypass, ins=[x3mine[:]],
        outs=[x3all[0:TTOT, :].rearrange("(s t) c -> s t c", s=NCORE)],
        replica_groups=ALL8), after=e_x3w, sem="CO", n=1)

    FSH = ctx.enter_context(nc.sbuf_tensor("FSH", [128, 8, C], BF16))
    idxT0 = ctx.enter_context(nc.sbuf_tensor("idxT0", [128, 64], I16))
    idxT1 = ctx.enter_context(nc.sbuf_tensor("idxT1", [128, 64], I16))
    wbt0 = ctx.enter_context(nc.sbuf_tensor("wbt0", [128, 8], F32))
    wbt1 = ctx.enter_context(nc.sbuf_tensor("wbt1", [128, 8], F32))
    IDXT = [idxT0, idxT1]
    WBT = [wbt0, wbt1]
    route_evs = []

    MT = ctx.enter_context(nc.sbuf_tensor("MT", [128, 22, TOK], BF16))
    if True:
        e_mt = []
        with nc.sbuf_tensor("X3T", [128, 2, 16, 512], BF16) as X3T:
            rec.stream("x3T")
            e_x3a = rec.op("gpsimd", lambda g: g.dma_gather(
                X3T[:, 0, :, :], x3mine[:], IDQS[0][:], num_idxs=512, num_idxs_reg=512,
                elem_size=C, transpose=True), after=[e_x3w[-1], e_c["idq0"]], sem="x3T", n=16)
            e_x3b = rec.op("gpsimd", lambda g: g.dma_gather(
                X3T[:, 1, :, :], x3mine[:], IDQS[1][:], num_idxs=512, num_idxs_reg=512,
                elem_size=C, transpose=True), after=[e_x3a, e_c["idq1"]], sem="x3T", n=16)
            e_x3T = [e_x3b] * 16

            with (
                nc.sbuf_tensor("LTs", [16, TOK], F32) as LTs,
                nc.sbuf_tensor("Ltok", [128, 8, 16], F32) as Ltok,
                nc.sbuf_tensor("gsc", [128, 8, 16], F32) as gsc,
                nc.sbuf_tensor("gst", [128, 10], F32) as gst,
                nc.sbuf_tensor("m8s", [128, 8], F32) as m8s,
                nc.sbuf_tensor("WMs", [16, TOK], F32) as WMs,
                nc.psum_tensor("pL", [16, 2, 512], F32) as pL,
                nc.psum_tensor("pTt", [128, 16], F32) as pTt,
                nc.psum_tensor("pwt", [16, 128], F32) as pwt,
            ):
                e_lcp = []
                for tc2 in range(2):
                    last = None
                    for k in range(16):
                        last = rec.op("tensor", lambda t, tc2=tc2, k=k: t.matmul(
                            pL[:, tc2, :], GWT[:, k, :], X3T[:, tc2, k, :],
                            start=(k == 0), stop=(k == 15)), after=[e_gwt, e_x3T[k]])
                    e_lcp.append(rec.op("vector", lambda v, tc2=tc2: v.tensor_copy(
                        LTs[:, tc2 * 512:(tc2 + 1) * 512], pL[:, tc2, :]), after=[last]))

                prev_t = EV0
                prev_w = EV0
                for tb in range(8):
                    e_tr = rec.op("tensor", lambda t, tb=tb: t.transpose(pTt[:, :], LTs[:, tb * 128:(tb + 1) * 128], IDN32[0:16, 0:16]),
                                  after=e_lcp + [e_c["i32"], prev_t])
                    e_ev = rec.op("vector", lambda v, tb=tb: v.tensor_copy(Ltok[:, tb, :], pTt[:, :]), after=[e_tr])
                    prev_t = e_ev
                    lt = Ltok[:, tb, :]
                    e_mx = rec.op("vector", lambda v, lt=lt: v.tensor_reduce(gst[:, 0:1], lt, axis=AX, op=OP.max), after=[e_ev])
                    e_nm = rec.op("vector", lambda v: v.tensor_scalar(gst[:, 1:2], gst[:, 0:1], -1.0, None, OP.mult), after=[e_mx])
                    e_p = rec.op("scalar", lambda s, lt=lt, tb=tb: s.activation(gsc[:, tb, :], lt, AF.Exp, bias=gst[:, 1:2],
                                                                                accum_out=gst[:, 2:3]), after=[e_nm])
                    e_m8 = rec.op("vector", lambda v, tb=tb: v.max(out=m8s[:, :], in_=gsc[:, tb, :]), after=[e_p])
                    e_m = rec.op("vector", lambda v, tb=tb: v.tensor_scalar(Ltok[:, tb, :], gsc[:, tb, :], m8s[:, 3:4], None, OP.is_ge),
                                 after=[e_m8])
                    e_pm = rec.op("vector", lambda v, tb=tb: v.tensor_tensor(gsc[:, tb, :], gsc[:, tb, :], Ltok[:, tb, :], OP.mult),
                                  after=[e_m])
                    e_s4 = rec.op("vector", lambda v, tb=tb: v.tensor_reduce(gst[:, 3:4], gsc[:, tb, :], axis=AX, op=OP.add), after=[e_pm])
                    e_zz = rec.op("vector", lambda v: v.tensor_scalar(gst[:, 4:5], gst[:, 2:3], 1e-20, None, OP.mult), after=[e_p, e_s4])
                    e_dn = rec.op("vector", lambda v: v.tensor_tensor(gst[:, 5:6], gst[:, 3:4], gst[:, 4:5], OP.add), after=[e_zz])
                    e_di = rec.op("vector", lambda v: v.reciprocal(gst[:, 6:7], gst[:, 5:6]), after=[e_dn])
                    e_w = rec.op("vector", lambda v, tb=tb: v.tensor_scalar(gsc[:, tb, :], gsc[:, tb, :], gst[:, 6:7], None, OP.mult),
                                 after=[e_di])
                    e_tw = rec.op("tensor", lambda t, tb=tb: t.transpose(pwt[:, :], gsc[:, tb, :], IDN32[:, :]),
                                  after=[e_w, prev_w])
                    e_wv2 = rec.op("vector", lambda v, tb=tb: v.tensor_copy(WMs[:, tb * 128:(tb + 1) * 128], pwt[:, :]), after=[e_tw])
                    prev_w = e_wv2
                e_wmd = rec.dma("scalar", "wmw", wm_mine[:], WMs[:], after=[prev_w])
                rec.emit()

            e_ag_wm = rec.op("gpsimd", lambda g: g.collective_compute(
                "AllGather", OP.bypass, ins=[wm_mine[:]], outs=[wm_all[:]], replica_groups=ALL8),
                after=[e_wmd], sem="CO", n=1)

            mrg = contextlib.ExitStack()
            _mt = lambda name, shape, dt: mrg.enter_context(nc.sbuf_tensor(name, shape, dt))
            wst = _mt("wst", [128, 2, 2, 16, 128], BF16)
            gsil = _mt("gsil", [128, 2, TOK], BF16)
            Wfull = _mt("Wfull", [128, TOK], F32)
            mfull = _mt("mfull", [128, TOK], F32)
            ones2 = _mt("ones2", [128, TOK], F32)
            Rx = _mt("Rx", [128, TOK], F32)
            keptf = _mt("keptf", [128, TOK], F32)
            TOKc = _mt("TOKc", [128, TOK], F32)
            CSUM = _mt("CSUM", [128, 128], F32)
            vtok = _mt("vtok", [128, TOK], F32)
            vw = _mt("vw", [128, TOK], F32)
            tot = _mt("tot", [128, 2], F32)
            sent = _mt("sent", [1, 2 * TOK], F32)
            sgi = _mt("sgi", [16, 2, 576], F32)
            sgo = _mt("sgo", [16, 144], F32)
            sgw = _mt("sgw", [16, 144], F32)
            nfo = _mt("nfo", [1, 4], U32)
            pbase = mrg.enter_context(nc.psum_tensor("pbase", [128, 1], F32))
            pm1 = mrg.enter_context(nc.psum_tensor("pm1", [128, 2, 512], F32))
            pm2 = mrg.enter_context(nc.psum_tensor("pm2", [128, 2, 512], F32))
            if True:
                sh_gate_rd = [EV0, EV0]
                sh_up_rd = [EV0, EV0]
                sh_silu = []
                sh_mul = []
                sh_mul_sl = [[], []]

                def emit_shared_ic(ic):
                    nonlocal sh_silu, sh_mul
                    sl = ic % 2
                    e_wg = rec.dma("scalar", f"sstg{sl}", wst[:, 0, sl, :, :], swg_b[ic],
                                   after=[sh_gate_rd[sl], e_swg])
                    lastg = [None, None]
                    for h2 in range(2):
                        for k in range(16):
                            aft = [e_wg, e_x3T[k]]
                            if k == 0:
                                aft += sh_silu
                            lastg[h2] = rec.op("tensor", lambda t, sl=sl, k=k, h2=h2: t.matmul(
                                pm1[:, h2, :], wst[:, 0, sl, k, :], X3T[:, h2, k, :],
                                start=(k == 0), stop=(k == 15)), after=aft)
                    sh_gate_rd[sl] = lastg[1]
                    e_sil = rec.op("scalar", lambda s, sl=sl: s.activation(gsil[:, sl, 0:512], pm1[:, 0, :], AF.Silu),
                                   after=[lastg[0]] + sh_mul_sl[sl])
                    e_sil2 = rec.op("scalar", lambda s, sl=sl: s.activation(gsil[:, sl, 512:1024], pm1[:, 1, :], AF.Silu),
                                    after=[lastg[1], e_sil])
                    sh_silu = [e_sil, e_sil2]
                    e_wu = rec.dma("scalar", f"sstu{sl}", wst[:, 1, sl, :, :], swu_b[ic],
                                   after=[sh_up_rd[sl], e_swu])
                    lastu = [None, None]
                    for h2 in range(2):
                        for k in range(16):
                            aft = [e_wu, e_x3T[k]]
                            if k == 0:
                                aft += sh_mul
                            lastu[h2] = rec.op("tensor", lambda t, sl=sl, k=k, h2=h2: t.matmul(
                                pm2[:, h2, :], wst[:, 1, sl, k, :], X3T[:, h2, k, :],
                                start=(k == 0), stop=(k == 15)), after=aft)
                    sh_up_rd[sl] = lastu[1]
                    e_mul1 = rec.op("vector", lambda v, ic=ic, sl=sl: v.tensor_tensor(
                        MT[:, ic, 0:512], pm2[:, 0, :], gsil[:, sl, 0:512], OP.mult),
                        after=[lastu[0], e_sil])
                    e_mul2 = rec.op("vector", lambda v, ic=ic, sl=sl: v.tensor_tensor(
                        MT[:, ic, 512:1024], pm2[:, 1, :], gsil[:, sl, 512:1024], OP.mult),
                        after=[lastu[1], e_sil2, e_mul1])
                    sh_mul = [e_mul1, e_mul2]
                    sh_mul_sl[sl] = [e_mul1, e_mul2]
                    e_mt.append(e_mul2)

                e_wf_l = [rec.dma("sync", "wmr", Wfull[e8 * 8:(e8 + 1) * 8, :], wm_all[:, e8, :], after=[e_ag_wm]) for e8 in range(E)]
                e_wf = [e_wf_l[-1]] * NCORE
                e_tc = rec.dma("sync", "tokc", TOKc[:, :], cTOK128[:])
                e_cs = rec.dma("sync", "csum", CSUM[:, :], cCSUM[:])
                e_mf = rec.op("vector", lambda v: v.tensor_scalar(mfull[:, :], Wfull[:, :], 0.0, None, OP.is_gt), after=e_wf)
                e_o2 = rec.op("vector", lambda v: v.memset(ones2[:, :], 1.0))
                e_sn0 = rec.op("vector", lambda v: v.memset(sent[:, 0:TOK], float(PADROW)))
                e_sn1 = rec.op("vector", lambda v: v.memset(sent[:, TOK:2 * TOK], 0.0))
                e_sc = rec.op("vector", lambda v: v.tensor_tensor_scan(Rx[:, :], ones2[:, :], mfull[:, :], 0.0, OP.mult, OP.add),
                              after=[e_mf, e_o2])
                e_tt = rec.op("vector", lambda v: v.tensor_copy(tot[:, 0:1], Rx[:, TOK - 1:TOK]), after=[e_sc])

                for ic in range(4):
                    emit_shared_ic(ic)

                e_bm = rec.op("tensor", lambda t: t.matmul(pbase[:, :], CSUM[:, :], tot[:, 0:1], start=True, stop=True),
                              after=[e_tt, e_cs])
                e_bc2 = rec.op("vector", lambda v: v.tensor_copy(tot[:, 1:2], pbase[:, :]), after=[e_bm])
                e_ad = rec.op("vector", lambda v: v.tensor_scalar(Rx[:, :], Rx[:, :], tot[:, 1:2], None, OP.add), after=[e_bc2])
                e_rx = rec.op("vector", lambda v: v.tensor_tensor(Rx[:, :], Rx[:, :], mfull[:, :], OP.subtract), after=[e_ad])
                e_ltc = rec.op("vector", lambda v: v.tensor_scalar(keptf[:, :], Rx[:, :], float(CAP), None, OP.is_lt), after=[e_rx])
                e_kp = rec.op("vector", lambda v: v.tensor_tensor(keptf[:, :], keptf[:, :], mfull[:, :], OP.mult), after=[e_ltc])
                e_vt0 = rec.op("vector", lambda v: v.tensor_tensor(vtok[:, :], keptf[:, :], TOKc[:, :], OP.mult),
                               after=[e_kp, e_tc])
                e_vt = rec.op("vector", lambda v: v.tensor_scalar(vtok[:, :], vtok[:, :], -1.0, None, OP.add), after=[e_vt0])
                e_vwa = rec.op("vector", lambda v: v.tensor_tensor(vw[:, :], keptf[:, :], Wfull[:, :], OP.mult), after=[e_kp])
                e_vwb = rec.op("vector", lambda v: v.tensor_scalar(TOKc[:, :], keptf[:, :], -1.0, None, OP.add), after=[e_vwa, e_vt0])
                e_vw0 = rec.op("vector", lambda v: v.tensor_tensor(vw[:, :], vw[:, :], TOKc[:, :], OP.add), after=[e_vwb])

                e_st0 = rec.dma("gpsimd", "sent", strm[0:1, TTOT:9216], sent[:, 0:TOK], after=[e_sn0])
                e_st1 = rec.dma("gpsimd", "sent", strm[1:2, TTOT:9216], sent[:, TOK:2 * TOK], after=[e_sn1])

                prev_strm = [e_vt, e_vw0, e_st0, e_st1]
                lib_sg = [False]
                for ei in range(EPC):
                    rec.stream("rows")

                    def emit_row_ei(g, ei=ei):
                        for c in range(NCORE):
                            with g.If_eq(rank_reg[0], c):
                                r = 2 * c + ei
                                g.dma_start(
                                    out=strm[0, 0:TTOT].rearrange("(s t) -> s t", t=TOK),
                                    in_=vtok[8 * r:8 * r + 8, :]).then_inc(rec.sems["rows"][0], 16)
                                g.dma_start(
                                    out=strm[1, 0:TTOT].rearrange("(s t) -> s t", t=TOK),
                                    in_=vw[8 * r:8 * r + 8, :]).then_inc(rec.sems["rows"][0], 16)
                            g.end_ifs()
                    rec.raw("gpsimd", emit_row_ei, after=prev_strm)
                    e_rows = rec.bump("rows", 2 * 16)
                    e_wrap0 = rec.dma("gpsimd", "wrap", sgi[:, 0, :], strm[0, :].rearrange("(f p) -> p f", p=16), after=[e_rows], allow_slow_non_contiguous=True)
                    e_wrap1 = rec.dma("gpsimd", "wrap", sgi[:, 1, :], strm[1, :].rearrange("(f p) -> p f", p=16), after=[e_wrap0], allow_slow_non_contiguous=True)
                    if not lib_sg[0]:
                        rec.raw("gpsimd", lambda g: g.load_library(lc.sparse_gather), after=[e_wrap1])
                        lib_sg[0] = True
                    e_sg0 = rec.op("gpsimd", lambda g: g.sparse_gather(sgo[:, :], sgi[:, 0, :], num_found=nfo[0:1, 0:1]),
                                   after=[e_wrap1])
                    e_sg1 = rec.op("gpsimd", lambda g: g.sparse_gather(sgw[:, :], sgi[:, 1, :], num_found=nfo[0:1, 1:2]),
                                   after=[e_sg0])
                    e_cl0 = rec.op("vector", lambda v: v.tensor_scalar(sgo[:, 0:64], sgo[:, 0:64], 0.0, float(PADROW), OP.max, OP.min),
                                   after=[e_sg0])
                    e_i16 = rec.op("vector", lambda v, ei=ei: v.tensor_copy(IDXT[ei][0:16, :], sgo[:, 0:64]), after=[e_cl0])
                    reps = [e_i16]
                    for kk in range(1, 8):
                        reps.append(rec.dma("sync", "irep", IDXT[ei][16 * kk:16 * (kk + 1), :], IDXT[ei][0:16, :], after=[e_i16]))
                    e_wb0 = rec.dma("gpsimd", "wbb", wbb[ei].rearrange("(f p) -> p f", p=16), sgw[:, 0:64], after=[e_sg1], allow_slow_non_contiguous=True)
                    e_wb1 = rec.dma("gpsimd", "wbb", WBT[ei][:, :], wbb[ei].rearrange("(f p) -> p f", p=128), after=[e_wb0], allow_slow_non_contiguous=True)
                    route_evs.append((reps, e_wb1))
                    prev_strm = [e_wrap1, e_sg1, e_cl0, e_i16, e_wb0]
                rec.raw("gpsimd", lambda g: g.drain())
                rec.raw("gpsimd", lambda g: g.load_library(lc.mlp), after=[route_evs[-1][1]])

                for ic in range(4, 22):
                    emit_shared_ic(ic)
                rec.emit()
            mrg.close()


    e_scat = []
    with (
        nc.sbuf_tensor("XGT", [128, 2, 16, 512], BF16) as XGT,
        nc.sbuf_tensor("wge", [128, 2, 2, 16, 128], BF16) as wge,
        nc.sbuf_tensor("MT2", [128, 11, CAP], BF16) as MT2,
        nc.sbuf_tensor("gsil2", [128, 2, CAP], BF16) as gsil2,
        nc.sbuf_tensor("eout", [128, 8, C], BF16) as eout,
        nc.sbuf_tensor("wde", [128, 2, 512], BF16) as wde,
    ):
        prev_scatter = EV0
        prev_xgt_users = EV0
        gate_rd = [EV0, EV0]
        up_rd = [EV0, EV0]
        silu_ev = [[], []]
        mul_ev = [[], []]
        prev_evac = []
        slot_last2 = [EV0, EV0]
        for ei in range(EPC):
            reps, e_wb = route_evs[ei]
            rec.stream("xg")
            e_xga = rec.op("gpsimd", lambda g, ei=ei: g.dma_gather(
                XGT[:, 0, :, :], x3all[:], IDXT[ei][:, 0:32], num_idxs=512, num_idxs_reg=512,
                elem_size=C, transpose=True),
                after=[e_ag_x3, prev_xgt_users, e_cast["zero"]] + reps, sem="xg", n=16)
            e_xg = rec.op("gpsimd", lambda g, ei=ei: g.dma_gather(
                XGT[:, 1, :, :], x3all[:], IDXT[ei][:, 32:64], num_idxs=512, num_idxs_reg=512,
                elem_size=C, transpose=True),
                after=[e_xga], sem="xg", n=16)
            e_mt2 = []
            pgctx = contextlib.ExitStack()
            pg1 = pgctx.enter_context(nc.psum_tensor(f"pg1_{ei}", [128, 2, 2, 512], F32))
            pg2 = pgctx.enter_context(nc.psum_tensor(f"pg2_{ei}", [128, 2, 2, 512], F32))
            for ic in range(11):
                sl = ic % 2
                e_wg2 = rec.dma("sync", f"wge_g{sl}", wge[:, 0, sl, :, :], weg_b[ei, ic],
                                after=[gate_rd[sl], e_weg])
                lastg = [None, None]
                for h2 in range(2):
                    for k in range(16):
                        aft = [e_wg2, e_xg]
                        if k == 0:
                            aft += silu_ev[sl] + prev_evac
                        lastg[h2] = rec.op("tensor", lambda t, k=k, h2=h2, sl=sl: t.matmul(
                            pg1[:, sl, h2, :], wge[:, 0, sl, k, :], XGT[:, h2, k, :],
                            start=(k == 0), stop=(k == 15)), after=aft)
                gate_rd[sl] = lastg[1]
                e_si1 = rec.op("scalar", lambda s, sl=sl: s.activation(gsil2[:, sl, 0:512], pg1[:, sl, 0, :], AF.Silu),
                               after=[lastg[0]] + mul_ev[sl])
                e_si2 = rec.op("scalar", lambda s, sl=sl: s.activation(gsil2[:, sl, 512:1024], pg1[:, sl, 1, :], AF.Silu),
                               after=[lastg[1], e_si1])
                silu_ev[sl] = [e_si1, e_si2]
                e_wu2 = rec.dma("sync", f"wge_u{sl}", wge[:, 1, sl, :, :], weu_b[ei, ic],
                                after=[up_rd[sl], e_weu])
                lastu = [None, None]
                for h2 in range(2):
                    for k in range(16):
                        aft = [e_wu2, e_xg]
                        if k == 0:
                            aft += mul_ev[sl] + prev_evac
                        lastu[h2] = rec.op("tensor", lambda t, k=k, h2=h2, sl=sl: t.matmul(
                            pg2[:, sl, h2, :], wge[:, 1, sl, k, :], XGT[:, h2, k, :],
                            start=(k == 0), stop=(k == 15)), after=aft)
                up_rd[sl] = lastu[1]
                e_m1 = rec.op("vector", lambda v, ic=ic, sl=sl: v.tensor_tensor(MT2[:, ic, 0:512], pg2[:, sl, 0, :], gsil2[:, sl, 0:512], OP.mult),
                              after=[lastu[0], e_si1])
                e_m2 = rec.op("vector", lambda v, ic=ic, sl=sl: v.tensor_tensor(MT2[:, ic, 512:1024], pg2[:, sl, 1, :], gsil2[:, sl, 512:1024], OP.mult),
                              after=[lastu[1], e_si2, e_m1])
                mul_ev[sl] = [e_m1, e_m2]
                e_mt2.append(e_m2)
            prev_xgt_users = e_mt2[-1]
            pgctx.close()

            pectx = contextlib.ExitStack()
            pe8 = pectx.enter_context(nc.psum_tensor(f"pe8_{ei}", [128, 8, 512], F32))
            prev_eo = {sb: e_mt2[-1] for sb in range(8)}
            for coq in range(4):
                for ic in range(11):
                    sl = ic % 2
                    e_ldd = rec.dma("sync", f"wde{sl}", wde[:, sl, :], wed_b[ei, ic, coq],
                                    after=[slot_last2[sl], e_wed])
                    last_mm = None
                    for sb in range(8):
                        aft = [e_ldd, e_mt2[ic]]
                        if ic == 0:
                            aft.append(prev_eo[sb])
                        last_mm = rec.op("tensor", lambda t, sb=sb, ic=ic, sl=sl: t.matmul(
                            pe8[:, sb, :], MT2[:, ic, sb * 128:(sb + 1) * 128], wde[:, sl, :],
                            start=(ic == 0), stop=(ic == 10)), after=aft)
                    slot_last2[sl] = last_mm
                pe_now = rec.now("PE")
                for sb in range(8):
                    prev_eo[sb] = rec.op("scalar", lambda s, sb=sb, coq=coq, ei=ei: s.activation(
                        eout[:, sb, coq * 512:(coq + 1) * 512], pe8[:, sb, :], AF.Copy, bias=0.0,
                        scale=WBT[ei][:, sb:sb + 1]), after=[pe_now, e_wb, prev_scatter])
            e_eo = [prev_eo[sb] for sb in range(8)]
            prev_evac = e_eo
            rec.stream("scat")
            e_sc2 = rec.op("gpsimd", lambda g, ei=ei: g.dma_scatter_add(
                yp[:], eout[:, :, :], IDXT[ei][:, :], num_idxs=CAP, num_idxs_reg=CAP, elem_size=C),
                after=e_eo + [e_cast["zero"], prev_scatter] + reps, sem="scat", n=16)
            prev_scatter = e_sc2
            e_scat.append(e_sc2)
            pectx.close()
        rec.emit()

    e_rs = rec.op("gpsimd", lambda g: g.collective_compute(
        "ReduceScatter", OP.add, ins=[yp[0:TTOT, :]], outs=[y_rs[:]], replica_groups=ALL8),
        after=e_scat, sem="CO", n=1)

    with (
        nc.sbuf_tensor("wds", [128, 2, 512], BF16) as wds,
        nc.psum_tensor("pf", [128, 8, 512], F32) as pf,
    ):
        prev_f = {tb: EV0 for tb in range(8)}
        slot_last = [EV0, EV0]
        for coq in range(4):
            for ic in range(22):
                sl = ic % 2
                e_ld = rec.dma("sync", f"swd{sl}", wds[:, sl, :], swd_b[ic, coq],
                               after=[slot_last[sl], e_swd])
                last_mm = None
                for tb in range(8):
                    aft = [e_ld, e_mt[ic]]
                    if ic == 0:
                        aft.append(prev_f[tb])
                    last_mm = rec.op("tensor", lambda t, tb=tb, ic=ic, sl=sl, coq=coq: t.matmul(
                        pf[:, tb, :], MT[:, ic, tb * 128:(tb + 1) * 128], wds[:, sl, :],
                        start=(ic == 0), stop=(ic == 21)), after=aft)
                slot_last[sl] = last_mm
            pe_now = rec.now("PE")
            for tb in range(8):
                prev_f[tb] = rec.op("scalar", lambda s, tb=tb, coq=coq: s.activation(
                    FSH[:, tb, coq * 512:(coq + 1) * 512], pf[:, tb, :], AF.Copy, bias=0.0, scale=1.0),
                    after=[pe_now])
        e_fsh = [prev_f[tb] for tb in range(8)]
        rec.emit()


    with (
        nc.sbuf_tensor("ym", [128, C], BF16) as ym,
        nc.sbuf_tensor("ff", [128, C], F32) as ff,
        nc.sbuf_tensor("x2r", [128, C], F32) as x2r,
        nc.sbuf_tensor("oo", [128, C], F32) as oo,
        nc.sbuf_tensor("sqd3", [128, C], BF16) as sqd3,
        nc.sbuf_tensor("st3", [128, 8], F32) as st3,
    ):
        e_outs = []
        prev = EV0
        for tb in range(8):
            e_ym = rec.dma("sync", "ymr", ym[:, :], y_rs[tb * 128:(tb + 1) * 128, :], after=[e_rs, prev])
            e_x2r = rec.dma("sync", "x2r", x2r[:, :], x2s[tb * 128:(tb + 1) * 128, :], after=[prev, ("x2w", 128)])
            e_ff = rec.op("vector", lambda v, tb=tb: v.tensor_tensor(ff[:, :], ym[:, :], FSH[:, tb, :], OP.add),
                          after=[e_ym, e_fsh[tb], prev])
            e_ri = _ln_stats(rec, st3, ff[:, :], sqd3, e_ff, prev, EPS[:, 0:1])
            e_ln = rec.op("vector", lambda v: v.tensor_scalar(ff[:, :], ff[:, :], st3[:, 2:3], st3[:, 6:7],
                                                              OP.subtract, OP.mult), after=[e_ri])
            e_o = rec.op("vector", lambda v: v.tensor_tensor(oo[:, :], x2r[:, :], ff[:, :], OP.add), after=[e_ln, e_x2r])
            e_w = rec.dma("sync", "outw", out_shard[tb * 128:(tb + 1) * 128, :], oo[:, :], after=[e_o])
            e_outs.append(e_w)
            prev = e_w
        rec.raw("sync", lambda s: None, after=e_outs)
        rec.raw("gpsimd", lambda g: None, after=e_outs)
        rec.emit()

    ctx.close()
    return nc


def kernel(**inputs):
    x = np.ascontiguousarray(inputs["L0_attn_x"])
    wq = np.ascontiguousarray(inputs["L0_attn_wq"])
    wk = np.ascontiguousarray(inputs["L0_attn_wk"])
    wv = np.ascontiguousarray(inputs["L0_attn_wv"])
    wo = np.ascontiguousarray(inputs["L0_attn_wo"])
    lqk = np.ascontiguousarray(np.stack([inputs["L0_attn_lq1"], inputs["L0_attn_lk1"],
                                         inputs["L0_attn_lq2"], inputs["L0_attn_lk2"]], axis=0))
    gate = np.ascontiguousarray(inputs["L0_ffn_gate_w"])
    wg = np.ascontiguousarray(inputs["L0_ffn_w_gate"])
    wu = np.ascontiguousarray(inputs["L0_ffn_w_up"])
    wd = np.ascontiguousarray(inputs["L0_ffn_w_down"])
    swg_i = np.ascontiguousarray(inputs["L0_ffn_sw_gate"])
    swu_i = np.ascontiguousarray(inputs["L0_ffn_sw_up"])
    swd_i = np.ascontiguousarray(inputs["L0_ffn_sw_down"])

    nc = build_nc()
    lower_extended_insts(nc)

    in_maps = []
    for c in range(NCORE):
        b, half = c // 2, c % 2
        hr = slice(1024 * half, 1024 * (half + 1))
        inv = (1.0 / 10000.0) ** (np.arange(0, HH, 2, dtype=np.float64) / HH)
        rot = np.zeros((128, 8), np.float32)
        for lh in range(4):
            h = 4 * half + lh
            th = h * inv
            rot[0:64, lh] = np.cos(th)
            rot[64:128, lh] = np.cos(th)
            rot[0:64, 4 + lh] = np.sin(th)
            rot[64:128, 4 + lh] = -np.sin(th)
        def tile_w(m):
            n = m.shape[0] // 512
            return np.ascontiguousarray(m.reshape(n, 512, 16, 128).transpose(0, 3, 2, 1))

        wg2 = wg[2 * c:2 * c + 2]
        wu2 = wu[2 * c:2 * c + 2]
        wd2 = wd[2 * c:2 * c + 2]
        in_maps.append({
            "xb": x[b],
            "rotpq": rot,
            "x_own": np.ascontiguousarray(x[b][1024 * half:1024 * (half + 1)]),
            "wq_t": tile_w(wq[hr]),
            "wk_t": tile_w(wk[hr]),
            "wv_t": tile_w(wv[hr]),
            "wo_t": tile_w(wo),
            "lqk": lqk,
            "gate_w": gate,
            "weg": np.ascontiguousarray(wg2.reshape(2, 16, 128, 11, 128).transpose(0, 3, 2, 1, 4)),
            "weu": np.ascontiguousarray(wu2.reshape(2, 16, 128, 11, 128).transpose(0, 3, 2, 1, 4)),
            "wed": np.ascontiguousarray(wd2.reshape(2, 11, 128, 4, 512).transpose(0, 1, 3, 2, 4)),
            "swg": np.ascontiguousarray(swg_i.reshape(16, 128, 22, 128).transpose(2, 1, 0, 3)),
            "swu": np.ascontiguousarray(swu_i.reshape(16, 128, 22, 128).transpose(2, 1, 0, 3)),
            "swd": np.ascontiguousarray(swd_i.reshape(22, 128, 4, 512).transpose(0, 2, 1, 3)),
        })
    res = run_bass_kernel_spmd(nc, in_maps, core_ids=list(range(NCORE)))
    shards = [res.results[c]["out_shard"] for c in range(NCORE)]
    out = np.concatenate(shards, axis=0).reshape(B, T, C)
    return out


# version 06
# speedup vs baseline: 1.3505x; 1.1368x over previous
import contextlib
import math

import numpy as np
import ml_dtypes

import concourse.bass as bass
import concourse.mybir as mybir
from concourse.alu_op_type import AluOpType
from concourse import library_config as lc
from concourse.bass_utils import run_bass_kernel_spmd
from concourse.library_overlay import lower_extended_insts

F32 = mybir.dt.float32
BF16 = mybir.dt.bfloat16
I16 = mybir.dt.int16
U32 = mybir.dt.uint32
AF = mybir.ActivationFunctionType
OP = AluOpType
AX = mybir.AxisListType.X

B, T, C = 4, 2048, 2048
NCORE = 8
TOK = 1024
HH, HD = 128, 256
E, CAP = 16, 1024
IEXP, ISH = 1408, 2816
EPC = 2
LN_EPS = 1e-5
RMS_EPS = 1.1920929e-07
LAMBDA_INIT = 0.2
ISQ = 1.0 / math.sqrt(128.0)
TTOT = B * T
PADROW = TTOT

PAIRS = [[0, 1], [2, 3], [4, 5], [6, 7]]
ALL8 = [list(range(NCORE))]

ENGINES = ("tensor", "vector", "scalar", "gpsimd", "sync")
CHAIN = {"tensor": "PE", "vector": "DV", "scalar": "AC", "gpsimd": "PL", "sync": "SY"}
QSEM = {"sync": "QS", "scalar": "QA", "gpsimd": "QG"}


class Rec:

    def __init__(self, nc, ctx):
        self.nc = nc
        self.ctx = ctx
        self.sems = {}
        for name in ["PE", "DV", "AC", "PL", "SY", "CO"]:
            self.sems[name] = [ctx.enter_context(nc.semaphore(name)), 0]
        self.ops = {e: [] for e in ENGINES}
        self.hw = {e: {} for e in ENGINES}
        self.drained = {e: 0 for e in ENGINES}
        self.block_start = {}
        self.bg = {"CO"}

    def stream(self, name, bg=False):
        if name not in self.sems:
            self.sems[name] = [self.ctx.enter_context(self.nc.semaphore(name)), 0]
        if bg:
            self.bg.add(name)
        return self.sems[name]

    def now(self, kind):
        return (kind, self.sems[kind][1])

    def _after(self, eng, after):
        waits, need_drain = [], False
        for ev in after:
            if ev is None:
                continue
            k, v = ev
            if v <= 0:
                continue
            if k == CHAIN[eng]:
                if v > self.drained[eng]:
                    need_drain = True
            else:
                waits.append((k, v))
        if need_drain:
            self.drained[eng] = self.sems[CHAIN[eng]][1]
        wmax = {}
        for (k, v) in waits:
            wmax[k] = max(wmax.get(k, 0), v)
        return sorted(wmax.items()), need_drain

    def op(self, eng, fn, after=(), sem=None, n=1):
        kind = sem or CHAIN[eng]
        waits, need_drain = self._after(eng, after)
        self.sems[kind][1] += n
        val = self.sems[kind][1]
        self.ops[eng].append((fn, waits, need_drain, kind, n))
        return (kind, val)

    def dma(self, eng, stream, out, in_, after=(), bg=False, **kw):
        self.stream(stream, bg=bg)
        return self.op(eng, lambda e: e.dma_start(out=out, in_=in_, **kw),
                       after=after, sem=stream, n=16)

    def raw(self, eng, fn, after=()):
        waits, need_drain = self._after(eng, after)
        self.ops[eng].append((fn, waits, need_drain, None, 0))

    def bump(self, kind, n):
        self.sems[kind][1] += n
        return (kind, self.sems[kind][1])

    def emit(self):
        nc = self.nc
        if not any(self.ops[e] for e in ENGINES):
            return
        fence = [(k, v) for k, v in self.block_start.items()
                 if v > 0 and k not in self.bg]
        with nc.Block() as blk:
            for eng_name in ENGINES:
                ops = self.ops[eng_name]
                if not ops:
                    continue

                def make_body(eng_name, ops):
                    def body(eng):
                        hw = self.hw[eng_name]
                        for (k, v) in fence:
                            if hw.get(k, 0) < v:
                                eng.wait_ge(self.sems[k][0], v)
                                hw[k] = v
                        for (fn, waits, need_drain, kind, n) in ops:
                            for (k, v) in waits:
                                if hw.get(k, 0) < v:
                                    eng.wait_ge(self.sems[k][0], v)
                                    hw[k] = v
                            if need_drain:
                                eng.drain()
                            inst = fn(eng)
                            if kind is not None and inst is not None:
                                inst.then_inc(self.sems[kind][0], n)
                    return body

                getattr(blk, eng_name)(make_body(eng_name, ops))
        self.ops = {e: [] for e in ENGINES}
        self.block_start = {k: s[1] for k, s in self.sems.items()
                            if k not in ("PE", "DV", "AC", "PL", "SY")}


def _consts():
    ku = np.arange(896)[None, :] >= (np.arange(128)[:, None] + 384)
    stair = np.asarray(ku.astype(ml_dtypes.bfloat16))
    idnb = np.asarray(np.eye(128, dtype=ml_dtypes.bfloat16))
    idn32 = np.eye(128, dtype=np.float32)
    onesb = np.asarray(np.ones((128, 1), dtype=ml_dtypes.bfloat16))
    tok128 = (np.arange(128)[:, None] % 8 * 1024 + np.arange(TOK)[None, :] + 1).astype(np.float32)
    pp = np.arange(128)
    csum = ((pp[:, None] // 8 == pp[None, :] // 8) & (pp[:, None] % 8 < pp[None, :] % 8)).astype(np.float32)
    idqs = []
    for b4 in range(4):
        w = np.zeros((128, 32), np.int16)
        for i in range(512):
            for g in range(8):
                w[(i % 16) + 16 * g, i // 16] = 512 * b4 + i
        idqs.append(w)
    return stair, idnb, idn32, onesb, tok128, csum, idqs


def _ln_stats(rec, stat, src_ap, sq_scratch, e_src, prev, eps_ap=None):
    e_s1 = rec.op("scalar", lambda s: s.activation(sq_scratch[:, :], src_ap, AF.Square, accum_out=stat[:, 1:2]),
                  after=[e_src, prev])
    e_s2 = rec.op("vector", lambda v: v.tensor_reduce(stat[:, 0:1], src_ap, axis=AX, op=OP.add),
                  after=[e_src, prev])
    e_m = rec.op("vector", lambda v: v.tensor_scalar(stat[:, 2:3], stat[:, 0:1], 1.0 / C, None, OP.mult), after=[e_s2])
    e_v1 = rec.op("vector", lambda v: v.tensor_tensor(stat[:, 3:4], stat[:, 2:3], stat[:, 2:3], OP.mult), after=[e_m])
    e_v2 = rec.op("vector", lambda v: v.scalar_tensor_tensor(stat[:, 4:5], stat[:, 1:2], 1.0 / C, stat[:, 3:4],
                                                             OP.mult, OP.subtract), after=[e_v1, e_s1])
    e_sd = rec.op("scalar", lambda s: s.activation(stat[:, 5:6], stat[:, 4:5], AF.Sqrt, bias=eps_ap), after=[e_v2])
    e_ri = rec.op("vector", lambda v: v.reciprocal(stat[:, 6:7], stat[:, 5:6]), after=[e_sd])
    return e_ri


def build_nc():
    nc = bass.Bass("TRN2", target_bir_lowering=False)
    ctx = contextlib.ExitStack()
    dram = lambda name, shape, dt, kind="Internal": nc.dram_tensor(name, shape, dt, kind=kind).ap()

    xb = dram("xb", [T, C], F32, "ExternalInput")
    x_own = dram("x_own", [TOK, C], F32, "ExternalInput")
    wq_t = dram("wq_t", [2, 128, 16, 512], F32, "ExternalInput")
    wk_t = dram("wk_t", [2, 128, 16, 512], F32, "ExternalInput")
    wv_t = dram("wv_t", [2, 128, 16, 512], F32, "ExternalInput")
    wo_t = dram("wo_t", [4, 128, 16, 512], F32, "ExternalInput")
    lqk = dram("lqk", [4, HD], F32, "ExternalInput")
    rotpq = dram("rotpq", [128, 8], F32, "ExternalInput")
    gate_w = dram("gate_w", [E, C], F32, "ExternalInput")
    weg = dram("weg", [EPC, 11, 128, 16, 128], F32, "ExternalInput")
    weu = dram("weu", [EPC, 11, 128, 16, 128], F32, "ExternalInput")
    wed = dram("wed", [EPC, 11, 4, 128, 512], F32, "ExternalInput")
    swg = dram("swg", [22, 128, 16, 128], F32, "ExternalInput")
    swu = dram("swu", [22, 128, 16, 128], F32, "ExternalInput")
    swd = dram("swd", [22, 4, 128, 512], F32, "ExternalInput")
    out_shard = dram("out_shard", [TOK, C], F32, "ExternalOutput")

    stair_h, idnb_h, idn32_h, onesb_h, tok128_h, csum_h, idqs_h = _consts()
    cSTAIR = nc.inline_tensor(stair_h, name="cSTAIR").ap()
    cIDNB = nc.inline_tensor(idnb_h, name="cIDNB").ap()
    cIDN32 = nc.inline_tensor(idn32_h, name="cIDN32").ap()
    cONESB = nc.inline_tensor(onesb_h, name="cONESB").ap()
    cTOK128 = nc.inline_tensor(tok128_h, name="cTOK128").ap()
    cIDQ = [nc.inline_tensor(idqs_h[b4], name=f"cIDQ{b4}").ap() for b4 in range(4)]
    cCSUM = nc.inline_tensor(csum_h, name="cCSUM").ap()

    x1s = dram("x1s", [T, C], BF16)
    wq_b = dram("wq_b", [2, 128, 16, 512], BF16)
    wk_b = dram("wk_b", [2, 128, 16, 512], BF16)
    wv_b = dram("wv_b", [2, 128, 16, 512], BF16)
    wo_b = dram("wo_b", [4, 128, 16, 512], BF16)
    weg_b = dram("weg_b", [EPC, 11, 128, 16, 128], BF16)
    weu_b = dram("weu_b", [EPC, 11, 128, 16, 128], BF16)
    wed_b = dram("wed_b", [EPC, 11, 4, 128, 512], BF16)
    swg_b = dram("swg_b", [22, 128, 16, 128], BF16)
    swu_b = dram("swu_b", [22, 128, 16, 128], BF16)
    swd_b = dram("swd_b", [22, 4, 128, 512], BF16)
    x2s = dram("x2s", [TOK, C], F32)
    yhalf = dram("yhalf", [128, 8, 1024], BF16)
    ypair = dram("ypair", [2, 128, 8, 1024], BF16)
    x3mine = dram("x3mine", [TOK, C], BF16)
    x3all = dram("x3all", [TTOT + 1, C], BF16)
    wm_mine = dram("wm_mine", [E, TOK], F32)
    wm_all = dram("wm_all", [NCORE, E, TOK], F32)
    yp = dram("yp", [TTOT + 1, C], BF16)
    y_rs = dram("y_rs", [TOK, C], BF16)
    strm = dram("strm", [2, 9216], F32)
    wbb = dram("wbb", [EPC, CAP], F32)

    sbt = lambda name, shape, dt: ctx.enter_context(nc.sbuf_tensor(name, shape, dt))
    ROTS = sbt("ROTS", [128, 8], F32)
    IDQS = [sbt(f"IDQS{b4}", [128, 32], I16) for b4 in range(4)]
    STAIR = sbt("STAIR", [128, 896], BF16)
    IDNB = sbt("IDNB", [128, 128], BF16)
    IDN32 = sbt("IDN32", [128, 128], F32)
    ONESB = sbt("ONESB", [128, 1], BF16)
    ZEROT = sbt("ZEROT", [128, 4096], BF16)
    LAM = sbt("LAM", [1, 4], F32)
    GWT = sbt("GWT", [128, 16, 16], BF16)
    EPS = sbt("EPS", [128, 2], F32)

    rec = Rec(nc, ctx)
    EV0 = ("PE", 0)

    rank_reg = [None]

    def init_g(g):
        g.load_library(lc.mlp)
        rank_reg[0] = g.to_reg(g.partition_id())
    rec.raw("gpsimd", init_g)


    e_c = {}
    for nm, dst, src in (("st", STAIR, cSTAIR), ("ib", IDNB, cIDNB),
                         ("i32", IDN32, cIDN32), ("ob", ONESB, cONESB)):
        e_c[nm] = rec.dma("sync", "cst_" + nm, dst[:], src[:])
    e_c["rot"] = rec.dma("sync", "cst_rot", ROTS[:], rotpq[:])
    for b4 in range(4):
        e_c[f"idq{b4}"] = rec.dma("sync", f"cst_idq{b4}", IDQS[b4][:], cIDQ[b4][:])
    e_zt = rec.op("vector", lambda v: v.memset(ZEROT[:], 0.0))
    rec.op("vector", lambda v: v.memset(EPS[:, 0:1], LN_EPS))
    rec.op("vector", lambda v: v.memset(EPS[:, 1:2], RMS_EPS))

    ZELEM = 128 * 4096
    flat_yp = yp.rearrange("r c -> (r c)")
    nzp = ((TTOT + 1) * C) // ZELEM
    zero_chunks = []
    for z in range(nzp):
        zero_chunks.append((flat_yp[z * ZELEM:(z + 1) * ZELEM].rearrange("(p f) -> p f", p=128), ZEROT[:]))
    rem = (TTOT + 1) * C - nzp * ZELEM
    if rem:
        zero_chunks.append((flat_yp[nzp * ZELEM:].rearrange("(p f) -> p f", p=128),
                            ZEROT[:, 0:rem // 128]))
    zero_chunks.append((x3all[PADROW:PADROW + 1, :], ZEROT[0:1, 0:C]))

    lvs = sbt("lvs", [1, 4 * HD], F32)
    lpr = sbt("lpr", [1, 2 * HD], F32)
    lred = sbt("lred", [1, 4], F32)
    e_lv = rec.dma("sync", "lqk", lvs[:], lqk.rearrange("a b -> (a b)").rearrange("(o f) -> o f", o=1))
    e_p1 = rec.op("vector", lambda v: v.tensor_tensor(lpr[:, 0:HD], lvs[:, 0:HD], lvs[:, HD:2 * HD], OP.mult), after=[e_lv])
    e_p2 = rec.op("vector", lambda v: v.tensor_tensor(lpr[:, HD:2 * HD], lvs[:, 2 * HD:3 * HD], lvs[:, 3 * HD:4 * HD], OP.mult), after=[e_lv, e_p1])
    e_r1 = rec.op("vector", lambda v: v.tensor_reduce(lred[:, 0:1], lpr[:, 0:HD], axis=AX, op=OP.add), after=[e_p2])
    e_r2 = rec.op("vector", lambda v: v.tensor_reduce(lred[:, 1:2], lpr[:, HD:2 * HD], axis=AX, op=OP.add), after=[e_r1])
    e_ex = rec.op("scalar", lambda s: s.activation(lred[:, 0:2], lred[:, 0:2], AF.Exp), after=[e_r2])
    e_l0 = rec.op("vector", lambda v: v.tensor_tensor(LAM[0:1, 0:1], lred[:, 0:1], lred[:, 1:2], OP.subtract), after=[e_ex])
    e_l1 = rec.op("vector", lambda v: v.tensor_scalar(LAM[0:1, 0:1], LAM[0:1, 0:1], LAMBDA_INIT, None, OP.add), after=[e_l0])
    e_lam = rec.op("vector", lambda v: v.tensor_scalar(LAM[0:1, 1:2], LAM[0:1, 0:1], -1.0, None, OP.mult), after=[e_l1])

    gwsb = sbt("gwsb", [16, C], BF16)
    e_gw = rec.dma("gpsimd", "gwl", gwsb[:], gate_w[:])
    with nc.psum_tensor("pgt", [128, 16], BF16) as pgt:
        ev_prev = EV0
        for k in range(16):
            e_t = rec.op("tensor", lambda t, k=k: t.transpose(pgt[:, :], gwsb[:, k * 128:(k + 1) * 128], IDNB[0:16, 0:16]),
                         after=[e_gw, e_c["ib"], ev_prev])
            ev_prev = rec.op("vector", lambda v, k=k: v.tensor_copy(GWT[:, k, :], pgt[:, :]), after=[e_t])
        e_gwt = ev_prev
        rec.emit()

    with (
        nc.sbuf_tensor("xt", [128, 2, C], F32) as xt,
        nc.sbuf_tensor("x1t", [128, 2, C], BF16) as x1t,
        nc.sbuf_tensor("sqd", [128, C], BF16) as sqd,
        nc.sbuf_tensor("st8", [128, 8], F32) as st8,
    ):
        e_wv = rec.dma("gpsimd", "c_wv", wv_b[:], wv_t[:], bg=True)
        e_wq = rec.dma("gpsimd", "c_wq", wq_b[:], wq_t[:], bg=True)
        e_wk = rec.dma("gpsimd", "c_wk", wk_b[:], wk_t[:], bg=True)
        prev_use = [EV0, EV0]
        prev_stat = EV0
        last_w = EV0
        for i in range(16):
            sl = i % 2
            e_x = rec.dma("sync", f"xls{sl}", xt[:, sl, :], xb[i * 128:(i + 1) * 128, :], after=[prev_use[sl]])
            e_ri = _ln_stats(rec, st8, xt[:, sl, :], sqd, e_x, prev_stat, EPS[:, 0:1])
            e_ap = rec.op("vector", lambda v, sl=sl: v.tensor_scalar(x1t[:, sl, :], xt[:, sl, :], st8[:, 2:3], st8[:, 6:7],
                                                                     OP.subtract, OP.mult), after=[e_ri])
            prev_stat = e_ap
            e_w = rec.dma("scalar", f"x1w{sl}", x1s[i * 128:(i + 1) * 128, :], x1t[:, sl, :], after=[e_ap])
            prev_use[sl] = e_w
            last_w = e_w
        e_x1_done = [prev_use[0], prev_use[1]]
        rec.emit()

    e_cast = {}
    bg_chunks = []
    for i in range(4):
        bg_chunks.append(("wo", wo_b[i], wo_t[i]))
    for lohi in range(2):
        bg_chunks.append(("swg", swg_b[11 * lohi:11 * (lohi + 1)], swg[11 * lohi:11 * (lohi + 1)]))
        bg_chunks.append(("swu", swu_b[11 * lohi:11 * (lohi + 1)], swu[11 * lohi:11 * (lohi + 1)]))
    for nm, db, sb_ in (("weg", weg_b, weg), ("weu", weu_b, weu), ("wed", wed_b, wed)):
        for ei_ in range(EPC):
            for lohi in range(2):
                sl_ = slice(6 * lohi, 6 + 5 * lohi)
                bg_chunks.append((nm, db[ei_, sl_], sb_[ei_, sl_]))
    for lohi in range(2):
        bg_chunks.append(("swd", swd_b[11 * lohi:11 * (lohi + 1)], swd[11 * lohi:11 * (lohi + 1)]))
    for dst, src_ in zero_chunks:
        bg_chunks.append(("zero", dst, src_, e_zt))

    def bg_pump(k=1):
        for _ in range(k):
            if bg_chunks:
                ch = bg_chunks.pop(0)
                key, dst, src = ch[0], ch[1], ch[2]
                aft = [ch[3]] if len(ch) > 3 else ()
                e_cast[key] = rec.dma("gpsimd", "cast_" + key, dst, src, bg=True, after=aft)

    with nc.sbuf_tensor("YT", [128, 8, T], BF16) as YT:
        for p in range(2):
            with (
                nc.sbuf_tensor(f"QT{p}", [128, 4, T], BF16) as QT,
                nc.sbuf_tensor(f"KT{p}", [128, 4, T], BF16) as KT,
                nc.sbuf_tensor(f"VS{p}", [128, 16, 512], BF16) as VS,
                nc.sbuf_tensor(f"WTS{p}", [128, 16, 512], BF16) as WTS,
            ):
                with nc.sbuf_tensor(f"X1T{p}", [128, 4, 16, 512], BF16) as X1T:
                    rec.stream("x1T")
                    e_x1T_l = []
                    prevg = e_x1_done
                    for g4 in range(4):
                        ev = rec.op("gpsimd", lambda g, g4=g4: g.dma_gather(
                            X1T[:, g4, :, :], x1s[:], IDQS[g4][:], num_idxs=512, num_idxs_reg=512,
                            elem_size=C, transpose=True),
                            after=list(prevg) + [e_c[f"idq{g4}"]] if g4 == 0 else [prevg, e_c[f"idq{g4}"]],
                            sem="x1T", n=16)
                        e_x1T_l.append(ev)
                        prevg = ev
                    e_x1T = [e_x1T_l[-1]] * 16
                    e_wts_1 = rec.dma("scalar", "wts", WTS[:, :, :], wv_b[p], after=[e_wv])
                    e_wts = [e_wts_1] * 16
                    e_vs = {}
                    with nc.psum_tensor(f"pv{p}", [128, 8, 512], F32) as pv:
                        for tb in range(16):
                            j = tb % 8
                            last = None
                            for cc in range(16):
                                aft = [e_x1T[cc], e_wts[cc]]
                                if cc == 0 and tb >= 8:
                                    aft.append(e_vs[tb - 8])
                                last = rec.op("tensor", lambda t, j=j, tb=tb, cc=cc: t.matmul(
                                    pv[:, j, :], X1T[:, tb // 4, cc, (tb % 4) * 128:(tb % 4 + 1) * 128], WTS[:, cc, :],
                                    start=(cc == 0), stop=(cc == 15)), after=aft)
                            e_vs[tb] = rec.op("vector", lambda v, j=j, tb=tb: v.tensor_copy(VS[:, tb, :], pv[:, j, :]),
                                              after=[last])
                    e_vs_all = [e_vs[tb] for tb in range(16)]
                    e_vs_last = rec.now("DV")
                    bg_pump()

                    def qk_proj(dst, w_b, e_w_ready, e_free, nm, init_ev):
                        e_w_1 = rec.dma("scalar", "wts", WTS[:, :, :], w_b[p], after=[e_w_ready, e_free])
                        e_w = [e_w_1] * 16
                        evs = []
                        with nc.psum_tensor(nm, [128, 2, 512], F32) as pqk:
                            prev = [init_ev, init_ev]
                            for cc in range(4):
                                for tc in range(4):
                                    sl = (cc * 4 + tc) % 2
                                    last = None
                                    for k in range(16):
                                        aft = [e_w[k], e_x1T[k]]
                                        if k == 0:
                                            aft.append(prev[sl])
                                        last = rec.op("tensor", lambda t, cc=cc, tc=tc, k=k, sl=sl: t.matmul(
                                            pqk[:, sl, :], WTS[:, k, cc * 128:(cc + 1) * 128],
                                            X1T[:, tc, k, :],
                                            start=(k == 0), stop=(k == 15)), after=aft)
                                    prev[sl] = rec.op("vector", lambda v, cc=cc, tc=tc, sl=sl: v.tensor_copy(
                                        dst[:, cc, tc * 512:(tc + 1) * 512], pqk[:, sl, :]), after=[last])
                                    evs.append(prev[sl])
                        return evs

                    e_q_evs = qk_proj(QT, wq_b, e_wq, rec.now("PE"), f"pqkq{p}", e_vs[15])
                    bg_pump()
                    e_k_evs = qk_proj(KT, wk_b, e_wk, rec.now("PE"), f"pqkk{p}", e_q_evs[-1])
                    bg_pump()
                rec.emit()

                rope_ev = {}
                with (
                    nc.sbuf_tensor(f"sqr{p}", [128, T], BF16) as sqr,
                    nc.sbuf_tensor(f"rr{p}", [1, 2 * T], F32) as rr,
                    nc.sbuf_tensor(f"rrb{p}", [1, T], BF16) as rrb,
                    nc.sbuf_tensor(f"rb{p}", [128, T], BF16) as rb,
                    nc.sbuf_tensor(f"TMPA{p}", [128, T], BF16) as TMPA,
                    nc.sbuf_tensor(f"TMPB{p}", [128, T], BF16) as TMPB,
                    nc.psum_tensor(f"pr{p}", [1, 4, 512], F32) as pr,
                ):
                    prev_rope = e_k_evs[-1]
                    prev_rb_rd = EV0
                    for mi, (src, src_evs) in enumerate(((QT, e_q_evs), (KT, e_k_evs))):
                        for ch in range(4):
                            tile = src[:, ch, :]
                            lh = ch // 2
                            e_sq = rec.op("scalar", lambda s, tile=tile: s.activation(sqr[:, :], tile, AF.Square),
                                          after=src_evs + [prev_rope])
                            e_ms = []
                            for q4 in range(4):
                                e_ms.append(rec.op("tensor", lambda t, q4=q4: t.matmul(
                                    pr[:, q4, :], ONESB[:, :], sqr[:, q4 * 512:(q4 + 1) * 512],
                                    start=True, stop=True), after=[e_sq, e_c["ob"], prev_rope]))
                            e_rs = rec.op("scalar", lambda s: s.activation(rr[:, 0:T], pr[:, :, :], AF.Sqrt,
                                                                           scale=1.0 / 128.0, bias=EPS[0:1, 1:2]), after=e_ms)
                            e_rc = rec.op("vector", lambda v: v.reciprocal(rr[:, T:2 * T], rr[:, 0:T]), after=[e_rs])
                            e_rb = rec.op("vector", lambda v: v.tensor_copy(rrb[:, :], rr[:, T:2 * T]), after=[e_rc])
                            e_bc = rec.op("gpsimd", lambda g: g.partition_broadcast(rb[:, :], rrb[:, :], channels=128),
                                          after=[e_rb, prev_rb_rd])
                            e_sw1 = rec.dma("sync", "swp", TMPB[0:64, :], tile[64:128, :], after=src_evs + [prev_rope])
                            e_sw2 = rec.dma("sync", "swp", TMPB[64:128, :], tile[0:64, :], after=[e_sw1])
                            e_t1 = rec.op("vector", lambda v, tile=tile, lh=lh: v.tensor_scalar(
                                TMPA[:, :], tile, ROTS[:, lh:lh + 1], None, OP.mult),
                                after=[e_c["rot"], e_sw2, prev_rope])
                            e_t2 = rec.op("vector", lambda v, tile=tile, lh=lh: v.scalar_tensor_tensor(
                                tile, TMPB[:, :], ROTS[:, 4 + lh:5 + lh], TMPA[:, :], OP.mult, OP.add),
                                after=[e_t1, e_sw2])
                            e_nb = rec.op("vector", lambda v, tile=tile: v.tensor_tensor(tile, tile, rb[:, :], OP.mult),
                                          after=[e_t2, e_bc])
                            prev_rb_rd = e_nb
                            rope_ev[(mi, ch)] = e_nb
                            prev_rope = e_nb
                    e_rope_all = prev_rope
                    bg_pump(2)
                    rec.emit()

                with (
                    nc.sbuf_tensor(f"PT{p}", [128, 2, 2, 512], BF16) as PT,
                    nc.sbuf_tensor(f"zrow{p}", [1, 2, 512], F32) as zrow,
                    nc.sbuf_tensor(f"zbf{p}", [128, 2, 512], F32) as zbf,
                    nc.sbuf_tensor(f"csc{p}", [128, 2, 512], F32) as csc,
                    nc.psum_tensor(f"pO{p}", [128, 4, 512], F32) as pO,
                    nc.psum_tensor(f"pS{p}", [128, 2, 512], F32) as pS,
                    nc.psum_tensor(f"pZ{p}", [1, 2, 512], F32) as pZ,
                ):
                    prev_exp = e_rope_all
                    prev_zread = e_rope_all
                    prev_zrow_rd = e_rope_all
                    prev_cmb = [e_rope_all] * 4
                    prev_zo = [[e_rope_all], [e_rope_all]]
                    for lh in range(2):
                        for qc in range(4):
                            nkb = (qc + 1) * 4
                            e_zl = None
                            e_ol = {}
                            qk_ev = {j: [rope_ev[(0, 2 * lh + j)], rope_ev[(1, 2 * lh + j)]] for j in range(2)}
                            e_s = {}
                            for j in range(2):
                                e_s[j] = rec.op("tensor", lambda t, lh=lh, j=j, qc=qc: t.matmul(
                                    pS[:, j, :], KT[:, 2 * lh + j, 0:128],
                                    QT[:, 2 * lh + j, qc * 512:(qc + 1) * 512],
                                    start=True, stop=True),
                                    after=qk_ev[j] + [prev_exp])
                            for kb in range(nkb):
                                par = kb % 2
                                e_e = rec.op("scalar", lambda s, par=par: s.activation(
                                    PT[:, par, :, :], pS[:, :, :], AF.Exp, scale=ISQ),
                                    after=[e_s[0], e_s[1]] + prev_zo[par])
                                prev_exp = e_e
                                if kb + 1 < nkb:
                                    for j in range(2):
                                        e_s[j] = rec.op("tensor", lambda t, lh=lh, j=j, kb=kb, qc=qc: t.matmul(
                                            pS[:, j, :], KT[:, 2 * lh + j, (kb + 1) * 128:(kb + 2) * 128],
                                            QT[:, 2 * lh + j, qc * 512:(qc + 1) * 512],
                                            start=True, stop=True),
                                            after=qk_ev[j] + [e_e])
                                evs_j = [e_e, e_e]
                                delta = kb * 128 - qc * 512
                                if 0 <= delta <= 384:
                                    for j in range(2):
                                        evs_j[j] = rec.op("vector", lambda v, j=j, par=par, delta=delta: v.tensor_tensor(
                                            PT[:, par, j, :], PT[:, par, j, :], STAIR[:, 384 - delta:896 - delta], OP.mult),
                                            after=[evs_j[j], e_c["st"]])
                                zo_evs = []
                                for j in range(2):
                                    e_z = rec.op("tensor", lambda t, j=j, par=par, kb=kb, nkb=nkb: t.matmul(
                                        pZ[:, j, :], ONESB[:, :], PT[:, par, j, :],
                                        start=(kb == 0), stop=(kb == nkb - 1)),
                                        after=[evs_j[j]] + ([prev_zread] if kb == 0 else []))
                                    e_zl = e_z
                                    zo_evs.append(e_z)
                                    for d2 in range(2):
                                        aft = [evs_j[j], e_vs_last]
                                        if kb == 0:
                                            aft.append(prev_cmb[2 * j + d2])
                                        e_ol[2 * j + d2] = rec.op("tensor", lambda t, lh=lh, j=j, d2=d2, par=par, kb=kb, nkb=nkb: t.matmul(
                                            pO[:, 2 * j + d2, :], VS[:, kb, 256 * lh + 128 * d2:256 * lh + 128 * (d2 + 1)],
                                            PT[:, par, j, :], start=(kb == 0), stop=(kb == nkb - 1)), after=aft)
                                        zo_evs.append(e_ol[2 * j + d2])
                                prev_zo[par] = zo_evs
                            e_zc = rec.op("vector", lambda v: v.tensor_copy(zrow[:, :, :], pZ[:, :, :]),
                                          after=[e_zl, prev_zrow_rd])
                            prev_zread = e_zc
                            e_zr = rec.op("vector", lambda v: v.reciprocal(zrow[:, :, :], zrow[:, :, :]), after=[e_zc])
                            e_zs = rec.op("vector", lambda v: v.tensor_scalar(zrow[:, 1, :], zrow[:, 1, :], LAM[0:1, 1:2], None, OP.mult),
                                          after=[e_zr, e_lam])
                            e_zb = rec.op("gpsimd", lambda g: g.partition_broadcast(zbf[:, :, :], zrow[:, :, :], channels=128),
                                          after=[e_zs])
                            prev_zrow_rd = e_zb
                            prev_y = e_zb
                            for d2 in range(2):
                                e_c1 = rec.op("vector", lambda v, d2=d2: v.tensor_tensor(csc[:, 0, :], pO[:, d2, :], zbf[:, 0, :], OP.mult),
                                              after=[e_zb, e_ol[d2], prev_cmb[d2], prev_y])
                                e_c2 = rec.op("vector", lambda v, d2=d2: v.tensor_tensor(csc[:, 1, :], pO[:, 2 + d2, :], zbf[:, 1, :], OP.mult),
                                              after=[e_c1, e_ol[2 + d2]])
                                ych = 4 * p + 2 * lh + d2
                                e_y = rec.op("vector", lambda v, ych=ych, qc=qc: v.tensor_tensor(
                                    YT[:, ych, qc * 512:(qc + 1) * 512], csc[:, 0, :], csc[:, 1, :], OP.add),
                                    after=[e_c2])
                                prev_cmb[d2] = e_y
                                prev_cmb[2 + d2] = e_y
                                prev_y = e_y
                            bg_pump()
                    rec.emit()
        e_yt_done = rec.now("DV")
        bg_pump(len(bg_chunks))
        e_wo, e_swg, e_swu = e_cast["wo"], e_cast["swg"], e_cast["swu"]
        e_weg, e_weu, e_wed, e_swd = e_cast["weg"], e_cast["weu"], e_cast["wed"], e_cast["swd"]

        rec.stream("yex")

        def emit_send(g):
            for c in range(NCORE):
                with g.If_eq(rank_reg[0], c):
                    off = 1024 if c % 2 == 0 else 0
                    g.dma_start(out=yhalf[:], in_=YT[:, :, off:off + 1024]).then_inc(rec.sems["yex"][0], 16)
                g.end_ifs()
        rec.raw("gpsimd", emit_send, after=[e_yt_done])
        e_send = rec.bump("yex", 16)

        e_ag_y = rec.op("gpsimd", lambda g: g.collective_compute(
            "AllGather", OP.bypass, ins=[yhalf[:]], outs=[ypair[:]], replica_groups=PAIRS),
            after=[e_send], sem="CO", n=1)

        e_fsh_dep = []
        with (
            nc.sbuf_tensor("YF", [128, 16, 1024], BF16) as YF,
            nc.sbuf_tensor("wt5", [128, 2, 16, 512], BF16) as wt5,
            nc.sbuf_tensor("a_sb", [128, 8, C], BF16) as a_sb,
        ):
            rec.stream("yasm")

            def emit_asm(g):
                for c in range(NCORE):
                    with g.If_eq(rank_reg[0], c):
                        if c % 2 == 0:
                            g.dma_start(out=YF[:, 0:8, :], in_=YT[:, :, 0:1024]).then_inc(rec.sems["yasm"][0], 16)
                            g.dma_start(out=YF[:, 8:16, :], in_=ypair[1]).then_inc(rec.sems["yasm"][0], 16)
                        else:
                            g.dma_start(out=YF[:, 0:8, :], in_=ypair[0]).then_inc(rec.sems["yasm"][0], 16)
                            g.dma_start(out=YF[:, 8:16, :], in_=YT[:, :, 1024:2048]).then_inc(rec.sems["yasm"][0], 16)
                    g.end_ifs()
            rec.raw("gpsimd", emit_asm, after=[e_ag_y, e_yt_done])
            e_yf = rec.bump("yasm", 32)

            with nc.psum_tensor("pa", [128, 8, 512], F32) as pa:
                rec.stream("wo")
                prev_ev = {tb: EV0 for tb in range(8)}
                slot_last_mm = [EV0, EV0]
                prev_lw = EV0
                for coq in range(4):
                    sl = coq % 2
                    e_lw = rec.dma("sync", f"wo{sl}", wt5[:, sl, :, :], wo_b[coq],
                                   after=[e_wo, slot_last_mm[sl]])
                    prev_lw = e_lw
                    last_mm = None
                    for cch in range(16):
                        for tb in range(8):
                            aft = [e_lw, e_yf]
                            if cch == 0:
                                aft.append(prev_ev[tb])
                            last_mm = rec.op("tensor", lambda t, tb=tb, cch=cch, sl=sl: t.matmul(
                                pa[:, tb, :], YF[:, cch, tb * 128:(tb + 1) * 128], wt5[:, sl, cch, :],
                                start=(cch == 0), stop=(cch == 15)), after=aft)
                    slot_last_mm[sl] = last_mm
                    pe_now = rec.now("PE")
                    for tb in range(8):
                        prev_ev[tb] = rec.op("scalar", lambda s, tb=tb, coq=coq: s.activation(
                            a_sb[:, tb, coq * 512:(coq + 1) * 512], pa[:, tb, :], AF.Copy, bias=0.0, scale=0.8),
                            after=[pe_now])
                e_a_done = [prev_ev[tb] for tb in range(8)]
            rec.emit()

            with (
                nc.sbuf_tensor("xo", [128, C], F32) as xo,
                nc.sbuf_tensor("lnb", [128, C], F32) as lnb,
                nc.sbuf_tensor("x2t", [128, C], F32) as x2t,
                nc.sbuf_tensor("x3t", [128, C], BF16) as x3t,
                nc.sbuf_tensor("sqd2", [128, C], BF16) as sqd2,
                nc.sbuf_tensor("st2", [128, 8], F32) as st2,
            ):
                e_x3w = []
                prev = EV0
                prev2 = EV0
                for tb in range(8):
                    atile = a_sb[:, tb, :]
                    e_xo = rec.dma("sync", "xow", xo[:, :], x_own[tb * 128:(tb + 1) * 128, :], after=[prev])
                    e_ri = _ln_stats(rec, st2, atile, sqd2, e_a_done[tb], prev, EPS[:, 0:1])
                    e_ln = rec.op("vector", lambda v, atile=atile: v.tensor_scalar(lnb[:, :], atile, st2[:, 2:3], st2[:, 6:7],
                                                                                   OP.subtract, OP.mult), after=[e_ri])
                    e_x2 = rec.op("vector", lambda v: v.tensor_tensor(x2t[:, :], xo[:, :], lnb[:, :], OP.add), after=[e_ln, e_xo, prev2])
                    e_w2 = rec.dma("scalar", "x2w", x2s[tb * 128:(tb + 1) * 128, :], x2t[:, :], after=[e_x2])
                    e_rib = _ln_stats(rec, st2, x2t[:, :], sqd2, e_x2, e_ln, EPS[:, 0:1])
                    e_x3 = rec.op("vector", lambda v: v.tensor_scalar(x3t[:, :], x2t[:, :], st2[:, 2:3], st2[:, 6:7],
                                                                      OP.subtract, OP.mult), after=[e_rib])
                    e_w3 = rec.dma("scalar", "x3w", x3mine[tb * 128:(tb + 1) * 128, :], x3t[:, :], after=[e_x3])
                    e_x3w.append(e_w3)
                    prev = e_w3
                    prev2 = e_w2
                rec.emit()

    e_ag_x3 = rec.op("gpsimd", lambda g: g.collective_compute(
        "AllGather", OP.bypass, ins=[x3mine[:]],
        outs=[x3all[0:TTOT, :].rearrange("(s t) c -> s t c", s=NCORE)],
        replica_groups=ALL8), after=e_x3w, sem="CO", n=1)

    FSH = ctx.enter_context(nc.sbuf_tensor("FSH", [128, 8, C], BF16))
    idxT0 = ctx.enter_context(nc.sbuf_tensor("idxT0", [128, 64], I16))
    idxT1 = ctx.enter_context(nc.sbuf_tensor("idxT1", [128, 64], I16))
    wbt0 = ctx.enter_context(nc.sbuf_tensor("wbt0", [128, 8], F32))
    wbt1 = ctx.enter_context(nc.sbuf_tensor("wbt1", [128, 8], F32))
    IDXT = [idxT0, idxT1]
    WBT = [wbt0, wbt1]
    route_evs = []

    MT = ctx.enter_context(nc.sbuf_tensor("MT", [128, 22, TOK], BF16))
    if True:
        e_mt = []
        with nc.sbuf_tensor("X3T", [128, 2, 16, 512], BF16) as X3T:
            rec.stream("x3T")
            e_x3a = rec.op("gpsimd", lambda g: g.dma_gather(
                X3T[:, 0, :, :], x3mine[:], IDQS[0][:], num_idxs=512, num_idxs_reg=512,
                elem_size=C, transpose=True), after=[e_x3w[-1], e_c["idq0"]], sem="x3T", n=16)
            e_x3b = rec.op("gpsimd", lambda g: g.dma_gather(
                X3T[:, 1, :, :], x3mine[:], IDQS[1][:], num_idxs=512, num_idxs_reg=512,
                elem_size=C, transpose=True), after=[e_x3a, e_c["idq1"]], sem="x3T", n=16)
            e_x3T = [e_x3b] * 16

            with (
                nc.sbuf_tensor("LTs", [16, TOK], F32) as LTs,
                nc.sbuf_tensor("Ltok", [128, 8, 16], F32) as Ltok,
                nc.sbuf_tensor("gsc", [128, 8, 16], F32) as gsc,
                nc.sbuf_tensor("gst", [128, 10], F32) as gst,
                nc.sbuf_tensor("m8s", [128, 8], F32) as m8s,
                nc.sbuf_tensor("WMs", [16, TOK], F32) as WMs,
                nc.psum_tensor("pL", [16, 2, 512], F32) as pL,
                nc.psum_tensor("pTt", [128, 16], F32) as pTt,
                nc.psum_tensor("pwt", [16, 128], F32) as pwt,
            ):
                e_lcp = []
                for tc2 in range(2):
                    last = None
                    for k in range(16):
                        last = rec.op("tensor", lambda t, tc2=tc2, k=k: t.matmul(
                            pL[:, tc2, :], GWT[:, k, :], X3T[:, tc2, k, :],
                            start=(k == 0), stop=(k == 15)), after=[e_gwt, e_x3T[k]])
                    e_lcp.append(rec.op("vector", lambda v, tc2=tc2: v.tensor_copy(
                        LTs[:, tc2 * 512:(tc2 + 1) * 512], pL[:, tc2, :]), after=[last]))

                prev_t = EV0
                prev_w = EV0
                for tb in range(8):
                    e_tr = rec.op("tensor", lambda t, tb=tb: t.transpose(pTt[:, :], LTs[:, tb * 128:(tb + 1) * 128], IDN32[0:16, 0:16]),
                                  after=e_lcp + [e_c["i32"], prev_t])
                    e_ev = rec.op("vector", lambda v, tb=tb: v.tensor_copy(Ltok[:, tb, :], pTt[:, :]), after=[e_tr])
                    prev_t = e_ev
                    lt = Ltok[:, tb, :]
                    e_mx = rec.op("vector", lambda v, lt=lt: v.tensor_reduce(gst[:, 0:1], lt, axis=AX, op=OP.max), after=[e_ev])
                    e_nm = rec.op("vector", lambda v: v.tensor_scalar(gst[:, 1:2], gst[:, 0:1], -1.0, None, OP.mult), after=[e_mx])
                    e_p = rec.op("scalar", lambda s, lt=lt, tb=tb: s.activation(gsc[:, tb, :], lt, AF.Exp, bias=gst[:, 1:2],
                                                                                accum_out=gst[:, 2:3]), after=[e_nm])
                    e_m8 = rec.op("vector", lambda v, tb=tb: v.max(out=m8s[:, :], in_=gsc[:, tb, :]), after=[e_p])
                    e_m = rec.op("vector", lambda v, tb=tb: v.tensor_scalar(Ltok[:, tb, :], gsc[:, tb, :], m8s[:, 3:4], None, OP.is_ge),
                                 after=[e_m8])
                    e_pm = rec.op("vector", lambda v, tb=tb: v.tensor_tensor(gsc[:, tb, :], gsc[:, tb, :], Ltok[:, tb, :], OP.mult),
                                  after=[e_m])
                    e_s4 = rec.op("vector", lambda v, tb=tb: v.tensor_reduce(gst[:, 3:4], gsc[:, tb, :], axis=AX, op=OP.add), after=[e_pm])
                    e_zz = rec.op("vector", lambda v: v.tensor_scalar(gst[:, 4:5], gst[:, 2:3], 1e-20, None, OP.mult), after=[e_p, e_s4])
                    e_dn = rec.op("vector", lambda v: v.tensor_tensor(gst[:, 5:6], gst[:, 3:4], gst[:, 4:5], OP.add), after=[e_zz])
                    e_di = rec.op("vector", lambda v: v.reciprocal(gst[:, 6:7], gst[:, 5:6]), after=[e_dn])
                    e_w = rec.op("vector", lambda v, tb=tb: v.tensor_scalar(gsc[:, tb, :], gsc[:, tb, :], gst[:, 6:7], None, OP.mult),
                                 after=[e_di])
                    e_tw = rec.op("tensor", lambda t, tb=tb: t.transpose(pwt[:, :], gsc[:, tb, :], IDN32[:, :]),
                                  after=[e_w, prev_w])
                    e_wv2 = rec.op("vector", lambda v, tb=tb: v.tensor_copy(WMs[:, tb * 128:(tb + 1) * 128], pwt[:, :]), after=[e_tw])
                    prev_w = e_wv2
                e_wmd = rec.dma("scalar", "wmw", wm_mine[:], WMs[:], after=[prev_w])
                rec.emit()

            e_ag_wm = rec.op("gpsimd", lambda g: g.collective_compute(
                "AllGather", OP.bypass, ins=[wm_mine[:]], outs=[wm_all[:]], replica_groups=ALL8),
                after=[e_wmd], sem="CO", n=1)

            mrg = contextlib.ExitStack()
            _mt = lambda name, shape, dt: mrg.enter_context(nc.sbuf_tensor(name, shape, dt))
            wst = _mt("wst", [128, 2, 2, 16, 128], BF16)
            gsil = _mt("gsil", [128, 2, TOK], BF16)
            Wfull = _mt("Wfull", [128, TOK], F32)
            mfull = _mt("mfull", [128, TOK], F32)
            ones2 = _mt("ones2", [128, TOK], F32)
            Rx = _mt("Rx", [128, TOK], F32)
            keptf = _mt("keptf", [128, TOK], F32)
            TOKc = _mt("TOKc", [128, TOK], F32)
            CSUM = _mt("CSUM", [128, 128], F32)
            vtok = _mt("vtok", [128, TOK], F32)
            vw = _mt("vw", [128, TOK], F32)
            tot = _mt("tot", [128, 2], F32)
            sent = _mt("sent", [1, 2 * TOK], F32)
            sgi = _mt("sgi", [16, 2, 576], F32)
            sgo = _mt("sgo", [16, 144], F32)
            sgw = _mt("sgw", [16, 144], F32)
            nfo = _mt("nfo", [1, 4], U32)
            pbase = mrg.enter_context(nc.psum_tensor("pbase", [128, 1], F32))
            pm1 = mrg.enter_context(nc.psum_tensor("pm1", [128, 2, 512], F32))
            pm2 = mrg.enter_context(nc.psum_tensor("pm2", [128, 2, 512], F32))
            if True:
                sh_gate_rd = [EV0, EV0]
                sh_up_rd = [EV0, EV0]
                sh_silu = []
                sh_mul = []
                sh_mul_sl = [[], []]

                def emit_shared_ic(ic):
                    nonlocal sh_silu, sh_mul
                    sl = ic % 2
                    e_wg = rec.dma("scalar", f"sstg{sl}", wst[:, 0, sl, :, :], swg_b[ic],
                                   after=[sh_gate_rd[sl], e_swg])
                    lastg = [None, None]
                    for h2 in range(2):
                        for k in range(16):
                            aft = [e_wg, e_x3T[k]]
                            if k == 0:
                                aft += sh_silu
                            lastg[h2] = rec.op("tensor", lambda t, sl=sl, k=k, h2=h2: t.matmul(
                                pm1[:, h2, :], wst[:, 0, sl, k, :], X3T[:, h2, k, :],
                                start=(k == 0), stop=(k == 15)), after=aft)
                    sh_gate_rd[sl] = lastg[1]
                    e_sil = rec.op("scalar", lambda s, sl=sl: s.activation(gsil[:, sl, 0:512], pm1[:, 0, :], AF.Silu),
                                   after=[lastg[0]] + sh_mul_sl[sl])
                    e_sil2 = rec.op("scalar", lambda s, sl=sl: s.activation(gsil[:, sl, 512:1024], pm1[:, 1, :], AF.Silu),
                                    after=[lastg[1], e_sil])
                    sh_silu = [e_sil, e_sil2]
                    e_wu = rec.dma("scalar", f"sstu{sl}", wst[:, 1, sl, :, :], swu_b[ic],
                                   after=[sh_up_rd[sl], e_swu])
                    lastu = [None, None]
                    for h2 in range(2):
                        for k in range(16):
                            aft = [e_wu, e_x3T[k]]
                            if k == 0:
                                aft += sh_mul
                            lastu[h2] = rec.op("tensor", lambda t, sl=sl, k=k, h2=h2: t.matmul(
                                pm2[:, h2, :], wst[:, 1, sl, k, :], X3T[:, h2, k, :],
                                start=(k == 0), stop=(k == 15)), after=aft)
                    sh_up_rd[sl] = lastu[1]
                    e_mul1 = rec.op("vector", lambda v, ic=ic, sl=sl: v.tensor_tensor(
                        MT[:, ic, 0:512], pm2[:, 0, :], gsil[:, sl, 0:512], OP.mult),
                        after=[lastu[0], e_sil])
                    e_mul2 = rec.op("vector", lambda v, ic=ic, sl=sl: v.tensor_tensor(
                        MT[:, ic, 512:1024], pm2[:, 1, :], gsil[:, sl, 512:1024], OP.mult),
                        after=[lastu[1], e_sil2, e_mul1])
                    sh_mul = [e_mul1, e_mul2]
                    sh_mul_sl[sl] = [e_mul1, e_mul2]
                    e_mt.append(e_mul2)

                e_wf_l = [rec.dma("sync", "wmr", Wfull[e8 * 8:(e8 + 1) * 8, :], wm_all[:, e8, :], after=[e_ag_wm]) for e8 in range(E)]
                e_wf = [e_wf_l[-1]] * NCORE
                e_tc = rec.dma("sync", "tokc", TOKc[:, :], cTOK128[:])
                e_cs = rec.dma("sync", "csum", CSUM[:, :], cCSUM[:])
                e_mf = rec.op("vector", lambda v: v.tensor_scalar(mfull[:, :], Wfull[:, :], 0.0, None, OP.is_gt), after=e_wf)
                e_o2 = rec.op("vector", lambda v: v.memset(ones2[:, :], 1.0))
                e_sn0 = rec.op("vector", lambda v: v.memset(sent[:, 0:TOK], float(PADROW)))
                e_sn1 = rec.op("vector", lambda v: v.memset(sent[:, TOK:2 * TOK], 0.0))
                e_sc = rec.op("vector", lambda v: v.tensor_tensor_scan(Rx[:, :], ones2[:, :], mfull[:, :], 0.0, OP.mult, OP.add),
                              after=[e_mf, e_o2])
                e_tt = rec.op("vector", lambda v: v.tensor_copy(tot[:, 0:1], Rx[:, TOK - 1:TOK]), after=[e_sc])

                for ic in range(4):
                    emit_shared_ic(ic)

                e_bm = rec.op("tensor", lambda t: t.matmul(pbase[:, :], CSUM[:, :], tot[:, 0:1], start=True, stop=True),
                              after=[e_tt, e_cs])
                e_bc2 = rec.op("vector", lambda v: v.tensor_copy(tot[:, 1:2], pbase[:, :]), after=[e_bm])
                e_ad = rec.op("vector", lambda v: v.tensor_scalar(Rx[:, :], Rx[:, :], tot[:, 1:2], None, OP.add), after=[e_bc2])
                e_rx = rec.op("vector", lambda v: v.tensor_tensor(Rx[:, :], Rx[:, :], mfull[:, :], OP.subtract), after=[e_ad])
                e_ltc = rec.op("vector", lambda v: v.tensor_scalar(keptf[:, :], Rx[:, :], float(CAP), None, OP.is_lt), after=[e_rx])
                e_kp = rec.op("vector", lambda v: v.tensor_tensor(keptf[:, :], keptf[:, :], mfull[:, :], OP.mult), after=[e_ltc])
                e_vt0 = rec.op("vector", lambda v: v.tensor_tensor(vtok[:, :], keptf[:, :], TOKc[:, :], OP.mult),
                               after=[e_kp, e_tc])
                e_vt = rec.op("vector", lambda v: v.tensor_scalar(vtok[:, :], vtok[:, :], -1.0, None, OP.add), after=[e_vt0])
                e_vwa = rec.op("vector", lambda v: v.tensor_tensor(vw[:, :], keptf[:, :], Wfull[:, :], OP.mult), after=[e_kp])
                e_vwb = rec.op("vector", lambda v: v.tensor_scalar(TOKc[:, :], keptf[:, :], -1.0, None, OP.add), after=[e_vwa, e_vt0])
                e_vw0 = rec.op("vector", lambda v: v.tensor_tensor(vw[:, :], vw[:, :], TOKc[:, :], OP.add), after=[e_vwb])

                e_st0 = rec.dma("gpsimd", "sent", strm[0:1, TTOT:9216], sent[:, 0:TOK], after=[e_sn0])
                e_st1 = rec.dma("gpsimd", "sent", strm[1:2, TTOT:9216], sent[:, TOK:2 * TOK], after=[e_sn1])

                prev_strm = [e_vt, e_vw0, e_st0, e_st1]
                lib_sg = [False]
                for ei in range(EPC):
                    rec.stream("rows")

                    def emit_row_ei(g, ei=ei):
                        for c in range(NCORE):
                            with g.If_eq(rank_reg[0], c):
                                r = 2 * c + ei
                                g.dma_start(
                                    out=strm[0, 0:TTOT].rearrange("(s t) -> s t", t=TOK),
                                    in_=vtok[8 * r:8 * r + 8, :]).then_inc(rec.sems["rows"][0], 16)
                                g.dma_start(
                                    out=strm[1, 0:TTOT].rearrange("(s t) -> s t", t=TOK),
                                    in_=vw[8 * r:8 * r + 8, :]).then_inc(rec.sems["rows"][0], 16)
                            g.end_ifs()
                    rec.raw("gpsimd", emit_row_ei, after=prev_strm)
                    e_rows = rec.bump("rows", 2 * 16)
                    e_wrap0 = rec.dma("gpsimd", "wrap", sgi[:, 0, :], strm[0, :].rearrange("(f p) -> p f", p=16), after=[e_rows], allow_slow_non_contiguous=True)
                    e_wrap1 = rec.dma("gpsimd", "wrap", sgi[:, 1, :], strm[1, :].rearrange("(f p) -> p f", p=16), after=[e_wrap0], allow_slow_non_contiguous=True)
                    if not lib_sg[0]:
                        rec.raw("gpsimd", lambda g: g.load_library(lc.sparse_gather), after=[e_wrap1])
                        lib_sg[0] = True
                    e_sg0 = rec.op("gpsimd", lambda g: g.sparse_gather(sgo[:, :], sgi[:, 0, :], num_found=nfo[0:1, 0:1]),
                                   after=[e_wrap1])
                    e_sg1 = rec.op("gpsimd", lambda g: g.sparse_gather(sgw[:, :], sgi[:, 1, :], num_found=nfo[0:1, 1:2]),
                                   after=[e_sg0])
                    e_cl0 = rec.op("vector", lambda v: v.tensor_scalar(sgo[:, 0:64], sgo[:, 0:64], 0.0, float(PADROW), OP.max, OP.min),
                                   after=[e_sg0])
                    e_i16 = rec.op("vector", lambda v, ei=ei: v.tensor_copy(IDXT[ei][0:16, :], sgo[:, 0:64]), after=[e_cl0])
                    reps = [e_i16]
                    for kk in range(1, 8):
                        reps.append(rec.dma("sync", "irep", IDXT[ei][16 * kk:16 * (kk + 1), :], IDXT[ei][0:16, :], after=[e_i16]))
                    e_wb0 = rec.dma("gpsimd", "wbb", wbb[ei].rearrange("(f p) -> p f", p=16), sgw[:, 0:64], after=[e_sg1], allow_slow_non_contiguous=True)
                    e_wb1 = rec.dma("gpsimd", "wbb", WBT[ei][:, :], wbb[ei].rearrange("(f p) -> p f", p=128), after=[e_wb0], allow_slow_non_contiguous=True)
                    route_evs.append((reps, e_wb1))
                    prev_strm = [e_wrap1, e_sg1, e_cl0, e_i16, e_wb0]
                rec.raw("gpsimd", lambda g: g.drain())
                rec.raw("gpsimd", lambda g: g.load_library(lc.mlp), after=[route_evs[-1][1]])

                for ic in range(4, 22):
                    emit_shared_ic(ic)
                rec.emit()
            mrg.close()


    e_scat = []
    with (
        nc.sbuf_tensor("XGT", [128, 2, 16, 512], BF16) as XGT,
        nc.sbuf_tensor("wge", [128, 2, 2, 16, 128], BF16) as wge,
        nc.sbuf_tensor("MT2", [128, 11, CAP], BF16) as MT2,
        nc.sbuf_tensor("gsil2", [128, 2, CAP], BF16) as gsil2,
        nc.sbuf_tensor("eout", [128, 8, C], BF16) as eout,
        nc.sbuf_tensor("wde", [128, 2, 512], BF16) as wde,
    ):
        prev_scatter = EV0
        prev_xgt_users = EV0
        gate_rd = [EV0, EV0]
        up_rd = [EV0, EV0]
        silu_ev = [[], []]
        mul_ev = [[], []]
        prev_evac = []
        slot_last2 = [EV0, EV0]
        for ei in range(EPC):
            reps, e_wb = route_evs[ei]
            rec.stream("xg")
            e_xga = rec.op("gpsimd", lambda g, ei=ei: g.dma_gather(
                XGT[:, 0, :, :], x3all[:], IDXT[ei][:, 0:32], num_idxs=512, num_idxs_reg=512,
                elem_size=C, transpose=True),
                after=[e_ag_x3, prev_xgt_users, e_cast["zero"]] + reps, sem="xg", n=16)
            e_xg = rec.op("gpsimd", lambda g, ei=ei: g.dma_gather(
                XGT[:, 1, :, :], x3all[:], IDXT[ei][:, 32:64], num_idxs=512, num_idxs_reg=512,
                elem_size=C, transpose=True),
                after=[e_xga], sem="xg", n=16)
            e_mt2 = []
            pgctx = contextlib.ExitStack()
            pg1 = pgctx.enter_context(nc.psum_tensor(f"pg1_{ei}", [128, 2, 2, 512], F32))
            pg2 = pgctx.enter_context(nc.psum_tensor(f"pg2_{ei}", [128, 2, 2, 512], F32))
            for ic in range(11):
                sl = ic % 2
                e_wg2 = rec.dma("sync", f"wge_g{sl}", wge[:, 0, sl, :, :], weg_b[ei, ic],
                                after=[gate_rd[sl], e_weg])
                lastg = [None, None]
                for h2 in range(2):
                    for k in range(16):
                        aft = [e_wg2, e_xg]
                        if k == 0:
                            aft += silu_ev[sl] + prev_evac
                        lastg[h2] = rec.op("tensor", lambda t, k=k, h2=h2, sl=sl: t.matmul(
                            pg1[:, sl, h2, :], wge[:, 0, sl, k, :], XGT[:, h2, k, :],
                            start=(k == 0), stop=(k == 15)), after=aft)
                gate_rd[sl] = lastg[1]
                e_si1 = rec.op("scalar", lambda s, sl=sl: s.activation(gsil2[:, sl, 0:512], pg1[:, sl, 0, :], AF.Silu),
                               after=[lastg[0]] + mul_ev[sl])
                e_si2 = rec.op("scalar", lambda s, sl=sl: s.activation(gsil2[:, sl, 512:1024], pg1[:, sl, 1, :], AF.Silu),
                               after=[lastg[1], e_si1])
                silu_ev[sl] = [e_si1, e_si2]
                e_wu2 = rec.dma("sync", f"wge_u{sl}", wge[:, 1, sl, :, :], weu_b[ei, ic],
                                after=[up_rd[sl], e_weu])
                lastu = [None, None]
                for h2 in range(2):
                    for k in range(16):
                        aft = [e_wu2, e_xg]
                        if k == 0:
                            aft += mul_ev[sl] + prev_evac
                        lastu[h2] = rec.op("tensor", lambda t, k=k, h2=h2, sl=sl: t.matmul(
                            pg2[:, sl, h2, :], wge[:, 1, sl, k, :], XGT[:, h2, k, :],
                            start=(k == 0), stop=(k == 15)), after=aft)
                up_rd[sl] = lastu[1]
                e_m1 = rec.op("vector", lambda v, ic=ic, sl=sl: v.tensor_tensor(MT2[:, ic, 0:512], pg2[:, sl, 0, :], gsil2[:, sl, 0:512], OP.mult),
                              after=[lastu[0], e_si1])
                e_m2 = rec.op("vector", lambda v, ic=ic, sl=sl: v.tensor_tensor(MT2[:, ic, 512:1024], pg2[:, sl, 1, :], gsil2[:, sl, 512:1024], OP.mult),
                              after=[lastu[1], e_si2, e_m1])
                mul_ev[sl] = [e_m1, e_m2]
                e_mt2.append(e_m2)
            prev_xgt_users = e_mt2[-1]
            pgctx.close()

            pectx = contextlib.ExitStack()
            pe8 = pectx.enter_context(nc.psum_tensor(f"pe8_{ei}", [128, 8, 512], F32))
            prev_eo = {sb: e_mt2[-1] for sb in range(8)}
            for coq in range(4):
                for ic in range(11):
                    sl = ic % 2
                    e_ldd = rec.dma("sync", f"wde{sl}", wde[:, sl, :], wed_b[ei, ic, coq],
                                    after=[slot_last2[sl], e_wed])
                    last_mm = None
                    for sb in range(8):
                        aft = [e_ldd, e_mt2[ic]]
                        if ic == 0:
                            aft.append(prev_eo[sb])
                        last_mm = rec.op("tensor", lambda t, sb=sb, ic=ic, sl=sl: t.matmul(
                            pe8[:, sb, :], MT2[:, ic, sb * 128:(sb + 1) * 128], wde[:, sl, :],
                            start=(ic == 0), stop=(ic == 10)), after=aft)
                    slot_last2[sl] = last_mm
                pe_now = rec.now("PE")
                for sb in range(8):
                    prev_eo[sb] = rec.op("scalar", lambda s, sb=sb, coq=coq, ei=ei: s.activation(
                        eout[:, sb, coq * 512:(coq + 1) * 512], pe8[:, sb, :], AF.Copy, bias=0.0,
                        scale=WBT[ei][:, sb:sb + 1]), after=[pe_now, e_wb, prev_scatter])
            e_eo = [prev_eo[sb] for sb in range(8)]
            prev_evac = e_eo
            rec.stream("scat")
            e_sc2 = rec.op("gpsimd", lambda g, ei=ei: g.dma_scatter_add(
                yp[:], eout[:, :, :], IDXT[ei][:, :], num_idxs=CAP, num_idxs_reg=CAP, elem_size=C),
                after=e_eo + [e_cast["zero"], prev_scatter] + reps, sem="scat", n=16)
            prev_scatter = e_sc2
            e_scat.append(e_sc2)
            pectx.close()
        rec.emit()

    e_rs = rec.op("gpsimd", lambda g: g.collective_compute(
        "ReduceScatter", OP.add, ins=[yp[0:TTOT, :]], outs=[y_rs[:]], replica_groups=ALL8),
        after=e_scat, sem="CO", n=1)

    with (
        nc.sbuf_tensor("wds", [128, 2, 512], BF16) as wds,
        nc.psum_tensor("pf", [128, 8, 512], F32) as pf,
    ):
        prev_f = {tb: EV0 for tb in range(8)}
        slot_last = [EV0, EV0]
        for coq in range(4):
            for ic in range(22):
                sl = ic % 2
                e_ld = rec.dma("sync", f"swd{sl}", wds[:, sl, :], swd_b[ic, coq],
                               after=[slot_last[sl], e_swd])
                last_mm = None
                for tb in range(8):
                    aft = [e_ld, e_mt[ic]]
                    if ic == 0:
                        aft.append(prev_f[tb])
                    last_mm = rec.op("tensor", lambda t, tb=tb, ic=ic, sl=sl, coq=coq: t.matmul(
                        pf[:, tb, :], MT[:, ic, tb * 128:(tb + 1) * 128], wds[:, sl, :],
                        start=(ic == 0), stop=(ic == 21)), after=aft)
                slot_last[sl] = last_mm
            pe_now = rec.now("PE")
            for tb in range(8):
                prev_f[tb] = rec.op("scalar", lambda s, tb=tb, coq=coq: s.activation(
                    FSH[:, tb, coq * 512:(coq + 1) * 512], pf[:, tb, :], AF.Copy, bias=0.0, scale=1.0),
                    after=[pe_now])
        e_fsh = [prev_f[tb] for tb in range(8)]
        rec.emit()


    with (
        nc.sbuf_tensor("ym", [128, C], BF16) as ym,
        nc.sbuf_tensor("ff", [128, C], F32) as ff,
        nc.sbuf_tensor("x2r", [128, C], F32) as x2r,
        nc.sbuf_tensor("oo", [128, C], F32) as oo,
        nc.sbuf_tensor("sqd3", [128, C], BF16) as sqd3,
        nc.sbuf_tensor("st3", [128, 8], F32) as st3,
    ):
        e_outs = []
        prev = EV0
        for tb in range(8):
            e_ym = rec.dma("sync", "ymr", ym[:, :], y_rs[tb * 128:(tb + 1) * 128, :], after=[e_rs, prev])
            e_x2r = rec.dma("sync", "x2r", x2r[:, :], x2s[tb * 128:(tb + 1) * 128, :], after=[prev, ("x2w", 128)])
            e_ff = rec.op("vector", lambda v, tb=tb: v.tensor_tensor(ff[:, :], ym[:, :], FSH[:, tb, :], OP.add),
                          after=[e_ym, e_fsh[tb], prev])
            e_ri = _ln_stats(rec, st3, ff[:, :], sqd3, e_ff, prev, EPS[:, 0:1])
            e_ln = rec.op("vector", lambda v: v.tensor_scalar(ff[:, :], ff[:, :], st3[:, 2:3], st3[:, 6:7],
                                                              OP.subtract, OP.mult), after=[e_ri])
            e_o = rec.op("vector", lambda v: v.tensor_tensor(oo[:, :], x2r[:, :], ff[:, :], OP.add), after=[e_ln, e_x2r])
            e_w = rec.dma("sync", "outw", out_shard[tb * 128:(tb + 1) * 128, :], oo[:, :], after=[e_o])
            e_outs.append(e_w)
            prev = e_w
        rec.raw("sync", lambda s: None, after=e_outs)
        rec.raw("gpsimd", lambda g: None, after=e_outs)
        rec.emit()

    ctx.close()
    return nc


def kernel(**inputs):
    x = np.ascontiguousarray(inputs["L0_attn_x"])
    wq = np.ascontiguousarray(inputs["L0_attn_wq"])
    wk = np.ascontiguousarray(inputs["L0_attn_wk"])
    wv = np.ascontiguousarray(inputs["L0_attn_wv"])
    wo = np.ascontiguousarray(inputs["L0_attn_wo"])
    lqk = np.ascontiguousarray(np.stack([inputs["L0_attn_lq1"], inputs["L0_attn_lk1"],
                                         inputs["L0_attn_lq2"], inputs["L0_attn_lk2"]], axis=0))
    gate = np.ascontiguousarray(inputs["L0_ffn_gate_w"])
    wg = np.ascontiguousarray(inputs["L0_ffn_w_gate"])
    wu = np.ascontiguousarray(inputs["L0_ffn_w_up"])
    wd = np.ascontiguousarray(inputs["L0_ffn_w_down"])
    swg_i = np.ascontiguousarray(inputs["L0_ffn_sw_gate"])
    swu_i = np.ascontiguousarray(inputs["L0_ffn_sw_up"])
    swd_i = np.ascontiguousarray(inputs["L0_ffn_sw_down"])

    nc = build_nc()
    lower_extended_insts(nc)

    in_maps = []
    for c in range(NCORE):
        b, half = c // 2, c % 2
        hr = slice(1024 * half, 1024 * (half + 1))
        inv = (1.0 / 10000.0) ** (np.arange(0, HH, 2, dtype=np.float64) / HH)
        rot = np.zeros((128, 8), np.float32)
        for lh in range(4):
            h = 4 * half + lh
            th = h * inv
            rot[0:64, lh] = np.cos(th)
            rot[64:128, lh] = np.cos(th)
            rot[0:64, 4 + lh] = np.sin(th)
            rot[64:128, 4 + lh] = -np.sin(th)
        def tile_w(m):
            n = m.shape[0] // 512
            return np.ascontiguousarray(m.reshape(n, 512, 16, 128).transpose(0, 3, 2, 1))

        wg2 = wg[2 * c:2 * c + 2]
        wu2 = wu[2 * c:2 * c + 2]
        wd2 = wd[2 * c:2 * c + 2]
        in_maps.append({
            "xb": x[b],
            "rotpq": rot,
            "x_own": np.ascontiguousarray(x[b][1024 * half:1024 * (half + 1)]),
            "wq_t": tile_w(wq[hr]),
            "wk_t": tile_w(wk[hr]),
            "wv_t": tile_w(wv[hr]),
            "wo_t": tile_w(wo),
            "lqk": lqk,
            "gate_w": gate,
            "weg": np.ascontiguousarray(wg2.reshape(2, 16, 128, 11, 128).transpose(0, 3, 2, 1, 4)),
            "weu": np.ascontiguousarray(wu2.reshape(2, 16, 128, 11, 128).transpose(0, 3, 2, 1, 4)),
            "wed": np.ascontiguousarray(wd2.reshape(2, 11, 128, 4, 512).transpose(0, 1, 3, 2, 4)),
            "swg": np.ascontiguousarray(swg_i.reshape(16, 128, 22, 128).transpose(2, 1, 0, 3)),
            "swu": np.ascontiguousarray(swu_i.reshape(16, 128, 22, 128).transpose(2, 1, 0, 3)),
            "swd": np.ascontiguousarray(swd_i.reshape(22, 128, 4, 512).transpose(0, 2, 1, 3)),
        })
    res = run_bass_kernel_spmd(nc, in_maps, core_ids=list(range(NCORE)))
    shards = [res.results[c]["out_shard"] for c in range(NCORE)]
    out = np.concatenate(shards, axis=0).reshape(B, T, C)
    return out


# version 07 (final)
# speedup vs baseline: 1.3706x; 1.0149x over previous
import contextlib
import math

import numpy as np
import ml_dtypes

import concourse.bass as bass
import concourse.mybir as mybir
from concourse.alu_op_type import AluOpType
from concourse import library_config as lc
from concourse.bass_utils import run_bass_kernel_spmd
from concourse.library_overlay import lower_extended_insts

F32 = mybir.dt.float32
BF16 = mybir.dt.bfloat16
I16 = mybir.dt.int16
U32 = mybir.dt.uint32
AF = mybir.ActivationFunctionType
OP = AluOpType
AX = mybir.AxisListType.X

B, T, C = 4, 2048, 2048
NCORE = 8
TOK = 1024
HH, HD = 128, 256
E, CAP = 16, 1024
IEXP, ISH = 1408, 2816
EPC = 2
LN_EPS = 1e-5
RMS_EPS = 1.1920929e-07
LAMBDA_INIT = 0.2
ISQ = 1.0 / math.sqrt(128.0)
TTOT = B * T
PADROW = TTOT

PAIRS = [[0, 1], [2, 3], [4, 5], [6, 7]]
ALL8 = [list(range(NCORE))]

ENGINES = ("tensor", "vector", "scalar", "gpsimd", "sync")
CHAIN = {"tensor": "PE", "vector": "DV", "scalar": "AC", "gpsimd": "PL", "sync": "SY"}
QSEM = {"sync": "QS", "scalar": "QA", "gpsimd": "QG"}


class Rec:

    def __init__(self, nc, ctx):
        self.nc = nc
        self.ctx = ctx
        self.sems = {}
        for name in ["PE", "DV", "AC", "PL", "SY", "CO"]:
            self.sems[name] = [ctx.enter_context(nc.semaphore(name)), 0]
        self.ops = {e: [] for e in ENGINES}
        self.hw = {e: {} for e in ENGINES}
        self.drained = {e: 0 for e in ENGINES}
        self.block_start = {}
        self.bg = {"CO"}

    def stream(self, name, bg=False):
        if name not in self.sems:
            self.sems[name] = [self.ctx.enter_context(self.nc.semaphore(name)), 0]
        if bg:
            self.bg.add(name)
        return self.sems[name]

    def now(self, kind):
        return (kind, self.sems[kind][1])

    def _after(self, eng, after):
        waits, need_drain = [], False
        for ev in after:
            if ev is None:
                continue
            k, v = ev
            if v <= 0:
                continue
            if k == CHAIN[eng]:
                if v > self.drained[eng]:
                    need_drain = True
            else:
                waits.append((k, v))
        if need_drain:
            self.drained[eng] = self.sems[CHAIN[eng]][1]
        wmax = {}
        for (k, v) in waits:
            wmax[k] = max(wmax.get(k, 0), v)
        return sorted(wmax.items()), need_drain

    def op(self, eng, fn, after=(), sem=None, n=1):
        kind = sem or CHAIN[eng]
        waits, need_drain = self._after(eng, after)
        self.sems[kind][1] += n
        val = self.sems[kind][1]
        self.ops[eng].append((fn, waits, need_drain, kind, n))
        return (kind, val)

    def dma(self, eng, stream, out, in_, after=(), bg=False, **kw):
        self.stream(stream, bg=bg)
        return self.op(eng, lambda e: e.dma_start(out=out, in_=in_, **kw),
                       after=after, sem=stream, n=16)

    def raw(self, eng, fn, after=()):
        waits, need_drain = self._after(eng, after)
        self.ops[eng].append((fn, waits, need_drain, None, 0))

    def bump(self, kind, n):
        self.sems[kind][1] += n
        return (kind, self.sems[kind][1])

    def emit(self):
        nc = self.nc
        if not any(self.ops[e] for e in ENGINES):
            return
        fence = [(k, v) for k, v in self.block_start.items()
                 if v > 0 and k not in self.bg]
        with nc.Block() as blk:
            for eng_name in ENGINES:
                ops = self.ops[eng_name]
                if not ops:
                    continue

                def make_body(eng_name, ops):
                    def body(eng):
                        hw = self.hw[eng_name]
                        for (k, v) in fence:
                            if hw.get(k, 0) < v:
                                eng.wait_ge(self.sems[k][0], v)
                                hw[k] = v
                        for (fn, waits, need_drain, kind, n) in ops:
                            for (k, v) in waits:
                                if hw.get(k, 0) < v:
                                    eng.wait_ge(self.sems[k][0], v)
                                    hw[k] = v
                            if need_drain:
                                eng.drain()
                            inst = fn(eng)
                            if kind is not None and inst is not None:
                                inst.then_inc(self.sems[kind][0], n)
                    return body

                getattr(blk, eng_name)(make_body(eng_name, ops))
        self.ops = {e: [] for e in ENGINES}
        self.block_start = {k: s[1] for k, s in self.sems.items()
                            if k not in ("PE", "DV", "AC", "PL", "SY")}


def _consts():
    ku = np.arange(896)[None, :] >= (np.arange(128)[:, None] + 384)
    stair = np.asarray(ku.astype(ml_dtypes.bfloat16))
    idnb = np.asarray(np.eye(128, dtype=ml_dtypes.bfloat16))
    idn32 = np.eye(128, dtype=np.float32)
    onesb = np.asarray(np.ones((128, 1), dtype=ml_dtypes.bfloat16))
    tok128 = (np.arange(128)[:, None] % 8 * 1024 + np.arange(TOK)[None, :] + 1).astype(np.float32)
    pp = np.arange(128)
    csum = ((pp[:, None] // 8 == pp[None, :] // 8) & (pp[:, None] % 8 < pp[None, :] % 8)).astype(np.float32)
    idqs = []
    for b4 in range(4):
        w = np.zeros((128, 32), np.int16)
        for i in range(512):
            for g in range(8):
                w[(i % 16) + 16 * g, i // 16] = 512 * b4 + i
        idqs.append(w)
    return stair, idnb, idn32, onesb, tok128, csum, idqs


def _ln_stats(rec, stat, src_ap, sq_scratch, e_src, prev, eps_ap=None):
    e_s1 = rec.op("scalar", lambda s: s.activation(sq_scratch[:, :], src_ap, AF.Square, accum_out=stat[:, 1:2]),
                  after=[e_src, prev])
    e_s2 = rec.op("vector", lambda v: v.tensor_reduce(stat[:, 0:1], src_ap, axis=AX, op=OP.add),
                  after=[e_src, prev])
    e_m = rec.op("vector", lambda v: v.tensor_scalar(stat[:, 2:3], stat[:, 0:1], 1.0 / C, None, OP.mult), after=[e_s2])
    e_v1 = rec.op("vector", lambda v: v.tensor_tensor(stat[:, 3:4], stat[:, 2:3], stat[:, 2:3], OP.mult), after=[e_m])
    e_v2 = rec.op("vector", lambda v: v.scalar_tensor_tensor(stat[:, 4:5], stat[:, 1:2], 1.0 / C, stat[:, 3:4],
                                                             OP.mult, OP.subtract), after=[e_v1, e_s1])
    e_sd = rec.op("scalar", lambda s: s.activation(stat[:, 5:6], stat[:, 4:5], AF.Sqrt, bias=eps_ap), after=[e_v2])
    e_ri = rec.op("vector", lambda v: v.reciprocal(stat[:, 6:7], stat[:, 5:6]), after=[e_sd])
    return e_ri


def build_nc():
    nc = bass.Bass("TRN2", target_bir_lowering=False)
    ctx = contextlib.ExitStack()
    dram = lambda name, shape, dt, kind="Internal": nc.dram_tensor(name, shape, dt, kind=kind).ap()

    xb = dram("xb", [T, C], F32, "ExternalInput")
    x_own = dram("x_own", [TOK, C], F32, "ExternalInput")
    wq_t = dram("wq_t", [2, 128, 16, 512], F32, "ExternalInput")
    wk_t = dram("wk_t", [2, 128, 16, 512], F32, "ExternalInput")
    wv_t = dram("wv_t", [2, 128, 16, 512], F32, "ExternalInput")
    wo_t = dram("wo_t", [4, 128, 16, 512], F32, "ExternalInput")
    lqk = dram("lqk", [4, HD], F32, "ExternalInput")
    rotpq = dram("rotpq", [128, 8], F32, "ExternalInput")
    gate_w = dram("gate_w", [E, C], F32, "ExternalInput")
    weg = dram("weg", [EPC, 11, 128, 16, 128], F32, "ExternalInput")
    weu = dram("weu", [EPC, 11, 128, 16, 128], F32, "ExternalInput")
    wed = dram("wed", [EPC, 11, 4, 128, 512], F32, "ExternalInput")
    swg = dram("swg", [22, 128, 16, 128], F32, "ExternalInput")
    swu = dram("swu", [22, 128, 16, 128], F32, "ExternalInput")
    swd = dram("swd", [22, 4, 128, 512], F32, "ExternalInput")
    out_shard = dram("out_shard", [TOK, C], F32, "ExternalOutput")

    stair_h, idnb_h, idn32_h, onesb_h, tok128_h, csum_h, idqs_h = _consts()
    cSTAIR = nc.inline_tensor(stair_h, name="cSTAIR").ap()
    cIDNB = nc.inline_tensor(idnb_h, name="cIDNB").ap()
    cIDN32 = nc.inline_tensor(idn32_h, name="cIDN32").ap()
    cONESB = nc.inline_tensor(onesb_h, name="cONESB").ap()
    cTOK128 = nc.inline_tensor(tok128_h, name="cTOK128").ap()
    cIDQ = [nc.inline_tensor(idqs_h[b4], name=f"cIDQ{b4}").ap() for b4 in range(4)]
    cCSUM = nc.inline_tensor(csum_h, name="cCSUM").ap()

    x1s = dram("x1s", [T, C], BF16)
    wq_b = dram("wq_b", [2, 128, 16, 512], BF16)
    wk_b = dram("wk_b", [2, 128, 16, 512], BF16)
    wv_b = dram("wv_b", [2, 128, 16, 512], BF16)
    wo_b = dram("wo_b", [4, 128, 16, 512], BF16)
    weg_b = dram("weg_b", [EPC, 11, 128, 16, 128], BF16)
    weu_b = dram("weu_b", [EPC, 11, 128, 16, 128], BF16)
    wed_b = dram("wed_b", [EPC, 11, 4, 128, 512], BF16)
    swg_b = dram("swg_b", [22, 128, 16, 128], BF16)
    swu_b = dram("swu_b", [22, 128, 16, 128], BF16)
    swd_b = dram("swd_b", [22, 4, 128, 512], BF16)
    x2s = dram("x2s", [TOK, C], F32)
    yhalf = dram("yhalf", [128, 8, 1024], BF16)
    ypair = dram("ypair", [2, 128, 8, 1024], BF16)
    x3mine = dram("x3mine", [TOK, C], BF16)
    x3all = dram("x3all", [TTOT + 1, C], BF16)
    wm_mine = dram("wm_mine", [E, TOK], F32)
    wm_all = dram("wm_all", [NCORE, E, TOK], F32)
    yp = dram("yp", [TTOT + 1, C], BF16)
    y_rs = dram("y_rs", [TOK, C], BF16)
    strm = dram("strm", [2, 9216], F32)
    wbb = dram("wbb", [EPC, CAP], F32)

    sbt = lambda name, shape, dt: ctx.enter_context(nc.sbuf_tensor(name, shape, dt))
    ROTS = sbt("ROTS", [128, 8], F32)
    IDQS = [sbt(f"IDQS{b4}", [128, 32], I16) for b4 in range(4)]
    STAIR = sbt("STAIR", [128, 896], BF16)
    IDNB = sbt("IDNB", [128, 128], BF16)
    IDN32 = sbt("IDN32", [128, 128], F32)
    ONESB = sbt("ONESB", [128, 1], BF16)
    ZEROT = sbt("ZEROT", [128, 4096], BF16)
    LAM = sbt("LAM", [1, 4], F32)
    GWT = sbt("GWT", [128, 16, 16], BF16)
    EPS = sbt("EPS", [128, 2], F32)

    rec = Rec(nc, ctx)
    EV0 = ("PE", 0)

    rank_reg = [None]

    def init_g(g):
        g.load_library(lc.mlp)
        rank_reg[0] = g.to_reg(g.partition_id())
    rec.raw("gpsimd", init_g)


    e_c = {}
    for nm, dst, src in (("st", STAIR, cSTAIR), ("ib", IDNB, cIDNB),
                         ("i32", IDN32, cIDN32), ("ob", ONESB, cONESB)):
        e_c[nm] = rec.dma("sync", "cst_" + nm, dst[:], src[:])
    e_c["rot"] = rec.dma("sync", "cst_rot", ROTS[:], rotpq[:])
    for b4 in range(4):
        e_c[f"idq{b4}"] = rec.dma("sync", f"cst_idq{b4}", IDQS[b4][:], cIDQ[b4][:])
    e_zt = rec.op("vector", lambda v: v.memset(ZEROT[:], 0.0))
    rec.op("vector", lambda v: v.memset(EPS[:, 0:1], LN_EPS))
    rec.op("vector", lambda v: v.memset(EPS[:, 1:2], RMS_EPS))

    ZELEM = 128 * 4096
    flat_yp = yp.rearrange("r c -> (r c)")
    nzp = ((TTOT + 1) * C) // ZELEM
    zero_chunks = []
    for z in range(nzp):
        zero_chunks.append((flat_yp[z * ZELEM:(z + 1) * ZELEM].rearrange("(p f) -> p f", p=128), ZEROT[:]))
    rem = (TTOT + 1) * C - nzp * ZELEM
    if rem:
        zero_chunks.append((flat_yp[nzp * ZELEM:].rearrange("(p f) -> p f", p=128),
                            ZEROT[:, 0:rem // 128]))
    zero_chunks.append((x3all[PADROW:PADROW + 1, :], ZEROT[0:1, 0:C]))

    lvs = sbt("lvs", [1, 4 * HD], F32)
    lpr = sbt("lpr", [1, 2 * HD], F32)
    lred = sbt("lred", [1, 4], F32)
    e_lv = rec.dma("sync", "lqk", lvs[:], lqk.rearrange("a b -> (a b)").rearrange("(o f) -> o f", o=1))
    e_p1 = rec.op("vector", lambda v: v.tensor_tensor(lpr[:, 0:HD], lvs[:, 0:HD], lvs[:, HD:2 * HD], OP.mult), after=[e_lv])
    e_p2 = rec.op("vector", lambda v: v.tensor_tensor(lpr[:, HD:2 * HD], lvs[:, 2 * HD:3 * HD], lvs[:, 3 * HD:4 * HD], OP.mult), after=[e_lv, e_p1])
    e_r1 = rec.op("vector", lambda v: v.tensor_reduce(lred[:, 0:1], lpr[:, 0:HD], axis=AX, op=OP.add), after=[e_p2])
    e_r2 = rec.op("vector", lambda v: v.tensor_reduce(lred[:, 1:2], lpr[:, HD:2 * HD], axis=AX, op=OP.add), after=[e_r1])
    e_ex = rec.op("scalar", lambda s: s.activation(lred[:, 0:2], lred[:, 0:2], AF.Exp), after=[e_r2])
    e_l0 = rec.op("vector", lambda v: v.tensor_tensor(LAM[0:1, 0:1], lred[:, 0:1], lred[:, 1:2], OP.subtract), after=[e_ex])
    e_l1 = rec.op("vector", lambda v: v.tensor_scalar(LAM[0:1, 0:1], LAM[0:1, 0:1], LAMBDA_INIT, None, OP.add), after=[e_l0])
    e_lam = rec.op("vector", lambda v: v.tensor_scalar(LAM[0:1, 1:2], LAM[0:1, 0:1], -1.0, None, OP.mult), after=[e_l1])

    gwsb = sbt("gwsb", [16, C], BF16)
    e_gw = rec.dma("gpsimd", "gwl", gwsb[:], gate_w[:])
    with nc.psum_tensor("pgt", [128, 16], BF16) as pgt:
        ev_prev = EV0
        for k in range(16):
            e_t = rec.op("tensor", lambda t, k=k: t.transpose(pgt[:, :], gwsb[:, k * 128:(k + 1) * 128], IDNB[0:16, 0:16]),
                         after=[e_gw, e_c["ib"], ev_prev])
            ev_prev = rec.op("vector", lambda v, k=k: v.tensor_copy(GWT[:, k, :], pgt[:, :]), after=[e_t])
        e_gwt = ev_prev
        rec.emit()

    with (
        nc.sbuf_tensor("xt", [128, 2, C], F32) as xt,
        nc.sbuf_tensor("x1t", [128, 2, C], BF16) as x1t,
        nc.sbuf_tensor("sqd", [128, C], BF16) as sqd,
        nc.sbuf_tensor("st8", [128, 8], F32) as st8,
    ):
        e_wv = rec.dma("gpsimd", "c_wv", wv_b[:], wv_t[:], bg=True)
        e_wq = rec.dma("gpsimd", "c_wq", wq_b[:], wq_t[:], bg=True)
        e_wk = rec.dma("gpsimd", "c_wk", wk_b[:], wk_t[:], bg=True)
        prev_use = [EV0, EV0]
        prev_stat = EV0
        last_w = EV0
        for i in range(16):
            sl = i % 2
            e_x = rec.dma("sync", f"xls{sl}", xt[:, sl, :], xb[i * 128:(i + 1) * 128, :], after=[prev_use[sl]])
            e_ri = _ln_stats(rec, st8, xt[:, sl, :], sqd, e_x, prev_stat, EPS[:, 0:1])
            e_ap = rec.op("vector", lambda v, sl=sl: v.tensor_scalar(x1t[:, sl, :], xt[:, sl, :], st8[:, 2:3], st8[:, 6:7],
                                                                     OP.subtract, OP.mult), after=[e_ri])
            prev_stat = e_ap
            e_w = rec.dma("scalar", f"x1w{sl}", x1s[i * 128:(i + 1) * 128, :], x1t[:, sl, :], after=[e_ap])
            prev_use[sl] = e_w
            last_w = e_w
        e_x1_done = [prev_use[0], prev_use[1]]
        rec.emit()

    e_cast = {}
    bg_chunks = []
    for i in range(4):
        bg_chunks.append(("wo", wo_b[i], wo_t[i]))
    for lohi in range(2):
        bg_chunks.append(("swg", swg_b[11 * lohi:11 * (lohi + 1)], swg[11 * lohi:11 * (lohi + 1)]))
        bg_chunks.append(("swu", swu_b[11 * lohi:11 * (lohi + 1)], swu[11 * lohi:11 * (lohi + 1)]))
    for nm, db, sb_ in (("weg", weg_b, weg), ("weu", weu_b, weu), ("wed", wed_b, wed)):
        for ei_ in range(EPC):
            for lohi in range(2):
                sl_ = slice(6 * lohi, 6 + 5 * lohi)
                bg_chunks.append((nm, db[ei_, sl_], sb_[ei_, sl_]))
    for lohi in range(2):
        bg_chunks.append(("swd", swd_b[11 * lohi:11 * (lohi + 1)], swd[11 * lohi:11 * (lohi + 1)]))
    for dst, src_ in zero_chunks:
        bg_chunks.append(("zero", dst, src_, e_zt))

    def bg_pump(k=1):
        for _ in range(k):
            if bg_chunks:
                ch = bg_chunks.pop(0)
                key, dst, src = ch[0], ch[1], ch[2]
                aft = [ch[3]] if len(ch) > 3 else ()
                e_cast[key] = rec.dma("gpsimd", "cast_" + key, dst, src, bg=True, after=aft)

    with nc.sbuf_tensor("YT", [128, 8, T], BF16) as YT:
        for p in range(2):
            with (
                nc.sbuf_tensor(f"QT{p}", [128, 4, T], BF16) as QT,
                nc.sbuf_tensor(f"KT{p}", [128, 4, T], BF16) as KT,
                nc.sbuf_tensor(f"VS{p}", [128, 16, 512], BF16) as VS,
                nc.sbuf_tensor(f"WTS{p}", [128, 16, 512], BF16) as WTS,
            ):
                with nc.sbuf_tensor(f"X1T{p}", [128, 4, 16, 512], BF16) as X1T:
                    rec.stream("x1T")
                    e_x1T_l = []
                    prevg = e_x1_done
                    for g4 in range(4):
                        ev = rec.op("gpsimd", lambda g, g4=g4: g.dma_gather(
                            X1T[:, g4, :, :], x1s[:], IDQS[g4][:], num_idxs=512, num_idxs_reg=512,
                            elem_size=C, transpose=True),
                            after=list(prevg) + [e_c[f"idq{g4}"]] if g4 == 0 else [prevg, e_c[f"idq{g4}"]],
                            sem="x1T", n=16)
                        e_x1T_l.append(ev)
                        prevg = ev
                    e_x1T = [e_x1T_l[-1]] * 16
                    e_wts_1 = rec.dma("scalar", "wts", WTS[:, :, :], wv_b[p], after=[e_wv])
                    e_wts = [e_wts_1] * 16
                    e_vs = {}
                    with nc.psum_tensor(f"pv{p}", [128, 8, 512], F32) as pv:
                        for tb in range(16):
                            j = tb % 8
                            last = None
                            for cc in range(16):
                                aft = [e_x1T[cc], e_wts[cc]]
                                if cc == 0 and tb >= 8:
                                    aft.append(e_vs[tb - 8])
                                last = rec.op("tensor", lambda t, j=j, tb=tb, cc=cc: t.matmul(
                                    pv[:, j, :], X1T[:, tb // 4, cc, (tb % 4) * 128:(tb % 4 + 1) * 128], WTS[:, cc, :],
                                    start=(cc == 0), stop=(cc == 15)), after=aft)
                            e_vs[tb] = rec.op("vector", lambda v, j=j, tb=tb: v.tensor_copy(VS[:, tb, :], pv[:, j, :]),
                                              after=[last])
                    e_vs_all = [e_vs[tb] for tb in range(16)]
                    e_vs_last = rec.now("DV")
                    bg_pump()

                    def qk_proj(dst, w_b, e_w_ready, e_free, nm, init_ev):
                        e_w_1 = rec.dma("scalar", "wts", WTS[:, :, :], w_b[p], after=[e_w_ready, e_free])
                        e_w = [e_w_1] * 16
                        evs = []
                        with nc.psum_tensor(nm, [128, 2, 512], F32) as pqk:
                            prev = [init_ev, init_ev]
                            for cc in range(4):
                                for tc in range(4):
                                    sl = (cc * 4 + tc) % 2
                                    last = None
                                    for k in range(16):
                                        aft = [e_w[k], e_x1T[k]]
                                        if k == 0:
                                            aft.append(prev[sl])
                                        last = rec.op("tensor", lambda t, cc=cc, tc=tc, k=k, sl=sl: t.matmul(
                                            pqk[:, sl, :], WTS[:, k, cc * 128:(cc + 1) * 128],
                                            X1T[:, tc, k, :],
                                            start=(k == 0), stop=(k == 15)), after=aft)
                                    prev[sl] = rec.op("vector", lambda v, cc=cc, tc=tc, sl=sl: v.tensor_copy(
                                        dst[:, cc, tc * 512:(tc + 1) * 512], pqk[:, sl, :]), after=[last])
                                    evs.append(prev[sl])
                        return evs

                    e_q_evs = qk_proj(QT, wq_b, e_wq, rec.now("PE"), f"pqkq{p}", e_vs[15])
                    bg_pump()
                    e_k_evs = qk_proj(KT, wk_b, e_wk, rec.now("PE"), f"pqkk{p}", e_q_evs[-1])
                    bg_pump()
                rec.emit()

                rope_ev = {}
                with (
                    nc.sbuf_tensor(f"sqr{p}", [128, 2, T], BF16) as sqr,
                    nc.sbuf_tensor(f"rr{p}", [1, 2, 2 * T], F32) as rr,
                    nc.sbuf_tensor(f"rrb{p}", [1, 2, T], BF16) as rrb,
                    nc.sbuf_tensor(f"rb{p}", [128, 2, T], BF16) as rb,
                    nc.sbuf_tensor(f"TMPA{p}", [128, 2, T], BF16) as TMPA,
                    nc.sbuf_tensor(f"TMPB{p}", [128, 2, T], BF16) as TMPB,
                    nc.psum_tensor(f"pr{p}", [1, 2, 4, 512], F32) as pr,
                ):
                    prev_rope_p = [e_k_evs[-1], e_k_evs[-1]]
                    prev_rb_rd_p = [EV0, EV0]
                    for it, (mi, ch) in enumerate([(m, c) for m in range(2) for c in range(4)]):
                        src, src_evs = ((QT, e_q_evs), (KT, e_k_evs))[mi]
                        pa = it % 2
                        prev_rope = prev_rope_p[pa]
                        prev_rb_rd = prev_rb_rd_p[pa]
                        if True:
                            tile = src[:, ch, :]
                            lh = ch // 2
                            e_sq = rec.op("scalar", lambda s, tile=tile, pa=pa: s.activation(sqr[:, pa, :], tile, AF.Square),
                                          after=src_evs + [prev_rope])
                            e_ms = []
                            for q4 in range(4):
                                e_ms.append(rec.op("tensor", lambda t, q4=q4, pa=pa: t.matmul(
                                    pr[:, pa, q4, :], ONESB[:, :], sqr[:, pa, q4 * 512:(q4 + 1) * 512],
                                    start=True, stop=True), after=[e_sq, e_c["ob"], prev_rope]))
                            e_rs = rec.op("scalar", lambda s, pa=pa: s.activation(rr[:, pa, 0:T], pr[:, pa, :, :], AF.Sqrt,
                                                                           scale=1.0 / 128.0, bias=EPS[0:1, 1:2]), after=e_ms)
                            e_rc = rec.op("vector", lambda v, pa=pa: v.reciprocal(rr[:, pa, T:2 * T], rr[:, pa, 0:T]), after=[e_rs])
                            e_rb = rec.op("vector", lambda v, pa=pa: v.tensor_copy(rrb[:, pa, :], rr[:, pa, T:2 * T]), after=[e_rc])
                            e_bc = rec.op("gpsimd", lambda g, pa=pa: g.partition_broadcast(rb[:, pa, :], rrb[:, pa, :], channels=128),
                                          after=[e_rb, prev_rb_rd])
                            e_sw1 = rec.dma("sync", f"swp{pa}", TMPB[0:64, pa, :], tile[64:128, :], after=src_evs + [prev_rope])
                            e_sw2 = rec.dma("sync", f"swp{pa}", TMPB[64:128, pa, :], tile[0:64, :], after=[e_sw1])
                            e_t1 = rec.op("vector", lambda v, tile=tile, lh=lh, pa=pa: v.tensor_scalar(
                                TMPA[:, pa, :], tile, ROTS[:, lh:lh + 1], None, OP.mult),
                                after=[e_c["rot"], e_sw2, prev_rope])
                            e_t2 = rec.op("vector", lambda v, tile=tile, lh=lh, pa=pa: v.scalar_tensor_tensor(
                                tile, TMPB[:, pa, :], ROTS[:, 4 + lh:5 + lh], TMPA[:, pa, :], OP.mult, OP.add),
                                after=[e_t1, e_sw2])
                            e_nb = rec.op("vector", lambda v, tile=tile, pa=pa: v.tensor_tensor(tile, tile, rb[:, pa, :], OP.mult),
                                          after=[e_t2, e_bc])
                            prev_rb_rd_p[pa] = e_nb
                            rope_ev[(mi, ch)] = e_nb
                            prev_rope_p[pa] = e_nb
                    e_rope_all = prev_rope_p[1]
                    e_rope_all2 = prev_rope_p[0]
                    bg_pump(2)
                    rec.emit()

                with (
                    nc.sbuf_tensor(f"PT{p}", [128, 2, 2, 512], BF16) as PT,
                    nc.sbuf_tensor(f"zrow{p}", [1, 2, 512], F32) as zrow,
                    nc.sbuf_tensor(f"zbf{p}", [128, 2, 512], F32) as zbf,
                    nc.sbuf_tensor(f"csc{p}", [128, 2, 512], F32) as csc,
                    nc.psum_tensor(f"pO{p}", [128, 4, 512], F32) as pO,
                    nc.psum_tensor(f"pS{p}", [128, 2, 512], F32) as pS,
                    nc.psum_tensor(f"pZ{p}", [1, 2, 512], F32) as pZ,
                ):
                    prev_exp = e_rope_all
                    prev_zread = e_rope_all
                    prev_zrow_rd = e_rope_all
                    prev_cmb = [e_rope_all, e_rope_all2, e_rope_all, e_rope_all2]
                    prev_zo = [[e_rope_all], [e_rope_all2]]
                    for lh in range(2):
                        for qc in range(4):
                            nkb = (qc + 1) * 4
                            e_zl = None
                            e_ol = {}
                            qk_ev = {j: [rope_ev[(0, 2 * lh + j)], rope_ev[(1, 2 * lh + j)]] for j in range(2)}
                            e_s = {}
                            for j in range(2):
                                e_s[j] = rec.op("tensor", lambda t, lh=lh, j=j, qc=qc: t.matmul(
                                    pS[:, j, :], KT[:, 2 * lh + j, 0:128],
                                    QT[:, 2 * lh + j, qc * 512:(qc + 1) * 512],
                                    start=True, stop=True),
                                    after=qk_ev[j] + [prev_exp])
                            for kb in range(nkb):
                                par = kb % 2
                                e_e = rec.op("scalar", lambda s, par=par: s.activation(
                                    PT[:, par, :, :], pS[:, :, :], AF.Exp, scale=ISQ),
                                    after=[e_s[0], e_s[1]] + prev_zo[par])
                                prev_exp = e_e
                                if kb + 1 < nkb:
                                    for j in range(2):
                                        e_s[j] = rec.op("tensor", lambda t, lh=lh, j=j, kb=kb, qc=qc: t.matmul(
                                            pS[:, j, :], KT[:, 2 * lh + j, (kb + 1) * 128:(kb + 2) * 128],
                                            QT[:, 2 * lh + j, qc * 512:(qc + 1) * 512],
                                            start=True, stop=True),
                                            after=qk_ev[j] + [e_e])
                                evs_j = [e_e, e_e]
                                delta = kb * 128 - qc * 512
                                if 0 <= delta <= 384:
                                    for j in range(2):
                                        evs_j[j] = rec.op("vector", lambda v, j=j, par=par, delta=delta: v.tensor_tensor(
                                            PT[:, par, j, :], PT[:, par, j, :], STAIR[:, 384 - delta:896 - delta], OP.mult),
                                            after=[evs_j[j], e_c["st"]])
                                zo_evs = []
                                for j in range(2):
                                    e_z = rec.op("tensor", lambda t, j=j, par=par, kb=kb, nkb=nkb: t.matmul(
                                        pZ[:, j, :], ONESB[:, :], PT[:, par, j, :],
                                        start=(kb == 0), stop=(kb == nkb - 1)),
                                        after=[evs_j[j]] + ([prev_zread] if kb == 0 else []))
                                    e_zl = e_z
                                    zo_evs.append(e_z)
                                    for d2 in range(2):
                                        aft = [evs_j[j], e_vs_last]
                                        if kb == 0:
                                            aft.append(prev_cmb[2 * j + d2])
                                        e_ol[2 * j + d2] = rec.op("tensor", lambda t, lh=lh, j=j, d2=d2, par=par, kb=kb, nkb=nkb: t.matmul(
                                            pO[:, 2 * j + d2, :], VS[:, kb, 256 * lh + 128 * d2:256 * lh + 128 * (d2 + 1)],
                                            PT[:, par, j, :], start=(kb == 0), stop=(kb == nkb - 1)), after=aft)
                                        zo_evs.append(e_ol[2 * j + d2])
                                prev_zo[par] = zo_evs
                            e_zc = rec.op("vector", lambda v: v.tensor_copy(zrow[:, :, :], pZ[:, :, :]),
                                          after=[e_zl, prev_zrow_rd])
                            prev_zread = e_zc
                            e_zr = rec.op("vector", lambda v: v.reciprocal(zrow[:, :, :], zrow[:, :, :]), after=[e_zc])
                            e_zs = rec.op("vector", lambda v: v.tensor_scalar(zrow[:, 1, :], zrow[:, 1, :], LAM[0:1, 1:2], None, OP.mult),
                                          after=[e_zr, e_lam])
                            e_zb = rec.op("gpsimd", lambda g: g.partition_broadcast(zbf[:, :, :], zrow[:, :, :], channels=128),
                                          after=[e_zs])
                            prev_zrow_rd = e_zb
                            prev_y = e_zb
                            for d2 in range(2):
                                e_c1 = rec.op("vector", lambda v, d2=d2: v.tensor_tensor(csc[:, 0, :], pO[:, d2, :], zbf[:, 0, :], OP.mult),
                                              after=[e_zb, e_ol[d2], prev_cmb[d2], prev_y])
                                e_c2 = rec.op("vector", lambda v, d2=d2: v.tensor_tensor(csc[:, 1, :], pO[:, 2 + d2, :], zbf[:, 1, :], OP.mult),
                                              after=[e_c1, e_ol[2 + d2]])
                                ych = 4 * p + 2 * lh + d2
                                e_y = rec.op("vector", lambda v, ych=ych, qc=qc: v.tensor_tensor(
                                    YT[:, ych, qc * 512:(qc + 1) * 512], csc[:, 0, :], csc[:, 1, :], OP.add),
                                    after=[e_c2])
                                prev_cmb[d2] = e_y
                                prev_cmb[2 + d2] = e_y
                                prev_y = e_y
                            bg_pump()
                    rec.emit()
        e_yt_done = rec.now("DV")
        bg_pump(len(bg_chunks))
        e_wo, e_swg, e_swu = e_cast["wo"], e_cast["swg"], e_cast["swu"]
        e_weg, e_weu, e_wed, e_swd = e_cast["weg"], e_cast["weu"], e_cast["wed"], e_cast["swd"]

        rec.stream("yex")

        def emit_send(g):
            for c in range(NCORE):
                with g.If_eq(rank_reg[0], c):
                    off = 1024 if c % 2 == 0 else 0
                    g.dma_start(out=yhalf[:], in_=YT[:, :, off:off + 1024]).then_inc(rec.sems["yex"][0], 16)
                g.end_ifs()
        rec.raw("gpsimd", emit_send, after=[e_yt_done])
        e_send = rec.bump("yex", 16)

        e_ag_y = rec.op("gpsimd", lambda g: g.collective_compute(
            "AllGather", OP.bypass, ins=[yhalf[:]], outs=[ypair[:]], replica_groups=PAIRS),
            after=[e_send], sem="CO", n=1)

        e_fsh_dep = []
        with (
            nc.sbuf_tensor("YF", [128, 16, 1024], BF16) as YF,
            nc.sbuf_tensor("wt5", [128, 2, 16, 512], BF16) as wt5,
            nc.sbuf_tensor("a_sb", [128, 8, C], BF16) as a_sb,
        ):
            rec.stream("yasm")

            def emit_asm(g):
                for c in range(NCORE):
                    with g.If_eq(rank_reg[0], c):
                        if c % 2 == 0:
                            g.dma_start(out=YF[:, 0:8, :], in_=YT[:, :, 0:1024]).then_inc(rec.sems["yasm"][0], 16)
                            g.dma_start(out=YF[:, 8:16, :], in_=ypair[1]).then_inc(rec.sems["yasm"][0], 16)
                        else:
                            g.dma_start(out=YF[:, 0:8, :], in_=ypair[0]).then_inc(rec.sems["yasm"][0], 16)
                            g.dma_start(out=YF[:, 8:16, :], in_=YT[:, :, 1024:2048]).then_inc(rec.sems["yasm"][0], 16)
                    g.end_ifs()
            rec.raw("gpsimd", emit_asm, after=[e_ag_y, e_yt_done])
            e_yf = rec.bump("yasm", 32)

            with nc.psum_tensor("pa", [128, 8, 512], F32) as pa:
                rec.stream("wo")
                prev_ev = {tb: EV0 for tb in range(8)}
                slot_last_mm = [EV0, EV0]
                prev_lw = EV0
                for coq in range(4):
                    sl = coq % 2
                    e_lw = rec.dma("sync", f"wo{sl}", wt5[:, sl, :, :], wo_b[coq],
                                   after=[e_wo, slot_last_mm[sl]])
                    prev_lw = e_lw
                    last_mm = None
                    for cch in range(16):
                        for tb in range(8):
                            aft = [e_lw, e_yf]
                            if cch == 0:
                                aft.append(prev_ev[tb])
                            last_mm = rec.op("tensor", lambda t, tb=tb, cch=cch, sl=sl: t.matmul(
                                pa[:, tb, :], YF[:, cch, tb * 128:(tb + 1) * 128], wt5[:, sl, cch, :],
                                start=(cch == 0), stop=(cch == 15)), after=aft)
                    slot_last_mm[sl] = last_mm
                    pe_now = rec.now("PE")
                    for tb in range(8):
                        prev_ev[tb] = rec.op("scalar", lambda s, tb=tb, coq=coq: s.activation(
                            a_sb[:, tb, coq * 512:(coq + 1) * 512], pa[:, tb, :], AF.Copy, bias=0.0, scale=0.8),
                            after=[pe_now])
                e_a_done = [prev_ev[tb] for tb in range(8)]
            rec.emit()

            with (
                nc.sbuf_tensor("xo", [128, C], F32) as xo,
                nc.sbuf_tensor("lnb", [128, C], F32) as lnb,
                nc.sbuf_tensor("x2t", [128, C], F32) as x2t,
                nc.sbuf_tensor("x3t", [128, C], BF16) as x3t,
                nc.sbuf_tensor("sqd2", [128, C], BF16) as sqd2,
                nc.sbuf_tensor("st2", [128, 8], F32) as st2,
            ):
                e_x3w = []
                prev = EV0
                prev2 = EV0
                for tb in range(8):
                    atile = a_sb[:, tb, :]
                    e_xo = rec.dma("sync", "xow", xo[:, :], x_own[tb * 128:(tb + 1) * 128, :], after=[prev])
                    e_ri = _ln_stats(rec, st2, atile, sqd2, e_a_done[tb], prev, EPS[:, 0:1])
                    e_ln = rec.op("vector", lambda v, atile=atile: v.tensor_scalar(lnb[:, :], atile, st2[:, 2:3], st2[:, 6:7],
                                                                                   OP.subtract, OP.mult), after=[e_ri])
                    e_x2 = rec.op("vector", lambda v: v.tensor_tensor(x2t[:, :], xo[:, :], lnb[:, :], OP.add), after=[e_ln, e_xo, prev2])
                    e_w2 = rec.dma("scalar", "x2w", x2s[tb * 128:(tb + 1) * 128, :], x2t[:, :], after=[e_x2])
                    e_rib = _ln_stats(rec, st2, x2t[:, :], sqd2, e_x2, e_ln, EPS[:, 0:1])
                    e_x3 = rec.op("vector", lambda v: v.tensor_scalar(x3t[:, :], x2t[:, :], st2[:, 2:3], st2[:, 6:7],
                                                                      OP.subtract, OP.mult), after=[e_rib])
                    e_w3 = rec.dma("scalar", "x3w", x3mine[tb * 128:(tb + 1) * 128, :], x3t[:, :], after=[e_x3])
                    e_x3w.append(e_w3)
                    prev = e_w3
                    prev2 = e_w2
                rec.emit()

    e_ag_x3 = rec.op("gpsimd", lambda g: g.collective_compute(
        "AllGather", OP.bypass, ins=[x3mine[:]],
        outs=[x3all[0:TTOT, :].rearrange("(s t) c -> s t c", s=NCORE)],
        replica_groups=ALL8), after=e_x3w, sem="CO", n=1)

    FSH = ctx.enter_context(nc.sbuf_tensor("FSH", [128, 8, C], BF16))
    idxT0 = ctx.enter_context(nc.sbuf_tensor("idxT0", [128, 64], I16))
    idxT1 = ctx.enter_context(nc.sbuf_tensor("idxT1", [128, 64], I16))
    wbt0 = ctx.enter_context(nc.sbuf_tensor("wbt0", [128, 8], F32))
    wbt1 = ctx.enter_context(nc.sbuf_tensor("wbt1", [128, 8], F32))
    IDXT = [idxT0, idxT1]
    WBT = [wbt0, wbt1]
    route_evs = []

    MT = ctx.enter_context(nc.sbuf_tensor("MT", [128, 22, TOK], BF16))
    if True:
        e_mt = []
        with nc.sbuf_tensor("X3T", [128, 2, 16, 512], BF16) as X3T:
            rec.stream("x3T")
            e_x3a = rec.op("gpsimd", lambda g: g.dma_gather(
                X3T[:, 0, :, :], x3mine[:], IDQS[0][:], num_idxs=512, num_idxs_reg=512,
                elem_size=C, transpose=True), after=[e_x3w[-1], e_c["idq0"]], sem="x3T", n=16)
            e_x3b = rec.op("gpsimd", lambda g: g.dma_gather(
                X3T[:, 1, :, :], x3mine[:], IDQS[1][:], num_idxs=512, num_idxs_reg=512,
                elem_size=C, transpose=True), after=[e_x3a, e_c["idq1"]], sem="x3T", n=16)
            e_x3T = [e_x3b] * 16

            with (
                nc.sbuf_tensor("LTs", [16, TOK], F32) as LTs,
                nc.sbuf_tensor("Ltok", [128, 8, 16], F32) as Ltok,
                nc.sbuf_tensor("gsc", [128, 8, 16], F32) as gsc,
                nc.sbuf_tensor("gst", [128, 10], F32) as gst,
                nc.sbuf_tensor("m8s", [128, 8], F32) as m8s,
                nc.sbuf_tensor("WMs", [16, TOK], F32) as WMs,
                nc.psum_tensor("pL", [16, 2, 512], F32) as pL,
                nc.psum_tensor("pTt", [128, 16], F32) as pTt,
                nc.psum_tensor("pwt", [16, 128], F32) as pwt,
            ):
                e_lcp = []
                for tc2 in range(2):
                    last = None
                    for k in range(16):
                        last = rec.op("tensor", lambda t, tc2=tc2, k=k: t.matmul(
                            pL[:, tc2, :], GWT[:, k, :], X3T[:, tc2, k, :],
                            start=(k == 0), stop=(k == 15)), after=[e_gwt, e_x3T[k]])
                    e_lcp.append(rec.op("vector", lambda v, tc2=tc2: v.tensor_copy(
                        LTs[:, tc2 * 512:(tc2 + 1) * 512], pL[:, tc2, :]), after=[last]))

                prev_t = EV0
                prev_w = EV0
                for tb in range(8):
                    e_tr = rec.op("tensor", lambda t, tb=tb: t.transpose(pTt[:, :], LTs[:, tb * 128:(tb + 1) * 128], IDN32[0:16, 0:16]),
                                  after=e_lcp + [e_c["i32"], prev_t])
                    e_ev = rec.op("vector", lambda v, tb=tb: v.tensor_copy(Ltok[:, tb, :], pTt[:, :]), after=[e_tr])
                    prev_t = e_ev
                    lt = Ltok[:, tb, :]
                    e_mx = rec.op("vector", lambda v, lt=lt: v.tensor_reduce(gst[:, 0:1], lt, axis=AX, op=OP.max), after=[e_ev])
                    e_nm = rec.op("vector", lambda v: v.tensor_scalar(gst[:, 1:2], gst[:, 0:1], -1.0, None, OP.mult), after=[e_mx])
                    e_p = rec.op("scalar", lambda s, lt=lt, tb=tb: s.activation(gsc[:, tb, :], lt, AF.Exp, bias=gst[:, 1:2],
                                                                                accum_out=gst[:, 2:3]), after=[e_nm])
                    e_m8 = rec.op("vector", lambda v, tb=tb: v.max(out=m8s[:, :], in_=gsc[:, tb, :]), after=[e_p])
                    e_m = rec.op("vector", lambda v, tb=tb: v.tensor_scalar(Ltok[:, tb, :], gsc[:, tb, :], m8s[:, 3:4], None, OP.is_ge),
                                 after=[e_m8])
                    e_pm = rec.op("vector", lambda v, tb=tb: v.tensor_tensor(gsc[:, tb, :], gsc[:, tb, :], Ltok[:, tb, :], OP.mult),
                                  after=[e_m])
                    e_s4 = rec.op("vector", lambda v, tb=tb: v.tensor_reduce(gst[:, 3:4], gsc[:, tb, :], axis=AX, op=OP.add), after=[e_pm])
                    e_zz = rec.op("vector", lambda v: v.tensor_scalar(gst[:, 4:5], gst[:, 2:3], 1e-20, None, OP.mult), after=[e_p, e_s4])
                    e_dn = rec.op("vector", lambda v: v.tensor_tensor(gst[:, 5:6], gst[:, 3:4], gst[:, 4:5], OP.add), after=[e_zz])
                    e_di = rec.op("vector", lambda v: v.reciprocal(gst[:, 6:7], gst[:, 5:6]), after=[e_dn])
                    e_w = rec.op("vector", lambda v, tb=tb: v.tensor_scalar(gsc[:, tb, :], gsc[:, tb, :], gst[:, 6:7], None, OP.mult),
                                 after=[e_di])
                    e_tw = rec.op("tensor", lambda t, tb=tb: t.transpose(pwt[:, :], gsc[:, tb, :], IDN32[:, :]),
                                  after=[e_w, prev_w])
                    e_wv2 = rec.op("vector", lambda v, tb=tb: v.tensor_copy(WMs[:, tb * 128:(tb + 1) * 128], pwt[:, :]), after=[e_tw])
                    prev_w = e_wv2
                e_wmd = rec.dma("scalar", "wmw", wm_mine[:], WMs[:], after=[prev_w])
                rec.emit()

            e_ag_wm = rec.op("gpsimd", lambda g: g.collective_compute(
                "AllGather", OP.bypass, ins=[wm_mine[:]], outs=[wm_all[:]], replica_groups=ALL8),
                after=[e_wmd], sem="CO", n=1)

            mrg = contextlib.ExitStack()
            _mt = lambda name, shape, dt: mrg.enter_context(nc.sbuf_tensor(name, shape, dt))
            wst = _mt("wst", [128, 2, 2, 16, 128], BF16)
            gsil = _mt("gsil", [128, 2, TOK], BF16)
            Wfull = _mt("Wfull", [128, TOK], F32)
            mfull = _mt("mfull", [128, TOK], F32)
            ones2 = _mt("ones2", [128, TOK], F32)
            Rx = _mt("Rx", [128, TOK], F32)
            keptf = _mt("keptf", [128, TOK], F32)
            TOKc = _mt("TOKc", [128, TOK], F32)
            CSUM = _mt("CSUM", [128, 128], F32)
            vtok = _mt("vtok", [128, TOK], F32)
            vw = _mt("vw", [128, TOK], F32)
            tot = _mt("tot", [128, 2], F32)
            sent = _mt("sent", [1, 2 * TOK], F32)
            sgi = _mt("sgi", [16, 2, 576], F32)
            sgo = _mt("sgo", [16, 144], F32)
            sgw = _mt("sgw", [16, 144], F32)
            nfo = _mt("nfo", [1, 4], U32)
            pbase = mrg.enter_context(nc.psum_tensor("pbase", [128, 1], F32))
            pm1 = mrg.enter_context(nc.psum_tensor("pm1", [128, 2, 512], F32))
            pm2 = mrg.enter_context(nc.psum_tensor("pm2", [128, 2, 512], F32))
            if True:
                sh_gate_rd = [EV0, EV0]
                sh_up_rd = [EV0, EV0]
                sh_silu = []
                sh_mul = []
                sh_mul_sl = [[], []]

                def emit_shared_ic(ic):
                    nonlocal sh_silu, sh_mul
                    sl = ic % 2
                    e_wg = rec.dma("scalar", f"sstg{sl}", wst[:, 0, sl, :, :], swg_b[ic],
                                   after=[sh_gate_rd[sl], e_swg])
                    lastg = [None, None]
                    for h2 in range(2):
                        for k in range(16):
                            aft = [e_wg, e_x3T[k]]
                            if k == 0:
                                aft += sh_silu
                            lastg[h2] = rec.op("tensor", lambda t, sl=sl, k=k, h2=h2: t.matmul(
                                pm1[:, h2, :], wst[:, 0, sl, k, :], X3T[:, h2, k, :],
                                start=(k == 0), stop=(k == 15)), after=aft)
                    sh_gate_rd[sl] = lastg[1]
                    e_sil = rec.op("scalar", lambda s, sl=sl: s.activation(gsil[:, sl, 0:512], pm1[:, 0, :], AF.Silu),
                                   after=[lastg[0]] + sh_mul_sl[sl])
                    e_sil2 = rec.op("scalar", lambda s, sl=sl: s.activation(gsil[:, sl, 512:1024], pm1[:, 1, :], AF.Silu),
                                    after=[lastg[1], e_sil])
                    sh_silu = [e_sil, e_sil2]
                    e_wu = rec.dma("scalar", f"sstu{sl}", wst[:, 1, sl, :, :], swu_b[ic],
                                   after=[sh_up_rd[sl], e_swu])
                    lastu = [None, None]
                    for h2 in range(2):
                        for k in range(16):
                            aft = [e_wu, e_x3T[k]]
                            if k == 0:
                                aft += sh_mul
                            lastu[h2] = rec.op("tensor", lambda t, sl=sl, k=k, h2=h2: t.matmul(
                                pm2[:, h2, :], wst[:, 1, sl, k, :], X3T[:, h2, k, :],
                                start=(k == 0), stop=(k == 15)), after=aft)
                    sh_up_rd[sl] = lastu[1]
                    e_mul1 = rec.op("vector", lambda v, ic=ic, sl=sl: v.tensor_tensor(
                        MT[:, ic, 0:512], pm2[:, 0, :], gsil[:, sl, 0:512], OP.mult),
                        after=[lastu[0], e_sil])
                    e_mul2 = rec.op("vector", lambda v, ic=ic, sl=sl: v.tensor_tensor(
                        MT[:, ic, 512:1024], pm2[:, 1, :], gsil[:, sl, 512:1024], OP.mult),
                        after=[lastu[1], e_sil2, e_mul1])
                    sh_mul = [e_mul1, e_mul2]
                    sh_mul_sl[sl] = [e_mul1, e_mul2]
                    e_mt.append(e_mul2)

                e_wf_l = [rec.dma("sync", "wmr", Wfull[e8 * 8:(e8 + 1) * 8, :], wm_all[:, e8, :], after=[e_ag_wm]) for e8 in range(E)]
                e_wf = [e_wf_l[-1]] * NCORE
                e_tc = rec.dma("sync", "tokc", TOKc[:, :], cTOK128[:])
                e_cs = rec.dma("sync", "csum", CSUM[:, :], cCSUM[:])
                e_mf = rec.op("vector", lambda v: v.tensor_scalar(mfull[:, :], Wfull[:, :], 0.0, None, OP.is_gt), after=e_wf)
                e_o2 = rec.op("vector", lambda v: v.memset(ones2[:, :], 1.0))
                e_sn0 = rec.op("vector", lambda v: v.memset(sent[:, 0:TOK], float(PADROW)))
                e_sn1 = rec.op("vector", lambda v: v.memset(sent[:, TOK:2 * TOK], 0.0))
                e_sc = rec.op("vector", lambda v: v.tensor_tensor_scan(Rx[:, :], ones2[:, :], mfull[:, :], 0.0, OP.mult, OP.add),
                              after=[e_mf, e_o2])
                e_tt = rec.op("vector", lambda v: v.tensor_copy(tot[:, 0:1], Rx[:, TOK - 1:TOK]), after=[e_sc])

                for ic in range(4):
                    emit_shared_ic(ic)

                e_bm = rec.op("tensor", lambda t: t.matmul(pbase[:, :], CSUM[:, :], tot[:, 0:1], start=True, stop=True),
                              after=[e_tt, e_cs])
                e_bc2 = rec.op("vector", lambda v: v.tensor_copy(tot[:, 1:2], pbase[:, :]), after=[e_bm])
                e_ad = rec.op("vector", lambda v: v.tensor_scalar(Rx[:, :], Rx[:, :], tot[:, 1:2], None, OP.add), after=[e_bc2])
                e_rx = rec.op("vector", lambda v: v.tensor_tensor(Rx[:, :], Rx[:, :], mfull[:, :], OP.subtract), after=[e_ad])
                e_ltc = rec.op("vector", lambda v: v.tensor_scalar(keptf[:, :], Rx[:, :], float(CAP), None, OP.is_lt), after=[e_rx])
                e_kp = rec.op("vector", lambda v: v.tensor_tensor(keptf[:, :], keptf[:, :], mfull[:, :], OP.mult), after=[e_ltc])
                e_vt0 = rec.op("vector", lambda v: v.tensor_tensor(vtok[:, :], keptf[:, :], TOKc[:, :], OP.mult),
                               after=[e_kp, e_tc])
                e_vt = rec.op("vector", lambda v: v.tensor_scalar(vtok[:, :], vtok[:, :], -1.0, None, OP.add), after=[e_vt0])
                e_vwa = rec.op("vector", lambda v: v.tensor_tensor(vw[:, :], keptf[:, :], Wfull[:, :], OP.mult), after=[e_kp])
                e_vwb = rec.op("vector", lambda v: v.tensor_scalar(TOKc[:, :], keptf[:, :], -1.0, None, OP.add), after=[e_vwa, e_vt0])
                e_vw0 = rec.op("vector", lambda v: v.tensor_tensor(vw[:, :], vw[:, :], TOKc[:, :], OP.add), after=[e_vwb])

                e_st0 = rec.dma("gpsimd", "sent", strm[0:1, TTOT:9216], sent[:, 0:TOK], after=[e_sn0])
                e_st1 = rec.dma("gpsimd", "sent", strm[1:2, TTOT:9216], sent[:, TOK:2 * TOK], after=[e_sn1])

                prev_strm = [e_vt, e_vw0, e_st0, e_st1]
                lib_sg = [False]
                for ei in range(EPC):
                    rec.stream("rows")

                    def emit_row_ei(g, ei=ei):
                        for c in range(NCORE):
                            with g.If_eq(rank_reg[0], c):
                                r = 2 * c + ei
                                g.dma_start(
                                    out=strm[0, 0:TTOT].rearrange("(s t) -> s t", t=TOK),
                                    in_=vtok[8 * r:8 * r + 8, :]).then_inc(rec.sems["rows"][0], 16)
                                g.dma_start(
                                    out=strm[1, 0:TTOT].rearrange("(s t) -> s t", t=TOK),
                                    in_=vw[8 * r:8 * r + 8, :]).then_inc(rec.sems["rows"][0], 16)
                            g.end_ifs()
                    rec.raw("gpsimd", emit_row_ei, after=prev_strm)
                    e_rows = rec.bump("rows", 2 * 16)
                    e_wrap0 = rec.dma("gpsimd", "wrap", sgi[:, 0, :], strm[0, :].rearrange("(f p) -> p f", p=16), after=[e_rows], allow_slow_non_contiguous=True)
                    e_wrap1 = rec.dma("gpsimd", "wrap", sgi[:, 1, :], strm[1, :].rearrange("(f p) -> p f", p=16), after=[e_wrap0], allow_slow_non_contiguous=True)
                    if not lib_sg[0]:
                        rec.raw("gpsimd", lambda g: g.load_library(lc.sparse_gather), after=[e_wrap1])
                        lib_sg[0] = True
                    e_sg0 = rec.op("gpsimd", lambda g: g.sparse_gather(sgo[:, :], sgi[:, 0, :], num_found=nfo[0:1, 0:1]),
                                   after=[e_wrap1])
                    e_sg1 = rec.op("gpsimd", lambda g: g.sparse_gather(sgw[:, :], sgi[:, 1, :], num_found=nfo[0:1, 1:2]),
                                   after=[e_sg0])
                    e_cl0 = rec.op("vector", lambda v: v.tensor_scalar(sgo[:, 0:64], sgo[:, 0:64], 0.0, float(PADROW), OP.max, OP.min),
                                   after=[e_sg0])
                    e_i16 = rec.op("vector", lambda v, ei=ei: v.tensor_copy(IDXT[ei][0:16, :], sgo[:, 0:64]), after=[e_cl0])
                    reps = [e_i16]
                    for kk in range(1, 8):
                        reps.append(rec.dma("sync", "irep", IDXT[ei][16 * kk:16 * (kk + 1), :], IDXT[ei][0:16, :], after=[e_i16]))
                    e_wb0 = rec.dma("gpsimd", "wbb", wbb[ei].rearrange("(f p) -> p f", p=16), sgw[:, 0:64], after=[e_sg1], allow_slow_non_contiguous=True)
                    e_wb1 = rec.dma("gpsimd", "wbb", WBT[ei][:, :], wbb[ei].rearrange("(f p) -> p f", p=128), after=[e_wb0], allow_slow_non_contiguous=True)
                    route_evs.append((reps, e_wb1))
                    prev_strm = [e_wrap1, e_sg1, e_cl0, e_i16, e_wb0]
                rec.raw("gpsimd", lambda g: g.drain())
                rec.raw("gpsimd", lambda g: g.load_library(lc.mlp), after=[route_evs[-1][1]])

                for ic in range(4, 22):
                    emit_shared_ic(ic)
                rec.emit()
            mrg.close()


    e_scat = []
    with (
        nc.sbuf_tensor("XGT", [128, 2, 16, 512], BF16) as XGT,
        nc.sbuf_tensor("wge", [128, 2, 2, 16, 128], BF16) as wge,
        nc.sbuf_tensor("MT2", [128, 11, CAP], BF16) as MT2,
        nc.sbuf_tensor("gsil2", [128, 2, CAP], BF16) as gsil2,
        nc.sbuf_tensor("eout", [128, 8, C], BF16) as eout,
        nc.sbuf_tensor("wde", [128, 2, 512], BF16) as wde,
    ):
        prev_scatter = EV0
        prev_xgt_users = EV0
        gate_rd = [EV0, EV0]
        up_rd = [EV0, EV0]
        silu_ev = [[], []]
        mul_ev = [[], []]
        prev_evac = []
        slot_last2 = [EV0, EV0]
        for ei in range(EPC):
            reps, e_wb = route_evs[ei]
            rec.stream("xg")
            e_xga = rec.op("gpsimd", lambda g, ei=ei: g.dma_gather(
                XGT[:, 0, :, :], x3all[:], IDXT[ei][:, 0:32], num_idxs=512, num_idxs_reg=512,
                elem_size=C, transpose=True),
                after=[e_ag_x3, prev_xgt_users, e_cast["zero"]] + reps, sem="xg", n=16)
            e_xg = rec.op("gpsimd", lambda g, ei=ei: g.dma_gather(
                XGT[:, 1, :, :], x3all[:], IDXT[ei][:, 32:64], num_idxs=512, num_idxs_reg=512,
                elem_size=C, transpose=True),
                after=[e_xga], sem="xg", n=16)
            e_mt2 = []
            pgctx = contextlib.ExitStack()
            pg1 = pgctx.enter_context(nc.psum_tensor(f"pg1_{ei}", [128, 2, 2, 512], F32))
            pg2 = pgctx.enter_context(nc.psum_tensor(f"pg2_{ei}", [128, 2, 2, 512], F32))
            for ic in range(11):
                sl = ic % 2
                e_wg2 = rec.dma("sync", f"wge_g{sl}", wge[:, 0, sl, :, :], weg_b[ei, ic],
                                after=[gate_rd[sl], e_weg])
                lastg = [None, None]
                for h2 in range(2):
                    for k in range(16):
                        aft = [e_wg2, e_xg]
                        if k == 0:
                            aft += silu_ev[sl] + prev_evac
                        lastg[h2] = rec.op("tensor", lambda t, k=k, h2=h2, sl=sl: t.matmul(
                            pg1[:, sl, h2, :], wge[:, 0, sl, k, :], XGT[:, h2, k, :],
                            start=(k == 0), stop=(k == 15)), after=aft)
                gate_rd[sl] = lastg[1]
                e_si1 = rec.op("scalar", lambda s, sl=sl: s.activation(gsil2[:, sl, 0:512], pg1[:, sl, 0, :], AF.Silu),
                               after=[lastg[0]] + mul_ev[sl])
                e_si2 = rec.op("scalar", lambda s, sl=sl: s.activation(gsil2[:, sl, 512:1024], pg1[:, sl, 1, :], AF.Silu),
                               after=[lastg[1], e_si1])
                silu_ev[sl] = [e_si1, e_si2]
                e_wu2 = rec.dma("sync", f"wge_u{sl}", wge[:, 1, sl, :, :], weu_b[ei, ic],
                                after=[up_rd[sl], e_weu])
                lastu = [None, None]
                for h2 in range(2):
                    for k in range(16):
                        aft = [e_wu2, e_xg]
                        if k == 0:
                            aft += mul_ev[sl] + prev_evac
                        lastu[h2] = rec.op("tensor", lambda t, k=k, h2=h2, sl=sl: t.matmul(
                            pg2[:, sl, h2, :], wge[:, 1, sl, k, :], XGT[:, h2, k, :],
                            start=(k == 0), stop=(k == 15)), after=aft)
                up_rd[sl] = lastu[1]
                e_m1 = rec.op("vector", lambda v, ic=ic, sl=sl: v.tensor_tensor(MT2[:, ic, 0:512], pg2[:, sl, 0, :], gsil2[:, sl, 0:512], OP.mult),
                              after=[lastu[0], e_si1])
                e_m2 = rec.op("vector", lambda v, ic=ic, sl=sl: v.tensor_tensor(MT2[:, ic, 512:1024], pg2[:, sl, 1, :], gsil2[:, sl, 512:1024], OP.mult),
                              after=[lastu[1], e_si2, e_m1])
                mul_ev[sl] = [e_m1, e_m2]
                e_mt2.append(e_m2)
            prev_xgt_users = e_mt2[-1]
            pgctx.close()

            pectx = contextlib.ExitStack()
            pe8 = pectx.enter_context(nc.psum_tensor(f"pe8_{ei}", [128, 8, 512], F32))
            prev_eo = {sb: e_mt2[-1] for sb in range(8)}
            for coq in range(4):
                for ic in range(11):
                    sl = ic % 2
                    e_ldd = rec.dma("sync", f"wde{sl}", wde[:, sl, :], wed_b[ei, ic, coq],
                                    after=[slot_last2[sl], e_wed])
                    last_mm = None
                    for sb in range(8):
                        aft = [e_ldd, e_mt2[ic]]
                        if ic == 0:
                            aft.append(prev_eo[sb])
                        last_mm = rec.op("tensor", lambda t, sb=sb, ic=ic, sl=sl: t.matmul(
                            pe8[:, sb, :], MT2[:, ic, sb * 128:(sb + 1) * 128], wde[:, sl, :],
                            start=(ic == 0), stop=(ic == 10)), after=aft)
                    slot_last2[sl] = last_mm
                pe_now = rec.now("PE")
                for sb in range(8):
                    prev_eo[sb] = rec.op("scalar", lambda s, sb=sb, coq=coq, ei=ei: s.activation(
                        eout[:, sb, coq * 512:(coq + 1) * 512], pe8[:, sb, :], AF.Copy, bias=0.0,
                        scale=WBT[ei][:, sb:sb + 1]), after=[pe_now, e_wb, prev_scatter])
            e_eo = [prev_eo[sb] for sb in range(8)]
            prev_evac = e_eo
            rec.stream("scat")
            e_sc2 = rec.op("gpsimd", lambda g, ei=ei: g.dma_scatter_add(
                yp[:], eout[:, :, :], IDXT[ei][:, :], num_idxs=CAP, num_idxs_reg=CAP, elem_size=C),
                after=e_eo + [e_cast["zero"], prev_scatter] + reps, sem="scat", n=16)
            prev_scatter = e_sc2
            e_scat.append(e_sc2)
            pectx.close()
        rec.emit()

    e_rs = rec.op("gpsimd", lambda g: g.collective_compute(
        "ReduceScatter", OP.add, ins=[yp[0:TTOT, :]], outs=[y_rs[:]], replica_groups=ALL8),
        after=e_scat, sem="CO", n=1)

    with (
        nc.sbuf_tensor("wds", [128, 2, 512], BF16) as wds,
        nc.psum_tensor("pf", [128, 8, 512], F32) as pf,
    ):
        prev_f = {tb: EV0 for tb in range(8)}
        slot_last = [EV0, EV0]
        for coq in range(4):
            for ic in range(22):
                sl = ic % 2
                e_ld = rec.dma("sync", f"swd{sl}", wds[:, sl, :], swd_b[ic, coq],
                               after=[slot_last[sl], e_swd])
                last_mm = None
                for tb in range(8):
                    aft = [e_ld, e_mt[ic]]
                    if ic == 0:
                        aft.append(prev_f[tb])
                    last_mm = rec.op("tensor", lambda t, tb=tb, ic=ic, sl=sl, coq=coq: t.matmul(
                        pf[:, tb, :], MT[:, ic, tb * 128:(tb + 1) * 128], wds[:, sl, :],
                        start=(ic == 0), stop=(ic == 21)), after=aft)
                slot_last[sl] = last_mm
            pe_now = rec.now("PE")
            for tb in range(8):
                prev_f[tb] = rec.op("scalar", lambda s, tb=tb, coq=coq: s.activation(
                    FSH[:, tb, coq * 512:(coq + 1) * 512], pf[:, tb, :], AF.Copy, bias=0.0, scale=1.0),
                    after=[pe_now])
        e_fsh = [prev_f[tb] for tb in range(8)]
        rec.emit()


    with (
        nc.sbuf_tensor("ym", [128, C], BF16) as ym,
        nc.sbuf_tensor("ff", [128, C], F32) as ff,
        nc.sbuf_tensor("x2r", [128, C], F32) as x2r,
        nc.sbuf_tensor("oo", [128, C], F32) as oo,
        nc.sbuf_tensor("sqd3", [128, C], BF16) as sqd3,
        nc.sbuf_tensor("st3", [128, 8], F32) as st3,
    ):
        e_outs = []
        prev = EV0
        for tb in range(8):
            e_ym = rec.dma("sync", "ymr", ym[:, :], y_rs[tb * 128:(tb + 1) * 128, :], after=[e_rs, prev])
            e_x2r = rec.dma("sync", "x2r", x2r[:, :], x2s[tb * 128:(tb + 1) * 128, :], after=[prev, ("x2w", 128)])
            e_ff = rec.op("vector", lambda v, tb=tb: v.tensor_tensor(ff[:, :], ym[:, :], FSH[:, tb, :], OP.add),
                          after=[e_ym, e_fsh[tb], prev])
            e_ri = _ln_stats(rec, st3, ff[:, :], sqd3, e_ff, prev, EPS[:, 0:1])
            e_ln = rec.op("vector", lambda v: v.tensor_scalar(ff[:, :], ff[:, :], st3[:, 2:3], st3[:, 6:7],
                                                              OP.subtract, OP.mult), after=[e_ri])
            e_o = rec.op("vector", lambda v: v.tensor_tensor(oo[:, :], x2r[:, :], ff[:, :], OP.add), after=[e_ln, e_x2r])
            e_w = rec.dma("sync", "outw", out_shard[tb * 128:(tb + 1) * 128, :], oo[:, :], after=[e_o])
            e_outs.append(e_w)
            prev = e_w
        rec.raw("sync", lambda s: None, after=e_outs)
        rec.raw("gpsimd", lambda g: None, after=e_outs)
        rec.emit()

    ctx.close()
    return nc


def kernel(**inputs):
    x = np.ascontiguousarray(inputs["L0_attn_x"])
    wq = np.ascontiguousarray(inputs["L0_attn_wq"])
    wk = np.ascontiguousarray(inputs["L0_attn_wk"])
    wv = np.ascontiguousarray(inputs["L0_attn_wv"])
    wo = np.ascontiguousarray(inputs["L0_attn_wo"])
    lqk = np.ascontiguousarray(np.stack([inputs["L0_attn_lq1"], inputs["L0_attn_lk1"],
                                         inputs["L0_attn_lq2"], inputs["L0_attn_lk2"]], axis=0))
    gate = np.ascontiguousarray(inputs["L0_ffn_gate_w"])
    wg = np.ascontiguousarray(inputs["L0_ffn_w_gate"])
    wu = np.ascontiguousarray(inputs["L0_ffn_w_up"])
    wd = np.ascontiguousarray(inputs["L0_ffn_w_down"])
    swg_i = np.ascontiguousarray(inputs["L0_ffn_sw_gate"])
    swu_i = np.ascontiguousarray(inputs["L0_ffn_sw_up"])
    swd_i = np.ascontiguousarray(inputs["L0_ffn_sw_down"])

    nc = build_nc()
    lower_extended_insts(nc)

    in_maps = []
    for c in range(NCORE):
        b, half = c // 2, c % 2
        hr = slice(1024 * half, 1024 * (half + 1))
        inv = (1.0 / 10000.0) ** (np.arange(0, HH, 2, dtype=np.float64) / HH)
        rot = np.zeros((128, 8), np.float32)
        for lh in range(4):
            h = 4 * half + lh
            th = h * inv
            rot[0:64, lh] = np.cos(th)
            rot[64:128, lh] = np.cos(th)
            rot[0:64, 4 + lh] = np.sin(th)
            rot[64:128, 4 + lh] = -np.sin(th)
        def tile_w(m):
            n = m.shape[0] // 512
            return np.ascontiguousarray(m.reshape(n, 512, 16, 128).transpose(0, 3, 2, 1))

        wg2 = wg[2 * c:2 * c + 2]
        wu2 = wu[2 * c:2 * c + 2]
        wd2 = wd[2 * c:2 * c + 2]
        in_maps.append({
            "xb": x[b],
            "rotpq": rot,
            "x_own": np.ascontiguousarray(x[b][1024 * half:1024 * (half + 1)]),
            "wq_t": tile_w(wq[hr]),
            "wk_t": tile_w(wk[hr]),
            "wv_t": tile_w(wv[hr]),
            "wo_t": tile_w(wo),
            "lqk": lqk,
            "gate_w": gate,
            "weg": np.ascontiguousarray(wg2.reshape(2, 16, 128, 11, 128).transpose(0, 3, 2, 1, 4)),
            "weu": np.ascontiguousarray(wu2.reshape(2, 16, 128, 11, 128).transpose(0, 3, 2, 1, 4)),
            "wed": np.ascontiguousarray(wd2.reshape(2, 11, 128, 4, 512).transpose(0, 1, 3, 2, 4)),
            "swg": np.ascontiguousarray(swg_i.reshape(16, 128, 22, 128).transpose(2, 1, 0, 3)),
            "swu": np.ascontiguousarray(swu_i.reshape(16, 128, 22, 128).transpose(2, 1, 0, 3)),
            "swd": np.ascontiguousarray(swd_i.reshape(22, 128, 4, 512).transpose(0, 2, 1, 3)),
        })
    res = run_bass_kernel_spmd(nc, in_maps, core_ids=list(range(NCORE)))
    shards = [res.results[c]["out_shard"] for c in range(NCORE)]
    out = np.concatenate(shards, axis=0).reshape(B, T, C)
    return out


# version 09
# speedup vs baseline: 1.3779x; 1.0053x over previous
import contextlib
import math

import numpy as np
import ml_dtypes

import concourse.bass as bass
import concourse.mybir as mybir
from concourse.alu_op_type import AluOpType
from concourse import library_config as lc
from concourse.bass_utils import run_bass_kernel_spmd
from concourse.library_overlay import lower_extended_insts

F32 = mybir.dt.float32
BF16 = mybir.dt.bfloat16
I16 = mybir.dt.int16
U32 = mybir.dt.uint32
AF = mybir.ActivationFunctionType
OP = AluOpType
AX = mybir.AxisListType.X

B, T, C = 4, 2048, 2048
NCORE = 8
TOK = 1024
HH, HD = 128, 256
E, CAP = 16, 1024
IEXP, ISH = 1408, 2816
EPC = 2
LN_EPS = 1e-5
RMS_EPS = 1.1920929e-07
LAMBDA_INIT = 0.2
ISQ = 1.0 / math.sqrt(128.0)
TTOT = B * T
PADROW = TTOT

PAIRS = [[0, 1], [2, 3], [4, 5], [6, 7]]
ALL8 = [list(range(NCORE))]

ENGINES = ("tensor", "vector", "scalar", "gpsimd", "sync")
CHAIN = {"tensor": "PE", "vector": "DV", "scalar": "AC", "gpsimd": "PL", "sync": "SY"}
QSEM = {"sync": "QS", "scalar": "QA", "gpsimd": "QG"}


class Rec:

    def __init__(self, nc, ctx):
        self.nc = nc
        self.ctx = ctx
        self.sems = {}
        for name in ["PE", "DV", "AC", "PL", "SY", "CO"]:
            self.sems[name] = [ctx.enter_context(nc.semaphore(name)), 0]
        self.ops = {e: [] for e in ENGINES}
        self.hw = {e: {} for e in ENGINES}
        self.drained = {e: 0 for e in ENGINES}
        self.block_start = {}
        self.bg = {"CO"}

    def stream(self, name, bg=False):
        if name not in self.sems:
            self.sems[name] = [self.ctx.enter_context(self.nc.semaphore(name)), 0]
        if bg:
            self.bg.add(name)
        return self.sems[name]

    def now(self, kind):
        return (kind, self.sems[kind][1])

    def _after(self, eng, after):
        waits, need_drain = [], False
        for ev in after:
            if ev is None:
                continue
            k, v = ev
            if v <= 0:
                continue
            if k == CHAIN[eng]:
                if v > self.drained[eng]:
                    need_drain = True
            else:
                waits.append((k, v))
        if need_drain:
            self.drained[eng] = self.sems[CHAIN[eng]][1]
        wmax = {}
        for (k, v) in waits:
            wmax[k] = max(wmax.get(k, 0), v)
        return sorted(wmax.items()), need_drain

    def op(self, eng, fn, after=(), sem=None, n=1):
        kind = sem or CHAIN[eng]
        waits, need_drain = self._after(eng, after)
        self.sems[kind][1] += n
        val = self.sems[kind][1]
        self.ops[eng].append((fn, waits, need_drain, kind, n))
        return (kind, val)

    def dma(self, eng, stream, out, in_, after=(), bg=False, **kw):
        self.stream(stream, bg=bg)
        return self.op(eng, lambda e: e.dma_start(out=out, in_=in_, **kw),
                       after=after, sem=stream, n=16)

    def raw(self, eng, fn, after=()):
        waits, need_drain = self._after(eng, after)
        self.ops[eng].append((fn, waits, need_drain, None, 0))

    def bump(self, kind, n):
        self.sems[kind][1] += n
        return (kind, self.sems[kind][1])

    def emit(self):
        nc = self.nc
        if not any(self.ops[e] for e in ENGINES):
            return
        fence = [(k, v) for k, v in self.block_start.items()
                 if v > 0 and k not in self.bg]
        with nc.Block() as blk:
            for eng_name in ENGINES:
                ops = self.ops[eng_name]
                if not ops:
                    continue

                def make_body(eng_name, ops):
                    def body(eng):
                        hw = self.hw[eng_name]
                        for (k, v) in fence:
                            if hw.get(k, 0) < v:
                                eng.wait_ge(self.sems[k][0], v)
                                hw[k] = v
                        for (fn, waits, need_drain, kind, n) in ops:
                            for (k, v) in waits:
                                if hw.get(k, 0) < v:
                                    eng.wait_ge(self.sems[k][0], v)
                                    hw[k] = v
                            if need_drain:
                                eng.drain()
                            inst = fn(eng)
                            if kind is not None and inst is not None:
                                inst.then_inc(self.sems[kind][0], n)
                    return body

                getattr(blk, eng_name)(make_body(eng_name, ops))
        self.ops = {e: [] for e in ENGINES}
        self.block_start = {k: s[1] for k, s in self.sems.items()
                            if k not in ("PE", "DV", "AC", "PL", "SY")}


def _consts():
    ku = np.arange(896)[None, :] >= (np.arange(128)[:, None] + 384)
    stair = np.asarray(ku.astype(ml_dtypes.bfloat16))
    idnb = np.asarray(np.eye(128, dtype=ml_dtypes.bfloat16))
    idn32 = np.eye(128, dtype=np.float32)
    onesb = np.asarray(np.ones((128, 1), dtype=ml_dtypes.bfloat16))
    tok128 = (np.arange(128)[:, None] % 8 * 1024 + np.arange(TOK)[None, :] + 1).astype(np.float32)
    pp = np.arange(128)
    csum = ((pp[:, None] // 8 == pp[None, :] // 8) & (pp[:, None] % 8 < pp[None, :] % 8)).astype(np.float32)
    idqs = []
    for b4 in range(4):
        w = np.zeros((128, 32), np.int16)
        for i in range(512):
            for g in range(8):
                w[(i % 16) + 16 * g, i // 16] = 512 * b4 + i
        idqs.append(w)
    return stair, idnb, idn32, onesb, tok128, csum, idqs


def _ln_stats(rec, stat, src_ap, sq_scratch, e_src, prev, eps_ap=None):
    e_s1 = rec.op("scalar", lambda s: s.activation(sq_scratch[:, :], src_ap, AF.Square, accum_out=stat[:, 1:2]),
                  after=[e_src, prev])
    e_s2 = rec.op("vector", lambda v: v.tensor_reduce(stat[:, 0:1], src_ap, axis=AX, op=OP.add),
                  after=[e_src, prev])
    e_m = rec.op("vector", lambda v: v.tensor_scalar(stat[:, 2:3], stat[:, 0:1], 1.0 / C, None, OP.mult), after=[e_s2])
    e_v1 = rec.op("vector", lambda v: v.tensor_tensor(stat[:, 3:4], stat[:, 2:3], stat[:, 2:3], OP.mult), after=[e_m])
    e_v2 = rec.op("vector", lambda v: v.scalar_tensor_tensor(stat[:, 4:5], stat[:, 1:2], 1.0 / C, stat[:, 3:4],
                                                             OP.mult, OP.subtract), after=[e_v1, e_s1])
    e_sd = rec.op("scalar", lambda s: s.activation(stat[:, 5:6], stat[:, 4:5], AF.Sqrt, bias=eps_ap), after=[e_v2])
    e_ri = rec.op("vector", lambda v: v.reciprocal(stat[:, 6:7], stat[:, 5:6]), after=[e_sd])
    return e_ri


def build_nc():
    nc = bass.Bass("TRN2", target_bir_lowering=False)
    ctx = contextlib.ExitStack()
    dram = lambda name, shape, dt, kind="Internal": nc.dram_tensor(name, shape, dt, kind=kind).ap()

    xb = dram("xb", [T, C], F32, "ExternalInput")
    x_own = dram("x_own", [TOK, C], F32, "ExternalInput")
    wq_t = dram("wq_t", [2, 128, 16, 512], F32, "ExternalInput")
    wk_t = dram("wk_t", [2, 128, 16, 512], F32, "ExternalInput")
    wv_t = dram("wv_t", [2, 128, 16, 512], F32, "ExternalInput")
    wo_t = dram("wo_t", [4, 128, 16, 512], F32, "ExternalInput")
    lqk = dram("lqk", [4, HD], F32, "ExternalInput")
    rotpq = dram("rotpq", [128, 8], F32, "ExternalInput")
    gate_w = dram("gate_w", [E, C], F32, "ExternalInput")
    weg = dram("weg", [EPC, 11, 128, 16, 128], F32, "ExternalInput")
    weu = dram("weu", [EPC, 11, 128, 16, 128], F32, "ExternalInput")
    wed = dram("wed", [EPC, 11, 4, 128, 512], F32, "ExternalInput")
    swg = dram("swg", [22, 128, 16, 128], F32, "ExternalInput")
    swu = dram("swu", [22, 128, 16, 128], F32, "ExternalInput")
    swd = dram("swd", [22, 4, 128, 512], F32, "ExternalInput")
    out_shard = dram("out_shard", [TOK, C], F32, "ExternalOutput")

    stair_h, idnb_h, idn32_h, onesb_h, tok128_h, csum_h, idqs_h = _consts()
    cSTAIR = nc.inline_tensor(stair_h, name="cSTAIR").ap()
    cIDNB = nc.inline_tensor(idnb_h, name="cIDNB").ap()
    cIDN32 = nc.inline_tensor(idn32_h, name="cIDN32").ap()
    cONESB = nc.inline_tensor(onesb_h, name="cONESB").ap()
    cTOK128 = nc.inline_tensor(tok128_h, name="cTOK128").ap()
    cIDQ = [nc.inline_tensor(idqs_h[b4], name=f"cIDQ{b4}").ap() for b4 in range(4)]
    cCSUM = nc.inline_tensor(csum_h, name="cCSUM").ap()

    x1s = dram("x1s", [T, C], BF16)
    wq_b = dram("wq_b", [2, 128, 16, 512], BF16)
    wk_b = dram("wk_b", [2, 128, 16, 512], BF16)
    wv_b = dram("wv_b", [2, 128, 16, 512], BF16)
    wo_b = dram("wo_b", [4, 128, 16, 512], BF16)
    weg_b = dram("weg_b", [EPC, 11, 128, 16, 128], BF16)
    weu_b = dram("weu_b", [EPC, 11, 128, 16, 128], BF16)
    wed_b = dram("wed_b", [EPC, 11, 4, 128, 512], BF16)
    swg_b = dram("swg_b", [22, 128, 16, 128], BF16)
    swu_b = dram("swu_b", [22, 128, 16, 128], BF16)
    swd_b = dram("swd_b", [22, 4, 128, 512], BF16)
    x2s = dram("x2s", [TOK, C], F32)
    yhalf = dram("yhalf", [128, 8, 1024], BF16)
    ypair = dram("ypair", [2, 128, 8, 1024], BF16)
    x3mine = dram("x3mine", [TOK, C], BF16)
    x3all = dram("x3all", [TTOT + 1, C], BF16)
    wm_mine = dram("wm_mine", [E, TOK], F32)
    wm_all = dram("wm_all", [NCORE, E, TOK], F32)
    yp = dram("yp", [TTOT + 1, C], BF16)
    y_rs = dram("y_rs", [TOK, C], BF16)
    strm = dram("strm", [2, 9216], F32)
    wbb = dram("wbb", [EPC, CAP], F32)

    sbt = lambda name, shape, dt: ctx.enter_context(nc.sbuf_tensor(name, shape, dt))
    ROTS = sbt("ROTS", [128, 8], F32)
    IDQS = [sbt(f"IDQS{b4}", [128, 32], I16) for b4 in range(4)]
    STAIR = sbt("STAIR", [128, 896], BF16)
    IDNB = sbt("IDNB", [128, 128], BF16)
    IDN32 = sbt("IDN32", [128, 128], F32)
    ONESB = sbt("ONESB", [128, 1], BF16)
    ZEROT = sbt("ZEROT", [128, 4096], BF16)
    LAM = sbt("LAM", [1, 4], F32)
    GWT = sbt("GWT", [128, 16, 16], BF16)
    EPS = sbt("EPS", [128, 2], F32)

    rec = Rec(nc, ctx)
    EV0 = ("PE", 0)

    rank_reg = [None]

    def init_g(g):
        g.load_library(lc.mlp)
        rank_reg[0] = g.to_reg(g.partition_id())
    rec.raw("gpsimd", init_g)


    e_c = {}
    for nm, dst, src in (("st", STAIR, cSTAIR), ("ib", IDNB, cIDNB),
                         ("i32", IDN32, cIDN32), ("ob", ONESB, cONESB)):
        e_c[nm] = rec.dma("sync", "cst_" + nm, dst[:], src[:])
    e_c["rot"] = rec.dma("sync", "cst_rot", ROTS[:], rotpq[:])
    for b4 in range(4):
        e_c[f"idq{b4}"] = rec.dma("sync", f"cst_idq{b4}", IDQS[b4][:], cIDQ[b4][:])
    e_zt = rec.op("vector", lambda v: v.memset(ZEROT[:], 0.0))
    rec.op("vector", lambda v: v.memset(EPS[:, 0:1], LN_EPS))
    rec.op("vector", lambda v: v.memset(EPS[:, 1:2], RMS_EPS))

    ZELEM = 128 * 4096
    flat_yp = yp.rearrange("r c -> (r c)")
    nzp = ((TTOT + 1) * C) // ZELEM
    zero_chunks = []
    for z in range(nzp):
        zero_chunks.append((flat_yp[z * ZELEM:(z + 1) * ZELEM].rearrange("(p f) -> p f", p=128), ZEROT[:]))
    rem = (TTOT + 1) * C - nzp * ZELEM
    if rem:
        zero_chunks.append((flat_yp[nzp * ZELEM:].rearrange("(p f) -> p f", p=128),
                            ZEROT[:, 0:rem // 128]))
    zero_chunks.append((x3all[PADROW:PADROW + 1, :], ZEROT[0:1, 0:C]))

    lvs = sbt("lvs", [1, 4 * HD], F32)
    lpr = sbt("lpr", [1, 2 * HD], F32)
    lred = sbt("lred", [1, 4], F32)
    e_lv = rec.dma("sync", "lqk", lvs[:], lqk.rearrange("a b -> (a b)").rearrange("(o f) -> o f", o=1))
    e_p1 = rec.op("vector", lambda v: v.tensor_tensor(lpr[:, 0:HD], lvs[:, 0:HD], lvs[:, HD:2 * HD], OP.mult), after=[e_lv])
    e_p2 = rec.op("vector", lambda v: v.tensor_tensor(lpr[:, HD:2 * HD], lvs[:, 2 * HD:3 * HD], lvs[:, 3 * HD:4 * HD], OP.mult), after=[e_lv, e_p1])
    e_r1 = rec.op("vector", lambda v: v.tensor_reduce(lred[:, 0:1], lpr[:, 0:HD], axis=AX, op=OP.add), after=[e_p2])
    e_r2 = rec.op("vector", lambda v: v.tensor_reduce(lred[:, 1:2], lpr[:, HD:2 * HD], axis=AX, op=OP.add), after=[e_r1])
    e_ex = rec.op("scalar", lambda s: s.activation(lred[:, 0:2], lred[:, 0:2], AF.Exp), after=[e_r2])
    e_l0 = rec.op("vector", lambda v: v.tensor_tensor(LAM[0:1, 0:1], lred[:, 0:1], lred[:, 1:2], OP.subtract), after=[e_ex])
    e_l1 = rec.op("vector", lambda v: v.tensor_scalar(LAM[0:1, 0:1], LAM[0:1, 0:1], LAMBDA_INIT, None, OP.add), after=[e_l0])
    e_lam = rec.op("vector", lambda v: v.tensor_scalar(LAM[0:1, 1:2], LAM[0:1, 0:1], -1.0, None, OP.mult), after=[e_l1])

    gwsb = sbt("gwsb", [16, C], BF16)
    e_gw = rec.dma("gpsimd", "gwl", gwsb[:], gate_w[:])
    with nc.psum_tensor("pgt", [128, 16], BF16) as pgt:
        ev_prev = EV0
        for k in range(16):
            e_t = rec.op("tensor", lambda t, k=k: t.transpose(pgt[:, :], gwsb[:, k * 128:(k + 1) * 128], IDNB[0:16, 0:16]),
                         after=[e_gw, e_c["ib"], ev_prev])
            ev_prev = rec.op("vector", lambda v, k=k: v.tensor_copy(GWT[:, k, :], pgt[:, :]), after=[e_t])
        e_gwt = ev_prev
        rec.emit()

    with (
        nc.sbuf_tensor("xt", [128, 2, C], F32) as xt,
        nc.sbuf_tensor("x1t", [128, 2, C], BF16) as x1t,
        nc.sbuf_tensor("sqd", [128, C], BF16) as sqd,
        nc.sbuf_tensor("st8", [128, 8], F32) as st8,
    ):
        e_wv0 = rec.dma("gpsimd", "c_wv", wv_b[0], wv_t[0], bg=True)
        e_wq0 = rec.dma("gpsimd", "c_wq", wq_b[0], wq_t[0], bg=True)
        e_wk0 = rec.dma("gpsimd", "c_wk", wk_b[0], wk_t[0], bg=True)
        prev_use = [EV0, EV0]
        prev_stat = EV0
        last_w = EV0
        for i in range(16):
            sl = i % 2
            e_x = rec.dma("sync", f"xls{sl}", xt[:, sl, :], xb[i * 128:(i + 1) * 128, :], after=[prev_use[sl]])
            e_ri = _ln_stats(rec, st8, xt[:, sl, :], sqd, e_x, prev_stat, EPS[:, 0:1])
            e_ap = rec.op("vector", lambda v, sl=sl: v.tensor_scalar(x1t[:, sl, :], xt[:, sl, :], st8[:, 2:3], st8[:, 6:7],
                                                                     OP.subtract, OP.mult), after=[e_ri])
            prev_stat = e_ap
            e_w = rec.dma("scalar", f"x1w{sl}", x1s[i * 128:(i + 1) * 128, :], x1t[:, sl, :], after=[e_ap])
            prev_use[sl] = e_w
            last_w = e_w
        e_x1_done = [prev_use[0], prev_use[1]]
        rec.emit()

    e_cast = {}
    bg_chunks = [("wv1", wv_b[1], wv_t[1]), ("wq1", wq_b[1], wq_t[1]),
                 ("wk1", wk_b[1], wk_t[1])]
    for i in range(4):
        bg_chunks.append(("wo", wo_b[i], wo_t[i]))
    for lohi in range(2):
        bg_chunks.append(("swg", swg_b[11 * lohi:11 * (lohi + 1)], swg[11 * lohi:11 * (lohi + 1)]))
        bg_chunks.append(("swu", swu_b[11 * lohi:11 * (lohi + 1)], swu[11 * lohi:11 * (lohi + 1)]))
    for nm, db, sb_ in (("weg", weg_b, weg), ("weu", weu_b, weu), ("wed", wed_b, wed)):
        for ei_ in range(EPC):
            for lohi in range(2):
                sl_ = slice(6 * lohi, 6 + 5 * lohi)
                bg_chunks.append((nm, db[ei_, sl_], sb_[ei_, sl_]))
    for lohi in range(2):
        bg_chunks.append(("swd", swd_b[11 * lohi:11 * (lohi + 1)], swd[11 * lohi:11 * (lohi + 1)]))
    for dst, src_ in zero_chunks:
        bg_chunks.append(("zero", dst, src_, e_zt))

    def bg_pump(k=1):
        for _ in range(k):
            if bg_chunks:
                ch = bg_chunks.pop(0)
                key, dst, src = ch[0], ch[1], ch[2]
                aft = [ch[3]] if len(ch) > 3 else ()
                e_cast[key] = rec.dma("gpsimd", "cast_" + key, dst, src, bg=True, after=aft)

    with nc.sbuf_tensor("YT", [128, 8, T], BF16) as YT:
        for p in range(2):
            with (
                nc.sbuf_tensor(f"QT{p}", [128, 4, T], BF16) as QT,
                nc.sbuf_tensor(f"KT{p}", [128, 4, T], BF16) as KT,
                nc.sbuf_tensor(f"VS{p}", [128, 16, 512], BF16) as VS,
                nc.sbuf_tensor(f"WTS{p}", [128, 16, 512], BF16) as WTS,
            ):
                with nc.sbuf_tensor(f"X1T{p}", [128, 4, 16, 512], BF16) as X1T:
                    rec.stream("x1T")
                    e_x1T_l = []
                    prevg = e_x1_done
                    for g4 in range(4):
                        ev = rec.op("gpsimd", lambda g, g4=g4: g.dma_gather(
                            X1T[:, g4, :, :], x1s[:], IDQS[g4][:], num_idxs=512, num_idxs_reg=512,
                            elem_size=C, transpose=True),
                            after=list(prevg) + [e_c[f"idq{g4}"]] if g4 == 0 else [prevg, e_c[f"idq{g4}"]],
                            sem="x1T", n=16)
                        e_x1T_l.append(ev)
                        prevg = ev
                    e_x1T = [e_x1T_l[-1]] * 16
                    e_wts_1 = rec.dma("scalar", "wts", WTS[:, :, :], wv_b[p],
                                      after=[e_wv0 if p == 0 else e_cast["wv1"]])
                    e_wts = [e_wts_1] * 16
                    e_vs = {}
                    with nc.psum_tensor(f"pv{p}", [128, 8, 512], F32) as pv:
                        for tb in range(16):
                            j = tb % 8
                            last = None
                            for cc in range(16):
                                aft = [e_x1T[cc], e_wts[cc]]
                                if cc == 0 and tb >= 8:
                                    aft.append(e_vs[tb - 8])
                                last = rec.op("tensor", lambda t, j=j, tb=tb, cc=cc: t.matmul(
                                    pv[:, j, :], X1T[:, tb // 4, cc, (tb % 4) * 128:(tb % 4 + 1) * 128], WTS[:, cc, :],
                                    start=(cc == 0), stop=(cc == 15)), after=aft)
                            e_vs[tb] = rec.op("vector", lambda v, j=j, tb=tb: v.tensor_copy(VS[:, tb, :], pv[:, j, :]),
                                              after=[last])
                    e_vs_all = [e_vs[tb] for tb in range(16)]
                    e_vs_last = rec.now("DV")
                    bg_pump()

                    def qk_proj(dst, w_b, e_w_ready, e_free, nm, init_ev):
                        e_w_1 = rec.dma("scalar", "wts", WTS[:, :, :], w_b[p], after=[e_w_ready, e_free])
                        e_w = [e_w_1] * 16
                        evs = []
                        with nc.psum_tensor(nm, [128, 2, 512], F32) as pqk:
                            prev = [init_ev, init_ev]
                            for cc in range(4):
                                for tc in range(4):
                                    sl = (cc * 4 + tc) % 2
                                    last = None
                                    for k in range(16):
                                        aft = [e_w[k], e_x1T[k]]
                                        if k == 0:
                                            aft.append(prev[sl])
                                        last = rec.op("tensor", lambda t, cc=cc, tc=tc, k=k, sl=sl: t.matmul(
                                            pqk[:, sl, :], WTS[:, k, cc * 128:(cc + 1) * 128],
                                            X1T[:, tc, k, :],
                                            start=(k == 0), stop=(k == 15)), after=aft)
                                    prev[sl] = rec.op("vector", lambda v, cc=cc, tc=tc, sl=sl: v.tensor_copy(
                                        dst[:, cc, tc * 512:(tc + 1) * 512], pqk[:, sl, :]), after=[last])
                                    evs.append(prev[sl])
                        return evs

                    e_q_evs = qk_proj(QT, wq_b, e_wq0 if p == 0 else e_cast["wq1"],
                                      rec.now("PE"), f"pqkq{p}", e_vs[15])
                    bg_pump()
                    e_k_evs = qk_proj(KT, wk_b, e_wk0 if p == 0 else e_cast["wk1"],
                                      rec.now("PE"), f"pqkk{p}", e_q_evs[-1])
                    bg_pump()
                rec.emit()

                rope_ev = {}
                with (
                    nc.sbuf_tensor(f"sqr{p}", [128, 2, T], BF16) as sqr,
                    nc.sbuf_tensor(f"rr{p}", [1, 2, 2 * T], F32) as rr,
                    nc.sbuf_tensor(f"rrb{p}", [1, 2, T], BF16) as rrb,
                    nc.sbuf_tensor(f"rb{p}", [128, 2, T], BF16) as rb,
                    nc.sbuf_tensor(f"TMPA{p}", [128, 2, T], BF16) as TMPA,
                    nc.sbuf_tensor(f"TMPB{p}", [128, 2, T], BF16) as TMPB,
                    nc.psum_tensor(f"pr{p}", [1, 2, 4, 512], F32) as pr,
                ):
                    prev_rope_p = [e_k_evs[-1], e_k_evs[-1]]
                    prev_rb_rd_p = [EV0, EV0]
                    for it, (mi, ch) in enumerate([(m, c) for m in range(2) for c in range(4)]):
                        src, src_evs = ((QT, e_q_evs), (KT, e_k_evs))[mi]
                        pa = it % 2
                        prev_rope = prev_rope_p[pa]
                        prev_rb_rd = prev_rb_rd_p[pa]
                        if True:
                            tile = src[:, ch, :]
                            lh = ch // 2
                            e_sq = rec.op("scalar", lambda s, tile=tile, pa=pa: s.activation(sqr[:, pa, :], tile, AF.Square),
                                          after=src_evs + [prev_rope])
                            e_ms = []
                            for q4 in range(4):
                                e_ms.append(rec.op("tensor", lambda t, q4=q4, pa=pa: t.matmul(
                                    pr[:, pa, q4, :], ONESB[:, :], sqr[:, pa, q4 * 512:(q4 + 1) * 512],
                                    start=True, stop=True), after=[e_sq, e_c["ob"], prev_rope]))
                            e_rs = rec.op("scalar", lambda s, pa=pa: s.activation(rr[:, pa, 0:T], pr[:, pa, :, :], AF.Sqrt,
                                                                           scale=1.0 / 128.0, bias=EPS[0:1, 1:2]), after=e_ms)
                            e_rc = rec.op("vector", lambda v, pa=pa: v.reciprocal(rr[:, pa, T:2 * T], rr[:, pa, 0:T]), after=[e_rs])
                            e_rb = rec.op("vector", lambda v, pa=pa: v.tensor_copy(rrb[:, pa, :], rr[:, pa, T:2 * T]), after=[e_rc])
                            e_bc = rec.op("gpsimd", lambda g, pa=pa: g.partition_broadcast(rb[:, pa, :], rrb[:, pa, :], channels=128),
                                          after=[e_rb, prev_rb_rd])
                            e_sw1 = rec.dma("sync", f"swp{pa}", TMPB[0:64, pa, :], tile[64:128, :], after=src_evs + [prev_rope])
                            e_sw2 = rec.dma("sync", f"swp{pa}", TMPB[64:128, pa, :], tile[0:64, :], after=[e_sw1])
                            e_t1 = rec.op("vector", lambda v, tile=tile, lh=lh, pa=pa: v.tensor_scalar(
                                TMPA[:, pa, :], tile, ROTS[:, lh:lh + 1], None, OP.mult),
                                after=[e_c["rot"], e_sw2, prev_rope])
                            e_t2 = rec.op("vector", lambda v, tile=tile, lh=lh, pa=pa: v.scalar_tensor_tensor(
                                tile, TMPB[:, pa, :], ROTS[:, 4 + lh:5 + lh], TMPA[:, pa, :], OP.mult, OP.add),
                                after=[e_t1, e_sw2])
                            e_nb = rec.op("vector", lambda v, tile=tile, pa=pa: v.tensor_tensor(tile, tile, rb[:, pa, :], OP.mult),
                                          after=[e_t2, e_bc])
                            prev_rb_rd_p[pa] = e_nb
                            rope_ev[(mi, ch)] = e_nb
                            prev_rope_p[pa] = e_nb
                    e_rope_all = prev_rope_p[1]
                    e_rope_all2 = prev_rope_p[0]
                    bg_pump(2)
                    rec.emit()

                with (
                    nc.sbuf_tensor(f"PT{p}", [128, 2, 2, 512], BF16) as PT,
                    nc.sbuf_tensor(f"zrow{p}", [1, 2, 512], F32) as zrow,
                    nc.sbuf_tensor(f"zbf{p}", [128, 2, 512], F32) as zbf,
                    nc.sbuf_tensor(f"csc{p}", [128, 2, 512], F32) as csc,
                    nc.psum_tensor(f"pO{p}", [128, 4, 512], F32) as pO,
                    nc.psum_tensor(f"pS{p}", [128, 2, 512], F32) as pS,
                    nc.psum_tensor(f"pZ{p}", [1, 2, 512], F32) as pZ,
                ):
                    prev_exp = e_rope_all
                    prev_zread = e_rope_all
                    prev_zrow_rd = e_rope_all
                    prev_cmb = [e_rope_all, e_rope_all2, e_rope_all, e_rope_all2]
                    prev_zo = [[e_rope_all], [e_rope_all2]]
                    for lh in range(2):
                        for qc in range(4):
                            nkb = (qc + 1) * 4
                            e_zl = None
                            e_ol = {}
                            qk_ev = {j: [rope_ev[(0, 2 * lh + j)], rope_ev[(1, 2 * lh + j)]] for j in range(2)}
                            e_s = {}
                            for j in range(2):
                                e_s[j] = rec.op("tensor", lambda t, lh=lh, j=j, qc=qc: t.matmul(
                                    pS[:, j, :], KT[:, 2 * lh + j, 0:128],
                                    QT[:, 2 * lh + j, qc * 512:(qc + 1) * 512],
                                    start=True, stop=True),
                                    after=qk_ev[j] + [prev_exp])
                            for kb in range(nkb):
                                par = kb % 2
                                e_e = rec.op("scalar", lambda s, par=par: s.activation(
                                    PT[:, par, :, :], pS[:, :, :], AF.Exp, scale=ISQ),
                                    after=[e_s[0], e_s[1]] + prev_zo[par])
                                prev_exp = e_e
                                if kb + 1 < nkb:
                                    for j in range(2):
                                        e_s[j] = rec.op("tensor", lambda t, lh=lh, j=j, kb=kb, qc=qc: t.matmul(
                                            pS[:, j, :], KT[:, 2 * lh + j, (kb + 1) * 128:(kb + 2) * 128],
                                            QT[:, 2 * lh + j, qc * 512:(qc + 1) * 512],
                                            start=True, stop=True),
                                            after=qk_ev[j] + [e_e])
                                evs_j = [e_e, e_e]
                                delta = kb * 128 - qc * 512
                                if 0 <= delta <= 384:
                                    for j in range(2):
                                        evs_j[j] = rec.op("vector", lambda v, j=j, par=par, delta=delta: v.tensor_tensor(
                                            PT[:, par, j, :], PT[:, par, j, :], STAIR[:, 384 - delta:896 - delta], OP.mult),
                                            after=[evs_j[j], e_c["st"]])
                                zo_evs = []
                                for j in range(2):
                                    e_z = rec.op("tensor", lambda t, j=j, par=par, kb=kb, nkb=nkb: t.matmul(
                                        pZ[:, j, :], ONESB[:, :], PT[:, par, j, :],
                                        start=(kb == 0), stop=(kb == nkb - 1)),
                                        after=[evs_j[j]] + ([prev_zread] if kb == 0 else []))
                                    e_zl = e_z
                                    zo_evs.append(e_z)
                                    for d2 in range(2):
                                        aft = [evs_j[j], e_vs_last]
                                        if kb == 0:
                                            aft.append(prev_cmb[2 * j + d2])
                                        e_ol[2 * j + d2] = rec.op("tensor", lambda t, lh=lh, j=j, d2=d2, par=par, kb=kb, nkb=nkb: t.matmul(
                                            pO[:, 2 * j + d2, :], VS[:, kb, 256 * lh + 128 * d2:256 * lh + 128 * (d2 + 1)],
                                            PT[:, par, j, :], start=(kb == 0), stop=(kb == nkb - 1)), after=aft)
                                        zo_evs.append(e_ol[2 * j + d2])
                                prev_zo[par] = zo_evs
                            e_zc = rec.op("vector", lambda v: v.tensor_copy(zrow[:, :, :], pZ[:, :, :]),
                                          after=[e_zl, prev_zrow_rd])
                            prev_zread = e_zc
                            e_zr = rec.op("vector", lambda v: v.reciprocal(zrow[:, :, :], zrow[:, :, :]), after=[e_zc])
                            e_zs = rec.op("vector", lambda v: v.tensor_scalar(zrow[:, 1, :], zrow[:, 1, :], LAM[0:1, 1:2], None, OP.mult),
                                          after=[e_zr, e_lam])
                            e_zb = rec.op("gpsimd", lambda g: g.partition_broadcast(zbf[:, :, :], zrow[:, :, :], channels=128),
                                          after=[e_zs])
                            prev_zrow_rd = e_zb
                            prev_y = e_zb
                            for d2 in range(2):
                                e_c1 = rec.op("vector", lambda v, d2=d2: v.tensor_tensor(csc[:, 0, :], pO[:, d2, :], zbf[:, 0, :], OP.mult),
                                              after=[e_zb, e_ol[d2], prev_cmb[d2], prev_y])
                                e_c2 = rec.op("vector", lambda v, d2=d2: v.tensor_tensor(csc[:, 1, :], pO[:, 2 + d2, :], zbf[:, 1, :], OP.mult),
                                              after=[e_c1, e_ol[2 + d2]])
                                ych = 4 * p + 2 * lh + d2
                                e_y = rec.op("vector", lambda v, ych=ych, qc=qc: v.tensor_tensor(
                                    YT[:, ych, qc * 512:(qc + 1) * 512], csc[:, 0, :], csc[:, 1, :], OP.add),
                                    after=[e_c2])
                                prev_cmb[d2] = e_y
                                prev_cmb[2 + d2] = e_y
                                prev_y = e_y
                            bg_pump()
                    rec.emit()
        e_yt_done = rec.now("DV")
        bg_pump(len(bg_chunks))
        e_wo, e_swg, e_swu = e_cast["wo"], e_cast["swg"], e_cast["swu"]
        e_weg, e_weu, e_wed, e_swd = e_cast["weg"], e_cast["weu"], e_cast["wed"], e_cast["swd"]

        rec.stream("yex")

        def emit_send(g):
            for c in range(NCORE):
                with g.If_eq(rank_reg[0], c):
                    off = 1024 if c % 2 == 0 else 0
                    g.dma_start(out=yhalf[:], in_=YT[:, :, off:off + 1024]).then_inc(rec.sems["yex"][0], 16)
                g.end_ifs()
        rec.raw("gpsimd", emit_send, after=[e_yt_done])
        e_send = rec.bump("yex", 16)

        e_ag_y = rec.op("gpsimd", lambda g: g.collective_compute(
            "AllGather", OP.bypass, ins=[yhalf[:]], outs=[ypair[:]], replica_groups=PAIRS),
            after=[e_send], sem="CO", n=1)

        e_fsh_dep = []
        with (
            nc.sbuf_tensor("YF", [128, 16, 1024], BF16) as YF,
            nc.sbuf_tensor("wt5", [128, 2, 16, 512], BF16) as wt5,
            nc.sbuf_tensor("a_sb", [128, 8, C], BF16) as a_sb,
        ):
            rec.stream("yasm")

            def emit_asm(g):
                for c in range(NCORE):
                    with g.If_eq(rank_reg[0], c):
                        if c % 2 == 0:
                            g.dma_start(out=YF[:, 0:8, :], in_=YT[:, :, 0:1024]).then_inc(rec.sems["yasm"][0], 16)
                            g.dma_start(out=YF[:, 8:16, :], in_=ypair[1]).then_inc(rec.sems["yasm"][0], 16)
                        else:
                            g.dma_start(out=YF[:, 0:8, :], in_=ypair[0]).then_inc(rec.sems["yasm"][0], 16)
                            g.dma_start(out=YF[:, 8:16, :], in_=YT[:, :, 1024:2048]).then_inc(rec.sems["yasm"][0], 16)
                    g.end_ifs()
            rec.raw("gpsimd", emit_asm, after=[e_ag_y, e_yt_done])
            e_yf = rec.bump("yasm", 32)

            with nc.psum_tensor("pa", [128, 8, 512], F32) as pa:
                rec.stream("wo")
                prev_ev = {tb: EV0 for tb in range(8)}
                slot_last_mm = [EV0, EV0]
                prev_lw = EV0
                for coq in range(4):
                    sl = coq % 2
                    e_lw = rec.dma("sync", f"wo{sl}", wt5[:, sl, :, :], wo_b[coq],
                                   after=[e_wo, slot_last_mm[sl]])
                    prev_lw = e_lw
                    last_mm = None
                    for cch in range(16):
                        for tb in range(8):
                            aft = [e_lw, e_yf]
                            if cch == 0:
                                aft.append(prev_ev[tb])
                            last_mm = rec.op("tensor", lambda t, tb=tb, cch=cch, sl=sl: t.matmul(
                                pa[:, tb, :], YF[:, cch, tb * 128:(tb + 1) * 128], wt5[:, sl, cch, :],
                                start=(cch == 0), stop=(cch == 15)), after=aft)
                    slot_last_mm[sl] = last_mm
                    pe_now = rec.now("PE")
                    for tb in range(8):
                        prev_ev[tb] = rec.op("scalar", lambda s, tb=tb, coq=coq: s.activation(
                            a_sb[:, tb, coq * 512:(coq + 1) * 512], pa[:, tb, :], AF.Copy, bias=0.0, scale=0.8),
                            after=[pe_now])
                e_a_done = [prev_ev[tb] for tb in range(8)]
            rec.emit()

            with (
                nc.sbuf_tensor("xo", [128, C], F32) as xo,
                nc.sbuf_tensor("lnb", [128, C], F32) as lnb,
                nc.sbuf_tensor("x2t", [128, C], F32) as x2t,
                nc.sbuf_tensor("x3t", [128, C], BF16) as x3t,
                nc.sbuf_tensor("sqd2", [128, C], BF16) as sqd2,
                nc.sbuf_tensor("st2", [128, 8], F32) as st2,
            ):
                e_x3w = []
                prev = EV0
                prev2 = EV0
                for tb in range(8):
                    atile = a_sb[:, tb, :]
                    e_xo = rec.dma("sync", "xow", xo[:, :], x_own[tb * 128:(tb + 1) * 128, :], after=[prev])
                    e_ri = _ln_stats(rec, st2, atile, sqd2, e_a_done[tb], prev, EPS[:, 0:1])
                    e_ln = rec.op("vector", lambda v, atile=atile: v.tensor_scalar(lnb[:, :], atile, st2[:, 2:3], st2[:, 6:7],
                                                                                   OP.subtract, OP.mult), after=[e_ri])
                    e_x2 = rec.op("vector", lambda v: v.tensor_tensor(x2t[:, :], xo[:, :], lnb[:, :], OP.add), after=[e_ln, e_xo, prev2])
                    e_w2 = rec.dma("scalar", "x2w", x2s[tb * 128:(tb + 1) * 128, :], x2t[:, :], after=[e_x2])
                    e_rib = _ln_stats(rec, st2, x2t[:, :], sqd2, e_x2, e_ln, EPS[:, 0:1])
                    e_x3 = rec.op("vector", lambda v: v.tensor_scalar(x3t[:, :], x2t[:, :], st2[:, 2:3], st2[:, 6:7],
                                                                      OP.subtract, OP.mult), after=[e_rib])
                    e_w3 = rec.dma("scalar", "x3w", x3mine[tb * 128:(tb + 1) * 128, :], x3t[:, :], after=[e_x3])
                    e_x3w.append(e_w3)
                    prev = e_w3
                    prev2 = e_w2
                rec.emit()

    e_ag_x3 = rec.op("gpsimd", lambda g: g.collective_compute(
        "AllGather", OP.bypass, ins=[x3mine[:]],
        outs=[x3all[0:TTOT, :].rearrange("(s t) c -> s t c", s=NCORE)],
        replica_groups=ALL8), after=e_x3w, sem="CO", n=1)

    FSH = ctx.enter_context(nc.sbuf_tensor("FSH", [128, 8, C], BF16))
    idxT0 = ctx.enter_context(nc.sbuf_tensor("idxT0", [128, 64], I16))
    idxT1 = ctx.enter_context(nc.sbuf_tensor("idxT1", [128, 64], I16))
    wbt0 = ctx.enter_context(nc.sbuf_tensor("wbt0", [128, 8], F32))
    wbt1 = ctx.enter_context(nc.sbuf_tensor("wbt1", [128, 8], F32))
    IDXT = [idxT0, idxT1]
    WBT = [wbt0, wbt1]
    route_evs = []

    MT = ctx.enter_context(nc.sbuf_tensor("MT", [128, 22, TOK], BF16))
    if True:
        e_mt = []
        with nc.sbuf_tensor("X3T", [128, 2, 16, 512], BF16) as X3T:
            rec.stream("x3T")
            e_x3a = rec.op("gpsimd", lambda g: g.dma_gather(
                X3T[:, 0, :, :], x3mine[:], IDQS[0][:], num_idxs=512, num_idxs_reg=512,
                elem_size=C, transpose=True), after=[e_x3w[-1], e_c["idq0"]], sem="x3T", n=16)
            e_x3b = rec.op("gpsimd", lambda g: g.dma_gather(
                X3T[:, 1, :, :], x3mine[:], IDQS[1][:], num_idxs=512, num_idxs_reg=512,
                elem_size=C, transpose=True), after=[e_x3a, e_c["idq1"]], sem="x3T", n=16)
            e_x3T = [e_x3b] * 16

            with (
                nc.sbuf_tensor("LTs", [16, TOK], F32) as LTs,
                nc.sbuf_tensor("Ltok", [128, 8, 16], F32) as Ltok,
                nc.sbuf_tensor("gsc", [128, 8, 16], F32) as gsc,
                nc.sbuf_tensor("gst", [128, 10], F32) as gst,
                nc.sbuf_tensor("m8s", [128, 8], F32) as m8s,
                nc.sbuf_tensor("WMs", [16, TOK], F32) as WMs,
                nc.psum_tensor("pL", [16, 2, 512], F32) as pL,
                nc.psum_tensor("pTt", [128, 16], F32) as pTt,
                nc.psum_tensor("pwt", [16, 128], F32) as pwt,
            ):
                e_lcp = []
                for tc2 in range(2):
                    last = None
                    for k in range(16):
                        last = rec.op("tensor", lambda t, tc2=tc2, k=k: t.matmul(
                            pL[:, tc2, :], GWT[:, k, :], X3T[:, tc2, k, :],
                            start=(k == 0), stop=(k == 15)), after=[e_gwt, e_x3T[k]])
                    e_lcp.append(rec.op("vector", lambda v, tc2=tc2: v.tensor_copy(
                        LTs[:, tc2 * 512:(tc2 + 1) * 512], pL[:, tc2, :]), after=[last]))

                prev_t = EV0
                prev_w = EV0
                for tb in range(8):
                    e_tr = rec.op("tensor", lambda t, tb=tb: t.transpose(pTt[:, :], LTs[:, tb * 128:(tb + 1) * 128], IDN32[0:16, 0:16]),
                                  after=e_lcp + [e_c["i32"], prev_t])
                    e_ev = rec.op("vector", lambda v, tb=tb: v.tensor_copy(Ltok[:, tb, :], pTt[:, :]), after=[e_tr])
                    prev_t = e_ev
                    lt = Ltok[:, tb, :]
                    e_mx = rec.op("vector", lambda v, lt=lt: v.tensor_reduce(gst[:, 0:1], lt, axis=AX, op=OP.max), after=[e_ev])
                    e_nm = rec.op("vector", lambda v: v.tensor_scalar(gst[:, 1:2], gst[:, 0:1], -1.0, None, OP.mult), after=[e_mx])
                    e_p = rec.op("scalar", lambda s, lt=lt, tb=tb: s.activation(gsc[:, tb, :], lt, AF.Exp, bias=gst[:, 1:2],
                                                                                accum_out=gst[:, 2:3]), after=[e_nm])
                    e_m8 = rec.op("vector", lambda v, tb=tb: v.max(out=m8s[:, :], in_=gsc[:, tb, :]), after=[e_p])
                    e_m = rec.op("vector", lambda v, tb=tb: v.tensor_scalar(Ltok[:, tb, :], gsc[:, tb, :], m8s[:, 3:4], None, OP.is_ge),
                                 after=[e_m8])
                    e_pm = rec.op("vector", lambda v, tb=tb: v.tensor_tensor(gsc[:, tb, :], gsc[:, tb, :], Ltok[:, tb, :], OP.mult),
                                  after=[e_m])
                    e_s4 = rec.op("vector", lambda v, tb=tb: v.tensor_reduce(gst[:, 3:4], gsc[:, tb, :], axis=AX, op=OP.add), after=[e_pm])
                    e_zz = rec.op("vector", lambda v: v.tensor_scalar(gst[:, 4:5], gst[:, 2:3], 1e-20, None, OP.mult), after=[e_p, e_s4])
                    e_dn = rec.op("vector", lambda v: v.tensor_tensor(gst[:, 5:6], gst[:, 3:4], gst[:, 4:5], OP.add), after=[e_zz])
                    e_di = rec.op("vector", lambda v: v.reciprocal(gst[:, 6:7], gst[:, 5:6]), after=[e_dn])
                    e_w = rec.op("vector", lambda v, tb=tb: v.tensor_scalar(gsc[:, tb, :], gsc[:, tb, :], gst[:, 6:7], None, OP.mult),
                                 after=[e_di])
                    e_tw = rec.op("tensor", lambda t, tb=tb: t.transpose(pwt[:, :], gsc[:, tb, :], IDN32[:, :]),
                                  after=[e_w, prev_w])
                    e_wv2 = rec.op("vector", lambda v, tb=tb: v.tensor_copy(WMs[:, tb * 128:(tb + 1) * 128], pwt[:, :]), after=[e_tw])
                    prev_w = e_wv2
                e_wmd = rec.dma("scalar", "wmw", wm_mine[:], WMs[:], after=[prev_w])
                rec.emit()

            e_ag_wm = rec.op("gpsimd", lambda g: g.collective_compute(
                "AllGather", OP.bypass, ins=[wm_mine[:]], outs=[wm_all[:]], replica_groups=ALL8),
                after=[e_wmd], sem="CO", n=1)

            mrg = contextlib.ExitStack()
            _mt = lambda name, shape, dt: mrg.enter_context(nc.sbuf_tensor(name, shape, dt))
            wst = _mt("wst", [128, 2, 2, 16, 128], BF16)
            gsil = _mt("gsil", [128, 2, TOK], BF16)
            Wfull = _mt("Wfull", [128, TOK], F32)
            mfull = _mt("mfull", [128, TOK], F32)
            ones2 = _mt("ones2", [128, TOK], F32)
            Rx = _mt("Rx", [128, TOK], F32)
            keptf = _mt("keptf", [128, TOK], F32)
            TOKc = _mt("TOKc", [128, TOK], F32)
            CSUM = _mt("CSUM", [128, 128], F32)
            vtok = _mt("vtok", [128, TOK], F32)
            vw = _mt("vw", [128, TOK], F32)
            tot = _mt("tot", [128, 2], F32)
            sent = _mt("sent", [1, 2 * TOK], F32)
            sgi = _mt("sgi", [16, 2, 576], F32)
            sgo = _mt("sgo", [16, 144], F32)
            sgw = _mt("sgw", [16, 144], F32)
            nfo = _mt("nfo", [1, 4], U32)
            pbase = mrg.enter_context(nc.psum_tensor("pbase", [128, 1], F32))
            pm1 = mrg.enter_context(nc.psum_tensor("pm1", [128, 2, 512], F32))
            pm2 = mrg.enter_context(nc.psum_tensor("pm2", [128, 2, 512], F32))
            if True:
                sh_gate_rd = [EV0, EV0]
                sh_up_rd = [EV0, EV0]
                sh_silu = []
                sh_mul = []
                sh_mul_sl = [[], []]

                def emit_shared_ic(ic):
                    nonlocal sh_silu, sh_mul
                    sl = ic % 2
                    e_wg = rec.dma("scalar", f"sstg{sl}", wst[:, 0, sl, :, :], swg_b[ic],
                                   after=[sh_gate_rd[sl], e_swg])
                    lastg = [None, None]
                    for h2 in range(2):
                        for k in range(16):
                            aft = [e_wg, e_x3T[k]]
                            if k == 0:
                                aft += sh_silu
                            lastg[h2] = rec.op("tensor", lambda t, sl=sl, k=k, h2=h2: t.matmul(
                                pm1[:, h2, :], wst[:, 0, sl, k, :], X3T[:, h2, k, :],
                                start=(k == 0), stop=(k == 15)), after=aft)
                    sh_gate_rd[sl] = lastg[1]
                    e_sil = rec.op("scalar", lambda s, sl=sl: s.activation(gsil[:, sl, 0:512], pm1[:, 0, :], AF.Silu),
                                   after=[lastg[0]] + sh_mul_sl[sl])
                    e_sil2 = rec.op("scalar", lambda s, sl=sl: s.activation(gsil[:, sl, 512:1024], pm1[:, 1, :], AF.Silu),
                                    after=[lastg[1], e_sil])
                    sh_silu = [e_sil, e_sil2]
                    e_wu = rec.dma("scalar", f"sstu{sl}", wst[:, 1, sl, :, :], swu_b[ic],
                                   after=[sh_up_rd[sl], e_swu])
                    lastu = [None, None]
                    for h2 in range(2):
                        for k in range(16):
                            aft = [e_wu, e_x3T[k]]
                            if k == 0:
                                aft += sh_mul
                            lastu[h2] = rec.op("tensor", lambda t, sl=sl, k=k, h2=h2: t.matmul(
                                pm2[:, h2, :], wst[:, 1, sl, k, :], X3T[:, h2, k, :],
                                start=(k == 0), stop=(k == 15)), after=aft)
                    sh_up_rd[sl] = lastu[1]
                    e_mul1 = rec.op("vector", lambda v, ic=ic, sl=sl: v.tensor_tensor(
                        MT[:, ic, 0:512], pm2[:, 0, :], gsil[:, sl, 0:512], OP.mult),
                        after=[lastu[0], e_sil])
                    e_mul2 = rec.op("vector", lambda v, ic=ic, sl=sl: v.tensor_tensor(
                        MT[:, ic, 512:1024], pm2[:, 1, :], gsil[:, sl, 512:1024], OP.mult),
                        after=[lastu[1], e_sil2, e_mul1])
                    sh_mul = [e_mul1, e_mul2]
                    sh_mul_sl[sl] = [e_mul1, e_mul2]
                    e_mt.append(e_mul2)

                e_wf_l = [rec.dma("sync", "wmr", Wfull[e8 * 8:(e8 + 1) * 8, :], wm_all[:, e8, :], after=[e_ag_wm]) for e8 in range(E)]
                e_wf = [e_wf_l[-1]] * NCORE
                e_tc = rec.dma("sync", "tokc", TOKc[:, :], cTOK128[:])
                e_cs = rec.dma("sync", "csum", CSUM[:, :], cCSUM[:])
                e_mf = rec.op("vector", lambda v: v.tensor_scalar(mfull[:, :], Wfull[:, :], 0.0, None, OP.is_gt), after=e_wf)
                e_o2 = rec.op("vector", lambda v: v.memset(ones2[:, :], 1.0))
                e_sn0 = rec.op("vector", lambda v: v.memset(sent[:, 0:TOK], float(PADROW)))
                e_sn1 = rec.op("vector", lambda v: v.memset(sent[:, TOK:2 * TOK], 0.0))
                e_sc = rec.op("vector", lambda v: v.tensor_tensor_scan(Rx[:, :], ones2[:, :], mfull[:, :], 0.0, OP.mult, OP.add),
                              after=[e_mf, e_o2])
                e_tt = rec.op("vector", lambda v: v.tensor_copy(tot[:, 0:1], Rx[:, TOK - 1:TOK]), after=[e_sc])

                for ic in range(4):
                    emit_shared_ic(ic)

                e_bm = rec.op("tensor", lambda t: t.matmul(pbase[:, :], CSUM[:, :], tot[:, 0:1], start=True, stop=True),
                              after=[e_tt, e_cs])
                e_bc2 = rec.op("vector", lambda v: v.tensor_copy(tot[:, 1:2], pbase[:, :]), after=[e_bm])
                e_ad = rec.op("vector", lambda v: v.tensor_scalar(Rx[:, :], Rx[:, :], tot[:, 1:2], None, OP.add), after=[e_bc2])
                e_rx = rec.op("vector", lambda v: v.tensor_tensor(Rx[:, :], Rx[:, :], mfull[:, :], OP.subtract), after=[e_ad])
                e_ltc = rec.op("vector", lambda v: v.tensor_scalar(keptf[:, :], Rx[:, :], float(CAP), None, OP.is_lt), after=[e_rx])
                e_kp = rec.op("vector", lambda v: v.tensor_tensor(keptf[:, :], keptf[:, :], mfull[:, :], OP.mult), after=[e_ltc])
                e_vt0 = rec.op("vector", lambda v: v.tensor_tensor(vtok[:, :], keptf[:, :], TOKc[:, :], OP.mult),
                               after=[e_kp, e_tc])
                e_vt = rec.op("vector", lambda v: v.tensor_scalar(vtok[:, :], vtok[:, :], -1.0, None, OP.add), after=[e_vt0])
                e_vwa = rec.op("vector", lambda v: v.tensor_tensor(vw[:, :], keptf[:, :], Wfull[:, :], OP.mult), after=[e_kp])
                e_vwb = rec.op("vector", lambda v: v.tensor_scalar(TOKc[:, :], keptf[:, :], -1.0, None, OP.add), after=[e_vwa, e_vt0])
                e_vw0 = rec.op("vector", lambda v: v.tensor_tensor(vw[:, :], vw[:, :], TOKc[:, :], OP.add), after=[e_vwb])

                e_st0 = rec.dma("gpsimd", "sent", strm[0:1, TTOT:9216], sent[:, 0:TOK], after=[e_sn0])
                e_st1 = rec.dma("gpsimd", "sent", strm[1:2, TTOT:9216], sent[:, TOK:2 * TOK], after=[e_sn1])

                prev_strm = [e_vt, e_vw0, e_st0, e_st1]
                lib_sg = [False]
                for ei in range(EPC):
                    rec.stream("rows")

                    def emit_row_ei(g, ei=ei):
                        for c in range(NCORE):
                            with g.If_eq(rank_reg[0], c):
                                r = 2 * c + ei
                                g.dma_start(
                                    out=strm[0, 0:TTOT].rearrange("(s t) -> s t", t=TOK),
                                    in_=vtok[8 * r:8 * r + 8, :]).then_inc(rec.sems["rows"][0], 16)
                                g.dma_start(
                                    out=strm[1, 0:TTOT].rearrange("(s t) -> s t", t=TOK),
                                    in_=vw[8 * r:8 * r + 8, :]).then_inc(rec.sems["rows"][0], 16)
                            g.end_ifs()
                    rec.raw("gpsimd", emit_row_ei, after=prev_strm)
                    e_rows = rec.bump("rows", 2 * 16)
                    e_wrap0 = rec.dma("gpsimd", "wrap", sgi[:, 0, :], strm[0, :].rearrange("(f p) -> p f", p=16), after=[e_rows], allow_slow_non_contiguous=True)
                    e_wrap1 = rec.dma("gpsimd", "wrap", sgi[:, 1, :], strm[1, :].rearrange("(f p) -> p f", p=16), after=[e_wrap0], allow_slow_non_contiguous=True)
                    if not lib_sg[0]:
                        rec.raw("gpsimd", lambda g: g.load_library(lc.sparse_gather), after=[e_wrap1])
                        lib_sg[0] = True
                    e_sg0 = rec.op("gpsimd", lambda g: g.sparse_gather(sgo[:, :], sgi[:, 0, :], num_found=nfo[0:1, 0:1]),
                                   after=[e_wrap1])
                    e_sg1 = rec.op("gpsimd", lambda g: g.sparse_gather(sgw[:, :], sgi[:, 1, :], num_found=nfo[0:1, 1:2]),
                                   after=[e_sg0])
                    e_cl0 = rec.op("vector", lambda v: v.tensor_scalar(sgo[:, 0:64], sgo[:, 0:64], 0.0, float(PADROW), OP.max, OP.min),
                                   after=[e_sg0])
                    e_i16 = rec.op("vector", lambda v, ei=ei: v.tensor_copy(IDXT[ei][0:16, :], sgo[:, 0:64]), after=[e_cl0])
                    reps = [e_i16]
                    for kk in range(1, 8):
                        reps.append(rec.dma("sync", "irep", IDXT[ei][16 * kk:16 * (kk + 1), :], IDXT[ei][0:16, :], after=[e_i16]))
                    e_wb0 = rec.dma("gpsimd", "wbb", wbb[ei].rearrange("(f p) -> p f", p=16), sgw[:, 0:64], after=[e_sg1], allow_slow_non_contiguous=True)
                    e_wb1 = rec.dma("gpsimd", "wbb", WBT[ei][:, :], wbb[ei].rearrange("(f p) -> p f", p=128), after=[e_wb0], allow_slow_non_contiguous=True)
                    route_evs.append((reps, e_wb1))
                    prev_strm = [e_wrap1, e_sg1, e_cl0, e_i16, e_wb0]
                rec.raw("gpsimd", lambda g: g.drain())
                rec.raw("gpsimd", lambda g: g.load_library(lc.mlp), after=[route_evs[-1][1]])

                for ic in range(4, 22):
                    emit_shared_ic(ic)
                rec.emit()
            mrg.close()


    e_scat = []
    with (
        nc.sbuf_tensor("XGT", [128, 2, 16, 512], BF16) as XGT,
        nc.sbuf_tensor("wge", [128, 2, 2, 16, 128], BF16) as wge,
        nc.sbuf_tensor("MT2", [128, 11, CAP], BF16) as MT2,
        nc.sbuf_tensor("gsil2", [128, 2, CAP], BF16) as gsil2,
        nc.sbuf_tensor("eout", [128, 8, C], BF16) as eout,
        nc.sbuf_tensor("wde", [128, 2, 512], BF16) as wde,
    ):
        prev_scatter = EV0
        prev_xgt_users = EV0
        gate_rd = [EV0, EV0]
        up_rd = [EV0, EV0]
        silu_ev = [[], []]
        mul_ev = [[], []]
        prev_evac = []
        slot_last2 = [EV0, EV0]
        for ei in range(EPC):
            reps, e_wb = route_evs[ei]
            rec.stream("xg")
            e_xga = rec.op("gpsimd", lambda g, ei=ei: g.dma_gather(
                XGT[:, 0, :, :], x3all[:], IDXT[ei][:, 0:32], num_idxs=512, num_idxs_reg=512,
                elem_size=C, transpose=True),
                after=[e_ag_x3, prev_xgt_users, e_cast["zero"]] + reps, sem="xg", n=16)
            e_xg = rec.op("gpsimd", lambda g, ei=ei: g.dma_gather(
                XGT[:, 1, :, :], x3all[:], IDXT[ei][:, 32:64], num_idxs=512, num_idxs_reg=512,
                elem_size=C, transpose=True),
                after=[e_xga], sem="xg", n=16)
            e_mt2 = []
            pgctx = contextlib.ExitStack()
            pg1 = pgctx.enter_context(nc.psum_tensor(f"pg1_{ei}", [128, 2, 2, 512], F32))
            pg2 = pgctx.enter_context(nc.psum_tensor(f"pg2_{ei}", [128, 2, 2, 512], F32))
            for ic in range(11):
                sl = ic % 2
                e_wg2 = rec.dma("sync", f"wge_g{sl}", wge[:, 0, sl, :, :], weg_b[ei, ic],
                                after=[gate_rd[sl], e_weg])
                lastg = [None, None]
                for h2 in range(2):
                    for k in range(16):
                        aft = [e_wg2, e_xg]
                        if k == 0:
                            aft += silu_ev[sl] + prev_evac
                        lastg[h2] = rec.op("tensor", lambda t, k=k, h2=h2, sl=sl: t.matmul(
                            pg1[:, sl, h2, :], wge[:, 0, sl, k, :], XGT[:, h2, k, :],
                            start=(k == 0), stop=(k == 15)), after=aft)
                gate_rd[sl] = lastg[1]
                e_si1 = rec.op("scalar", lambda s, sl=sl: s.activation(gsil2[:, sl, 0:512], pg1[:, sl, 0, :], AF.Silu),
                               after=[lastg[0]] + mul_ev[sl])
                e_si2 = rec.op("scalar", lambda s, sl=sl: s.activation(gsil2[:, sl, 512:1024], pg1[:, sl, 1, :], AF.Silu),
                               after=[lastg[1], e_si1])
                silu_ev[sl] = [e_si1, e_si2]
                e_wu2 = rec.dma("sync", f"wge_u{sl}", wge[:, 1, sl, :, :], weu_b[ei, ic],
                                after=[up_rd[sl], e_weu])
                lastu = [None, None]
                for h2 in range(2):
                    for k in range(16):
                        aft = [e_wu2, e_xg]
                        if k == 0:
                            aft += mul_ev[sl] + prev_evac
                        lastu[h2] = rec.op("tensor", lambda t, k=k, h2=h2, sl=sl: t.matmul(
                            pg2[:, sl, h2, :], wge[:, 1, sl, k, :], XGT[:, h2, k, :],
                            start=(k == 0), stop=(k == 15)), after=aft)
                up_rd[sl] = lastu[1]
                e_m1 = rec.op("vector", lambda v, ic=ic, sl=sl: v.tensor_tensor(MT2[:, ic, 0:512], pg2[:, sl, 0, :], gsil2[:, sl, 0:512], OP.mult),
                              after=[lastu[0], e_si1])
                e_m2 = rec.op("vector", lambda v, ic=ic, sl=sl: v.tensor_tensor(MT2[:, ic, 512:1024], pg2[:, sl, 1, :], gsil2[:, sl, 512:1024], OP.mult),
                              after=[lastu[1], e_si2, e_m1])
                mul_ev[sl] = [e_m1, e_m2]
                e_mt2.append(e_m2)
            prev_xgt_users = e_mt2[-1]
            pgctx.close()

            pectx = contextlib.ExitStack()
            pe8 = pectx.enter_context(nc.psum_tensor(f"pe8_{ei}", [128, 8, 512], F32))
            prev_eo = {sb: e_mt2[-1] for sb in range(8)}
            for coq in range(4):
                for ic in range(11):
                    sl = ic % 2
                    e_ldd = rec.dma("sync", f"wde{sl}", wde[:, sl, :], wed_b[ei, ic, coq],
                                    after=[slot_last2[sl], e_wed])
                    last_mm = None
                    for sb in range(8):
                        aft = [e_ldd, e_mt2[ic]]
                        if ic == 0:
                            aft.append(prev_eo[sb])
                        last_mm = rec.op("tensor", lambda t, sb=sb, ic=ic, sl=sl: t.matmul(
                            pe8[:, sb, :], MT2[:, ic, sb * 128:(sb + 1) * 128], wde[:, sl, :],
                            start=(ic == 0), stop=(ic == 10)), after=aft)
                    slot_last2[sl] = last_mm
                pe_now = rec.now("PE")
                for sb in range(8):
                    prev_eo[sb] = rec.op("scalar", lambda s, sb=sb, coq=coq, ei=ei: s.activation(
                        eout[:, sb, coq * 512:(coq + 1) * 512], pe8[:, sb, :], AF.Copy, bias=0.0,
                        scale=WBT[ei][:, sb:sb + 1]), after=[pe_now, e_wb, prev_scatter])
            e_eo = [prev_eo[sb] for sb in range(8)]
            prev_evac = e_eo
            rec.stream("scat")
            e_sc2 = rec.op("gpsimd", lambda g, ei=ei: g.dma_scatter_add(
                yp[:], eout[:, :, :], IDXT[ei][:, :], num_idxs=CAP, num_idxs_reg=CAP, elem_size=C),
                after=e_eo + [e_cast["zero"], prev_scatter] + reps, sem="scat", n=16)
            prev_scatter = e_sc2
            e_scat.append(e_sc2)
            pectx.close()
        rec.emit()

    e_rs = rec.op("gpsimd", lambda g: g.collective_compute(
        "ReduceScatter", OP.add, ins=[yp[0:TTOT, :]], outs=[y_rs[:]], replica_groups=ALL8),
        after=e_scat, sem="CO", n=1)

    with (
        nc.sbuf_tensor("wds", [128, 2, 512], BF16) as wds,
        nc.psum_tensor("pf", [128, 8, 512], F32) as pf,
    ):
        prev_f = {tb: EV0 for tb in range(8)}
        slot_last = [EV0, EV0]
        for coq in range(4):
            for ic in range(22):
                sl = ic % 2
                e_ld = rec.dma("sync", f"swd{sl}", wds[:, sl, :], swd_b[ic, coq],
                               after=[slot_last[sl], e_swd])
                last_mm = None
                for tb in range(8):
                    aft = [e_ld, e_mt[ic]]
                    if ic == 0:
                        aft.append(prev_f[tb])
                    last_mm = rec.op("tensor", lambda t, tb=tb, ic=ic, sl=sl, coq=coq: t.matmul(
                        pf[:, tb, :], MT[:, ic, tb * 128:(tb + 1) * 128], wds[:, sl, :],
                        start=(ic == 0), stop=(ic == 21)), after=aft)
                slot_last[sl] = last_mm
            pe_now = rec.now("PE")
            for tb in range(8):
                prev_f[tb] = rec.op("scalar", lambda s, tb=tb, coq=coq: s.activation(
                    FSH[:, tb, coq * 512:(coq + 1) * 512], pf[:, tb, :], AF.Copy, bias=0.0, scale=1.0),
                    after=[pe_now])
        e_fsh = [prev_f[tb] for tb in range(8)]
        rec.emit()


    with (
        nc.sbuf_tensor("ym", [128, C], BF16) as ym,
        nc.sbuf_tensor("ff", [128, C], F32) as ff,
        nc.sbuf_tensor("x2r", [128, C], F32) as x2r,
        nc.sbuf_tensor("oo", [128, C], F32) as oo,
        nc.sbuf_tensor("sqd3", [128, C], BF16) as sqd3,
        nc.sbuf_tensor("st3", [128, 8], F32) as st3,
    ):
        e_outs = []
        prev = EV0
        for tb in range(8):
            e_ym = rec.dma("sync", "ymr", ym[:, :], y_rs[tb * 128:(tb + 1) * 128, :], after=[e_rs, prev])
            e_x2r = rec.dma("sync", "x2r", x2r[:, :], x2s[tb * 128:(tb + 1) * 128, :], after=[prev, ("x2w", 128)])
            e_ff = rec.op("vector", lambda v, tb=tb: v.tensor_tensor(ff[:, :], ym[:, :], FSH[:, tb, :], OP.add),
                          after=[e_ym, e_fsh[tb], prev])
            e_ri = _ln_stats(rec, st3, ff[:, :], sqd3, e_ff, prev, EPS[:, 0:1])
            e_ln = rec.op("vector", lambda v: v.tensor_scalar(ff[:, :], ff[:, :], st3[:, 2:3], st3[:, 6:7],
                                                              OP.subtract, OP.mult), after=[e_ri])
            e_o = rec.op("vector", lambda v: v.tensor_tensor(oo[:, :], x2r[:, :], ff[:, :], OP.add), after=[e_ln, e_x2r])
            e_w = rec.dma("sync", "outw", out_shard[tb * 128:(tb + 1) * 128, :], oo[:, :], after=[e_o])
            e_outs.append(e_w)
            prev = e_w
        rec.raw("sync", lambda s: None, after=e_outs)
        rec.raw("gpsimd", lambda g: None, after=e_outs)
        rec.emit()

    ctx.close()
    return nc


def kernel(**inputs):
    x = np.ascontiguousarray(inputs["L0_attn_x"])
    wq = np.ascontiguousarray(inputs["L0_attn_wq"])
    wk = np.ascontiguousarray(inputs["L0_attn_wk"])
    wv = np.ascontiguousarray(inputs["L0_attn_wv"])
    wo = np.ascontiguousarray(inputs["L0_attn_wo"])
    lqk = np.ascontiguousarray(np.stack([inputs["L0_attn_lq1"], inputs["L0_attn_lk1"],
                                         inputs["L0_attn_lq2"], inputs["L0_attn_lk2"]], axis=0))
    gate = np.ascontiguousarray(inputs["L0_ffn_gate_w"])
    wg = np.ascontiguousarray(inputs["L0_ffn_w_gate"])
    wu = np.ascontiguousarray(inputs["L0_ffn_w_up"])
    wd = np.ascontiguousarray(inputs["L0_ffn_w_down"])
    swg_i = np.ascontiguousarray(inputs["L0_ffn_sw_gate"])
    swu_i = np.ascontiguousarray(inputs["L0_ffn_sw_up"])
    swd_i = np.ascontiguousarray(inputs["L0_ffn_sw_down"])

    nc = build_nc()
    lower_extended_insts(nc)

    in_maps = []
    for c in range(NCORE):
        b, half = c // 2, c % 2
        hr = slice(1024 * half, 1024 * (half + 1))
        inv = (1.0 / 10000.0) ** (np.arange(0, HH, 2, dtype=np.float64) / HH)
        rot = np.zeros((128, 8), np.float32)
        for lh in range(4):
            h = 4 * half + lh
            th = h * inv
            rot[0:64, lh] = np.cos(th)
            rot[64:128, lh] = np.cos(th)
            rot[0:64, 4 + lh] = np.sin(th)
            rot[64:128, 4 + lh] = -np.sin(th)
        def tile_w(m):
            n = m.shape[0] // 512
            return np.ascontiguousarray(m.reshape(n, 512, 16, 128).transpose(0, 3, 2, 1))

        wg2 = wg[2 * c:2 * c + 2]
        wu2 = wu[2 * c:2 * c + 2]
        wd2 = wd[2 * c:2 * c + 2]
        in_maps.append({
            "xb": x[b],
            "rotpq": rot,
            "x_own": np.ascontiguousarray(x[b][1024 * half:1024 * (half + 1)]),
            "wq_t": tile_w(wq[hr]),
            "wk_t": tile_w(wk[hr]),
            "wv_t": tile_w(wv[hr]),
            "wo_t": tile_w(wo),
            "lqk": lqk,
            "gate_w": gate,
            "weg": np.ascontiguousarray(wg2.reshape(2, 16, 128, 11, 128).transpose(0, 3, 2, 1, 4)),
            "weu": np.ascontiguousarray(wu2.reshape(2, 16, 128, 11, 128).transpose(0, 3, 2, 1, 4)),
            "wed": np.ascontiguousarray(wd2.reshape(2, 11, 128, 4, 512).transpose(0, 1, 3, 2, 4)),
            "swg": np.ascontiguousarray(swg_i.reshape(16, 128, 22, 128).transpose(2, 1, 0, 3)),
            "swu": np.ascontiguousarray(swu_i.reshape(16, 128, 22, 128).transpose(2, 1, 0, 3)),
            "swd": np.ascontiguousarray(swd_i.reshape(22, 128, 4, 512).transpose(0, 2, 1, 3)),
        })
    res = run_bass_kernel_spmd(nc, in_maps, core_ids=list(range(NCORE)))
    shards = [res.results[c]["out_shard"] for c in range(NCORE)]
    out = np.concatenate(shards, axis=0).reshape(B, T, C)
    return out
